# version 08
# baseline (speedup 1.0000x reference)
import os

import numpy as np
import ml_dtypes
from contextlib import ExitStack

import concourse.bass as bass
import concourse.mybir as mybir
from concourse.bass_isa import ReduceOp
from concourse.ap import AP
from concourse.bass_utils import run_bass_kernel_spmd

F32 = mybir.dt.float32
BF16 = mybir.dt.bfloat16
AF = mybir.ActivationFunctionType
ALU = mybir.AluOpType
AXX = mybir.AxisListType.X

D = 1024
SEQ = 4096
NBATCH = 4
NCORES = 8
HID = 4096
NT = 32
OWN_NT = 16
OWN = 2048
SLABW = 130
SLABT = OWN_NT * SLABW
NFC = 8
NHC = 32
MRND = 12582912.0
NEGBIG = -1.0e30
EPS_LN = 1e-5

ENGINES = ("sync", "gpsimd", "vector", "scalar", "tensor")

STRICT_SYNC = os.environ.get("BASS_SIM_FP64") == "1"

HAT_LIMIT = None
HAT_ECS = None
LN1_STOP = None
HAT_STOP = None

I_CHUNKS = [(0, 512), (512, 512), (1024, 512), (1536, 512), (2048, 32)]
T_CHUNKS = [(0, 512), (512, 512), (1024, 512), (1536, 512)]


def _true_tiles(par):
    return [2 * k + par for k in range(OWN_NT)]


def _gtile(jt):
    return (jt % 2) * OWN_NT + jt // 2



def prep_core_inputs(I):
    x0 = np.ascontiguousarray(np.asarray(I["L0_attn_x"], dtype=np.float32))
    f32 = lambda a: np.ascontiguousarray(np.asarray(a, dtype=np.float32))
    T = lambda a: np.ascontiguousarray(np.asarray(a, dtype=np.float32).T)
    col = lambda a: np.ascontiguousarray(np.asarray(a, dtype=np.float32).reshape(-1, 1))

    shared = {
        "WqT0": T(I["L0_attn_Wq"]), "WkT0": T(I["L0_attn_Wk"]),
        "WvT0": T(I["L0_attn_Wv"]), "WoT0": T(I["L0_attn_Wo"]),
        "bo0": col(I["L0_attn_bo"]),
        "qs_col": col(np.asarray(I["L0_attn_q_scale"]).reshape(-1)),
        "ks_col": col(np.asarray(I["L0_attn_k_scale"]).reshape(-1)),
        "qs16": f32(np.asarray(I["L0_attn_q_scale"]).reshape(16, 64)),
        "ks16": f32(np.asarray(I["L0_attn_k_scale"]).reshape(16, 64)),
        "w1T": T(I["L0_ffn_w1"]), "w2T": T(I["L0_ffn_w2"]),
        "convw": f32(np.asarray(I["L0_ffn_conv_w"]).reshape(HID, 3)),
        "convb": col(I["L0_ffn_conv_b"]),
        "Wq1": f32(I["L1_attn_Wq"]), "Wk1": f32(I["L1_attn_Wk"]), "Wv1": f32(I["L1_attn_Wv"]),
        "bq1": col(I["L1_attn_bq"]), "bk1": col(I["L1_attn_bk"]), "bv1": col(I["L1_attn_bv"]),
        "wgT": T(I["L1_ffn_w_gate"]), "wuT": T(I["L1_ffn_w_up"]), "wdT": T(I["L1_ffn_w_down"]),
    }

    maps = []
    for c in range(NCORES):
        b, par = c // 2, c % 2
        xT = np.ascontiguousarray(x0[b].T)
        slab = np.zeros((D, SLABT), np.float32)
        for k, t in enumerate(_true_tiles(par)):
            lo, hi = 128 * t - 1, 128 * t + 129
            s0, s1 = max(lo, 0), min(hi, SEQ)
            slab[:, SLABW * k + (s0 - lo): SLABW * k + (s1 - lo)] = xT[:, s0:s1]
        amask = np.ones(SLABT, np.float32)
        if par == 0:
            amask[0] = 0.0
        else:
            amask[SLABT - 1] = 0.0
        ampad = np.zeros(17 * 128, np.float32)
        ampad[:SLABT] = amask
        amask_tm = np.ascontiguousarray(ampad.reshape(17, 128).T)
        stair = np.full((8, 128, 512), NEGBIG, np.float32)
        jj = np.arange(128)[:, None]
        ii = np.arange(512)[None, :]
        itrue = (2 * (ii // 128) + par) * 128 + (ii % 128)
        for s in range(8):
            stair[s][(s * 128 + jj) <= itrue] = 0.0
        m = dict(shared)
        m["x0T_full"] = xT
        m["x0T_slab"] = slab
        m["amask_tm"] = amask_tm
        m["stair"] = np.ascontiguousarray(stair.reshape(8 * 128, 512))
        maps.append(m)
    return maps


def assemble_output(results):
    out = np.empty((NBATCH, SEQ, D), np.float32)
    for c in range(NCORES):
        b, par = c // 2, c % 2
        sh = results[c]["out_shard"]
        for k, t in enumerate(_true_tiles(par)):
            out[b, 128 * t:128 * (t + 1), :] = sh[:, 128 * k:128 * (k + 1)].T
    return out



class Prog:
    def __init__(self, nc, ctx):
        self.nc = nc
        self.q = {e: [] for e in ENGINES}
        self.cnt = {}
        self.sem = {}
        self._last_wait = {e: {} for e in ENGINES}
        self.ctx = ctx

    def sem_new(self, *names):
        for n in names:
            self.sem[n] = self.ctx.enter_context(self.nc.semaphore(n))
            self.cnt[n] = 0

    def tok(self, sem):
        return (sem, self.cnt[sem])

    ENG_SEM = {"vector": "v", "scalar": "sc", "tensor": "mm", "gpsimd": "g",
               "sync": "sy"}

    @staticmethod
    def _flat(waits):
        out = []
        for wt in waits:
            if wt is None:
                continue
            if isinstance(wt[0], str):
                out.append(wt)
            else:
                out.extend(w for w in wt if w is not None)
        return out

    def all_toks(self):
        return [(n, self.cnt[n]) for n in self.cnt]

    def emit(self, eng, fn, waits=(), inc=None, inc_k=1, stream=None, stream_k=16):
        waits = self._flat(waits)
        auto = self.ENG_SEM[eng]
        if stream is not None:
            sname = "q_" + stream
            if sname not in self.sem:
                self.sem_new(sname)
            waits = waits + [(sname, self.cnt[sname])]
            if STRICT_SYNC or eng != "tensor":
                waits = waits + [(auto, self.cnt[auto])]
            inc, inc_k = sname, stream_k
        else:
            if STRICT_SYNC or eng != "tensor":
                waits = waits + [(auto, self.cnt[auto])]
            if inc is None:
                inc = auto
                inc_k = 1
        tok = None
        if inc is not None:
            self.cnt[inc] += inc_k
            tok = (inc, self.cnt[inc])
        lw = self._last_wait[eng]
        eff = []
        for wt in waits:
            s, v = wt
            if v <= 0 or lw.get(s, 0) >= v:
                continue
            eff.append((s, v))
            lw[s] = v
        sem = self.sem

        def thunk(e, eff=tuple(eff), fn=fn, inc=inc, inc_k=inc_k):
            for (s, v) in eff:
                e.wait_ge(sem[s], v)
            r = fn(e)
            if inc is not None:
                r.then_inc(sem[inc], inc_k)
        self.q[eng].append(thunk)
        return tok

    def wait(self, eng, *toks):
        lw = self._last_wait[eng]
        need = [wt for wt in self._flat(toks)
                if wt[1] > 0 and lw.get(wt[0], 0) < wt[1]]
        if not need:
            return
        self.emit(eng, lambda e: e.nop(), waits=need)

    def barrier(self):
        toks = [(n, self.cnt[n]) for n in self.cnt]
        for e in ENGINES:
            self.wait(e, *toks)

    def run_all(self, block):
        qs = self.q

        @block.sync
        def _(e):
            for f in qs["sync"]:
                f(e)

        @block.gpsimd
        def _(e):
            for f in qs["gpsimd"]:
                f(e)

        @block.vector
        def _(e):
            for f in qs["vector"]:
                f(e)

        @block.scalar
        def _(e):
            for f in qs["scalar"]:
                f(e)

        @block.tensor
        def _(e):
            for f in qs["tensor"]:
                f(e)



def mkap(t, offset, dims):
    return AP(t, offset, [list(d) for d in dims])



def build_nc(upto="full", dbg=()):
    nc = bass.Bass("TRN2", target_bir_lowering=False)
    dbg = set(dbg)

    dr = {}
    def din(name, shape, dt=F32):
        dr[name] = nc.dram_tensor(name, list(shape), dt, kind="ExternalInput")
    def dout(name, shape, dt=F32):
        dr[name] = nc.dram_tensor(name, list(shape), dt, kind="ExternalOutput")
    def dint(name, shape, dt=F32):
        dr[name] = nc.dram_tensor(name, list(shape), dt)

    din("x0T_full", (D, SEQ)); din("x0T_slab", (D, SLABT))
    din("WqT0", (D, D)); din("WkT0", (D, D)); din("WvT0", (D, D)); din("WoT0", (D, D))
    din("bo0", (D, 1)); din("qs_col", (D, 1)); din("ks_col", (D, 1))
    din("qs16", (16, 64)); din("ks16", (16, 64))
    din("w1T", (D, HID)); din("w2T", (HID, D)); din("convw", (HID, 3)); din("convb", (HID, 1))
    din("Wq1", (D, D)); din("Wk1", (D, D)); din("Wv1", (D, D))
    din("bq1", (D, 1)); din("bk1", (D, 1)); din("bv1", (D, 1))
    din("wgT", (D, HID)); din("wuT", (D, HID)); din("wdT", (HID, D))
    din("amask_tm", (128, 17)); din("stair", (8 * 128, 512))
    dout("out_shard", (D, OWN))

    dint("sp_xh1", (D, SEQ), BF16)
    dint("sp_xh1s", (D, SLABT), BF16)
    dint("sp_khat", (D, SEQ), BF16)
    dint("sp_qhat", (D, SLABT), BF16)
    dint("sp_x1", (D, SLABT))
    dint("sp_x2", (D, OWN))
    dint("sp_z", (D, OWN))
    dint("sp_x3", (D, OWN))
    dint("sp_k1", (D, SEQ), BF16)
    dint("sp_v1", (SEQ, D), BF16)
    dint("sp_w1t", (D, HID), BF16)
    dint("sp_w2t", (HID, D), BF16)
    dint("sp_wgt", (D, HID), BF16)
    dint("sp_wut", (D, HID), BF16)
    dint("sp_wdt", (HID, D), BF16)
    dint("cc_h3_in", (D, OWN), BF16)
    dint("cc_h3_out", (2 * D, OWN), BF16)
    dint("cc_st_in", (128, 64))
    dint("cc_st_out", (2 * 128, 64))

    with ExitStack() as ctx:
        E = ctx.enter_context
        P = Prog(nc, ctx)
        P.sem_new("mm", "v", "sc", "g", "sy")
        H = {"bank_free": {}}

        ones_col = E(nc.sbuf_tensor("ones_col", [128, 1], F32))
        ones_colb = E(nc.sbuf_tensor("ones_colb", [128, 1], BF16))
        ones_row = E(nc.sbuf_tensor("ones_row", [1, 128], F32))
        ident = E(nc.sbuf_tensor("ident", [128, 128], F32))
        onesA = E(nc.sbuf_tensor("onesA", [128, 1], F32))
        onesB = E(nc.sbuf_tensor("onesB", [128, 1], F32))
        rowsel0 = E(nc.sbuf_tensor("rowsel0", [1, 128], F32))
        rowsel1 = E(nc.sbuf_tensor("rowsel1", [1, 128], F32))
        ones64 = E(nc.sbuf_tensor("ones64", [128, 64], F32))
        ones128 = E(nc.sbuf_tensor("ones128", [128, 128], F32))
        thr_cols = E(nc.sbuf_tensor("thr_cols", [128, 16], F32))
        negB = E(nc.sbuf_tensor("negB", [128, 1], F32))
        epsl_col = E(nc.sbuf_tensor("epsl_col", [128, 1], F32))
        eps24_col = E(nc.sbuf_tensor("eps24_col", [128, 1], F32))

        P.emit("gpsimd", lambda e: e.memset(ones_col.ap()[:, :], 1.0), inc="g")
        P.emit("gpsimd", lambda e: e.memset(epsl_col.ap()[:, :], EPS_LN), inc="g")
        P.emit("gpsimd", lambda e: e.memset(eps24_col.ap()[:, :], 1e-24), inc="g")
        P.emit("gpsimd", lambda e: e.memset(ones_colb.ap()[:, :], 1.0), inc="g")
        P.emit("gpsimd", lambda e: e.memset(ones_row.ap()[:, :], 1.0), inc="g")
        P.emit("gpsimd", lambda e: e.memset(ident.ap()[:, :], 0.0), inc="g")
        P.emit("gpsimd", lambda e: e.affine_select(
            out=ident.ap()[:, :], in_=ident.ap()[:, :],
            compare_op=ALU.not_equal, fill=1.0, base=0,
            pattern=[[-1, 128]], channel_multiplier=1), inc="g")
        P.emit("gpsimd", lambda e: e.memset(onesA.ap()[:, :], 0.0), inc="g")
        P.emit("gpsimd", lambda e: e.memset(onesA.ap()[0:64, 0:1], 1.0), inc="g")
        P.emit("gpsimd", lambda e: e.memset(onesB.ap()[:, :], 0.0), inc="g")
        P.emit("gpsimd", lambda e: e.memset(onesB.ap()[64:128, 0:1], 1.0), inc="g")
        P.emit("gpsimd", lambda e: e.memset(rowsel0.ap()[:, :], 0.0), inc="g")
        P.emit("gpsimd", lambda e: e.memset(rowsel0.ap()[0:1, 0:64], 1.0), inc="g")
        P.emit("gpsimd", lambda e: e.memset(rowsel1.ap()[:, :], 0.0), inc="g")
        P.emit("gpsimd", lambda e: e.memset(rowsel1.ap()[0:1, 64:128], 1.0), inc="g")
        P.emit("gpsimd", lambda e: e.memset(ones128.ap()[:, :], 1.0), inc="g")
        P.emit("gpsimd", lambda e: e.memset(ones64.ap()[:, :], 0.0), inc="g")
        P.emit("gpsimd", lambda e: e.memset(ones64.ap()[64:65, :], 1.0), inc="g")
        t_const = P.tok("g")

        pA = E(nc.psum_tensor("pA", [128, 1024], F32))
        pB = E(nc.psum_tensor("pB", [128, 1024], F32))
        bank_s = [E(nc.psum_tensor(f"pb{i}", [128, 512], F32)) for i in range(4)]

        class Bk:
            __slots__ = ("t", "off")
            def __init__(self, t, off):
                self.t, self.off = t, off
            def ap(self):
                a = self.t.ap()
                return a[:, self.off:self.off + 512]

        bank = [Bk(pA, 0), Bk(pA, 512), Bk(pB, 0), Bk(pB, 512)] + \
               [Bk(b, 0) for b in bank_s]

        def bfree(b):
            return H["bank_free"].get(id(b))

        def bset(b, tok):
            H["bank_free"][id(b)] = tok


        WQ = [("w1T", "sp_w1t", D, HID), ("w2T", "sp_w2t", HID, D),
              ("wgT", "sp_wgt", D, HID), ("wuT", "sp_wut", D, HID),
              ("wdT", "sp_wdt", HID, D)]
        phWQ = ExitStack()
        EWQ = phWQ.enter_context
        wring = EWQ(nc.sbuf_tensor("wring", [128, 4096], F32))
        wacc = EWQ(nc.sbuf_tensor("wacc", [128, 32], F32))
        wrow = EWQ(nc.sbuf_tensor("wrow", [1, 32], F32))
        wb1 = EWQ(nc.sbuf_tensor("wb1", [128, 2048], F32))
        wb2 = EWQ(nc.sbuf_tensor("wb2", [128, 2048], F32))
        wtn = EWQ(nc.sbuf_tensor("wtn", [128, 2 * 2048], BF16))
        wq_mean_steps = []
        wq_tern_steps = []
        wq_state = {}

        def _wq_build():
            for wi, (src, dst, R, C) in enumerate(WQ):
                nch = R // 128
                colw = C
                nsl = colw // 512

                def mean_step(ch, wi=wi, src=src, R=R, C=C, nch=nch, colw=colw, nsl=nsl):
                    tld = P.emit("gpsimd", lambda e, src=src, ch=ch, colw=colw: e.dma_start(
                        out=wring.ap()[:, 0:colw],
                        in_=dr[src].ap()[128 * ch:128 * (ch + 1), :]),
                        waits=(wq_state.get(("mg",)), wq_state.get(("tg",))),
                        stream='wql')
                    ta = P.emit("vector", lambda e, ch=ch, colw=colw: e.reduce_sum(
                        wacc.ap()[:, ch:ch + 1], wring.ap()[:, 0:colw],
                        axis=AXX, apply_absolute_value=True),
                        waits=(tld, wq_state.get(("wacc_t",)) if ch == 0 else None), inc="v")
                    wq_state[("mg",)] = ta
                    wq_state[("macc", ch)] = ta
                    if ch == nch - 1:
                        tmm = P.emit("tensor", lambda e, nch=nch: e.matmul(
                            bank[6].ap()[0:1, 0:nch], ones_col.ap()[:, :],
                            wacc.ap()[:, 0:nch], start=True, stop=True),
                            waits=tuple(wq_state[("macc", c)] for c in range(nch)) +
                                  (t_const, bfree(bank[6])), inc="mm")
                        wq_state[("wacc_t",)] = tmm
                        trr = P.emit("vector", lambda e, wi=wi, nch=nch: e.reduce_sum(
                            wrow.ap()[0:1, 2 * wi:2 * wi + 1].rearrange("p a -> p a ()"),
                            bank[6].ap()[0:1, 0:nch].rearrange("p a -> p () a"), axis=AXX),
                            waits=(tmm,), inc="v")
                        tm = P.emit("vector", lambda e, wi=wi, R=R, C=C: e.tensor_scalar(
                            wrow.ap()[0:1, 2 * wi:2 * wi + 1], wrow.ap()[0:1, 2 * wi:2 * wi + 1],
                            1.0 / (R * C), 1e-5, ALU.mult, ALU.max), waits=(trr,), inc="v")
                        tb1 = P.emit("tensor", lambda e, wi=wi: e.matmul(
                            bank[6].ap()[:, 0:1], ones_row.ap()[0:1, :],
                            wrow.ap()[0:1, 2 * wi:2 * wi + 1],
                            start=True, stop=True), waits=(tm, trr), inc="mm")
                        tc1 = P.emit("vector", lambda e, wi=wi: e.tensor_copy(
                            thr_cols.ap()[:, 2 * wi + 1:2 * wi + 2], bank[6].ap()[:, 0:1]),
                            waits=(tb1,), inc="v")
                        tc2 = P.emit("vector", lambda e, wi=wi: e.tensor_single_scalar(
                            thr_cols.ap()[:, 2 * wi:2 * wi + 1], bank[6].ap()[:, 0:1], 0.5,
                            ALU.mult), waits=(tb1,), inc="v")
                        bset(bank[6], tc2)
                        tc3 = P.emit("vector", lambda e, wi=wi: e.tensor_single_scalar(
                            thr_cols.ap()[:, 10 + wi:11 + wi],
                            thr_cols.ap()[:, 2 * wi:2 * wi + 1], -1.0, ALU.mult),
                            waits=(tc2,), inc="v")
                        wq_state[("thr", wi)] = tc3

                for ch in range(nch):
                    wq_mean_steps.append((mean_step, ch))

                def tern_step(ch, wi=wi, src=src, dst=dst, colw=colw, nch=nch):
                    tld = P.emit("gpsimd", lambda e, src=src, ch=ch, colw=colw: e.dma_start(
                        out=wring.ap()[:, 0:colw],
                        in_=dr[src].ap()[128 * ch:128 * (ch + 1), :]),
                        waits=(wq_state.get(("tg",)), wq_state.get(("mg",))),
                        stream='wql')
                    tc2 = wq_state[("thr", wi)]
                    nhb = max(1, colw // 2048)
                    last = None
                    for hb in range(nhb):
                        hw = min(2048, colw)
                        off = 2048 * hb
                        rr = hb % 2
                        tg1 = P.emit("vector", lambda e, off=off, hw=hw, wi=wi: e.tensor_scalar(
                            wb1.ap()[:, 0:hw], wring.ap()[:, off:off + hw],
                            thr_cols.ap()[:, 2 * wi:2 * wi + 1], None, ALU.is_ge),
                            waits=(tld, tc2), inc="v")
                        tg2 = P.emit("vector", lambda e, off=off, hw=hw, wi=wi: e.tensor_scalar(
                            wb2.ap()[:, 0:hw], wring.ap()[:, off:off + hw],
                            thr_cols.ap()[:, 10 + wi:11 + wi], None, ALU.is_le),
                            inc="v")
                        tg3 = P.emit("vector", lambda e, off=off, hw=hw, rr=rr: e.tensor_tensor(
                            wtn.ap()[:, 2048 * rr:2048 * rr + hw], wb1.ap()[:, 0:hw],
                            wb2.ap()[:, 0:hw], ALU.subtract),
                            waits=(tg1, tg2, wq_state.get(("tsp", rr))), inc="v")
                        last = tg3
                        wq_state[("tsp", rr)] = P.emit(
                            "scalar", lambda e, dst=dst, ch=ch, hb=hb, hw=hw, rr=rr: e.dma_start(
                                out=dr[dst].ap()[128 * ch:128 * (ch + 1),
                                                 2048 * hb:2048 * hb + hw],
                                in_=wtn.ap()[:, 2048 * rr:2048 * rr + hw]),
                            waits=(tg3,), stream=f'wqs{rr}')
                    wq_state[("tg",)] = last
                    if ch == (128 if False else (dst and 0)) or True:
                        pass
                    if ch == nch - 1:
                        wq_state[("done", wi)] = (P.tok("q_wqs0"),
                                                  P.tok("q_wqs1") if "q_wqs1" in P.cnt else None)

                for ch in range(nch):
                    wq_tern_steps.append((tern_step, ch))

        _wq_build()
        _wq_i = [0, 0]

        def pump_mean(n):
            while n > 0 and _wq_i[0] < len(wq_mean_steps):
                f, a = wq_mean_steps[_wq_i[0]]
                f(a)
                _wq_i[0] += 1
                n -= 1

        def pump_tern(n):
            while n > 0 and _wq_i[1] < len(wq_tern_steps):
                if _wq_i[0] < len(wq_mean_steps):
                    pump_mean(1)
                    n -= 1
                    continue
                f, a = wq_tern_steps[_wq_i[1]]
                f(a)
                _wq_i[1] += 1
                n -= 1

        def wq_finish():
            pump_mean(10 ** 9)
            pump_tern(10 ** 9)

        def emit_ln(name, src_getter, n_cols, out_cb, rows, sqr, state=None):
            nblk = (n_cols + 511) // 512
            state = state if state is not None else {}
            ring_free = state.setdefault("ring", {})
            sq_guard = state.setdefault("sq", [None])
            ab_guard = state.setdefault("ab", [None, None])
            pend = state.setdefault("pend", [None])

            def finish(ctx2):
                cb, c0, w, r, tile, ld, t7, t8 = ctx2
                bA, bB = bank[4], bank[5]
                tA = P.emit("tensor", lambda e, w=w, bA=bA: e.matmul(
                    bA.ap()[:, 0:w], ones_row.ap()[0:1, :],
                    rows.ap()[0:1, 2048 * (cb % 2) + 1536: 2048 * (cb % 2) + 1536 + w],
                    start=True, stop=True), waits=(t7, bfree(bA)), inc="mm")
                tB = P.emit("tensor", lambda e, w=w, bB=bB: e.matmul(
                    bB.ap()[:, 0:w], ones_row.ap()[0:1, :],
                    rows.ap()[0:1, 2048 * (cb % 2) + 512: 2048 * (cb % 2) + 512 + w],
                    start=True, stop=True), waits=(t8, bfree(bB)), inc="mm")
                ab_guard[0] = tA
                ab_guard[1] = tB
                last = out_cb(cb, c0, w, r, bA, bB, (tA, tB), tile, ld)
                ring_free[r] = last
                bset(bA, last); bset(bB, last)

            for cb in range(nblk):
                c0 = 512 * cb
                w = min(512, n_cols - c0)
                r = cb % 2
                ro = 2048 * r
                bsum, bsq = bank[0 + 2 * r], bank[1 + 2 * r]
                ld, tile = src_getter(cb, c0, w, r, ring_free.get(r))
                sq_toks = []
                for fc in range(NFC):
                    st = P.emit("scalar", lambda e, fc=fc, w=w, tile=tile: e.activation(
                        sqr.ap()[:, 512 * fc:512 * fc + w], tile(fc), AF.Square,
                        bias=0.0, scale=1.0), waits=(ld, sq_guard[0]), inc="sc")
                    sq_toks.append(st)
                for fc in range(NFC):
                    P.emit("tensor", lambda e, fc=fc, w=w, tile=tile, bsum=bsum: e.matmul(
                        bsum.ap()[0:1, 0:w], ones_col.ap()[:, :], tile(fc),
                        start=(fc == 0), stop=(fc == NFC - 1)),
                        waits=(ld, t_const, bfree(bsum)) if fc == 0 else (),
                        inc="mm" if fc == NFC - 1 else None)
                t_sum = P.tok("mm")
                for fc in range(NFC):
                    P.emit("tensor", lambda e, fc=fc, w=w, bsq=bsq: e.matmul(
                        bsq.ap()[0:1, 0:w], ones_colb.ap()[:, :],
                        sqr.ap()[:, 512 * fc:512 * fc + w],
                        start=(fc == 0), stop=(fc == NFC - 1)),
                        waits=(sq_toks[fc], bfree(bsq)) if fc == 0 else (sq_toks[fc],),
                        inc="mm" if fc == NFC - 1 else None)
                t_sq = P.tok("mm")
                sq_guard[0] = t_sq
                t1 = P.emit("vector", lambda e, w=w, ro=ro, bsum=bsum: e.tensor_single_scalar(
                    rows.ap()[0:1, ro:ro + w], bsum.ap()[0:1, 0:w], 1.0 / D, ALU.mult),
                    waits=(t_sum,), inc="v")
                t2 = P.emit("vector", lambda e, w=w, ro=ro, bsq=bsq: e.tensor_single_scalar(
                    rows.ap()[0:1, ro + 512:ro + 512 + w], bsq.ap()[0:1, 0:w], 1.0 / D, ALU.mult),
                    waits=(t_sq, ab_guard[1]), inc="v")
                bset(bsum, t1); bset(bsq, t2)
                t3 = P.emit("vector", lambda e, w=w, ro=ro: e.tensor_tensor(
                    rows.ap()[0:1, ro + 1024:ro + 1024 + w], rows.ap()[0:1, ro:ro + w],
                    rows.ap()[0:1, ro:ro + w], ALU.mult), waits=(t1,), inc="v")
                t4 = P.emit("vector", lambda e, w=w, ro=ro: e.tensor_tensor(
                    rows.ap()[0:1, ro + 1024:ro + 1024 + w], rows.ap()[0:1, ro + 512:ro + 512 + w],
                    rows.ap()[0:1, ro + 1024:ro + 1024 + w], ALU.subtract), waits=(t2, t3), inc="v")
                t5 = P.emit("vector", lambda e, w=w, ro=ro: e.tensor_single_scalar(
                    rows.ap()[0:1, ro + 1024:ro + 1024 + w], rows.ap()[0:1, ro + 1024:ro + 1024 + w],
                    EPS_LN, ALU.add), waits=(t4,), inc="v")
                t6 = P.emit("vector", lambda e, w=w, ro=ro: e.reciprocal(
                    rows.ap()[0:1, ro + 1536:ro + 1536 + w], rows.ap()[0:1, ro + 1024:ro + 1024 + w]),
                    waits=(t5, ab_guard[0]), inc="v")
                t7 = P.emit("scalar", lambda e, w=w, ro=ro: e.activation(
                    rows.ap()[0:1, ro + 1536:ro + 1536 + w], rows.ap()[0:1, ro + 1536:ro + 1536 + w],
                    AF.Sqrt, bias=0.0, scale=1.0), waits=(t6,), inc="sc")
                t8 = P.emit("vector", lambda e, w=w, ro=ro: e.scalar_tensor_tensor(
                    rows.ap()[0:1, ro + 512:ro + 512 + w], rows.ap()[0:1, ro:ro + w], -1.0,
                    rows.ap()[0:1, ro + 1536:ro + 1536 + w], ALU.mult, ALU.mult),
                    waits=(t7,), inc="v")
                if pend[0] is not None:
                    finish(pend[0])
                pend[0] = (cb, c0, w, r, tile, ld, t7, t8)
            finish(pend[0])
            pend[0] = None

        with ExitStack() as ph0:
            E0 = ph0.enter_context
            xring = E0(nc.sbuf_tensor("ln0_ring", [128, 2 * NFC * 512], F32))
            rows0 = E0(nc.sbuf_tensor("ln0_rows", [1, 2 * 4 * 512], F32))
            sq0 = E0(nc.sbuf_tensor("ln0_sq", [128, NFC * 512], BF16))
            prod0 = E0(nc.sbuf_tensor("ln0_prod", [128, 512], F32))
            stg0 = E0(nc.sbuf_tensor("ln0_stg", [128, 2 * NFC * 512], BF16))
            stg_guard = {}

            def mk_ln0(src_name, spill_name, tag):
                def src(cb, c0, w, r, free_tok):
                    ld = P.emit("sync", lambda e, c0=c0, w=w, r=r: e.dma_start(
                        out=mkap(xring.ap().tensor, NFC * 512 * r,
                                 [[2 * NFC * 512, 128], [512, NFC], [1, w]]),
                        in_=dr[src_name].ap()[:, c0:c0 + w].rearrange("(f p) w -> p f w", p=128)),
                        waits=(free_tok,), stream=f'lnl{r}')
                    def tile(fc, r=r, w=w):
                        return xring.ap()[:, NFC * 512 * r + 512 * fc: NFC * 512 * r + 512 * fc + w]
                    return ld, tile

                def outcb(cb, c0, w, r, bA, bB, toks, tile, ld):
                    P.wait("vector", *toks)
                    P.wait("vector", stg_guard.get((tag, r)))
                    last = None
                    for fc in range(NFC):
                        P.emit("vector", lambda e, fc=fc, w=w, bA=bA, tile=tile: e.tensor_tensor(
                            prod0.ap()[:, 0:w], tile(fc), bA.ap()[:, 0:w], ALU.mult), inc="v")
                        last = P.emit("vector", lambda e, fc=fc, r=r, w=w, bB=bB: e.tensor_tensor(
                            stg0.ap()[:, NFC * 512 * r + 512 * fc: NFC * 512 * r + 512 * fc + w],
                            prod0.ap()[:, 0:w], bB.ap()[:, 0:w], ALU.add), inc="v")
                    tspl = None
                    for fc in range(NFC):
                        tspl = P.emit("scalar", lambda e, fc=fc, c0=c0, w=w, r=r: e.dma_start(
                            out=dr[spill_name].ap()[128 * fc:128 * (fc + 1), c0:c0 + w],
                            in_=stg0.ap()[:, NFC * 512 * r + 512 * fc: NFC * 512 * r + 512 * fc + w]),
                            waits=(last,), stream=f'ln0s{r}')
                    stg_guard[(tag, r)] = tspl
                    pump_mean(1)
                    return tspl
                return src, outcb

            ln0_state = {}
            s0, o0 = mk_ln0("x0T_full", "sp_xh1", "a")
            emit_ln("ln0", s0, SEQ, o0, rows0, sq0, state=ln0_state)
            s1, o1 = mk_ln0("x0T_slab", "sp_xh1s", "a")
            emit_ln("ln0s", s1, SLABT, o1, rows0, sq0, state=ln0_state)
            t_xh1_sp = (P.tok("q_ln0s0"), P.tok("q_ln0s1"))
            if "xh1" in dbg:
                dout("dbg_xh1", (D, SEQ), BF16)
                P.wait("sync", t_xh1_sp)
                P.emit("sync", lambda e: e.dma_start(
                    out=dr["dbg_xh1"].ap()[:, :], in_=dr["sp_xh1"].ap()[:, :]),
                    stream='dbg')
            P.barrier()

        if upto == "xh1":
            phWQ.close()
            P.wait("sync", *P.all_toks())
            block = E(nc.Block())
            P.run_all(block)
            return nc

        phv = ExitStack()
        v_sb = phv.enter_context(nc.sbuf_tensor("v_sb", [128, NT * 1040], BF16))
        qs_sb = phv.enter_context(nc.sbuf_tensor("qs_sb", [128, 8], F32))
        ks_sb = phv.enter_context(nc.sbuf_tensor("ks_sb", [128, 8], F32))
        bo_sb = phv.enter_context(nc.sbuf_tensor("bo_sb", [128, 8], F32))
        P.emit("gpsimd", lambda e: e.memset(
            v_sb.ap().rearrange("p (a b) -> p a b", b=65)[:, :, 64:65], 1.0), inc="g")
        t_vones = P.tok("g")
        P.emit("sync", lambda e: e.dma_start(
            out=qs_sb.ap().rearrange("p a -> p a ()"),
            in_=dr["qs_col"].ap().rearrange("(a p) o -> p a o", p=128),
            allow_slow_non_contiguous=True), stream='sm0')
        P.emit("sync", lambda e: e.dma_start(
            out=ks_sb.ap().rearrange("p a -> p a ()"),
            in_=dr["ks_col"].ap().rearrange("(a p) o -> p a o", p=128),
            allow_slow_non_contiguous=True), stream='sm1')
        P.emit("sync", lambda e: e.dma_start(
            out=bo_sb.ap().rearrange("p a -> p a ()"),
            in_=dr["bo0"].ap().rearrange("(a p) o -> p a o", p=128),
            allow_slow_non_contiguous=True), stream='sm2')
        t_small = (P.tok("q_sm0"), P.tok("q_sm1"), P.tok("q_sm2"))

        with ExitStack() as ph1:
            E1 = ph1.enter_context
            qs16r = E1(nc.sbuf_tensor("qs16r", [16, 64], F32))
            ks16r = E1(nc.sbuf_tensor("ks16r", [16, 64], F32))
            hmax = E1(nc.sbuf_tensor("hmax", [16, 3], F32))
            t1 = P.emit("sync", lambda e: e.dma_start(out=qs16r.ap()[:, :], in_=dr["qs16"].ap()[:, :]), stream='sm3')
            t2 = P.emit("sync", lambda e: e.dma_start(out=ks16r.ap()[:, :], in_=dr["ks16"].ap()[:, :]), stream='sm4')
            ta = P.emit("vector", lambda e: e.reduce_max(
                hmax.ap()[:, 0:1], qs16r.ap()[:, :], axis=AXX, apply_absolute_value=True),
                waits=(t1,), inc="v")
            tb = P.emit("vector", lambda e: e.reduce_max(
                hmax.ap()[:, 1:2], ks16r.ap()[:, :], axis=AXX, apply_absolute_value=True),
                waits=(t2,), inc="v")
            tc_ = P.emit("vector", lambda e: e.tensor_tensor(
                hmax.ap()[:, 2:3], hmax.ap()[:, 0:1], hmax.ap()[:, 1:2], ALU.mult),
                waits=(ta, tb), inc="v")
            hrow16 = E1(nc.sbuf_tensor("hrow16", [1, 16], F32))
            tdh = P.emit("scalar", lambda e: e.dma_start(
                out=hrow16.ap()[0:1, :], in_=hmax.ap()[:, 2:3],
                allow_slow_non_contiguous=True),
                waits=(tc_,), stream='sm3')
            trdh = P.emit("vector", lambda e: e.reduce_max(
                hmax.ap()[0:1, 2:3], hrow16.ap()[0:1, :], axis=AXX),
                waits=(tdh,), inc="v")
            tnb = P.emit("vector", lambda e: e.tensor_single_scalar(
                negB.ap()[0:1, 0:1], hmax.ap()[0:1, 2:3], -0.125, ALU.mult),
                waits=(trdh,), inc="v")
            tmm = P.emit("tensor", lambda e: e.matmul(
                bank[7].ap()[:, 0:1], ones_row.ap()[0:1, :], negB.ap()[0:1, 0:1],
                start=True, stop=True), waits=(tnb, t_const, bfree(bank[7])), inc="mm")
            tnb2 = P.emit("vector", lambda e: e.tensor_copy(negB.ap()[:, 0:1], bank[7].ap()[:, 0:1]),
                          waits=(tmm,), inc="v")
            bset(bank[7], tnb2)

            wq_sb = E1(nc.sbuf_tensor("wq_sb", [128, NFC * 1024], BF16))
            wk_sb = E1(nc.sbuf_tensor("wk_sb", [128, NFC * 1024], BF16))
            wv_sb = E1(nc.sbuf_tensor("wv_sb", [128, NFC * 1024], BF16))
            for wi_, (nm, sbw) in enumerate((("WqT0", wq_sb), ("WkT0", wk_sb), ("WvT0", wv_sb))):
                P.emit("gpsimd", lambda e, nm=nm, sbw=sbw: e.dma_start(
                    out=sbw.ap().rearrange("p (f n) -> p f n", f=NFC),
                    in_=dr[nm].ap().rearrange("(f p) n -> p f n", p=128)), stream=f'gw{wi_}')
            t_wload = (P.tok("q_gw0"), P.tok("q_gw1"), P.tok("q_gw2"))

            xr1 = E1(nc.sbuf_tensor("p1_xring", [128, 2 * NFC * 512], BF16))
            praw = E1(nc.sbuf_tensor("p1_raw", [128, 2 * 512], F32))
            psq1 = E1(nc.sbuf_tensor("p1_sq", [128, 512], F32))
            stg1 = E1(nc.sbuf_tensor("p1_stg", [128, 2 * 512], BF16))
            sumr = E1(nc.sbuf_tensor("p1_sumr", [1, 1024], F32))
            hg = {}

            def hat_pass(src_spill, n_cols, wsb, scol, spill):
                nblk = (n_cols + 511) // 512
                if HAT_LIMIT:
                    nblk = min(nblk, HAT_LIMIT)
                for cb in range(nblk):
                    c0 = 512 * cb
                    w = min(512, n_cols - c0)
                    r = cb % 2
                    ld = P.emit("sync", lambda e, c0=c0, w=w, r=r, src_spill=src_spill: e.dma_start(
                        out=mkap(xr1.ap().tensor, NFC * 512 * r,
                                 [[2 * NFC * 512, 128], [512, NFC], [1, w]]),
                        in_=dr[src_spill].ap()[:, c0:c0 + w].rearrange("(f p) w -> p f w", p=128)),
                        waits=(hg.get(("x", r)), t_xh1_sp), stream=f'lnl{r}')
                    last_r = None
                    hpend = [None]

                    def hat_norm(ec, bk, tmm, w=w, c0=c0):
                        nonlocal last_r
                        rr = ec % 2
                        trw = P.emit("vector", lambda e, rr=rr, w=w, bk=bk: e.tensor_copy(
                            praw.ap()[:, 512 * rr: 512 * rr + w], bk.ap()[:, 0:w]),
                            waits=(tmm, hg.get(("praw", rr))), inc="v")
                        bset(bk, trw)
                        if HAT_STOP and HAT_STOP < 2:
                            hg[("praw", rr)] = trw
                            last_r = trw
                            return
                        tsq = P.emit("scalar", lambda e, w=w, rr=rr: e.activation(
                            psq1.ap()[:, 0:w], praw.ap()[:, 512 * rr: 512 * rr + w],
                            AF.Square, bias=0.0, scale=1.0),
                            waits=(trw, hg.get(("psq",))), inc="sc")
                        if HAT_STOP and HAT_STOP < 3:
                            hg[("psq",)] = tsq
                            hg[("praw", rr)] = trw
                            last_r = tsq
                            return
                        P.emit("tensor", lambda e, w=w: e.matmul(
                            bank[2].ap()[0:1, 0:w], onesA.ap()[:, :], psq1.ap()[:, 0:w],
                            start=True, stop=True),
                            waits=(tsq, t_const, bfree(bank[2]), bfree(bank[3])), inc="mm")
                        tss = P.emit("tensor", lambda e, w=w: e.matmul(
                            bank[3].ap()[0:1, 0:w], onesB.ap()[:, :],
                            psq1.ap()[:, 0:w],
                            start=True, stop=True), inc="mm")
                        hg[("psq",)] = tss
                        if HAT_STOP and HAT_STOP < 4:
                            bset(bank[2], tss); bset(bank[3], tss)
                            hg[("praw", rr)] = trw
                            last_r = tss
                            return
                        tsc0 = P.emit("vector", lambda e, w=w: e.tensor_copy(
                            sumr.ap()[0:1, 0:w], bank[2].ap()[0:1, 0:w]),
                            waits=(tss, hg.get(("sumr",))), inc="v")
                        tsc1 = P.emit("vector", lambda e, w=w: e.tensor_copy(
                            sumr.ap()[0:1, 512:512 + w], bank[3].ap()[0:1, 0:w]), inc="v")
                        bset(bank[2], tsc1); bset(bank[3], tsc1)
                        ts1 = P.emit("scalar", lambda e, w=w: e.activation(
                            sumr.ap()[0:1, 0:w], sumr.ap()[0:1, 0:w],
                            AF.Sqrt, bias=0.0, scale=1.0),
                            waits=(tsc0,), inc="sc")
                        ts1b = P.emit("scalar", lambda e, w=w: e.activation(
                            sumr.ap()[0:1, 512:512 + w], sumr.ap()[0:1, 512:512 + w],
                            AF.Sqrt, bias=0.0, scale=1.0), waits=(tsc1,), inc="sc")
                        ts2 = P.emit("vector", lambda e, w=w: e.tensor_single_scalar(
                            sumr.ap()[0:1, 0:1024], sumr.ap()[0:1, 0:1024], 1e-12, ALU.max),
                            waits=(ts1, ts1b), inc="v")
                        ts3b = ts3 = P.emit("vector", lambda e, w=w: e.reciprocal(
                            sumr.ap()[0:1, 0:1024], sumr.ap()[0:1, 0:1024]),
                            waits=(ts2,), inc="v")
                        if HAT_STOP and HAT_STOP < 5:
                            hg[("sumr",)] = ts3b
                            hg[("praw", rr)] = trw
                            last_r = ts3b
                            return
                        if HAT_STOP and HAT_STOP < 6:
                            hg[("sumr",)] = ts3b
                            hg[("praw", rr)] = trw
                            last_r = ts3b
                            return
                        P.emit("tensor", lambda e, w=w: e.matmul(
                            bank[4].ap()[:, 0:w], rowsel0.ap()[0:1, :],
                            sumr.ap()[0:1, 0:w],
                            start=True, stop=False),
                            waits=(ts3, ts3b, bfree(bank[4])), inc="mm")
                        tbc = P.emit("tensor", lambda e, w=w: e.matmul(
                            bank[4].ap()[:, 0:w], rowsel1.ap()[0:1, :],
                            sumr.ap()[0:1, 512:512 + w],
                            start=False, stop=True), inc="mm")
                        hg[("sumr",)] = tbc
                        if HAT_STOP and HAT_STOP < 7:
                            bset(bank[4], tbc)
                            hg[("praw", rr)] = trw
                            last_r = tbc
                            return
                        tht = P.emit("vector", lambda e, ec=ec, rr=rr, w=w, scol=scol: e.scalar_tensor_tensor(
                            stg1.ap()[:, 512 * rr: 512 * rr + w],
                            praw.ap()[:, 512 * rr: 512 * rr + w], scol.ap()[:, ec:ec + 1],
                            bank[4].ap()[:, 0:w], ALU.mult, ALU.mult),
                            waits=(tbc, trw, t_small, hg.get(("stg", rr))), inc="v")
                        bset(bank[4], tht)
                        hg[("praw", rr)] = tht
                        if HAT_STOP and HAT_STOP < 8:
                            last_r = tht
                            hg[("stg", rr)] = tht
                            return
                        last_r = P.emit("scalar", lambda e, ec=ec, c0=c0, w=w, rr=rr, spill=spill: e.dma_start(
                            out=dr[spill].ap()[128 * ec:128 * (ec + 1), c0:c0 + w],
                            in_=stg1.ap()[:, 512 * rr: 512 * rr + w]),
                            waits=(tht,), stream=f'hats{rr}')
                        hg[("stg", rr)] = last_r

                    for ec in range(HAT_ECS or NFC):
                        bk = bank[0] if ec % 2 == 0 else bank[1]
                        for fc in range(NFC):
                            P.emit("tensor", lambda e, ec=ec, fc=fc, r=r, w=w, bk=bk, wsb=wsb: e.matmul(
                                bk.ap()[:, 0:w],
                                wsb.ap()[:, 1024 * fc + 128 * ec: 1024 * fc + 128 * (ec + 1)],
                                xr1.ap()[:, NFC * 512 * r + 512 * fc: NFC * 512 * r + 512 * fc + w],
                                start=(fc == 0), stop=(fc == NFC - 1)),
                                waits=(ld, t_wload, bfree(bk)) if fc == 0 else (),
                                inc="mm" if fc == NFC - 1 else None)
                        tmm = P.tok("mm")
                        if hpend[0] is not None:
                            hat_norm(*hpend[0])
                        hpend[0] = (ec, bk, tmm)
                    hat_norm(*hpend[0])
                    hg[("x", r)] = last_r
                    pump_mean(1)

            if upto != "negb":
                hat_pass("sp_xh1", SEQ, wk_sb, ks_sb, "sp_khat")
                t_khat = tuple(P.tok(q) for q in ("q_hats0", "q_hats1") if q in P.cnt)
            if upto in ("negb", "khat"):
                P.barrier()
                ph1.close()
                phv.close()
                phWQ.close()
                block = E(nc.Block())
                P.run_all(block)
                return nc
            hat_pass("sp_xh1s", SLABT, wq_sb, qs_sb, "sp_qhat")
            t_qhat = tuple(P.tok(q) for q in ("q_hats0", "q_hats1") if q in P.cnt)

            for cb in range(SEQ // 512):
                c0 = 512 * cb
                r = cb % 2
                ld = P.emit("sync", lambda e, c0=c0, r=r: e.dma_start(
                    out=xr1.ap()[:, NFC * 512 * r: NFC * 512 * (r + 1)].rearrange(
                        "p (f w) -> p f w", f=NFC),
                    in_=dr["sp_xh1"].ap()[:, c0:c0 + 512].rearrange("(f p) w -> p f w", p=128)),
                    waits=(hg.get(("x", r)), P.tok("mm") if cb < 2 else None),
                    stream=f'lnl{r}')
                last_mm = None
                for sub in range(4):
                    jt = 4 * cb + sub
                    for en in range(2):
                        bk = bank[0] if en == 0 else bank[1]
                        for fc in range(NFC):
                            P.emit("tensor", lambda e, sub=sub, en=en, fc=fc, r=r, bk=bk: e.matmul(
                                bk.ap()[:, 0:512],
                                xr1.ap()[:, NFC * 512 * r + 512 * fc + 128 * sub:
                                          NFC * 512 * r + 512 * fc + 128 * (sub + 1)],
                                wv_sb.ap()[:, 1024 * fc + 512 * en: 1024 * fc + 512 * (en + 1)],
                                start=(fc == 0), stop=(fc == NFC - 1)),
                                waits=(ld, t_wload, t_vones, bfree(bk)) if fc == 0 else (),
                                inc="mm" if fc == NFC - 1 else None)
                        tmm = P.tok("mm")
                        tev = P.emit("vector", lambda e, jt=jt, en=en, bk=bk: e.tensor_copy(
                            v_sb.ap()[:, 1040 * jt + 520 * en: 1040 * jt + 520 * (en + 1)].rearrange(
                                "p (h x) -> p h x", h=8)[:, :, 0:64],
                            bk.ap()[:, 0:512].rearrange("p (h x) -> p h x", h=8)),
                            waits=(tmm,), inc="v")
                        bset(bk, tev)
                hg[("x", r)] = P.tok("v")
                pump_mean(1)
            t_vproj = P.tok("v")
            P.barrier()

        if upto == "hat":
            P.barrier()
            phv.close()
            phWQ.close()
            block = E(nc.Block())
            P.run_all(block)
            return nc

        pump_mean(10 ** 9)

        attnout = [phv.enter_context(nc.sbuf_tensor(f"ao_{ec}", [128, SLABT], BF16))
                   for ec in range(NFC)]
        with ExitStack() as ph1b:
            E1b = ph1b.enter_context
            kh_ring = E1b(nc.sbuf_tensor("kh_ring", [128, 2 * SEQ], BF16))
            qh_ring = E1b(nc.sbuf_tensor("qh_ring", [128, 2 * SLABT], BF16))
            e_ring = E1b(nc.sbuf_tensor("e_ring", [128, 3 * 1024], BF16))
            rec_sb = E1b(nc.sbuf_tensor("rec_sb", [128, 2 * 512], F32))
            den_sb = E1b(nc.sbuf_tensor("den_sb", [128, 2 * 512], F32))
            h1tmp = E1b(nc.sbuf_tensor("h1tmp", [128, 512], BF16))
            kq_free = {}
            for ec in range(NFC):
                r = ec % 2
                tkh = P.emit("sync", lambda e, ec=ec, r=r: e.dma_start(
                    out=kh_ring.ap()[:, SEQ * r: SEQ * (r + 1)],
                    in_=dr["sp_khat"].ap()[128 * ec:128 * (ec + 1), :]),
                    waits=(kq_free.get(r), t_khat), stream=f'khl{r}')
                tqh = P.emit("sync", lambda e, ec=ec, r=r: e.dma_start(
                    out=qh_ring.ap()[:, SLABT * r: SLABT * (r + 1)],
                    in_=dr["sp_qhat"].ap()[128 * ec:128 * (ec + 1), :]),
                    waits=(kq_free.get(r), t_qhat), stream=f'qhl{r}')
                av_tok = {}
                for ic, (i0, iw) in enumerate(I_CHUNKS):
                    bR0, bR1 = bank[4], bank[5]
                    tevprev = (bfree(bR0), bfree(bR1))
                    pend = None
                    for jt in range(NT):
                        sp = pA if jt % 2 == 0 else pB
                        spk = ("S", jt % 2)
                        wA = (tkh, tqh, t_vproj, tnb2, H["bank_free"].get(spk))
                        if jt >= 2:
                            wA = (H["bank_free"].get(spk),)
                        P.emit("tensor", lambda e, r=r, jt=jt, i0=i0, iw=iw, sp=sp: e.matmul(
                            sp.ap()[:, 0:iw],
                            kh_ring.ap()[0:64, SEQ * r + 128 * jt: SEQ * r + 128 * (jt + 1)],
                            qh_ring.ap()[0:64, SLABT * r + i0: SLABT * r + i0 + iw],
                            start=True, stop=True, tile_position=(0, 0)),
                            waits=wA, inc=None)
                        tqk = P.emit("tensor", lambda e, r=r, jt=jt, i0=i0, iw=iw, sp=sp: e.matmul(
                            sp.ap()[:, 512:512 + iw],
                            kh_ring.ap()[64:128, SEQ * r + 128 * jt: SEQ * r + 128 * (jt + 1)],
                            qh_ring.ap()[64:128, SLABT * r + i0: SLABT * r + i0 + iw],
                            start=True, stop=True, tile_position=(64, 0)),
                            inc="mm")
                        er = jt % 3
                        if iw == 512:
                            tex = P.emit("scalar", lambda e, er=er, sp=sp: e.activation(
                                e_ring.ap()[:, 1024 * er: 1024 * (er + 1)], sp.ap()[:, :],
                                AF.Exp, bias=negB.ap()[:, 0:1], scale=0.125),
                                waits=(tqk, av_tok.get(er)), inc="sc")
                        else:
                            P.emit("scalar", lambda e, er=er, iw=iw, sp=sp: e.activation(
                                e_ring.ap()[:, 1024 * er: 1024 * er + iw], sp.ap()[:, 0:iw],
                                AF.Exp, bias=negB.ap()[:, 0:1], scale=0.125),
                                waits=(tqk, av_tok.get(er)), inc="sc")
                            tex = P.emit("scalar", lambda e, er=er, iw=iw, sp=sp: e.activation(
                                e_ring.ap()[:, 1024 * er + 512: 1024 * er + 512 + iw],
                                sp.ap()[:, 512:512 + iw],
                                AF.Exp, bias=negB.ap()[:, 0:1], scale=0.125), inc="sc")
                        H["bank_free"][spk] = tex

                        def emit_av(jt, er, tex, iw=iw, ec=ec, bR0=bR0, bR1=bR1,
                                    tevprev=tevprev):
                            P.emit("tensor", lambda e, jt=jt, er=er: e.matmul(
                                bR0.ap()[0:65, 0:iw],
                                v_sb.ap()[:, 1040 * jt + 65 * (2 * ec): 1040 * jt + 65 * (2 * ec) + 65],
                                e_ring.ap()[:, 1024 * er: 1024 * er + iw],
                                start=(jt == 0), stop=(jt == NT - 1)),
                                waits=(tex,) + (tevprev if jt == 0 else ()), inc=None)
                            tav = P.emit("tensor", lambda e, jt=jt, er=er: e.matmul(
                                bR1.ap()[0:65, 0:iw],
                                v_sb.ap()[:, 1040 * jt + 65 * (2 * ec + 1): 1040 * jt + 65 * (2 * ec + 1) + 65],
                                e_ring.ap()[:, 1024 * er + 512: 1024 * er + 512 + iw],
                                start=(jt == 0), stop=(jt == NT - 1)),
                                inc="mm")
                            av_tok[er] = tav
                            return tav

                        if pend is not None:
                            tav = emit_av(*pend)
                        pend = (jt, er, tex)
                    tav = emit_av(*pend)
                    tr0 = P.emit("vector", lambda e, iw=iw, bR0=bR0: e.tensor_copy(
                        rec_sb.ap()[64:65, 0:iw], bR0.ap()[64:65, 0:iw]), waits=(tav,), inc="v")
                    tr1 = P.emit("vector", lambda e, iw=iw, bR1=bR1: e.tensor_copy(
                        rec_sb.ap()[64:65, 512:512 + iw], bR1.ap()[64:65, 0:iw]), inc="v")
                    tb0 = P.emit("tensor", lambda e, iw=iw: e.matmul(
                        bank[6].ap()[0:64, 0:iw], ones64.ap()[64:65, :],
                        rec_sb.ap()[64:65, 0:iw], start=True, stop=True),
                        waits=(tr0, tr1, bfree(bank[6]), bfree(bank[7])), inc="mm")
                    tb1 = P.emit("tensor", lambda e, iw=iw: e.matmul(
                        bank[7].ap()[0:64, 0:iw], ones64.ap()[64:65, :],
                        rec_sb.ap()[64:65, 512:512 + iw], start=True, stop=True),
                        inc="mm")
                    td0 = P.emit("vector", lambda e, iw=iw: e.reciprocal(
                        den_sb.ap()[0:64, 0:iw], bank[6].ap()[0:64, 0:iw]),
                        waits=(tb0, tb1), inc="v")
                    td1 = P.emit("vector", lambda e, iw=iw: e.reciprocal(
                        den_sb.ap()[0:64, 512:512 + iw], bank[7].ap()[0:64, 0:iw]),
                        inc="v")
                    te0 = P.emit("vector", lambda e, ec=ec, i0=i0, iw=iw, bR0=bR0: e.tensor_tensor(
                        attnout[ec].ap()[0:64, i0:i0 + iw], bR0.ap()[0:64, 0:iw],
                        den_sb.ap()[0:64, 0:iw], ALU.mult), waits=(td0,), inc="v")
                    te1 = P.emit("vector", lambda e, iw=iw, bR1=bR1: e.tensor_tensor(
                        h1tmp.ap()[0:64, 0:iw], bR1.ap()[0:64, 0:iw],
                        den_sb.ap()[0:64, 512:512 + iw], ALU.mult),
                        waits=(td1, kq_free.get(("h1",))), inc="v")
                    tdm = P.emit("scalar", lambda e, ec=ec, i0=i0, iw=iw: e.dma_start(
                        out=attnout[ec].ap()[64:128, i0:i0 + iw], in_=h1tmp.ap()[0:64, 0:iw]),
                        waits=(te1,), stream='h1s')
                    kq_free[("h1",)] = tdm
                    bset(bank[6], td0); bset(bank[7], td1)
                    bset(bR0, te0); bset(bR1, te1)
                kq_free[r] = P.tok("mm")
                pump_tern(3)
            t_attn_done = (P.tok("v"), P.tok("q_h1s"))
            P.barrier()

        if upto == "attn":
            P.barrier()
            phv.close()
            phWQ.close()
            block = E(nc.Block())
            P.run_all(block)
            return nc

        with ExitStack() as ph1c:
            E1c = ph1c.enter_context
            wo_sb = E1c(nc.sbuf_tensor("wo_sb", [128, NFC * 1024], BF16))
            x0ring = E1c(nc.sbuf_tensor("x0ring", [128, 4 * 512], F32))
            x1stg = E1c(nc.sbuf_tensor("x1stg", [128, 4 * 512], F32))
            P.emit("gpsimd", lambda e: e.dma_start(
                out=wo_sb.ap().rearrange("p (f n) -> p f n", f=NFC),
                in_=dr["WoT0"].ap().rearrange("(f p) n -> p f n", p=128)), stream='gw0')
            t_wo = P.tok("q_gw0")
            ring_free = {}
            for ic, (i0, iw) in enumerate(I_CHUNKS):
                for dc in range(NFC):
                    r = (ic * NFC + dc) % 2
                    tx0 = P.emit("sync", lambda e, dc=dc, i0=i0, iw=iw, r=r: e.dma_start(
                        out=x0ring.ap()[:, 1024 * r: 1024 * r + iw],
                        in_=dr["x0T_slab"].ap()[128 * dc:128 * (dc + 1), i0:i0 + iw]),
                        waits=(ring_free.get(("x0", r)),), stream=f'x0l{r}')
                    bk = bank[0] if dc % 2 == 0 else bank[1]
                    for ec in range(NFC):
                        P.emit("tensor", lambda e, dc=dc, ec=ec, i0=i0, iw=iw, bk=bk: e.matmul(
                            bk.ap()[:, 0:iw],
                            wo_sb.ap()[:, 1024 * ec + 128 * dc: 1024 * ec + 128 * (dc + 1)],
                            attnout[ec].ap()[:, i0:i0 + iw],
                            start=(ec == 0), stop=(ec == NFC - 1)),
                            waits=((t_wo, bfree(bk)) + t_attn_done) if ec == 0 else (),
                            inc="mm" if ec == NFC - 1 else None)
                    tmm = P.tok("mm")
                    tev = P.emit("vector", lambda e, dc=dc, r=r, iw=iw, bk=bk: e.scalar_tensor_tensor(
                        x1stg.ap()[:, 1024 * r: 1024 * r + iw],
                        bk.ap()[:, 0:iw], bo_sb.ap()[:, dc:dc + 1],
                        x0ring.ap()[:, 1024 * r: 1024 * r + iw],
                        ALU.add, ALU.add),
                        waits=(tmm, tx0, t_small, ring_free.get(("x1", r))), inc="v")
                    bset(bk, tev)
                    ring_free[("x0", r)] = tev
                    ring_free[("x1", r)] = P.emit(
                        "scalar", lambda e, dc=dc, i0=i0, iw=iw, r=r: e.dma_start(
                            out=dr["sp_x1"].ap()[128 * dc:128 * (dc + 1), i0:i0 + iw],
                            in_=x1stg.ap()[:, 1024 * r: 1024 * r + iw]),
                        waits=(tev,), stream=f'x1s{r}')
                pump_tern(3)
            t_x1_sp = (P.tok("q_x1s0"), P.tok("q_x1s1"))
            if "x1" in dbg:
                dout("dbg_x1", (D, SLABT), F32)
                P.wait("sync", t_x1_sp)
                P.emit("sync", lambda e: e.dma_start(
                    out=dr["dbg_x1"].ap()[:, :], in_=dr["sp_x1"].ap()[:, :]), stream='dbg')
            P.barrier()

        wq_finish()
        P.barrier()
        phv.close()
        phWQ.close()

        if upto == "x1":
            P.wait("sync", *P.all_toks())
            block = E(nc.Block())
            P.run_all(block)
            return nc

        t_w1t = wq_state[("done", 0)]
        t_w2t = wq_state[("done", 1)]
        dr["sp_y"] = nc.dram_tensor("sp_y", [HID, OWN], BF16)
        dr["sp_yq"] = nc.dram_tensor("sp_yq", [HID, OWN], BF16)
        cw_sb = E(nc.sbuf_tensor("cw_sb", [128, NHC * 3], F32))
        cb_sb = E(nc.sbuf_tensor("cb_sb", [128, NHC], F32))
        amel = E(nc.sbuf_tensor("amel", [128, 17], F32))
        amtm = E(nc.sbuf_tensor("amtm", [128, 17], F32))
        P.emit("sync", lambda e: e.dma_start(
            out=cw_sb.ap().rearrange("p (h t) -> p h t", h=NHC),
            in_=dr["convw"].ap().rearrange("(h p) t -> p h t", p=128)), stream='sm0')
        P.emit("sync", lambda e: e.dma_start(
            out=cb_sb.ap().rearrange("p h -> p h ()"),
            in_=dr["convb"].ap().rearrange("(h p) o -> p h o", p=128),
            allow_slow_non_contiguous=True), stream='sm1')
        tam = P.emit("sync", lambda e: e.dma_start(out=amtm.ap()[:, :], in_=dr["amask_tm"].ap()[:, :]),
                     stream='sm2')
        P.emit("vector", lambda e: e.tensor_single_scalar(
            amel.ap()[:, :], amtm.ap()[:, :], 1.0 / 127.0, ALU.mult), waits=(tam,), inc="v")
        t_cw = (P.tok("q_sm0"), P.tok("q_sm1"), P.tok("q_sm2"))
        t_amel = P.tok("v")

        with ExitStack() as ph2:
            E2o = ph2.enter_context
            xq2 = [E2o(nc.sbuf_tensor(f"xq2_{fc}", [128, SLABT], BF16)) for fc in range(NFC)]
            alpha = E2o(nc.sbuf_tensor("alpha_sb", [128, SLABT], F32))
            my_sb = E2o(nc.sbuf_tensor("my_sb", [128, OWN], F32))

            with ExitStack() as ph2a:
                E2 = ph2a.enter_context
                ring2 = E2(nc.sbuf_tensor("ln1_ring", [128, 2 * NFC * 512], F32))
                rows2 = E2(nc.sbuf_tensor("ln1_rows", [1, 2 * 4 * 512], F32))
                sq2 = E2(nc.sbuf_tensor("ln1_sq", [128, NFC * 512], BF16))
                xh2r = E2(nc.sbuf_tensor("xh2r", [128, NFC * 512], F32))
                mfold = E2(nc.sbuf_tensor("mfold", [128, 512], F32))
                fscr2 = E2(nc.sbuf_tensor("fscr2", [128, 512], F32))
                amax2 = E2(nc.sbuf_tensor("amax2", [128, 4], F32))
                srep = E2(nc.sbuf_tensor("srep2", [128, 129], F32))
                s1b = E2(nc.sbuf_tensor("s1b", [128, 512], F32))
                P.emit("gpsimd", lambda e: e.memset(amax2.ap()[:, :], 1.0), inc="g")
                t_am2i = P.tok("g")
                lng = {"mf": None, "sr": None}

                def ln1_src(cb, c0, w, r, free_tok):
                    ld = P.emit("sync", lambda e, c0=c0, w=w, r=r: e.dma_start(
                        out=mkap(ring2.ap().tensor, NFC * 512 * r,
                                 [[2 * NFC * 512, 128], [512, NFC], [1, w]]),
                        in_=dr["sp_x1"].ap()[:, c0:c0 + w].rearrange("(f p) w -> p f w", p=128)),
                        waits=(free_tok, t_x1_sp), stream=f'lnl{r}')
                    def tile(fc, r=r, w=w):
                        return ring2.ap()[:, NFC * 512 * r + 512 * fc: NFC * 512 * r + 512 * fc + w]
                    return ld, tile

                def ln1_out(cb, c0, w, r, bA, bB, toks, tile, ld):
                    P.wait("vector", *toks)
                    I32 = mybir.dt.int32
                    tts = []
                    for fc in range(NFC):
                        P.emit("vector", lambda e, fc=fc, w=w, bA=bA, tile=tile: e.tensor_tensor(
                            xh2r.ap()[:, 512 * fc:512 * fc + w], tile(fc), bA.ap()[:, 0:w],
                            ALU.mult), inc="v")
                        tt = P.emit("vector", lambda e, fc=fc, w=w, bB=bB: e.tensor_tensor(
                            xh2r.ap()[:, 512 * fc:512 * fc + w], xh2r.ap()[:, 512 * fc:512 * fc + w],
                            bB.ap()[:, 0:w], ALU.add), inc="v")
                        tts.append(tt)
                    if LN1_STOP and LN1_STOP < 2:
                        return tts[-1]
                    tf = P.emit("vector", lambda e, w=w: e.tensor_single_scalar(
                        mfold.ap()[:, 0:w].bitcast(I32), xh2r.ap()[:, 0:w].bitcast(I32),
                        0x7FFFFFFF, ALU.bitwise_and), waits=(tts[0], lng["mf"]), inc="v")
                    for fc in range(1, NFC):
                        P.emit("vector", lambda e, fc=fc, w=w: e.tensor_single_scalar(
                            fscr2.ap()[:, 0:w].bitcast(I32),
                            xh2r.ap()[:, 512 * fc:512 * fc + w].bitcast(I32),
                            0x7FFFFFFF, ALU.bitwise_and), waits=(tts[fc],), inc="v")
                        tf = P.emit("vector", lambda e, w=w: e.tensor_tensor(
                            mfold.ap()[:, 0:w], mfold.ap()[:, 0:w], fscr2.ap()[:, 0:w], ALU.max),
                            inc="v")
                    if LN1_STOP and LN1_STOP < 3:
                        return tf
                    ng = (w + 127) // 128
                    tam_ = []
                    for g in range(ng):
                        gw = min(128, w - 128 * g)
                        ttr = P.emit("tensor", lambda e, g=g, gw=gw: e.transpose(
                            bank[2].ap()[0:gw, 0:128],
                            mfold.ap()[:, 128 * g:128 * g + gw], ident.ap()[:, :]),
                            waits=(tf, t_am2i, t_const, bfree(bank[2])),
                            inc="mm")
                        tam_.append(P.emit("vector", lambda e, g=g, gw=gw: e.reduce_max(
                            amax2.ap()[0:gw, g:g + 1], bank[2].ap()[0:gw, 0:128],
                            axis=AXX), waits=(ttr,), inc="v"))
                        bset(bank[2], tam_[-1])
                        lng["mf"] = ttr
                    tcl = P.emit("vector", lambda e, ng=ng: e.tensor_single_scalar(
                        amax2.ap()[:, 0:ng], amax2.ap()[:, 0:ng], 1e-5, ALU.max),
                        waits=tuple(tam_), inc="v")
                    if LN1_STOP and LN1_STOP < 4:
                        return tcl
                    for g in range(ng):
                        gw = min(128, w - 128 * g)
                        gabs = c0 // 128 + g
                        trc = P.emit("vector", lambda e, g=g: e.reciprocal(
                            srep.ap()[:, 128:129], amax2.ap()[:, g:g + 1]), waits=(tcl,), inc="v")
                        trp = P.emit("vector", lambda e: e.tensor_scalar(
                            srep.ap()[:, 0:128], ones128.ap()[:, :],
                            srep.ap()[:, 128:129], 127.0, ALU.mult, ALU.mult),
                            waits=(trc, lng["sr"]), inc="v")
                        ttr = P.emit("tensor", lambda e: e.transpose(
                            bank[3].ap()[:, 0:128], srep.ap()[:, 0:128], ident.ap()[:, :]),
                            waits=(trp, bfree(bank[3])), inc="mm")
                        tcp = P.emit("vector", lambda e, g=g, gw=gw: e.tensor_copy(
                            s1b.ap()[:, 128 * g:128 * g + gw], bank[3].ap()[:, 0:gw]),
                            waits=(ttr,), inc="v")
                        bset(bank[3], tcp)
                        if LN1_STOP and LN1_STOP < 5:
                            last = tcp
                            continue
                        ta1 = P.emit("vector", lambda e, g=g, gabs=gabs: e.tensor_scalar(
                            srep.ap()[:, 128:129], amax2.ap()[:, g:g + 1],
                            thr_cols.ap()[:, 1:2], amel.ap()[:, gabs:gabs + 1],
                            ALU.mult, ALU.mult),
                            waits=(tcl, t_amel), inc="v")
                        tra = P.emit("vector", lambda e: e.tensor_scalar(
                            srep.ap()[:, 0:128], ones128.ap()[:, :],
                            srep.ap()[:, 128:129], None, ALU.mult),
                            waits=(ta1, ttr), inc="v")
                        ttr2 = P.emit("tensor", lambda e: e.transpose(
                            bank[3].ap()[:, 0:128], srep.ap()[:, 0:128], ident.ap()[:, :]),
                            waits=(tra, bfree(bank[3])), inc="mm")
                        lng["sr"] = ttr2
                        tca = P.emit("vector", lambda e, gabs=gabs, gw=gw: e.tensor_copy(
                            alpha.ap()[:, 128 * gabs:128 * gabs + gw], bank[3].ap()[:, 0:gw]),
                            waits=(ttr2,), inc="v")
                        bset(bank[3], tca)
                        last = tca
                        if LN1_STOP and LN1_STOP < 6:
                            continue
                        for fc in range(NFC):
                            P.emit("vector", lambda e, fc=fc, g=g, gw=gw: e.tensor_tensor(
                                fscr2.ap()[:, 0:gw],
                                xh2r.ap()[:, 512 * fc + 128 * g: 512 * fc + 128 * g + gw],
                                s1b.ap()[:, 128 * g:128 * g + gw], ALU.mult),
                                waits=(tcp,), inc="v")
                            last = P.emit("vector", lambda e, fc=fc, c0=c0, g=g, gw=gw: e.tensor_scalar(
                                xq2[fc].ap()[:, c0 + 128 * g: c0 + 128 * g + gw],
                                fscr2.ap()[:, 0:gw],
                                MRND, MRND, ALU.add, ALU.subtract), inc="v")
                    return last

                emit_ln("ln1", ln1_src, SLABT, ln1_out, rows2, sq2)
                t_xq2 = P.tok("v")
                P.barrier()
                if "ln1dbg" in dbg:
                    dout("dbg_amax2", (128, 4), F32)
                    dout("dbg_mfold", (128, 512), F32)
                    dout("dbg_xh2r", (128, NFC * 512), F32)
                    P.emit("sync", lambda e: e.dma_start(
                        out=dr["dbg_amax2"].ap()[:, :], in_=amax2.ap()[:, :]), stream='dbg')
                    P.emit("sync", lambda e: e.dma_start(
                        out=dr["dbg_mfold"].ap()[:, :], in_=mfold.ap()[:, :]), stream='dbg')
                    P.emit("sync", lambda e: e.dma_start(
                        out=dr["dbg_xh2r"].ap()[:, :], in_=xh2r.ap()[:, :]), stream='dbg')
                    P.barrier()
                if "ffn1" in dbg:
                    dout("dbg_xq2", (D, SLABT), BF16)
                    dout("dbg_alpha", (128, SLABT), F32)
                    for fc in range(NFC):
                        P.emit("sync", lambda e, fc=fc: e.dma_start(
                            out=dr["dbg_xq2"].ap()[128 * fc:128 * (fc + 1), :],
                            in_=xq2[fc].ap()[:, :]), stream='dbg')
                    P.emit("sync", lambda e: e.dma_start(
                        out=dr["dbg_alpha"].ap()[:, :], in_=alpha.ap()[:, :]), stream='dbg')
                    P.barrier()

            if upto == "ln1":
                P.barrier()
                ph2.close()
                block = E(nc.Block())
                P.run_all(block)
                return nc

            with ExitStack() as ph2b:
                E2 = ph2b.enter_context
                w1ring = E2(nc.sbuf_tensor("w1ring", [128, 2 * NFC * 128], BF16))
                hrow = E2(nc.sbuf_tensor("hrow", [128, 2 * SLABT], F32))
                cscr = E2(nc.sbuf_tensor("cscr", [128, OWN], F32))
                ystg = E2(nc.sbuf_tensor("ystg", [128, 2 * OWN], BF16))
                yabs = E2(nc.sbuf_tensor("yabs", [128, OWN], BF16))
                w1_free = {}
                hrow_free = {}
                yst_free = {}
                y_tok = {}
                tmy = None
                for hc in range(NHC):
                    r = hc % 2
                    tw = P.emit("sync", lambda e, hc=hc, r=r: e.dma_start(
                        out=w1ring.ap()[:, NFC * 128 * r: NFC * 128 * (r + 1)].rearrange(
                            "p (f w) -> p f w", f=NFC),
                        in_=dr["sp_w1t"].ap()[:, 128 * hc:128 * (hc + 1)].rearrange(
                            "(f p) w -> p f w", p=128)),
                        waits=(w1_free.get(r), t_w1t), stream=f'w1l{r}')
                    rh = hc % 2
                    tev_all = []
                    for tb, (i0, iw) in enumerate(I_CHUNKS):
                        bk = bank[tb % 4]
                        for fc in range(NFC):
                            P.emit("tensor", lambda e, fc=fc, r=r, i0=i0, iw=iw, bk=bk: e.matmul(
                                bk.ap()[:, 0:iw],
                                w1ring.ap()[:, NFC * 128 * r + 128 * fc: NFC * 128 * r + 128 * (fc + 1)],
                                xq2[fc].ap()[:, i0:i0 + iw],
                                start=(fc == 0), stop=(fc == NFC - 1)),
                                waits=(tw, t_xq2, bfree(bk)) if fc == 0 else (),
                                inc="mm" if fc == NFC - 1 else None)
                        tmm = P.tok("mm")
                        tev = P.emit("vector", lambda e, rh=rh, i0=i0, iw=iw, bk=bk: e.tensor_tensor(
                            hrow.ap()[:, SLABT * rh + i0: SLABT * rh + i0 + iw],
                            bk.ap()[:, 0:iw], alpha.ap()[:, i0:i0 + iw], ALU.mult),
                            waits=(tmm, hrow_free.get(rh)), inc="v")
                        bset(bk, tev)
                        tev_all.append(tev)
                    w1_free[r] = tev_all[-1]
                    h3d = hrow.ap()[:, SLABT * rh: SLABT * (rh + 1)].rearrange(
                        "p (s u) -> p s u", u=SLABW)
                    cv = cscr.ap()[:, 0:OWN].rearrange("p (s u) -> p s u", u=128)
                    tc1 = P.emit("vector", lambda e, hc=hc, h3d=h3d, cv=cv: e.tensor_scalar(
                        cv, h3d[:, :, 0:128], cw_sb.ap()[:, 3 * hc: 3 * hc + 1],
                        cb_sb.ap()[:, hc:hc + 1], ALU.mult, ALU.add),
                        waits=tuple(tev_all) + (t_cw, hrow_free.get(("cs",))), inc="v")
                    tc2 = P.emit("vector", lambda e, hc=hc, h3d=h3d, cv=cv: e.scalar_tensor_tensor(
                        cv, h3d[:, :, 1:129], cw_sb.ap()[:, 3 * hc + 1: 3 * hc + 2],
                        cv, ALU.mult, ALU.add), inc="v")
                    tc3 = P.emit("vector", lambda e, hc=hc, h3d=h3d, cv=cv: e.scalar_tensor_tensor(
                        cv, h3d[:, :, 2:130], cw_sb.ap()[:, 3 * hc + 2: 3 * hc + 3],
                        cv, ALU.mult, ALU.add), inc="v")
                    hrow_free[rh] = tc3
                    tsi = P.emit("scalar", lambda e, hc=hc, r=r: e.activation(
                        ystg.ap()[:, OWN * r: OWN * (r + 1)], cscr.ap()[:, 0:OWN],
                        AF.Silu, bias=0.0, scale=1.0),
                        waits=(tc3, yst_free.get(("d", r)), yst_free.get(("f", r))), inc="sc")
                    hrow_free[("cs",)] = tsi
                    yst_free[("d", r)] = P.emit("scalar", lambda e, hc=hc, r=r: e.dma_start(
                        out=dr["sp_y"].ap()[128 * hc:128 * (hc + 1), :],
                        in_=ystg.ap()[:, OWN * r: OWN * (r + 1)]),
                        waits=(tsi,), stream=f'ys{r}')
                    y_tok[hc] = yst_free[("d", r)]
                    tya = P.emit("scalar", lambda e, r=r: e.activation(
                        yabs.ap()[:, :], ystg.ap()[:, OWN * r: OWN * (r + 1)],
                        AF.Abs, bias=0.0, scale=1.0), waits=(tsi, tmy), inc="sc")
                    if hc == 0:
                        tmy = P.emit("vector", lambda e: e.tensor_copy(my_sb.ap()[:, :], yabs.ap()[:, :]),
                                     waits=(tya,), inc="v")
                    else:
                        tmy = P.emit("vector", lambda e: e.tensor_tensor(
                            my_sb.ap()[:, :], my_sb.ap()[:, :], yabs.ap()[:, :], ALU.max),
                            waits=(tya, tmy), inc="v")
                    yst_free[("f", r)] = tmy
                t_y_sp = (P.tok("q_ys0"), P.tok("q_ys1"))
                t_my = tmy
                P.barrier()

            if upto == "w1y":
                P.barrier()
                ph2.close()
                block = E(nc.Block())
                P.run_all(block)
                return nc

            with ExitStack() as ph2c:
                E2 = ph2c.enter_context
                amaxg = E2(nc.sbuf_tensor("amaxg", [128, 16], F32))
                s2b = E2(nc.sbuf_tensor("s2b", [128, OWN], F32))
                gmb = E2(nc.sbuf_tensor("gmb", [128, OWN], F32))
                srepc = E2(nc.sbuf_tensor("srep_c", [128, 129], F32))
                yring = E2(nc.sbuf_tensor("yring", [128, 2 * OWN], BF16))
                yqs = E2(nc.sbuf_tensor("yqs", [128, 2 * OWN], BF16))
                qscr = E2(nc.sbuf_tensor("qscr", [128, OWN], F32))
                P.emit("gpsimd", lambda e: e.memset(amaxg.ap()[:, :], 1.0), inc="g")
                t_agi = P.tok("g")
                trd = None
                for g in range(16):
                    ttr = P.emit("tensor", lambda e, g=g: e.transpose(
                        bank[2].ap()[:, 0:128], my_sb.ap()[:, 128 * g:128 * (g + 1)],
                        ident.ap()[:, :]), waits=(t_my, t_agi, bfree(bank[2])), inc="mm")
                    trd = P.emit("vector", lambda e, g=g: e.reduce_max(
                        amaxg.ap()[:, g:g + 1], bank[2].ap()[:, 0:128], axis=AXX),
                        waits=(ttr,), inc="v")
                    bset(bank[2], trd)
                tcl = P.emit("vector", lambda e: e.tensor_single_scalar(
                    amaxg.ap()[:, :], amaxg.ap()[:, :], 1e-5, ALU.max), waits=(trd,), inc="v")
                tcp = tcg = None
                cg_sr = None
                for g in range(16):
                    trc = P.emit("vector", lambda e, g=g: e.reciprocal(
                        srepc.ap()[:, 128:129], amaxg.ap()[:, g:g + 1]), waits=(tcl,), inc="v")
                    trp = P.emit("vector", lambda e: e.tensor_scalar(
                        srepc.ap()[:, 0:128], ones128.ap()[:, :],
                        srepc.ap()[:, 128:129], 127.0, ALU.mult, ALU.mult),
                        waits=(trc, cg_sr), inc="v")
                    ttr = P.emit("tensor", lambda e: e.transpose(
                        bank[3].ap()[:, 0:128], srepc.ap()[:, 0:128], ident.ap()[:, :]),
                        waits=(trp, bfree(bank[3])), inc="mm")
                    tcp = P.emit("vector", lambda e, g=g: e.tensor_copy(
                        s2b.ap()[:, 128 * g:128 * (g + 1)], bank[3].ap()[:, 0:128]),
                        waits=(ttr,), inc="v")
                    bset(bank[3], tcp)
                    ta1 = P.emit("vector", lambda e, g=g: e.tensor_scalar(
                        srepc.ap()[:, 128:129], amaxg.ap()[:, g:g + 1],
                        thr_cols.ap()[:, 3:4], 1.0 / 127.0, ALU.mult, ALU.mult),
                        waits=(tcl,), inc="v")
                    tra = P.emit("vector", lambda e: e.tensor_scalar(
                        srepc.ap()[:, 0:128], ones128.ap()[:, :],
                        srepc.ap()[:, 128:129], None, ALU.mult),
                        waits=(ta1, ttr), inc="v")
                    ttr2 = P.emit("tensor", lambda e: e.transpose(
                        bank[3].ap()[:, 0:128], srepc.ap()[:, 0:128], ident.ap()[:, :]),
                        waits=(tra, bfree(bank[3])), inc="mm")
                    cg_sr = ttr2
                    tcg = P.emit("vector", lambda e, g=g: e.tensor_copy(
                        gmb.ap()[:, 128 * g:128 * (g + 1)], bank[3].ap()[:, 0:128]),
                        waits=(ttr2,), inc="v")
                    bset(bank[3], tcg)
                t_s2 = P.tok("v")
                yq_free = {}
                yq_tok = {}
                for hc in range(NHC):
                    r = hc % 2
                    tyl = P.emit("sync", lambda e, hc=hc, r=r: e.dma_start(
                        out=yring.ap()[:, OWN * r: OWN * (r + 1)],
                        in_=dr["sp_y"].ap()[128 * hc:128 * (hc + 1), :]),
                        waits=(yq_free.get(("l", r)), y_tok[hc]), stream=f'yl{r}')
                    tq1 = P.emit("vector", lambda e, r=r: e.tensor_tensor(
                        qscr.ap()[:, :], yring.ap()[:, OWN * r: OWN * (r + 1)],
                        s2b.ap()[:, :], ALU.mult), waits=(tyl, t_s2), inc="v")
                    yq_free[("l", r)] = tq1
                    tq2 = P.emit("vector", lambda e, r=r: e.tensor_scalar(
                        yqs.ap()[:, OWN * r: OWN * (r + 1)], qscr.ap()[:, :],
                        MRND, MRND, ALU.add, ALU.subtract),
                        waits=(yq_free.get(("s", r)),), inc="v")
                    yq_free[("s", r)] = P.emit("scalar", lambda e, hc=hc, r=r: e.dma_start(
                        out=dr["sp_yq"].ap()[128 * hc:128 * (hc + 1), :],
                        in_=yqs.ap()[:, OWN * r: OWN * (r + 1)]),
                        waits=(tq2,), stream=f'yqs{r}')
                    yq_tok[hc] = yq_free[("s", r)]
                t_yq_sp = (P.tok("q_yqs0"), P.tok("q_yqs1"))
                if upto == "yq":
                    P.barrier()
                    ph2c.close()
                    ph2.close()
                    block = E(nc.Block())
                    P.run_all(block)
                    return nc

                w2ring = E2(nc.sbuf_tensor("w2ring", [128, 2 * 4 * 128], BF16))
                yqring = E2(nc.sbuf_tensor("yqring", [128, 2 * 512], BF16))
                x1ring = E2(nc.sbuf_tensor("x1iring", [128, 512], F32))
                x2stg = E2(nc.sbuf_tensor("x2stg", [128, 2 * 512], F32))
                w2_free = {}
                mm_slot = {}
                for quad in range(2):
                    for tb, (i0, iw) in enumerate(T_CHUNKS):
                        bks = [bank[2 * i + (tb % 2)] for i in range(4)]
                        for hc in range(NHC):
                            r = hc % 2
                            tw2 = P.emit("sync", lambda e, quad=quad, hc=hc, r=r: e.dma_start(
                                out=w2ring.ap()[:, 512 * r: 512 * (r + 1)].rearrange(
                                    "p (d w) -> p d w", d=4),
                                in_=dr["sp_w2t"].ap()[128 * hc:128 * (hc + 1),
                                                      512 * quad:512 * (quad + 1)].rearrange(
                                    "p (d w) -> p d w", d=4)),
                                waits=(mm_slot.get(r), t_w2t), stream=f'w2l{r}')
                            tyq = P.emit("scalar", lambda e, hc=hc, i0=i0, r=r: e.dma_start(
                                out=yqring.ap()[:, 512 * r: 512 * (r + 1)],
                                in_=dr["sp_yq"].ap()[128 * hc:128 * (hc + 1), i0:i0 + 512]),
                                waits=(mm_slot.get(r), yq_tok[hc]), stream=f'yql{r}')
                            for dx in range(4):
                                P.emit("tensor", lambda e, dx=dx, r=r, hc=hc, bk=bks[dx]: e.matmul(
                                    bk.ap()[:, 0:512],
                                    w2ring.ap()[:, 512 * r + 128 * dx: 512 * r + 128 * (dx + 1)],
                                    yqring.ap()[:, 512 * r: 512 * (r + 1)],
                                    start=(hc == 0), stop=(hc == NHC - 1)),
                                    waits=((tw2, tyq) +
                                           tuple(bfree(bks[i]) for i in range(4))) if dx == 0 else (),
                                    inc="mm" if dx == 3 else None)
                            mm_slot[r] = P.tok("mm")
                        tmm = P.tok("mm")
                        for dx in range(4):
                            dc = 4 * quad + dx
                            bk = bks[dx]
                            tx1 = P.emit("scalar", lambda e, dc=dc, i0=i0: e.dma_start(
                                out=x1ring.ap()[:, 0:512],
                                in_=mkap(dr["sp_x1"].ap().tensor,
                                         128 * dc * SLABT + 1 + (i0 // 128) * SLABW,
                                         [[SLABT, 128], [SLABW, 4], [1, 128]])),
                                waits=(w2_free.get(("x1",)),), stream='x1i')
                            tg1 = P.emit("vector", lambda e, dx=dx, i0=i0, bk=bk: e.tensor_tensor(
                                x2stg.ap()[:, 512 * (dx % 2): 512 * (dx % 2) + 512],
                                bk.ap()[:, 0:512], gmb.ap()[:, i0:i0 + 512], ALU.mult),
                                waits=(tmm, w2_free.get(("st", dx % 2))), inc="v")
                            tg2 = P.emit("vector", lambda e, dx=dx: e.scalar_tensor_tensor(
                                x2stg.ap()[:, 512 * (dx % 2): 512 * (dx % 2) + 512],
                                x1ring.ap()[:, 0:512], 1.0,
                                x2stg.ap()[:, 512 * (dx % 2): 512 * (dx % 2) + 512],
                                ALU.mult, ALU.add), waits=(tx1,), inc="v")
                            bset(bk, tg2)
                            w2_free[("x1",)] = tg2
                            w2_free[("st", dx % 2)] = P.emit(
                                "scalar", lambda e, dc=dc, i0=i0, dx=dx: e.dma_start(
                                    out=dr["sp_x2"].ap()[128 * dc:128 * (dc + 1), i0:i0 + 512],
                                    in_=x2stg.ap()[:, 512 * (dx % 2): 512 * (dx % 2) + 512]),
                                waits=(tg2,), stream=f'x2s{dx % 2}')
                t_x2_sp = (P.tok("q_x2s0"), P.tok("q_x2s1"))
                if "x2" in dbg:
                    dout("dbg_x2", (D, OWN), F32)
                    P.wait("sync", t_x2_sp)
                    P.emit("sync", lambda e: e.dma_start(
                        out=dr["dbg_x2"].ap()[:, :], in_=dr["sp_x2"].ap()[:, :]), stream='dbg')
                P.barrier()

        if upto == "x2":
            P.wait("sync", *P.all_toks())
            block = E(nc.Block())
            P.run_all(block)
            return nc

        ph4o = ExitStack()
        E4o = ph4o.enter_context
        bq_sb = E4o(nc.sbuf_tensor("bq_sb", [128, 8], F32))
        bk_sb = E4o(nc.sbuf_tensor("bk_sb", [128, 8], F32))
        bv_sb = E4o(nc.sbuf_tensor("bv_sb", [128, 8], F32))
        for bi_, (nm, sbb) in enumerate((("bq1", bq_sb), ("bk1", bk_sb), ("bv1", bv_sb))):
            P.emit("sync", lambda e, nm=nm, sbb=sbb: e.dma_start(
                out=sbb.ap().rearrange("p a -> p a ()"),
                in_=dr[nm].ap().rearrange("(a p) o -> p a o", p=128),
                allow_slow_non_contiguous=True), stream=f'sm{bi_}')
        t_b1 = (P.tok("q_sm0"), P.tok("q_sm1"), P.tok("q_sm2"))

        e_off = {}
        _off = 0
        for jt in range(NT):
            e_off[jt] = _off
            _off += (4 - jt // 8) * 512
        E_COLS = _off

        negm_sb = E4o(nc.sbuf_tensor("negm_sb", [128, NT], F32))
        sown_sb = E4o(nc.sbuf_tensor("sown_sb", [128, NT], F32))
        fst_sb = E4o(nc.sbuf_tensor("fst_sb", [128, NT], F32))
        fbf_sb = E4o(nc.sbuf_tensor("fbf_sb", [128, NT], BF16))

        ph45 = ExitStack()
        qT = [ph45.enter_context(nc.sbuf_tensor(f"qT_{fc}", [128, OWN], BF16))
              for fc in range(NFC)]

        ph34 = ExitStack()
        h3own = [ph34.enter_context(nc.sbuf_tensor(f"h3o_{fc}", [128, OWN], BF16))
                 for fc in range(NFC)]
        with ExitStack() as ph3:
            E3 = ph3.enter_context
            ring3 = E3(nc.sbuf_tensor("ln2_ring", [128, 2 * NFC * 512], F32))
            rows3 = E3(nc.sbuf_tensor("ln2_rows", [1, 2 * 4 * 512], F32))
            sq3 = E3(nc.sbuf_tensor("ln2_sq", [128, NFC * 512], BF16))
            h3scr = E3(nc.sbuf_tensor("h3scr", [128, 512], F32))
            zstg = E3(nc.sbuf_tensor("zstg", [128, 2 * 512], F32))
            zs_g = {}

            def ln2_src(cb, c0, w, r, free_tok):
                ld = P.emit("sync", lambda e, c0=c0, w=w, r=r: e.dma_start(
                    out=mkap(ring3.ap().tensor, NFC * 512 * r,
                             [[2 * NFC * 512, 128], [512, NFC], [1, w]]),
                    in_=dr["sp_x2"].ap()[:, c0:c0 + w].rearrange("(f p) w -> p f w", p=128)),
                    waits=(free_tok, t_x2_sp), stream=f'lnl{r}')
                def tile(fc, r=r, w=w):
                    return ring3.ap()[:, NFC * 512 * r + 512 * fc: NFC * 512 * r + 512 * fc + w]
                return ld, tile

            def ln2_out(cb, c0, w, r, bA, bB, toks, tile, ld):
                P.wait("vector", *toks)
                for fc in range(NFC):
                    sl = fc % 2
                    P.emit("vector", lambda e, fc=fc, w=w, bA=bA, tile=tile: e.tensor_tensor(
                        h3scr.ap()[:, 0:w], tile(fc), bA.ap()[:, 0:w], ALU.mult), inc="v")
                    P.emit("vector", lambda e, fc=fc, w=w, bB=bB: e.tensor_tensor(
                        h3scr.ap()[:, 0:w], h3scr.ap()[:, 0:w], bB.ap()[:, 0:w], ALU.add),
                        inc="v")
                    P.emit("vector", lambda e, fc=fc, c0=c0, w=w: e.tensor_copy(
                        h3own[fc].ap()[:, c0:c0 + w], h3scr.ap()[:, 0:w]), inc="v")
                    tz = P.emit("vector", lambda e, fc=fc, sl=sl, w=w, tile=tile: e.tensor_tensor(
                        zstg.ap()[:, 512 * sl:512 * sl + w], tile(fc), h3scr.ap()[:, 0:w],
                        ALU.add), waits=(zs_g.get(sl),), inc="v")
                    zs_g[sl] = P.emit("scalar", lambda e, fc=fc, c0=c0, w=w, sl=sl: e.dma_start(
                        out=dr["sp_z"].ap()[128 * fc:128 * (fc + 1), c0:c0 + w],
                        in_=zstg.ap()[:, 512 * sl:512 * sl + w]),
                        waits=(tz,), stream=f'zs{sl}')
                    th3s = P.emit("scalar", lambda e, fc=fc, c0=c0, w=w: e.dma_start(
                        out=dr["cc_h3_in"].ap()[128 * fc:128 * (fc + 1), c0:c0 + w],
                        in_=h3own[fc].ap()[:, c0:c0 + w]),
                        waits=(P.tok("v"),), stream='h3s')
                return th3s

            emit_ln("ln2", ln2_src, OWN, ln2_out, rows3, sq3)
            t_h3 = P.tok("v")
            t_h3_sp = (P.tok("q_zs0"), P.tok("q_zs1"), P.tok("q_h3s"))
            P.wait("gpsimd", t_h3_sp)
            P.emit("gpsimd", lambda e: e.collective_compute(
                "AllGather", ALU.bypass,
                replica_groups=[[0, 1], [2, 3], [4, 5], [6, 7]],
                ins=[dr["cc_h3_in"].ap().opt()],
                outs=[dr["cc_h3_out"].ap().opt()]), stream="cc1", stream_k=1)
            t_ag = P.tok("q_cc1")

        with ExitStack() as phq:
            t_ph3q = tuple(t for t in P.all_toks() if t[0] != "q_cc1")
            wq1_sb = phq.enter_context(nc.sbuf_tensor("wq1_sb", [128, NFC * 1024], BF16))
            P.emit("gpsimd", lambda e: e.dma_start(
                out=wq1_sb.ap().rearrange("p (f n) -> p f n", f=NFC),
                in_=dr["Wq1"].ap().rearrange("(f p) n -> p f n", p=128)),
                waits=(t_ph3q,), stream='gw0')
            t_wq1 = P.tok("q_gw0")
            for tc, (i0, iw) in enumerate(T_CHUNKS):
                for ec in range(NFC):
                    bk = bank[0] if ec % 2 == 0 else bank[1]
                    for fc in range(NFC):
                        P.emit("tensor", lambda e, ec=ec, fc=fc, i0=i0, bk=bk: e.matmul(
                            bk.ap()[:, 0:512],
                            wq1_sb.ap()[:, 1024 * fc + 128 * ec: 1024 * fc + 128 * (ec + 1)],
                            h3own[fc].ap()[:, i0:i0 + 512],
                            start=(fc == 0), stop=(fc == NFC - 1)),
                            waits=(t_wq1, t_h3, bfree(bk)) if fc == 0 else (),
                            inc="mm" if fc == NFC - 1 else None)
                    tmm = P.tok("mm")
                    tev = P.emit("vector", lambda e, ec=ec, i0=i0, bk=bk: e.tensor_scalar(
                        qT[ec].ap()[:, i0:i0 + 512], bk.ap()[:, 0:512],
                        bq_sb.ap()[:, ec:ec + 1], None, ALU.add),
                        waits=(tmm, t_b1), inc="v")
                    bset(bk, tev)
            t_qT = P.tok("v")

        with ExitStack() as phkv:
            E4 = phkv.enter_context
            wk1_sb = E4(nc.sbuf_tensor("wk1_sb", [128, NFC * 1024], BF16))
            wv1_sb = E4(nc.sbuf_tensor("wv1_sb", [128, NFC * 1024], BF16))
            t_phqq = tuple(t for t in P.all_toks() if t[0] != "q_cc1")
            for ki_, (nm, sbw) in enumerate((("Wk1", wk1_sb), ("Wv1", wv1_sb))):
                P.emit("gpsimd", lambda e, nm=nm, sbw=sbw: e.dma_start(
                    out=sbw.ap().rearrange("p (f n) -> p f n", f=NFC),
                    in_=dr[nm].ap().rearrange("(f p) n -> p f n", p=128)),
                    waits=(t_phqq,), stream=f'gw{ki_ + 1}')
            t_w1l = (P.tok("q_gw1"), P.tok("q_gw2"))
            gring = E4(nc.sbuf_tensor("gring", [128, 2 * NFC * 512], BF16))
            kvstg = E4(nc.sbuf_tensor("kvstg", [128, 4 * 512], BF16))
            kvg = {}
            gfree = {}
            for gc in range(NT // 4):
                r = gc % 2
                rank, loc = gc // 4, (gc % 4) * 512
                tld = P.emit("sync", lambda e, rank=rank, loc=loc, r=r: e.dma_start(
                    out=gring.ap()[:, NFC * 512 * r: NFC * 512 * (r + 1)].rearrange(
                        "p (f w) -> p f w", f=NFC),
                    in_=dr["cc_h3_out"].ap()[1024 * rank:1024 * (rank + 1), loc:loc + 512].rearrange(
                        "(f p) w -> p f w", p=128)),
                    waits=(gfree.get(r), t_ag), stream=f'gl{r}')
                last = None
                for ec in range(NFC):
                    bk = bank[0] if ec % 2 == 0 else bank[1]
                    for fc in range(NFC):
                        P.emit("tensor", lambda e, ec=ec, fc=fc, r=r, bk=bk: e.matmul(
                            bk.ap()[:, 0:512],
                            wk1_sb.ap()[:, 1024 * fc + 128 * ec: 1024 * fc + 128 * (ec + 1)],
                            gring.ap()[:, NFC * 512 * r + 512 * fc: NFC * 512 * r + 512 * fc + 512],
                            start=(fc == 0), stop=(fc == NFC - 1)),
                            waits=(tld, t_w1l, bfree(bk)) if fc == 0 else (),
                            inc="mm" if fc == NFC - 1 else None)
                    tmm = P.tok("mm")
                    rr = ec % 2
                    tev = P.emit("vector", lambda e, ec=ec, rr=rr, bk=bk: e.tensor_scalar(
                        kvstg.ap()[:, 512 * rr:512 * (rr + 1)], bk.ap()[:, 0:512],
                        bk_sb.ap()[:, ec:ec + 1], None, ALU.add),
                        waits=(tmm, t_b1, kvg.get(("k", rr))), inc="v")
                    bset(bk, tev)
                    last = P.emit("scalar", lambda e, ec=ec, gc=gc, rr=rr: e.dma_start(
                        out=dr["sp_k1"].ap()[128 * ec:128 * (ec + 1), 512 * gc:512 * (gc + 1)],
                        in_=kvstg.ap()[:, 512 * rr:512 * (rr + 1)]),
                        waits=(tev,), stream=f'ks{rr}')
                    kvg[("k", rr)] = last
                for sub in range(4):
                    for en in range(2):
                        bk = bank[0] if en == 0 else bank[1]
                        for fc in range(NFC):
                            P.emit("tensor", lambda e, sub=sub, en=en, fc=fc, r=r, bk=bk: e.matmul(
                                bk.ap()[:, 0:512],
                                gring.ap()[:, NFC * 512 * r + 512 * fc + 128 * sub:
                                            NFC * 512 * r + 512 * fc + 128 * (sub + 1)],
                                wv1_sb.ap()[:, 1024 * fc + 512 * en: 1024 * fc + 512 * (en + 1)],
                                start=(fc == 0), stop=(fc == NFC - 1)),
                                waits=(tld, t_w1l, bfree(bk)) if fc == 0 else (),
                                inc="mm" if fc == NFC - 1 else None)
                        tmm = P.tok("mm")
                        rr = 2 + en
                        tev = P.emit("vector", lambda e, rr=rr, bk=bk: e.tensor_copy(
                            kvstg.ap()[:, 512 * rr:512 * (rr + 1)], bk.ap()[:, 0:512]),
                            waits=(tmm, kvg.get(("v", rr))), inc="v")
                        bset(bk, tev)
                        last = P.emit("scalar", lambda e, gc=gc, sub=sub, en=en, rr=rr: e.dma_start(
                            out=dr["sp_v1"].ap()[512 * gc + 128 * sub: 512 * gc + 128 * (sub + 1),
                                                 512 * en:512 * (en + 1)],
                            in_=kvstg.ap()[:, 512 * rr:512 * (rr + 1)]),
                            waits=(tev,), stream=f'vs{en}')
                        kvg[("v", rr)] = last
                gfree[r] = last
            t_kv_sp = (P.tok("q_ks0"), P.tok("q_ks1"), P.tok("q_vs0"), P.tok("q_vs1"))
            P.barrier()
        ph34.close()

        ph4e = ExitStack()
        E4e = ph4e.enter_context
        e_sb = E4e(nc.sbuf_tensor("e_sb", [128, E_COLS], BF16))
        ststg = E4e(nc.sbuf_tensor("ststg", [128, 128], F32))
        cmb = E4e(nc.sbuf_tensor("cmb", [128, 5 * NT], F32))

        with ExitStack() as ph4ab:
            E4 = ph4ab.enter_context
            stair_sb = E4(nc.sbuf_tensor("stair_sb", [128, 8 * 512], BF16))
            P.emit("gpsimd", lambda e: e.dma_start(
                out=stair_sb.ap().rearrange("p (s w) -> p s w", s=8),
                in_=dr["stair"].ap().rearrange("(s p) w -> p s w", p=128)), stream='gw0')
            t_stair = P.tok("q_gw0")

            kring = E4(nc.sbuf_tensor("kring", [128, 2 * NFC * 128], BF16))
            mscr = E4(nc.sbuf_tensor("mscr", [128, 6], F32))
            kfree = {}
            tacc_prev = None
            for jt in range(NT):
                r = jt % 2
                gt = _gtile(jt)
                tk = P.emit("sync", lambda e, gt=gt, r=r: e.dma_start(
                    out=kring.ap()[:, NFC * 128 * r: NFC * 128 * (r + 1)].rearrange(
                        "p (f w) -> p f w", f=NFC),
                    in_=dr["sp_k1"].ap()[:, 128 * gt:128 * (gt + 1)].rearrange(
                        "(f p) w -> p f w", p=128)),
                    waits=(kfree.get(r), t_kv_sp), stream=f'kl{r}')
                cmin = jt // 8
                sm = jt % 8
                trd = None
                for ci, C in enumerate(range(cmin, 4)):
                    i0 = 512 * C
                    bk = bank[0] if C % 2 == 0 else bank[2]
                    for fc in range(NFC):
                        P.emit("tensor", lambda e, fc=fc, r=r, i0=i0, bk=bk: e.matmul(
                            bk.ap()[:, 0:512],
                            kring.ap()[:, NFC * 128 * r + 128 * fc: NFC * 128 * r + 128 * (fc + 1)],
                            qT[fc].ap()[:, i0:i0 + 512],
                            start=(fc == 0), stop=(fc == NFC - 1)),
                            waits=(tk, t_qT, t_stair, bfree(bk)) if fc == 0 else (),
                            inc="mm" if fc == NFC - 1 else None)
                    tmm = P.tok("mm")
                    tlast = tmm
                    if C == cmin:
                        tlast = P.emit("vector", lambda e, sm=sm, bk=bk: e.tensor_tensor(
                            bk.ap()[:, 0:512], bk.ap()[:, 0:512],
                            stair_sb.ap()[:, 512 * sm:512 * (sm + 1)], ALU.add),
                            waits=(tmm,), inc="v")
                    if ci == 0:
                        trd = P.emit("vector", lambda e, jt=jt, bk=bk: e.reduce_max(
                            negm_sb.ap()[:, jt:jt + 1], bk.ap()[:, 0:512], axis=AXX,
                            negate=True), waits=(tlast,), inc="v")
                    else:
                        trd0 = P.emit("vector", lambda e, bk=bk: e.reduce_max(
                            mscr.ap()[:, 0:1], bk.ap()[:, 0:512], axis=AXX, negate=True),
                            waits=(tlast,), inc="v")
                        trd = P.emit("vector", lambda e, jt=jt: e.tensor_tensor(
                            negm_sb.ap()[:, jt:jt + 1], negm_sb.ap()[:, jt:jt + 1],
                            mscr.ap()[:, 0:1], ALU.min), waits=(trd0,), inc="v")
                    bset(bk, trd)
                for ci, C in enumerate(range(cmin, 4)):
                    i0 = 512 * C
                    bk = bank[0] if C % 2 == 0 else bank[2]
                    for fc in range(NFC):
                        P.emit("tensor", lambda e, fc=fc, r=r, i0=i0, bk=bk: e.matmul(
                            bk.ap()[:, 0:512],
                            kring.ap()[:, NFC * 128 * r + 128 * fc: NFC * 128 * r + 128 * (fc + 1)],
                            qT[fc].ap()[:, i0:i0 + 512],
                            start=(fc == 0), stop=(fc == NFC - 1)),
                            waits=(trd, bfree(bk)) if fc == 0 else (),
                            inc="mm" if fc == NFC - 1 else None)
                    tmm = P.tok("mm")
                    tlast = tmm
                    if C == cmin:
                        tlast = P.emit("vector", lambda e, sm=sm, bk=bk: e.tensor_tensor(
                            bk.ap()[:, 0:512], bk.ap()[:, 0:512],
                            stair_sb.ap()[:, 512 * sm:512 * (sm + 1)], ALU.add),
                            waits=(tmm,), inc="v")
                    eoff = e_off[jt] + ci * 512
                    if ci == 0:
                        tex = P.emit("scalar", lambda e, jt=jt, eoff=eoff, bk=bk: e.activation(
                            e_sb.ap()[:, eoff:eoff + 512], bk.ap()[:, 0:512], AF.Exp,
                            bias=negm_sb.ap()[:, jt:jt + 1], scale=1.0,
                            accum_out=sown_sb.ap()[:, jt:jt + 1]),
                            waits=(tlast, tacc_prev), inc="sc")
                        tacc = tex
                    else:
                        tex = P.emit("scalar", lambda e, jt=jt, eoff=eoff, ci=ci, bk=bk: e.activation(
                            e_sb.ap()[:, eoff:eoff + 512], bk.ap()[:, 0:512], AF.Exp,
                            bias=negm_sb.ap()[:, jt:jt + 1], scale=1.0,
                            accum_out=mscr.ap()[:, 1 + ci:2 + ci]),
                            waits=(tlast,), inc="sc")
                        tacc = P.emit("vector", lambda e, jt=jt, ci=ci: e.tensor_tensor(
                            sown_sb.ap()[:, jt:jt + 1], sown_sb.ap()[:, jt:jt + 1],
                            mscr.ap()[:, 1 + ci:2 + ci], ALU.add), waits=(tex,), inc="v")
                    bset(bk, tex)
                tacc_prev = tacc
                kfree[r] = P.tok("sc")
            t_ab = (P.tok("v"), P.tok("sc"))

            P.wait("vector", *t_ab)
            tp1 = P.emit("vector", lambda e: e.tensor_copy(
                ststg.ap()[:, 0:NT], negm_sb.ap()[:, :]), inc="v")
            tp2 = P.emit("vector", lambda e: e.tensor_copy(
                ststg.ap()[:, NT:2 * NT], sown_sb.ap()[:, :]), inc="v")
            tsp = P.emit("scalar", lambda e: e.dma_start(
                out=dr["cc_st_in"].ap()[:, :], in_=ststg.ap()[:, 0:64]),
                waits=(tp1, tp2), stream='sts')
            P.wait("gpsimd", tsp)
            tag2 = P.emit("gpsimd", lambda e: e.collective_compute(
                "AllGather", ALU.bypass,
                replica_groups=[[0, 1], [2, 3], [4, 5], [6, 7]],
                ins=[dr["cc_st_in"].ap().opt()],
                outs=[dr["cc_st_out"].ap().opt()]), stream="cc2", stream_k=1)
            tld2 = P.emit("sync", lambda e: e.dma_start(
                out=ststg.ap()[:, 0:128].rearrange("p (r c) -> p r c", r=2),
                in_=dr["cc_st_out"].ap().rearrange("(r p) c -> p r c", p=128)),
                waits=(tag2,), stream='stl')
            u1 = P.emit("vector", lambda e: e.tensor_tensor(
                cmb.ap()[:, 0:NT], ststg.ap()[:, 0:NT], ststg.ap()[:, 64:64 + NT], ALU.min),
                waits=(tld2,), inc="v")
            u2 = P.emit("vector", lambda e: e.tensor_tensor(
                cmb.ap()[:, NT:2 * NT], cmb.ap()[:, 0:NT], ststg.ap()[:, 0:NT], ALU.subtract),
                waits=(u1,), inc="v")
            u3 = P.emit("vector", lambda e: e.tensor_tensor(
                cmb.ap()[:, 2 * NT:3 * NT], cmb.ap()[:, 0:NT], ststg.ap()[:, 64:64 + NT],
                ALU.subtract), waits=(u1,), inc="v")
            u4 = P.emit("scalar", lambda e: e.activation(
                cmb.ap()[:, NT:2 * NT], cmb.ap()[:, NT:2 * NT], AF.Exp, bias=0.0, scale=1.0),
                waits=(u2,), inc="sc")
            u5 = P.emit("scalar", lambda e: e.activation(
                cmb.ap()[:, 2 * NT:3 * NT], cmb.ap()[:, 2 * NT:3 * NT], AF.Exp, bias=0.0,
                scale=1.0), waits=(u3,), inc="sc")
            u6 = P.emit("vector", lambda e: e.tensor_tensor(
                cmb.ap()[:, NT:2 * NT], cmb.ap()[:, NT:2 * NT], ststg.ap()[:, NT:2 * NT],
                ALU.mult), waits=(u4,), inc="v")
            u7 = P.emit("vector", lambda e: e.tensor_tensor(
                cmb.ap()[:, 2 * NT:3 * NT], cmb.ap()[:, 2 * NT:3 * NT],
                ststg.ap()[:, 64 + NT:64 + 2 * NT], ALU.mult), waits=(u5,), inc="v")
            u8 = P.emit("vector", lambda e: e.tensor_tensor(
                cmb.ap()[:, 3 * NT:4 * NT], cmb.ap()[:, NT:2 * NT], cmb.ap()[:, 2 * NT:3 * NT],
                ALU.add), waits=(u6, u7), inc="v")
            u9 = P.emit("vector", lambda e: e.tensor_tensor(
                cmb.ap()[:, 4 * NT:5 * NT], cmb.ap()[:, 0:NT], negm_sb.ap()[:, :], ALU.subtract),
                waits=(u8,), inc="v")
            u10 = P.emit("scalar", lambda e: e.activation(
                cmb.ap()[:, 4 * NT:5 * NT], cmb.ap()[:, 4 * NT:5 * NT], AF.Exp, bias=0.0,
                scale=1.0), waits=(u9,), inc="sc")
            u11 = P.emit("vector", lambda e: e.reciprocal(
                cmb.ap()[:, 3 * NT:4 * NT], cmb.ap()[:, 3 * NT:4 * NT]), waits=(u8,), inc="v")
            u12 = P.emit("vector", lambda e: e.tensor_tensor(
                fst_sb.ap()[:, :], cmb.ap()[:, 4 * NT:5 * NT], cmb.ap()[:, 3 * NT:4 * NT],
                ALU.mult), waits=(u10, u11), inc="v")
            u13 = P.emit("vector", lambda e: e.tensor_single_scalar(
                fst_sb.ap()[:, :], fst_sb.ap()[:, :], 1.0 / 32.0, ALU.mult),
                waits=(u12,), inc="v")
            u14 = P.emit("vector", lambda e: e.tensor_copy(fbf_sb.ap()[:, :], fst_sb.ap()[:, :]),
                         waits=(u13,), inc="v")
            t_f = u14
            P.barrier()

        with ExitStack() as ph4c:
            E4 = ph4c.enter_context
            vring = E4(nc.sbuf_tensor("vring", [128, 2 * 1024], BF16))
            zring = E4(nc.sbuf_tensor("zring", [128, NFC * 512], F32))
            rrow = E4(nc.sbuf_tensor("rrow", [1, 512], F32))
            rb_sb = E4(nc.sbuf_tensor("rb_sb", [128, 512], F32))
            c3scr = E4(nc.sbuf_tensor("c3scr", [128, 512], F32))
            x3stg = E4(nc.sbuf_tensor("x3stg", [128, 2 * 512], F32))
            vg = {}
            for C in range(4):
                i0 = 512 * C
                njt = 8 * C + 8
                tz = P.emit("sync", lambda e, i0=i0: e.dma_start(
                    out=zring.ap().rearrange("p (f w) -> p f w", f=NFC),
                    in_=dr["sp_z"].ap()[:, i0:i0 + 512].rearrange("(f p) w -> p f w", p=128)),
                    waits=(P.tok("v"),), stream='zl')
                for jx in range(njt):
                    jt = jx
                    P.emit("tensor", lambda e, jt=jt, C=C, jx=jx, njt=njt: e.matmul(
                        bank[6].ap()[0:1, 0:512], fbf_sb.ap()[:, jt:jt + 1],
                        e_sb.ap()[:, e_off[jt] + (C - jt // 8) * 512:
                                  e_off[jt] + (C - jt // 8) * 512 + 512],
                        start=(jx == 0), stop=(jx == njt - 1)),
                        waits=(t_f, bfree(bank[6])) if jx == 0 else (),
                        inc="mm" if jx == njt - 1 else None)
                trho = P.tok("mm")
                trr = P.emit("vector", lambda e: e.tensor_copy(
                    rrow.ap()[0:1, :], bank[6].ap()[0:1, 0:512]), waits=(trho,), inc="v")
                bset(bank[6], trr)
                tbc = P.emit("tensor", lambda e: e.matmul(
                    bank[7].ap()[:, 0:512], ones_row.ap()[0:1, :], rrow.ap()[0:1, :],
                    start=True, stop=True), waits=(trr, bfree(bank[7])), inc="mm")
                trb = P.emit("vector", lambda e: e.tensor_copy(
                    rb_sb.ap()[:, :], bank[7].ap()[:, 0:512]), waits=(tbc,), inc="v")
                bset(bank[7], trb)
                for half in range(2):
                    for jx in range(njt):
                        jt = jx
                        r = jt % 2
                        gt = _gtile(jt)
                        tv = P.emit("sync", lambda e, gt=gt, r=r: e.dma_start(
                            out=vring.ap()[:, 1024 * r:1024 * (r + 1)],
                            in_=dr["sp_v1"].ap()[128 * gt:128 * (gt + 1), :]),
                            waits=(vg.get(("v", r)), t_kv_sp), stream=f'vl{r}')
                        tvs = P.emit("vector", lambda e, jt=jt, r=r: e.tensor_scalar(
                            vring.ap()[:, 1024 * r:1024 * (r + 1)],
                            vring.ap()[:, 1024 * r:1024 * (r + 1)],
                            fst_sb.ap()[:, jt:jt + 1], None, ALU.mult),
                            waits=(tv, t_f, vg.get(("vt", r))), inc="v")
                        for vx in range(4):
                            vc = 4 * half + vx
                            bk = bank[2 * vx]
                            P.emit("tensor", lambda e, jt=jt, C=C, vc=vc, r=r, bk=bk, jx=jx, njt=njt: e.matmul(
                                bk.ap()[:, 0:512],
                                vring.ap()[:, 1024 * r + 128 * vc: 1024 * r + 128 * (vc + 1)],
                                e_sb.ap()[:, e_off[jt] + (C - jt // 8) * 512:
                                          e_off[jt] + (C - jt // 8) * 512 + 512],
                                start=(jx == 0), stop=(jx == njt - 1)),
                                waits=(tvs, bfree(bk)) if jx == 0 else (tvs,),
                                inc="mm" if jx == njt - 1 else None)
                        vg[("vt", r)] = P.tok("mm")
                        vg[("v", r)] = vg[("vt", r)]
                    tmm_half = P.tok("mm")
                    for vx in range(4):
                        vc = 4 * half + vx
                        bk = bank[2 * vx]
                        sl = vc % 2
                        tc1 = P.emit("vector", lambda e, vc=vc, bk=bk: e.scalar_tensor_tensor(
                            c3scr.ap()[:, 0:512], rb_sb.ap()[:, :],
                            bv_sb.ap()[:, vc:vc + 1], bk.ap()[:, 0:512], ALU.mult, ALU.add),
                            waits=(tmm_half, trb, t_b1), inc="v")
                        tc2 = P.emit("vector", lambda e, vc=vc, sl=sl: e.tensor_tensor(
                            x3stg.ap()[:, 512 * sl:512 * (sl + 1)], c3scr.ap()[:, 0:512],
                            zring.ap()[:, 512 * vc:512 * (vc + 1)], ALU.add),
                            waits=(tz, vg.get(("x3", sl))), inc="v")
                        bset(bk, tc2)
                        vg[("x3", sl)] = P.emit(
                            "scalar", lambda e, vc=vc, i0=i0, sl=sl: e.dma_start(
                                out=dr["sp_x3"].ap()[128 * vc:128 * (vc + 1), i0:i0 + 512],
                                in_=x3stg.ap()[:, 512 * sl:512 * (sl + 1)]),
                            waits=(tc2,), stream=f'x3s{sl}')
            t_x3_sp = (P.tok("q_x3s0"), P.tok("q_x3s1"))
            P.barrier()
        ph4e.close()
        ph45.close()
        ph4o.close()

        if upto == "x3":
            if "x3" in dbg:
                dout("dbg_x3", (D, OWN), F32)
                P.wait("sync", t_x3_sp)
                P.emit("sync", lambda e: e.dma_start(
                    out=dr["dbg_x3"].ap()[:, :], in_=dr["sp_x3"].ap()[:, :]),
                    stream='dbg')
            P.wait("sync", *P.all_toks())
            block = E(nc.Block())
            P.run_all(block)
            return nc

        t_wgt = wq_state[("done", 2)]
        t_wut = wq_state[("done", 3)]
        t_wdt = wq_state[("done", 4)]
        with ExitStack() as ph5:
            E5 = ph5.enter_context
            xq4 = [E5(nc.sbuf_tensor(f"xq4_{fc}", [128, OWN], BF16)) for fc in range(NFC)]
            am5 = E5(nc.sbuf_tensor("am5", [128, 4], F32))
            am5b = E5(nc.sbuf_tensor("am5b", [128, 16], F32))
            srep5 = E5(nc.sbuf_tensor("srep5", [128, 129], F32))
            s4b = E5(nc.sbuf_tensor("s4b", [128, OWN], F32))
            dgb = E5(nc.sbuf_tensor("dgb", [128, OWN], F32))
            dub = E5(nc.sbuf_tensor("dub", [128, OWN], F32))
            ph5a = ExitStack()
            E5a = ph5a.enter_context
            ring5 = E5a(nc.sbuf_tensor("ln3_ring", [128, 2 * NFC * 512], F32))
            rows5 = E5a(nc.sbuf_tensor("ln3_rows", [1, 2 * 4 * 512], F32))
            sq5 = E5a(nc.sbuf_tensor("ln3_sq", [128, NFC * 512], BF16))
            xh4r = E5a(nc.sbuf_tensor("xh4r", [128, NFC * 512], F32))
            mf5 = E5a(nc.sbuf_tensor("mf5", [128, 512], F32))
            f5scr = E5a(nc.sbuf_tensor("f5scr", [128, 512], F32))
            P.emit("gpsimd", lambda e: e.memset(am5.ap()[:, :], 1.0), inc="g")
            P.emit("gpsimd", lambda e: e.memset(am5b.ap()[:, :], 1.0), inc="g")
            t_am5i = P.tok("g")
            I32 = mybir.dt.int32
            lg5 = {"mf": None, "sr": None}

            def ln3_src(cb, c0, w, r, free_tok):
                ld = P.emit("sync", lambda e, c0=c0, w=w, r=r: e.dma_start(
                    out=mkap(ring5.ap().tensor, NFC * 512 * r,
                             [[2 * NFC * 512, 128], [512, NFC], [1, w]]),
                    in_=dr["sp_x3"].ap()[:, c0:c0 + w].rearrange("(f p) w -> p f w", p=128)),
                    waits=(free_tok, t_x3_sp), stream=f'lnl{r}')
                def tile(fc, r=r, w=w):
                    return ring5.ap()[:, NFC * 512 * r + 512 * fc: NFC * 512 * r + 512 * fc + w]
                return ld, tile

            def ln3_out(cb, c0, w, r, bA, bB, toks, tile, ld):
                P.wait("vector", *toks)
                tts = []
                for fc in range(NFC):
                    P.emit("vector", lambda e, fc=fc, w=w, bA=bA, tile=tile: e.tensor_tensor(
                        xh4r.ap()[:, 512 * fc:512 * fc + w], tile(fc), bA.ap()[:, 0:w],
                        ALU.mult), inc="v")
                    tt = P.emit("vector", lambda e, fc=fc, w=w, bB=bB: e.tensor_tensor(
                        xh4r.ap()[:, 512 * fc:512 * fc + w], xh4r.ap()[:, 512 * fc:512 * fc + w],
                        bB.ap()[:, 0:w], ALU.add), inc="v")
                    tts.append(tt)
                tf = P.emit("vector", lambda e, w=w: e.tensor_single_scalar(
                    mf5.ap()[:, 0:w].bitcast(I32), xh4r.ap()[:, 0:w].bitcast(I32),
                    0x7FFFFFFF, ALU.bitwise_and), waits=(tts[0], lg5["mf"]), inc="v")
                for fc in range(1, NFC):
                    P.emit("vector", lambda e, fc=fc, w=w: e.tensor_single_scalar(
                        f5scr.ap()[:, 0:w].bitcast(I32),
                        xh4r.ap()[:, 512 * fc:512 * fc + w].bitcast(I32),
                        0x7FFFFFFF, ALU.bitwise_and), waits=(tts[fc],), inc="v")
                    tf = P.emit("vector", lambda e, w=w: e.tensor_tensor(
                        mf5.ap()[:, 0:w], mf5.ap()[:, 0:w], f5scr.ap()[:, 0:w], ALU.max),
                        inc="v")
                trd = None
                for g in range(4):
                    ttr = P.emit("tensor", lambda e, g=g: e.transpose(
                        bank[2].ap()[:, 0:128], mf5.ap()[:, 128 * g:128 * (g + 1)],
                        ident.ap()[:, :]), waits=(tf, t_am5i, bfree(bank[2])), inc="mm")
                    trd = P.emit("vector", lambda e, g=g: e.reduce_max(
                        am5.ap()[:, g:g + 1], bank[2].ap()[:, 0:128], axis=AXX),
                        waits=(ttr,), inc="v")
                    bset(bank[2], trd)
                    lg5["mf"] = ttr
                tcl = P.emit("vector", lambda e: e.tensor_single_scalar(
                    am5.ap()[:, :], am5.ap()[:, :], 1e-5, ALU.max), waits=(trd,), inc="v")
                tcp = None
                for g in range(4):
                    gabs = c0 // 128 + g
                    trc = P.emit("vector", lambda e, g=g: e.reciprocal(
                        srep5.ap()[:, 128:129], am5.ap()[:, g:g + 1]), waits=(tcl,), inc="v")
                    trp = P.emit("vector", lambda e: e.tensor_scalar(
                        srep5.ap()[:, 0:128], ones128.ap()[:, :],
                        srep5.ap()[:, 128:129], 127.0, ALU.mult, ALU.mult),
                        waits=(trc, lg5["sr"]), inc="v")
                    ttr = P.emit("tensor", lambda e: e.transpose(
                        bank[3].ap()[:, 0:128], srep5.ap()[:, 0:128], ident.ap()[:, :]),
                        waits=(trp, bfree(bank[3])), inc="mm")
                    tcp = P.emit("vector", lambda e, gabs=gabs: e.tensor_copy(
                        s4b.ap()[:, 128 * gabs:128 * (gabs + 1)], bank[3].ap()[:, 0:128]),
                        waits=(ttr,), inc="v")
                    bset(bank[3], tcp)
                    lg5["sr"] = ttr
                    for (dst, wqi) in ((dgb, 5), (dub, 7)):
                        ta1 = P.emit("vector", lambda e, g=g, wqi=wqi: e.tensor_scalar(
                            srep5.ap()[:, 128:129], am5.ap()[:, g:g + 1],
                            thr_cols.ap()[:, wqi:wqi + 1], 1.0 / 127.0, ALU.mult, ALU.mult),
                            waits=(tcl,), inc="v")
                        tra = P.emit("vector", lambda e: e.tensor_scalar(
                            srep5.ap()[:, 0:128], ones128.ap()[:, :],
                            srep5.ap()[:, 128:129], None, ALU.mult),
                            waits=(ta1, lg5["sr"]), inc="v")
                        ttr2 = P.emit("tensor", lambda e: e.transpose(
                            bank[3].ap()[:, 0:128], srep5.ap()[:, 0:128], ident.ap()[:, :]),
                            waits=(tra, bfree(bank[3])), inc="mm")
                        tcg = P.emit("vector", lambda e, gabs=gabs, dst=dst: e.tensor_copy(
                            dst.ap()[:, 128 * gabs:128 * (gabs + 1)], bank[3].ap()[:, 0:128]),
                            waits=(ttr2,), inc="v")
                        bset(bank[3], tcg)
                        lg5["sr"] = ttr2
                last = None
                for fc in range(NFC):
                    P.emit("vector", lambda e, fc=fc, c0=c0, w=w: e.tensor_tensor(
                        xh4r.ap()[:, 512 * fc:512 * fc + w], xh4r.ap()[:, 512 * fc:512 * fc + w],
                        s4b.ap()[:, c0:c0 + w], ALU.mult), waits=(tcp,), inc="v")
                    last = P.emit("vector", lambda e, fc=fc, c0=c0, w=w: e.tensor_scalar(
                        xq4[fc].ap()[:, c0:c0 + w], xh4r.ap()[:, 512 * fc:512 * fc + w],
                        MRND, MRND, ALU.add, ALU.subtract), inc="v")
                return last

            emit_ln("ln3", ln3_src, OWN, ln3_out, rows5, sq5)
            t_xq4 = P.tok("v")
            P.barrier()
            ph5a.close()
            mp5 = E5(nc.sbuf_tensor("mp5x", [128, 512], F32))

            wgr = E5(nc.sbuf_tensor("wgr", [128, 2 * NFC * 128], BF16))
            wur = E5(nc.sbuf_tensor("wur", [128, 2 * NFC * 128], BF16))
            wdr = E5(nc.sbuf_tensor("wdr", [128, 2 * NHC * 128], BF16))
            psl = E5(nc.sbuf_tensor("psl", [128, NHC * 512], BF16))
            gscr = E5(nc.sbuf_tensor("gscr", [128, 2 * 512], F32))
            uscr = E5(nc.sbuf_tensor("uscr", [128, 2 * 512], F32))
            pabs = E5(nc.sbuf_tensor("pabs", [128, 512], F32))
            s5b = E5(nc.sbuf_tensor("s5b", [128, 512], F32))
            ztb = E5(nc.sbuf_tensor("ztb", [128, 512], F32))
            outstg = E5(nc.sbuf_tensor("outstg", [128, 2 * 512], F32))
            x3iring = E5(nc.sbuf_tensor("x3iring", [128, 2 * 512], F32))
            t_prev_wd = None
            og = {}

            for tb, (i0, iw) in enumerate(T_CHUNKS):
                wfree = {}
                pm_tok = None
                for ic in range(NHC):
                    r = ic % 2
                    twg = P.emit("sync", lambda e, ic=ic, r=r: e.dma_start(
                        out=wgr.ap()[:, NFC * 128 * r: NFC * 128 * (r + 1)].rearrange(
                            "p (f w) -> p f w", f=NFC),
                        in_=dr["sp_wgt"].ap()[:, 128 * ic:128 * (ic + 1)].rearrange(
                            "(f p) w -> p f w", p=128)),
                        waits=(wfree.get(("g", r)), t_wgt), stream=f'wgl{r}')
                    twu = P.emit("sync", lambda e, ic=ic, r=r: e.dma_start(
                        out=wur.ap()[:, NFC * 128 * r: NFC * 128 * (r + 1)].rearrange(
                            "p (f w) -> p f w", f=NFC),
                        in_=dr["sp_wut"].ap()[:, 128 * ic:128 * (ic + 1)].rearrange(
                            "(f p) w -> p f w", p=128)),
                        waits=(wfree.get(("u", r)), t_wut), stream=f'wul{r}')
                    bg, bu = bank[0], bank[1]
                    for fc in range(NFC):
                        P.emit("tensor", lambda e, fc=fc, r=r, i0=i0, bg=bg: e.matmul(
                            bg.ap()[:, 0:512],
                            wgr.ap()[:, NFC * 128 * r + 128 * fc: NFC * 128 * r + 128 * (fc + 1)],
                            xq4[fc].ap()[:, i0:i0 + 512],
                            start=(fc == 0), stop=(fc == NFC - 1)),
                            waits=(twg, t_xq4, bfree(bg)) if fc == 0 else (),
                            inc="mm" if fc == NFC - 1 else None)
                    tg = P.tok("mm")
                    for fc in range(NFC):
                        P.emit("tensor", lambda e, fc=fc, r=r, i0=i0, bu=bu: e.matmul(
                            bu.ap()[:, 0:512],
                            wur.ap()[:, NFC * 128 * r + 128 * fc: NFC * 128 * r + 128 * (fc + 1)],
                            xq4[fc].ap()[:, i0:i0 + 512],
                            start=(fc == 0), stop=(fc == NFC - 1)),
                            waits=(twu, bfree(bu)) if fc == 0 else (),
                            inc="mm" if fc == NFC - 1 else None)
                    tu = P.tok("mm")
                    wfree[("g", r)] = tg
                    wfree[("u", r)] = tu
                    rr = ic % 2
                    tg1 = P.emit("vector", lambda e, rr=rr, i0=i0, bg=bg: e.tensor_tensor(
                        gscr.ap()[:, 512 * rr:512 * (rr + 1)], bg.ap()[:, 0:512],
                        dgb.ap()[:, i0:i0 + 512], ALU.mult),
                        waits=(tg, og.get(("gs", rr)), t_prev_wd), inc="v")
                    tsi = P.emit("scalar", lambda e, rr=rr: e.activation(
                        gscr.ap()[:, 512 * rr:512 * (rr + 1)], gscr.ap()[:, 512 * rr:512 * (rr + 1)],
                        AF.Silu, bias=0.0, scale=1.0), waits=(tg1,), inc="sc")
                    tu1 = P.emit("vector", lambda e, rr=rr, i0=i0, bu=bu: e.tensor_tensor(
                        uscr.ap()[:, 512 * rr:512 * (rr + 1)], bu.ap()[:, 0:512],
                        dub.ap()[:, i0:i0 + 512], ALU.mult),
                        waits=(tu, og.get(("us", rr))), inc="v")
                    bset(bg, tg1); bset(bu, tu1)
                    tp = P.emit("vector", lambda e, ic=ic, rr=rr: e.tensor_tensor(
                        psl.ap()[:, 512 * ic:512 * (ic + 1)], gscr.ap()[:, 512 * rr:512 * (rr + 1)],
                        uscr.ap()[:, 512 * rr:512 * (rr + 1)], ALU.mult),
                        waits=(tsi, tu1), inc="v")
                    og[("gs", rr)] = tp
                    og[("us", rr)] = tp
                    tya = P.emit("scalar", lambda e, ic=ic: e.activation(
                        pabs.ap()[:, :], psl.ap()[:, 512 * ic:512 * (ic + 1)],
                        AF.Abs, bias=0.0, scale=1.0), waits=(tp, pm_tok), inc="sc")
                    if ic == 0:
                        pm_tok = P.emit("vector", lambda e: e.tensor_copy(
                            mp5.ap()[:, :], pabs.ap()[:, :]),
                            waits=(tya, lg5.get("mp")), inc="v")
                    else:
                        pm_tok = P.emit("vector", lambda e: e.tensor_tensor(
                            mp5.ap()[:, :], mp5.ap()[:, :], pabs.ap()[:, :], ALU.max),
                            waits=(tya, pm_tok), inc="v")
                trd = None
                for g in range(4):
                    ttr = P.emit("tensor", lambda e, g=g: e.transpose(
                        bank[2].ap()[:, 0:128], mp5.ap()[:, 128 * g:128 * (g + 1)],
                        ident.ap()[:, :]), waits=(pm_tok, t_am5i, bfree(bank[2])), inc="mm")
                    trd = P.emit("vector", lambda e, g=g: e.reduce_max(
                        am5b.ap()[:, g:g + 1], bank[2].ap()[:, 0:128], axis=AXX),
                        waits=(ttr,), inc="v")
                    bset(bank[2], trd)
                    lg5["mp"] = ttr
                tcl = P.emit("vector", lambda e: e.tensor_single_scalar(
                    am5b.ap()[:, 0:4], am5b.ap()[:, 0:4], 1e-5, ALU.max), waits=(trd,), inc="v")
                tcz_last = None
                for g in range(4):
                    trc = P.emit("vector", lambda e, g=g: e.reciprocal(
                        srep5.ap()[:, 128:129], am5b.ap()[:, g:g + 1]), waits=(tcl,), inc="v")
                    trp = P.emit("vector", lambda e: e.tensor_scalar(
                        srep5.ap()[:, 0:128], ones128.ap()[:, :],
                        srep5.ap()[:, 128:129], 127.0, ALU.mult, ALU.mult),
                        waits=(trc, lg5["sr"]), inc="v")
                    ttr = P.emit("tensor", lambda e: e.transpose(
                        bank[3].ap()[:, 0:128], srep5.ap()[:, 0:128], ident.ap()[:, :]),
                        waits=(trp, bfree(bank[3])), inc="mm")
                    tcp = P.emit("vector", lambda e, g=g: e.tensor_copy(
                        s5b.ap()[:, 128 * g:128 * (g + 1)], bank[3].ap()[:, 0:128]),
                        waits=(ttr, og.get(("s5",)), og.get(("s5g",))), inc="v")
                    bset(bank[3], tcp)
                    lg5["sr"] = ttr
                    ta1 = P.emit("vector", lambda e, g=g: e.tensor_scalar(
                        srep5.ap()[:, 128:129], am5b.ap()[:, g:g + 1],
                        thr_cols.ap()[:, 9:10], 1.0 / 127.0, ALU.mult, ALU.mult),
                        waits=(tcl,), inc="v")
                    tra = P.emit("vector", lambda e: e.tensor_scalar(
                        srep5.ap()[:, 0:128], ones128.ap()[:, :],
                        srep5.ap()[:, 128:129], None, ALU.mult),
                        waits=(ta1, lg5["sr"]), inc="v")
                    ttr2 = P.emit("tensor", lambda e: e.transpose(
                        bank[3].ap()[:, 0:128], srep5.ap()[:, 0:128], ident.ap()[:, :]),
                        waits=(tra, bfree(bank[3])), inc="mm")
                    tcz_last = P.emit("vector", lambda e, g=g: e.tensor_copy(
                        ztb.ap()[:, 128 * g:128 * (g + 1)], bank[3].ap()[:, 0:128]),
                        waits=(ttr2,), inc="v")
                    bset(bank[3], tcz_last)
                    lg5["sr"] = ttr2
                t_s5 = P.tok("v")
                for ic in range(NHC):
                    rr = ic % 2
                    P.emit("vector", lambda e, ic=ic, rr=rr: e.tensor_tensor(
                        gscr.ap()[:, 512 * rr:512 * (rr + 1)], psl.ap()[:, 512 * ic:512 * (ic + 1)],
                        s5b.ap()[:, :], ALU.mult), waits=(t_s5, pm_tok), inc="v")
                    P.emit("vector", lambda e, ic=ic, rr=rr: e.tensor_scalar(
                        psl.ap()[:, 512 * ic:512 * (ic + 1)], gscr.ap()[:, 512 * rr:512 * (rr + 1)],
                        MRND, MRND, ALU.add, ALU.subtract), inc="v")
                t_pq = (P.tok("v"), P.tok("g"))
                og[("s5",)] = t_pq[0]
                og[("s5g",)] = t_pq[1]
                og[("gs", 0)] = t_pq[0]
                og[("gs", 1)] = t_pq[1]
                og[("us", 0)] = t_pq[0]
                og[("us", 1)] = t_pq[1]
                wdfree = {}
                for dc in range(NFC):
                    r = dc % 2
                    twd = P.emit("sync", lambda e, dc=dc, r=r: e.dma_start(
                        out=wdr.ap()[:, NHC * 128 * r: NHC * 128 * (r + 1)].rearrange(
                            "p (h w) -> p h w", h=NHC),
                        in_=dr["sp_wdt"].ap()[:, 128 * dc:128 * (dc + 1)].rearrange(
                            "(h p) w -> p h w", p=128)),
                        waits=(wdfree.get(r), t_wdt), stream=f'wdl{r}')
                    bk = bank[0] if dc % 2 == 0 else bank[1]
                    for ic in range(NHC):
                        P.emit("tensor", lambda e, ic=ic, dc=dc, r=r, bk=bk: e.matmul(
                            bk.ap()[:, 0:512],
                            wdr.ap()[:, NHC * 128 * r + 128 * ic: NHC * 128 * r + 128 * (ic + 1)],
                            psl.ap()[:, 512 * ic:512 * (ic + 1)],
                            start=(ic == 0), stop=(ic == NHC - 1)),
                            waits=((twd, bfree(bk)) + t_pq) if ic == 0 else (),
                            inc="mm" if ic == NHC - 1 else None)
                    tmm = P.tok("mm")
                    wdfree[r] = tmm
                    t_prev_wd = tmm
                    rr = dc % 2
                    tx3 = P.emit("scalar", lambda e, dc=dc, i0=i0, rr=rr: e.dma_start(
                        out=x3iring.ap()[:, 512 * rr:512 * (rr + 1)],
                        in_=dr["sp_x3"].ap()[128 * dc:128 * (dc + 1), i0:i0 + 512]),
                        waits=(og.get(("x3", rr)),), stream=f'x3i{rr}')
                    to1 = P.emit("vector", lambda e, rr=rr, bk=bk: e.tensor_tensor(
                        outstg.ap()[:, 512 * rr:512 * (rr + 1)], bk.ap()[:, 0:512],
                        ztb.ap()[:, :], ALU.mult),
                        waits=(tmm, tcz_last, og.get(("os", rr))), inc="v")
                    to2 = P.emit("vector", lambda e, rr=rr: e.tensor_tensor(
                        outstg.ap()[:, 512 * rr:512 * (rr + 1)],
                        outstg.ap()[:, 512 * rr:512 * (rr + 1)],
                        x3iring.ap()[:, 512 * rr:512 * (rr + 1)], ALU.add),
                        waits=(tx3,), inc="v")
                    og[("x3", rr)] = to2
                    bset(bk, to2)
                    og[("os", rr)] = P.emit("scalar", lambda e, dc=dc, i0=i0, rr=rr: e.dma_start(
                        out=dr["out_shard"].ap()[128 * dc:128 * (dc + 1), i0:i0 + 512],
                        in_=outstg.ap()[:, 512 * rr:512 * (rr + 1)]),
                        waits=(to2,), stream=f'os{rr}')

        P.wait("sync", *P.all_toks())
        block = E(nc.Block())
        P.run_all(block)
    return nc


def kernel(**inputs):
    maps = prep_core_inputs(inputs)
    nc = build_nc()
    res = run_bass_kernel_spmd(nc, maps, core_ids=list(range(NCORES)))
    return assemble_output(res.results)


# version 09
# speedup vs baseline: 1.0044x; 1.0044x over previous
import os

import numpy as np
import ml_dtypes
from contextlib import ExitStack

import concourse.bass as bass
import concourse.mybir as mybir
from concourse.bass_isa import ReduceOp
from concourse.ap import AP
from concourse.bass_utils import run_bass_kernel_spmd

F32 = mybir.dt.float32
BF16 = mybir.dt.bfloat16
AF = mybir.ActivationFunctionType
ALU = mybir.AluOpType
AXX = mybir.AxisListType.X

D = 1024
SEQ = 4096
NBATCH = 4
NCORES = 8
HID = 4096
NT = 32
OWN_NT = 16
OWN = 2048
SLABW = 130
SLABT = OWN_NT * SLABW
NFC = 8
NHC = 32
MRND = 12582912.0
NEGBIG = -1.0e30
EPS_LN = 1e-5

ENGINES = ("sync", "gpsimd", "vector", "scalar", "tensor")

STRICT_SYNC = os.environ.get("BASS_SIM_FP64") == "1"

HAT_LIMIT = None
HAT_ECS = None
LN1_STOP = None
HAT_STOP = None

I_CHUNKS = [(0, 512), (512, 512), (1024, 512), (1536, 512), (2048, 32)]
T_CHUNKS = [(0, 512), (512, 512), (1024, 512), (1536, 512)]


def _true_tiles(par):
    return [2 * k + par for k in range(OWN_NT)]


def _gtile(jt):
    return (jt % 2) * OWN_NT + jt // 2



def prep_core_inputs(I):
    x0 = np.ascontiguousarray(np.asarray(I["L0_attn_x"], dtype=np.float32))
    f32 = lambda a: np.ascontiguousarray(np.asarray(a, dtype=np.float32))
    T = lambda a: np.ascontiguousarray(np.asarray(a, dtype=np.float32).T)
    col = lambda a: np.ascontiguousarray(np.asarray(a, dtype=np.float32).reshape(-1, 1))

    shared = {
        "WqT0": T(I["L0_attn_Wq"]), "WkT0": T(I["L0_attn_Wk"]),
        "WvT0": T(I["L0_attn_Wv"]), "WoT0": T(I["L0_attn_Wo"]),
        "bo0": col(I["L0_attn_bo"]),
        "qs_col": col(np.asarray(I["L0_attn_q_scale"]).reshape(-1)),
        "ks_col": col(np.asarray(I["L0_attn_k_scale"]).reshape(-1)),
        "qs16": f32(np.asarray(I["L0_attn_q_scale"]).reshape(16, 64)),
        "ks16": f32(np.asarray(I["L0_attn_k_scale"]).reshape(16, 64)),
        "w1T": T(I["L0_ffn_w1"]), "w2T": T(I["L0_ffn_w2"]),
        "convw": f32(np.asarray(I["L0_ffn_conv_w"]).reshape(HID, 3)),
        "convb": col(I["L0_ffn_conv_b"]),
        "Wq1": f32(I["L1_attn_Wq"]), "Wk1": f32(I["L1_attn_Wk"]), "Wv1": f32(I["L1_attn_Wv"]),
        "bq1": col(I["L1_attn_bq"]), "bk1": col(I["L1_attn_bk"]), "bv1": col(I["L1_attn_bv"]),
        "wgT": T(I["L1_ffn_w_gate"]), "wuT": T(I["L1_ffn_w_up"]), "wdT": T(I["L1_ffn_w_down"]),
    }

    maps = []
    for c in range(NCORES):
        b, par = c // 2, c % 2
        xT = np.ascontiguousarray(x0[b].T)
        slab = np.zeros((D, SLABT), np.float32)
        for k, t in enumerate(_true_tiles(par)):
            lo, hi = 128 * t - 1, 128 * t + 129
            s0, s1 = max(lo, 0), min(hi, SEQ)
            slab[:, SLABW * k + (s0 - lo): SLABW * k + (s1 - lo)] = xT[:, s0:s1]
        amask = np.ones(SLABT, np.float32)
        if par == 0:
            amask[0] = 0.0
        else:
            amask[SLABT - 1] = 0.0
        ampad = np.zeros(17 * 128, np.float32)
        ampad[:SLABT] = amask
        amask_tm = np.ascontiguousarray(ampad.reshape(17, 128).T)
        stair = np.full((8, 128, 512), NEGBIG, np.float32)
        jj = np.arange(128)[:, None]
        ii = np.arange(512)[None, :]
        itrue = (2 * (ii // 128) + par) * 128 + (ii % 128)
        for s in range(8):
            stair[s][(s * 128 + jj) <= itrue] = 0.0
        m = dict(shared)
        m["x0T_full"] = xT
        m["x0T_slab"] = slab
        m["amask_tm"] = amask_tm
        m["stair"] = np.ascontiguousarray(stair.reshape(8 * 128, 512))
        maps.append(m)
    return maps


def assemble_output(results):
    out = np.empty((NBATCH, SEQ, D), np.float32)
    for c in range(NCORES):
        b, par = c // 2, c % 2
        sh = results[c]["out_shard"]
        for k, t in enumerate(_true_tiles(par)):
            out[b, 128 * t:128 * (t + 1), :] = sh[:, 128 * k:128 * (k + 1)].T
    return out



class Prog:
    def __init__(self, nc, ctx):
        self.nc = nc
        self.q = {e: [] for e in ENGINES}
        self.cnt = {}
        self.sem = {}
        self._last_wait = {e: {} for e in ENGINES}
        self.ctx = ctx

    def sem_new(self, *names):
        for n in names:
            self.sem[n] = self.ctx.enter_context(self.nc.semaphore(n))
            self.cnt[n] = 0

    def tok(self, sem):
        return (sem, self.cnt[sem])

    ENG_SEM = {"vector": "v", "scalar": "sc", "tensor": "mm", "gpsimd": "g",
               "sync": "sy"}

    @staticmethod
    def _flat(waits):
        out = []
        for wt in waits:
            if wt is None:
                continue
            if isinstance(wt[0], str):
                out.append(wt)
            else:
                out.extend(w for w in wt if w is not None)
        return out

    def all_toks(self):
        return [(n, self.cnt[n]) for n in self.cnt]

    def emit(self, eng, fn, waits=(), inc=None, inc_k=1, stream=None, stream_k=16):
        waits = self._flat(waits)
        auto = self.ENG_SEM[eng]
        if stream is not None:
            sname = "q_" + stream
            if sname not in self.sem:
                self.sem_new(sname)
            waits = waits + [(sname, self.cnt[sname])]
            if STRICT_SYNC or eng != "tensor":
                waits = waits + [(auto, self.cnt[auto])]
            inc, inc_k = sname, stream_k
        else:
            if STRICT_SYNC or eng != "tensor":
                waits = waits + [(auto, self.cnt[auto])]
            if inc is None:
                inc = auto
                inc_k = 1
        tok = None
        if inc is not None:
            self.cnt[inc] += inc_k
            tok = (inc, self.cnt[inc])
        lw = self._last_wait[eng]
        eff = []
        for wt in waits:
            s, v = wt
            if v <= 0 or lw.get(s, 0) >= v:
                continue
            eff.append((s, v))
            lw[s] = v
        sem = self.sem

        def thunk(e, eff=tuple(eff), fn=fn, inc=inc, inc_k=inc_k):
            for (s, v) in eff:
                e.wait_ge(sem[s], v)
            r = fn(e)
            if inc is not None:
                r.then_inc(sem[inc], inc_k)
        self.q[eng].append(thunk)
        return tok

    def wait(self, eng, *toks):
        lw = self._last_wait[eng]
        need = [wt for wt in self._flat(toks)
                if wt[1] > 0 and lw.get(wt[0], 0) < wt[1]]
        if not need:
            return
        self.emit(eng, lambda e: e.nop(), waits=need)

    def barrier(self):
        toks = [(n, self.cnt[n]) for n in self.cnt]
        for e in ENGINES:
            self.wait(e, *toks)

    def run_all(self, block):
        qs = self.q

        @block.sync
        def _(e):
            for f in qs["sync"]:
                f(e)

        @block.gpsimd
        def _(e):
            for f in qs["gpsimd"]:
                f(e)

        @block.vector
        def _(e):
            for f in qs["vector"]:
                f(e)

        @block.scalar
        def _(e):
            for f in qs["scalar"]:
                f(e)

        @block.tensor
        def _(e):
            for f in qs["tensor"]:
                f(e)



def mkap(t, offset, dims):
    return AP(t, offset, [list(d) for d in dims])



def build_nc(upto="full", dbg=()):
    nc = bass.Bass("TRN2", target_bir_lowering=False)
    dbg = set(dbg)

    dr = {}
    def din(name, shape, dt=F32):
        dr[name] = nc.dram_tensor(name, list(shape), dt, kind="ExternalInput")
    def dout(name, shape, dt=F32):
        dr[name] = nc.dram_tensor(name, list(shape), dt, kind="ExternalOutput")
    def dint(name, shape, dt=F32):
        dr[name] = nc.dram_tensor(name, list(shape), dt)

    din("x0T_full", (D, SEQ)); din("x0T_slab", (D, SLABT))
    din("WqT0", (D, D)); din("WkT0", (D, D)); din("WvT0", (D, D)); din("WoT0", (D, D))
    din("bo0", (D, 1)); din("qs_col", (D, 1)); din("ks_col", (D, 1))
    din("qs16", (16, 64)); din("ks16", (16, 64))
    din("w1T", (D, HID)); din("w2T", (HID, D)); din("convw", (HID, 3)); din("convb", (HID, 1))
    din("Wq1", (D, D)); din("Wk1", (D, D)); din("Wv1", (D, D))
    din("bq1", (D, 1)); din("bk1", (D, 1)); din("bv1", (D, 1))
    din("wgT", (D, HID)); din("wuT", (D, HID)); din("wdT", (HID, D))
    din("amask_tm", (128, 17)); din("stair", (8 * 128, 512))
    dout("out_shard", (D, OWN))

    dint("sp_xh1", (D, SEQ), BF16)
    dint("sp_xh1s", (D, SLABT), BF16)
    dint("sp_khat", (D, SEQ), BF16)
    dint("sp_qhat", (D, SLABT), BF16)
    dint("sp_x1", (D, SLABT))
    dint("sp_x2", (D, OWN))
    dint("sp_z", (D, OWN))
    dint("sp_x3", (D, OWN))
    dint("sp_k1", (D, SEQ), BF16)
    dint("sp_v1", (SEQ, D), BF16)
    dint("sp_w1t", (D, HID), BF16)
    dint("sp_w2t", (HID, D), BF16)
    dint("sp_wgt", (D, HID), BF16)
    dint("sp_wut", (D, HID), BF16)
    dint("sp_wdt", (HID, D), BF16)
    dint("cc_h3_in", (D, OWN), BF16)
    dint("cc_h3_out", (2 * D, OWN), BF16)
    dint("cc_st_in", (128, 64))
    dint("cc_st_out", (2 * 128, 64))

    with ExitStack() as ctx:
        E = ctx.enter_context
        P = Prog(nc, ctx)
        P.sem_new("mm", "v", "sc", "g", "sy")
        H = {"bank_free": {}}

        ones_col = E(nc.sbuf_tensor("ones_col", [128, 1], F32))
        ones_colb = E(nc.sbuf_tensor("ones_colb", [128, 1], BF16))
        ones_row = E(nc.sbuf_tensor("ones_row", [1, 128], F32))
        ident = E(nc.sbuf_tensor("ident", [128, 128], F32))
        onesA = E(nc.sbuf_tensor("onesA", [128, 1], F32))
        onesB = E(nc.sbuf_tensor("onesB", [128, 1], F32))
        rowsel0 = E(nc.sbuf_tensor("rowsel0", [1, 128], F32))
        rowsel1 = E(nc.sbuf_tensor("rowsel1", [1, 128], F32))
        ones64 = E(nc.sbuf_tensor("ones64", [128, 64], F32))
        ones128 = E(nc.sbuf_tensor("ones128", [128, 128], F32))
        thr_cols = E(nc.sbuf_tensor("thr_cols", [128, 16], F32))
        negB = E(nc.sbuf_tensor("negB", [128, 1], F32))
        epsl_col = E(nc.sbuf_tensor("epsl_col", [128, 1], F32))
        eps24_col = E(nc.sbuf_tensor("eps24_col", [128, 1], F32))

        P.emit("gpsimd", lambda e: e.memset(ones_col.ap()[:, :], 1.0), inc="g")
        P.emit("gpsimd", lambda e: e.memset(epsl_col.ap()[:, :], EPS_LN), inc="g")
        P.emit("gpsimd", lambda e: e.memset(eps24_col.ap()[:, :], 1e-24), inc="g")
        P.emit("gpsimd", lambda e: e.memset(ones_colb.ap()[:, :], 1.0), inc="g")
        P.emit("gpsimd", lambda e: e.memset(ones_row.ap()[:, :], 1.0), inc="g")
        P.emit("gpsimd", lambda e: e.memset(ident.ap()[:, :], 0.0), inc="g")
        P.emit("gpsimd", lambda e: e.affine_select(
            out=ident.ap()[:, :], in_=ident.ap()[:, :],
            compare_op=ALU.not_equal, fill=1.0, base=0,
            pattern=[[-1, 128]], channel_multiplier=1), inc="g")
        P.emit("gpsimd", lambda e: e.memset(onesA.ap()[:, :], 0.0), inc="g")
        P.emit("gpsimd", lambda e: e.memset(onesA.ap()[0:64, 0:1], 1.0), inc="g")
        P.emit("gpsimd", lambda e: e.memset(onesB.ap()[:, :], 0.0), inc="g")
        P.emit("gpsimd", lambda e: e.memset(onesB.ap()[64:128, 0:1], 1.0), inc="g")
        P.emit("gpsimd", lambda e: e.memset(rowsel0.ap()[:, :], 0.0), inc="g")
        P.emit("gpsimd", lambda e: e.memset(rowsel0.ap()[0:1, 0:64], 1.0), inc="g")
        P.emit("gpsimd", lambda e: e.memset(rowsel1.ap()[:, :], 0.0), inc="g")
        P.emit("gpsimd", lambda e: e.memset(rowsel1.ap()[0:1, 64:128], 1.0), inc="g")
        P.emit("gpsimd", lambda e: e.memset(ones128.ap()[:, :], 1.0), inc="g")
        P.emit("gpsimd", lambda e: e.memset(ones64.ap()[:, :], 0.0), inc="g")
        P.emit("gpsimd", lambda e: e.memset(ones64.ap()[64:65, :], 1.0), inc="g")
        t_const = P.tok("g")

        pA = E(nc.psum_tensor("pA", [128, 1024], F32))
        pB = E(nc.psum_tensor("pB", [128, 1024], F32))
        bank_s = [E(nc.psum_tensor(f"pb{i}", [128, 512], F32)) for i in range(4)]

        class Bk:
            __slots__ = ("t", "off")
            def __init__(self, t, off):
                self.t, self.off = t, off
            def ap(self):
                a = self.t.ap()
                return a[:, self.off:self.off + 512]

        bank = [Bk(pA, 0), Bk(pA, 512), Bk(pB, 0), Bk(pB, 512)] + \
               [Bk(b, 0) for b in bank_s]

        def bfree(b):
            return H["bank_free"].get(id(b))

        def bset(b, tok):
            H["bank_free"][id(b)] = tok


        WQ = [("w1T", "sp_w1t", D, HID), ("w2T", "sp_w2t", HID, D),
              ("wgT", "sp_wgt", D, HID), ("wuT", "sp_wut", D, HID),
              ("wdT", "sp_wdt", HID, D)]
        phWQ = ExitStack()
        EWQ = phWQ.enter_context
        wring = EWQ(nc.sbuf_tensor("wring", [128, 4096], F32))
        wacc = EWQ(nc.sbuf_tensor("wacc", [128, 32], F32))
        wrow = EWQ(nc.sbuf_tensor("wrow", [1, 32], F32))
        wb1 = EWQ(nc.sbuf_tensor("wb1", [128, 2048], F32))
        wb2 = EWQ(nc.sbuf_tensor("wb2", [128, 2048], F32))
        wtn = EWQ(nc.sbuf_tensor("wtn", [128, 2 * 2048], BF16))
        wq_mean_steps = []
        wq_tern_steps = []
        wq_state = {}

        def _wq_build():
            for wi, (src, dst, R, C) in enumerate(WQ):
                nch = R // 128
                colw = C
                nsl = colw // 512

                def mean_step(ch, wi=wi, src=src, R=R, C=C, nch=nch, colw=colw, nsl=nsl):
                    tld = P.emit("gpsimd", lambda e, src=src, ch=ch, colw=colw: e.dma_start(
                        out=wring.ap()[:, 0:colw],
                        in_=dr[src].ap()[128 * ch:128 * (ch + 1), :]),
                        waits=(wq_state.get(("mg",)), wq_state.get(("tg",))),
                        stream='wql')
                    ta = P.emit("vector", lambda e, ch=ch, colw=colw: e.reduce_sum(
                        wacc.ap()[:, ch:ch + 1], wring.ap()[:, 0:colw],
                        axis=AXX, apply_absolute_value=True),
                        waits=(tld, wq_state.get(("wacc_t",)) if ch == 0 else None), inc="v")
                    wq_state[("mg",)] = ta
                    wq_state[("macc", ch)] = ta
                    if ch == nch - 1:
                        tmm = P.emit("tensor", lambda e, nch=nch: e.matmul(
                            bank[6].ap()[0:1, 0:nch], ones_col.ap()[:, :],
                            wacc.ap()[:, 0:nch], start=True, stop=True),
                            waits=tuple(wq_state[("macc", c)] for c in range(nch)) +
                                  (t_const, bfree(bank[6])), inc="mm")
                        wq_state[("wacc_t",)] = tmm
                        trr = P.emit("vector", lambda e, wi=wi, nch=nch: e.reduce_sum(
                            wrow.ap()[0:1, 2 * wi:2 * wi + 1].rearrange("p a -> p a ()"),
                            bank[6].ap()[0:1, 0:nch].rearrange("p a -> p () a"), axis=AXX),
                            waits=(tmm,), inc="v")
                        tm = P.emit("vector", lambda e, wi=wi, R=R, C=C: e.tensor_scalar(
                            wrow.ap()[0:1, 2 * wi:2 * wi + 1], wrow.ap()[0:1, 2 * wi:2 * wi + 1],
                            1.0 / (R * C), 1e-5, ALU.mult, ALU.max), waits=(trr,), inc="v")
                        tb1 = P.emit("tensor", lambda e, wi=wi: e.matmul(
                            bank[6].ap()[:, 0:1], ones_row.ap()[0:1, :],
                            wrow.ap()[0:1, 2 * wi:2 * wi + 1],
                            start=True, stop=True), waits=(tm, trr), inc="mm")
                        tc1 = P.emit("vector", lambda e, wi=wi: e.tensor_copy(
                            thr_cols.ap()[:, 2 * wi + 1:2 * wi + 2], bank[6].ap()[:, 0:1]),
                            waits=(tb1,), inc="v")
                        tc2 = P.emit("vector", lambda e, wi=wi: e.tensor_single_scalar(
                            thr_cols.ap()[:, 2 * wi:2 * wi + 1], bank[6].ap()[:, 0:1], 0.5,
                            ALU.mult), waits=(tb1,), inc="v")
                        bset(bank[6], tc2)
                        tc3 = P.emit("vector", lambda e, wi=wi: e.tensor_single_scalar(
                            thr_cols.ap()[:, 10 + wi:11 + wi],
                            thr_cols.ap()[:, 2 * wi:2 * wi + 1], -1.0, ALU.mult),
                            waits=(tc2,), inc="v")
                        wq_state[("thr", wi)] = tc3

                for ch in range(nch):
                    wq_mean_steps.append((mean_step, ch))

                def tern_step(ch, wi=wi, src=src, dst=dst, colw=colw, nch=nch):
                    tld = P.emit("gpsimd", lambda e, src=src, ch=ch, colw=colw: e.dma_start(
                        out=wring.ap()[:, 0:colw],
                        in_=dr[src].ap()[128 * ch:128 * (ch + 1), :]),
                        waits=(wq_state.get(("tg",)), wq_state.get(("mg",))),
                        stream='wql')
                    tc2 = wq_state[("thr", wi)]
                    nhb = max(1, colw // 2048)
                    last = None
                    for hb in range(nhb):
                        hw = min(2048, colw)
                        off = 2048 * hb
                        rr = hb % 2
                        tg1 = P.emit("vector", lambda e, off=off, hw=hw, wi=wi: e.tensor_scalar(
                            wb1.ap()[:, 0:hw], wring.ap()[:, off:off + hw],
                            thr_cols.ap()[:, 2 * wi:2 * wi + 1], None, ALU.is_ge),
                            waits=(tld, tc2), inc="v")
                        tg2 = P.emit("vector", lambda e, off=off, hw=hw, wi=wi: e.tensor_scalar(
                            wb2.ap()[:, 0:hw], wring.ap()[:, off:off + hw],
                            thr_cols.ap()[:, 10 + wi:11 + wi], None, ALU.is_le),
                            inc="v")
                        tg3 = P.emit("vector", lambda e, off=off, hw=hw, rr=rr: e.tensor_tensor(
                            wtn.ap()[:, 2048 * rr:2048 * rr + hw], wb1.ap()[:, 0:hw],
                            wb2.ap()[:, 0:hw], ALU.subtract),
                            waits=(tg1, tg2, wq_state.get(("tsp", rr))), inc="v")
                        last = tg3
                        wq_state[("tsp", rr)] = P.emit(
                            "scalar", lambda e, dst=dst, ch=ch, hb=hb, hw=hw, rr=rr: e.dma_start(
                                out=dr[dst].ap()[128 * ch:128 * (ch + 1),
                                                 2048 * hb:2048 * hb + hw],
                                in_=wtn.ap()[:, 2048 * rr:2048 * rr + hw]),
                            waits=(tg3,), stream=f'wqs{rr}')
                    wq_state[("tg",)] = last
                    if ch == (128 if False else (dst and 0)) or True:
                        pass
                    if ch == nch - 1:
                        wq_state[("done", wi)] = (P.tok("q_wqs0"),
                                                  P.tok("q_wqs1") if "q_wqs1" in P.cnt else None)

                for ch in range(nch):
                    wq_tern_steps.append((tern_step, ch))

        _wq_build()
        _wq_i = [0, 0]

        def pump_mean(n):
            while n > 0 and _wq_i[0] < len(wq_mean_steps):
                f, a = wq_mean_steps[_wq_i[0]]
                f(a)
                _wq_i[0] += 1
                n -= 1

        def pump_tern(n):
            while n > 0 and _wq_i[1] < len(wq_tern_steps):
                if _wq_i[0] < len(wq_mean_steps):
                    pump_mean(1)
                    n -= 1
                    continue
                f, a = wq_tern_steps[_wq_i[1]]
                f(a)
                _wq_i[1] += 1
                n -= 1

        def wq_finish():
            pump_mean(10 ** 9)
            pump_tern(10 ** 9)

        def emit_ln(name, src_getter, n_cols, out_cb, rows, sqr, state=None):
            nblk = (n_cols + 511) // 512
            state = state if state is not None else {}
            ring_free = state.setdefault("ring", {})
            sq_guard = state.setdefault("sq", [None])
            ab_guard = state.setdefault("ab", [None, None])
            pend = state.setdefault("pend", [None])

            def finish(ctx2):
                cb, c0, w, r, tile, ld, t7, t8 = ctx2
                bA, bB = bank[4], bank[5]
                tA = P.emit("tensor", lambda e, w=w, bA=bA: e.matmul(
                    bA.ap()[:, 0:w], ones_row.ap()[0:1, :],
                    rows.ap()[0:1, 2048 * (cb % 2) + 1536: 2048 * (cb % 2) + 1536 + w],
                    start=True, stop=True), waits=(t7, bfree(bA)), inc="mm")
                tB = P.emit("tensor", lambda e, w=w, bB=bB: e.matmul(
                    bB.ap()[:, 0:w], ones_row.ap()[0:1, :],
                    rows.ap()[0:1, 2048 * (cb % 2) + 512: 2048 * (cb % 2) + 512 + w],
                    start=True, stop=True), waits=(t8, bfree(bB)), inc="mm")
                ab_guard[0] = tA
                ab_guard[1] = tB
                last = out_cb(cb, c0, w, r, bA, bB, (tA, tB), tile, ld)
                ring_free[r] = last
                bset(bA, last); bset(bB, last)

            for cb in range(nblk):
                c0 = 512 * cb
                w = min(512, n_cols - c0)
                r = cb % 2
                ro = 2048 * r
                bsum, bsq = bank[0 + 2 * r], bank[1 + 2 * r]
                ld, tile = src_getter(cb, c0, w, r, ring_free.get(r))
                sq_toks = []
                for fc in range(NFC):
                    st = P.emit("scalar", lambda e, fc=fc, w=w, tile=tile: e.activation(
                        sqr.ap()[:, 512 * fc:512 * fc + w], tile(fc), AF.Square,
                        bias=0.0, scale=1.0), waits=(ld, sq_guard[0]), inc="sc")
                    sq_toks.append(st)
                for fc in range(NFC):
                    P.emit("tensor", lambda e, fc=fc, w=w, tile=tile, bsum=bsum: e.matmul(
                        bsum.ap()[0:1, 0:w], ones_col.ap()[:, :], tile(fc),
                        start=(fc == 0), stop=(fc == NFC - 1)),
                        waits=(ld, t_const, bfree(bsum)) if fc == 0 else (),
                        inc="mm" if fc == NFC - 1 else None)
                t_sum = P.tok("mm")
                for fc in range(NFC):
                    P.emit("tensor", lambda e, fc=fc, w=w, bsq=bsq: e.matmul(
                        bsq.ap()[0:1, 0:w], ones_colb.ap()[:, :],
                        sqr.ap()[:, 512 * fc:512 * fc + w],
                        start=(fc == 0), stop=(fc == NFC - 1)),
                        waits=(sq_toks[fc], bfree(bsq)) if fc == 0 else (sq_toks[fc],),
                        inc="mm" if fc == NFC - 1 else None)
                t_sq = P.tok("mm")
                sq_guard[0] = t_sq
                t1 = P.emit("vector", lambda e, w=w, ro=ro, bsum=bsum: e.tensor_single_scalar(
                    rows.ap()[0:1, ro:ro + w], bsum.ap()[0:1, 0:w], 1.0 / D, ALU.mult),
                    waits=(t_sum,), inc="v")
                t2 = P.emit("vector", lambda e, w=w, ro=ro, bsq=bsq: e.tensor_single_scalar(
                    rows.ap()[0:1, ro + 512:ro + 512 + w], bsq.ap()[0:1, 0:w], 1.0 / D, ALU.mult),
                    waits=(t_sq, ab_guard[1]), inc="v")
                bset(bsum, t1); bset(bsq, t2)
                t3 = P.emit("vector", lambda e, w=w, ro=ro: e.tensor_tensor(
                    rows.ap()[0:1, ro + 1024:ro + 1024 + w], rows.ap()[0:1, ro:ro + w],
                    rows.ap()[0:1, ro:ro + w], ALU.mult), waits=(t1,), inc="v")
                t4 = P.emit("vector", lambda e, w=w, ro=ro: e.tensor_tensor(
                    rows.ap()[0:1, ro + 1024:ro + 1024 + w], rows.ap()[0:1, ro + 512:ro + 512 + w],
                    rows.ap()[0:1, ro + 1024:ro + 1024 + w], ALU.subtract), waits=(t2, t3), inc="v")
                t5 = P.emit("vector", lambda e, w=w, ro=ro: e.tensor_single_scalar(
                    rows.ap()[0:1, ro + 1024:ro + 1024 + w], rows.ap()[0:1, ro + 1024:ro + 1024 + w],
                    EPS_LN, ALU.add), waits=(t4,), inc="v")
                t6 = P.emit("vector", lambda e, w=w, ro=ro: e.reciprocal(
                    rows.ap()[0:1, ro + 1536:ro + 1536 + w], rows.ap()[0:1, ro + 1024:ro + 1024 + w]),
                    waits=(t5, ab_guard[0]), inc="v")
                t7 = P.emit("scalar", lambda e, w=w, ro=ro: e.activation(
                    rows.ap()[0:1, ro + 1536:ro + 1536 + w], rows.ap()[0:1, ro + 1536:ro + 1536 + w],
                    AF.Sqrt, bias=0.0, scale=1.0), waits=(t6,), inc="sc")
                t8 = P.emit("vector", lambda e, w=w, ro=ro: e.scalar_tensor_tensor(
                    rows.ap()[0:1, ro + 512:ro + 512 + w], rows.ap()[0:1, ro:ro + w], -1.0,
                    rows.ap()[0:1, ro + 1536:ro + 1536 + w], ALU.mult, ALU.mult),
                    waits=(t7,), inc="v")
                if pend[0] is not None:
                    finish(pend[0])
                pend[0] = (cb, c0, w, r, tile, ld, t7, t8)
            finish(pend[0])
            pend[0] = None

        with ExitStack() as ph0:
            E0 = ph0.enter_context
            xring = E0(nc.sbuf_tensor("ln0_ring", [128, 2 * NFC * 512], F32))
            rows0 = E0(nc.sbuf_tensor("ln0_rows", [1, 2 * 4 * 512], F32))
            sq0 = E0(nc.sbuf_tensor("ln0_sq", [128, NFC * 512], BF16))
            prod0 = E0(nc.sbuf_tensor("ln0_prod", [128, 512], F32))
            stg0 = E0(nc.sbuf_tensor("ln0_stg", [128, 2 * NFC * 512], BF16))
            stg_guard = {}

            def mk_ln0(src_name, spill_name, tag):
                def src(cb, c0, w, r, free_tok):
                    ld = P.emit("sync", lambda e, c0=c0, w=w, r=r: e.dma_start(
                        out=mkap(xring.ap().tensor, NFC * 512 * r,
                                 [[2 * NFC * 512, 128], [512, NFC], [1, w]]),
                        in_=dr[src_name].ap()[:, c0:c0 + w].rearrange("(f p) w -> p f w", p=128)),
                        waits=(free_tok,), stream=f'lnl{r}')
                    def tile(fc, r=r, w=w):
                        return xring.ap()[:, NFC * 512 * r + 512 * fc: NFC * 512 * r + 512 * fc + w]
                    return ld, tile

                def outcb(cb, c0, w, r, bA, bB, toks, tile, ld):
                    P.wait("vector", *toks)
                    P.wait("vector", stg_guard.get((tag, r)))
                    last = None
                    for fc in range(NFC):
                        P.emit("vector", lambda e, fc=fc, w=w, bA=bA, tile=tile: e.tensor_tensor(
                            prod0.ap()[:, 0:w], tile(fc), bA.ap()[:, 0:w], ALU.mult), inc="v")
                        last = P.emit("vector", lambda e, fc=fc, r=r, w=w, bB=bB: e.tensor_tensor(
                            stg0.ap()[:, NFC * 512 * r + 512 * fc: NFC * 512 * r + 512 * fc + w],
                            prod0.ap()[:, 0:w], bB.ap()[:, 0:w], ALU.add), inc="v")
                    tspl = None
                    for fc in range(NFC):
                        tspl = P.emit("scalar", lambda e, fc=fc, c0=c0, w=w, r=r: e.dma_start(
                            out=dr[spill_name].ap()[128 * fc:128 * (fc + 1), c0:c0 + w],
                            in_=stg0.ap()[:, NFC * 512 * r + 512 * fc: NFC * 512 * r + 512 * fc + w]),
                            waits=(last,), stream=f'ln0s{r}')
                    stg_guard[(tag, r)] = tspl
                    pump_mean(1)
                    return tspl
                return src, outcb

            ln0_state = {}
            s0, o0 = mk_ln0("x0T_full", "sp_xh1", "a")
            emit_ln("ln0", s0, SEQ, o0, rows0, sq0, state=ln0_state)
            s1, o1 = mk_ln0("x0T_slab", "sp_xh1s", "a")
            emit_ln("ln0s", s1, SLABT, o1, rows0, sq0, state=ln0_state)
            t_xh1_sp = (P.tok("q_ln0s0"), P.tok("q_ln0s1"))
            if "xh1" in dbg:
                dout("dbg_xh1", (D, SEQ), BF16)
                P.wait("sync", t_xh1_sp)
                P.emit("sync", lambda e: e.dma_start(
                    out=dr["dbg_xh1"].ap()[:, :], in_=dr["sp_xh1"].ap()[:, :]),
                    stream='dbg')
            P.barrier()

        if upto == "xh1":
            phWQ.close()
            P.wait("sync", *P.all_toks())
            block = E(nc.Block())
            P.run_all(block)
            return nc

        phv = ExitStack()
        v_sb = phv.enter_context(nc.sbuf_tensor("v_sb", [128, NT * 1040], BF16))
        qs_sb = phv.enter_context(nc.sbuf_tensor("qs_sb", [128, 8], F32))
        ks_sb = phv.enter_context(nc.sbuf_tensor("ks_sb", [128, 8], F32))
        bo_sb = phv.enter_context(nc.sbuf_tensor("bo_sb", [128, 8], F32))
        P.emit("gpsimd", lambda e: e.memset(
            v_sb.ap().rearrange("p (a b) -> p a b", b=65)[:, :, 64:65], 1.0), inc="g")
        t_vones = P.tok("g")
        P.emit("sync", lambda e: e.dma_start(
            out=qs_sb.ap().rearrange("p a -> p a ()"),
            in_=dr["qs_col"].ap().rearrange("(a p) o -> p a o", p=128),
            allow_slow_non_contiguous=True), stream='sm0')
        P.emit("sync", lambda e: e.dma_start(
            out=ks_sb.ap().rearrange("p a -> p a ()"),
            in_=dr["ks_col"].ap().rearrange("(a p) o -> p a o", p=128),
            allow_slow_non_contiguous=True), stream='sm1')
        P.emit("sync", lambda e: e.dma_start(
            out=bo_sb.ap().rearrange("p a -> p a ()"),
            in_=dr["bo0"].ap().rearrange("(a p) o -> p a o", p=128),
            allow_slow_non_contiguous=True), stream='sm2')
        t_small = (P.tok("q_sm0"), P.tok("q_sm1"), P.tok("q_sm2"))

        with ExitStack() as ph1:
            E1 = ph1.enter_context
            qs16r = E1(nc.sbuf_tensor("qs16r", [16, 64], F32))
            ks16r = E1(nc.sbuf_tensor("ks16r", [16, 64], F32))
            hmax = E1(nc.sbuf_tensor("hmax", [16, 3], F32))
            t1 = P.emit("sync", lambda e: e.dma_start(out=qs16r.ap()[:, :], in_=dr["qs16"].ap()[:, :]), stream='sm3')
            t2 = P.emit("sync", lambda e: e.dma_start(out=ks16r.ap()[:, :], in_=dr["ks16"].ap()[:, :]), stream='sm4')
            ta = P.emit("vector", lambda e: e.reduce_max(
                hmax.ap()[:, 0:1], qs16r.ap()[:, :], axis=AXX, apply_absolute_value=True),
                waits=(t1,), inc="v")
            tb = P.emit("vector", lambda e: e.reduce_max(
                hmax.ap()[:, 1:2], ks16r.ap()[:, :], axis=AXX, apply_absolute_value=True),
                waits=(t2,), inc="v")
            tc_ = P.emit("vector", lambda e: e.tensor_tensor(
                hmax.ap()[:, 2:3], hmax.ap()[:, 0:1], hmax.ap()[:, 1:2], ALU.mult),
                waits=(ta, tb), inc="v")
            hrow16 = E1(nc.sbuf_tensor("hrow16", [1, 16], F32))
            tdh = P.emit("scalar", lambda e: e.dma_start(
                out=hrow16.ap()[0:1, :], in_=hmax.ap()[:, 2:3],
                allow_slow_non_contiguous=True),
                waits=(tc_,), stream='sm3')
            trdh = P.emit("vector", lambda e: e.reduce_max(
                hmax.ap()[0:1, 2:3], hrow16.ap()[0:1, :], axis=AXX),
                waits=(tdh,), inc="v")
            tnb = P.emit("vector", lambda e: e.tensor_single_scalar(
                negB.ap()[0:1, 0:1], hmax.ap()[0:1, 2:3], -0.125, ALU.mult),
                waits=(trdh,), inc="v")
            tmm = P.emit("tensor", lambda e: e.matmul(
                bank[7].ap()[:, 0:1], ones_row.ap()[0:1, :], negB.ap()[0:1, 0:1],
                start=True, stop=True), waits=(tnb, t_const, bfree(bank[7])), inc="mm")
            tnb2 = P.emit("vector", lambda e: e.tensor_copy(negB.ap()[:, 0:1], bank[7].ap()[:, 0:1]),
                          waits=(tmm,), inc="v")
            bset(bank[7], tnb2)

            wq_sb = E1(nc.sbuf_tensor("wq_sb", [128, NFC * 1024], BF16))
            wk_sb = E1(nc.sbuf_tensor("wk_sb", [128, NFC * 1024], BF16))
            wv_sb = E1(nc.sbuf_tensor("wv_sb", [128, NFC * 1024], BF16))
            for wi_, (nm, sbw) in enumerate((("WqT0", wq_sb), ("WkT0", wk_sb), ("WvT0", wv_sb))):
                P.emit("gpsimd", lambda e, nm=nm, sbw=sbw: e.dma_start(
                    out=sbw.ap().rearrange("p (f n) -> p f n", f=NFC),
                    in_=dr[nm].ap().rearrange("(f p) n -> p f n", p=128)), stream=f'gw{wi_}')
            t_wload = (P.tok("q_gw0"), P.tok("q_gw1"), P.tok("q_gw2"))

            xr1 = E1(nc.sbuf_tensor("p1_xring", [128, 2 * NFC * 512], BF16))
            praw = E1(nc.sbuf_tensor("p1_raw", [128, 2 * 512], F32))
            psq1 = E1(nc.sbuf_tensor("p1_sq", [128, 512], F32))
            stg1 = E1(nc.sbuf_tensor("p1_stg", [128, 2 * 512], BF16))
            sumr = E1(nc.sbuf_tensor("p1_sumr", [1, 1024], F32))
            hg = {}

            def hat_pass(src_spill, n_cols, wsb, scol, spill):
                nblk = (n_cols + 511) // 512
                if HAT_LIMIT:
                    nblk = min(nblk, HAT_LIMIT)
                for cb in range(nblk):
                    c0 = 512 * cb
                    w = min(512, n_cols - c0)
                    r = cb % 2
                    ld = P.emit("sync", lambda e, c0=c0, w=w, r=r, src_spill=src_spill: e.dma_start(
                        out=mkap(xr1.ap().tensor, NFC * 512 * r,
                                 [[2 * NFC * 512, 128], [512, NFC], [1, w]]),
                        in_=dr[src_spill].ap()[:, c0:c0 + w].rearrange("(f p) w -> p f w", p=128)),
                        waits=(hg.get(("x", r)), t_xh1_sp), stream=f'lnl{r}')
                    last_r = None
                    hpend = [None]

                    def hat_norm(ec, bk, tmm, w=w, c0=c0):
                        nonlocal last_r
                        rr = ec % 2
                        trw = P.emit("vector", lambda e, rr=rr, w=w, bk=bk: e.tensor_copy(
                            praw.ap()[:, 512 * rr: 512 * rr + w], bk.ap()[:, 0:w]),
                            waits=(tmm, hg.get(("praw", rr))), inc="v")
                        bset(bk, trw)
                        if HAT_STOP and HAT_STOP < 2:
                            hg[("praw", rr)] = trw
                            last_r = trw
                            return
                        tsq = P.emit("scalar", lambda e, w=w, rr=rr: e.activation(
                            psq1.ap()[:, 0:w], praw.ap()[:, 512 * rr: 512 * rr + w],
                            AF.Square, bias=0.0, scale=1.0),
                            waits=(trw, hg.get(("psq",))), inc="sc")
                        if HAT_STOP and HAT_STOP < 3:
                            hg[("psq",)] = tsq
                            hg[("praw", rr)] = trw
                            last_r = tsq
                            return
                        P.emit("tensor", lambda e, w=w: e.matmul(
                            bank[2].ap()[0:1, 0:w], onesA.ap()[:, :], psq1.ap()[:, 0:w],
                            start=True, stop=True),
                            waits=(tsq, t_const, bfree(bank[2]), bfree(bank[3])), inc="mm")
                        tss = P.emit("tensor", lambda e, w=w: e.matmul(
                            bank[3].ap()[0:1, 0:w], onesB.ap()[:, :],
                            psq1.ap()[:, 0:w],
                            start=True, stop=True), inc="mm")
                        hg[("psq",)] = tss
                        if HAT_STOP and HAT_STOP < 4:
                            bset(bank[2], tss); bset(bank[3], tss)
                            hg[("praw", rr)] = trw
                            last_r = tss
                            return
                        tsc0 = P.emit("vector", lambda e, w=w: e.tensor_copy(
                            sumr.ap()[0:1, 0:w], bank[2].ap()[0:1, 0:w]),
                            waits=(tss, hg.get(("sumr",))), inc="v")
                        tsc1 = P.emit("vector", lambda e, w=w: e.tensor_copy(
                            sumr.ap()[0:1, 512:512 + w], bank[3].ap()[0:1, 0:w]), inc="v")
                        bset(bank[2], tsc1); bset(bank[3], tsc1)
                        ts1 = P.emit("scalar", lambda e, w=w: e.activation(
                            sumr.ap()[0:1, 0:w], sumr.ap()[0:1, 0:w],
                            AF.Sqrt, bias=0.0, scale=1.0),
                            waits=(tsc0,), inc="sc")
                        ts1b = P.emit("scalar", lambda e, w=w: e.activation(
                            sumr.ap()[0:1, 512:512 + w], sumr.ap()[0:1, 512:512 + w],
                            AF.Sqrt, bias=0.0, scale=1.0), waits=(tsc1,), inc="sc")
                        ts2 = P.emit("vector", lambda e, w=w: e.tensor_single_scalar(
                            sumr.ap()[0:1, 0:1024], sumr.ap()[0:1, 0:1024], 1e-12, ALU.max),
                            waits=(ts1, ts1b), inc="v")
                        ts3b = ts3 = P.emit("vector", lambda e, w=w: e.reciprocal(
                            sumr.ap()[0:1, 0:1024], sumr.ap()[0:1, 0:1024]),
                            waits=(ts2,), inc="v")
                        if HAT_STOP and HAT_STOP < 5:
                            hg[("sumr",)] = ts3b
                            hg[("praw", rr)] = trw
                            last_r = ts3b
                            return
                        if HAT_STOP and HAT_STOP < 6:
                            hg[("sumr",)] = ts3b
                            hg[("praw", rr)] = trw
                            last_r = ts3b
                            return
                        P.emit("tensor", lambda e, w=w: e.matmul(
                            bank[4].ap()[:, 0:w], rowsel0.ap()[0:1, :],
                            sumr.ap()[0:1, 0:w],
                            start=True, stop=False),
                            waits=(ts3, ts3b, bfree(bank[4])), inc="mm")
                        tbc = P.emit("tensor", lambda e, w=w: e.matmul(
                            bank[4].ap()[:, 0:w], rowsel1.ap()[0:1, :],
                            sumr.ap()[0:1, 512:512 + w],
                            start=False, stop=True), inc="mm")
                        hg[("sumr",)] = tbc
                        if HAT_STOP and HAT_STOP < 7:
                            bset(bank[4], tbc)
                            hg[("praw", rr)] = trw
                            last_r = tbc
                            return
                        tht = P.emit("vector", lambda e, ec=ec, rr=rr, w=w, scol=scol: e.scalar_tensor_tensor(
                            stg1.ap()[:, 512 * rr: 512 * rr + w],
                            praw.ap()[:, 512 * rr: 512 * rr + w], scol.ap()[:, ec:ec + 1],
                            bank[4].ap()[:, 0:w], ALU.mult, ALU.mult),
                            waits=(tbc, trw, t_small, hg.get(("stg", rr))), inc="v")
                        bset(bank[4], tht)
                        hg[("praw", rr)] = tht
                        if HAT_STOP and HAT_STOP < 8:
                            last_r = tht
                            hg[("stg", rr)] = tht
                            return
                        last_r = P.emit("scalar", lambda e, ec=ec, c0=c0, w=w, rr=rr, spill=spill: e.dma_start(
                            out=dr[spill].ap()[128 * ec:128 * (ec + 1), c0:c0 + w],
                            in_=stg1.ap()[:, 512 * rr: 512 * rr + w]),
                            waits=(tht,), stream=f'hats{rr}')
                        hg[("stg", rr)] = last_r

                    for ec in range(HAT_ECS or NFC):
                        bk = bank[0] if ec % 2 == 0 else bank[1]
                        for fc in range(NFC):
                            P.emit("tensor", lambda e, ec=ec, fc=fc, r=r, w=w, bk=bk, wsb=wsb: e.matmul(
                                bk.ap()[:, 0:w],
                                wsb.ap()[:, 1024 * fc + 128 * ec: 1024 * fc + 128 * (ec + 1)],
                                xr1.ap()[:, NFC * 512 * r + 512 * fc: NFC * 512 * r + 512 * fc + w],
                                start=(fc == 0), stop=(fc == NFC - 1)),
                                waits=(ld, t_wload, bfree(bk)) if fc == 0 else (),
                                inc="mm" if fc == NFC - 1 else None)
                        tmm = P.tok("mm")
                        if hpend[0] is not None:
                            hat_norm(*hpend[0])
                        hpend[0] = (ec, bk, tmm)
                    hat_norm(*hpend[0])
                    hg[("x", r)] = last_r
                    pump_mean(1)

            if upto != "negb":
                hat_pass("sp_xh1", SEQ, wk_sb, ks_sb, "sp_khat")
                t_khat = tuple(P.tok(q) for q in ("q_hats0", "q_hats1") if q in P.cnt)
            if upto in ("negb", "khat"):
                P.barrier()
                ph1.close()
                phv.close()
                phWQ.close()
                block = E(nc.Block())
                P.run_all(block)
                return nc
            hat_pass("sp_xh1s", SLABT, wq_sb, qs_sb, "sp_qhat")
            t_qhat = tuple(P.tok(q) for q in ("q_hats0", "q_hats1") if q in P.cnt)

            for cb in range(SEQ // 512):
                c0 = 512 * cb
                r = cb % 2
                ld = P.emit("sync", lambda e, c0=c0, r=r: e.dma_start(
                    out=xr1.ap()[:, NFC * 512 * r: NFC * 512 * (r + 1)].rearrange(
                        "p (f w) -> p f w", f=NFC),
                    in_=dr["sp_xh1"].ap()[:, c0:c0 + 512].rearrange("(f p) w -> p f w", p=128)),
                    waits=(hg.get(("x", r)), P.tok("mm") if cb < 2 else None),
                    stream=f'lnl{r}')
                last_mm = None
                for sub in range(4):
                    jt = 4 * cb + sub
                    for en in range(2):
                        bk = bank[0] if en == 0 else bank[1]
                        for fc in range(NFC):
                            P.emit("tensor", lambda e, sub=sub, en=en, fc=fc, r=r, bk=bk: e.matmul(
                                bk.ap()[:, 0:512],
                                xr1.ap()[:, NFC * 512 * r + 512 * fc + 128 * sub:
                                          NFC * 512 * r + 512 * fc + 128 * (sub + 1)],
                                wv_sb.ap()[:, 1024 * fc + 512 * en: 1024 * fc + 512 * (en + 1)],
                                start=(fc == 0), stop=(fc == NFC - 1)),
                                waits=(ld, t_wload, t_vones, bfree(bk)) if fc == 0 else (),
                                inc="mm" if fc == NFC - 1 else None)
                        tmm = P.tok("mm")
                        tev = P.emit("vector", lambda e, jt=jt, en=en, bk=bk: e.tensor_copy(
                            v_sb.ap()[:, 1040 * jt + 520 * en: 1040 * jt + 520 * (en + 1)].rearrange(
                                "p (h x) -> p h x", h=8)[:, :, 0:64],
                            bk.ap()[:, 0:512].rearrange("p (h x) -> p h x", h=8)),
                            waits=(tmm,), inc="v")
                        bset(bk, tev)
                hg[("x", r)] = P.tok("v")
                pump_mean(1)
            t_vproj = P.tok("v")
            P.barrier()

        if upto == "hat":
            P.barrier()
            phv.close()
            phWQ.close()
            block = E(nc.Block())
            P.run_all(block)
            return nc

        pump_mean(10 ** 9)

        attnout = [phv.enter_context(nc.sbuf_tensor(f"ao_{ec}", [128, SLABT], BF16))
                   for ec in range(NFC)]
        with ExitStack() as ph1b:
            E1b = ph1b.enter_context
            kh_ring = E1b(nc.sbuf_tensor("kh_ring", [128, 2 * SEQ], BF16))
            qh_ring = E1b(nc.sbuf_tensor("qh_ring", [128, 2 * SLABT], BF16))
            e_ring = E1b(nc.sbuf_tensor("e_ring", [128, 3 * 1024], BF16))
            rec_sb = E1b(nc.sbuf_tensor("rec_sb", [128, 2 * 512], F32))
            den_sb = E1b(nc.sbuf_tensor("den_sb", [128, 2 * 512], F32))
            h1tmp = E1b(nc.sbuf_tensor("h1tmp", [128, 512], BF16))
            kq_free = {}
            for ec in range(NFC):
                r = ec % 2
                tkh = P.emit("sync", lambda e, ec=ec, r=r: e.dma_start(
                    out=kh_ring.ap()[:, SEQ * r: SEQ * (r + 1)],
                    in_=dr["sp_khat"].ap()[128 * ec:128 * (ec + 1), :]),
                    waits=(kq_free.get(r), t_khat), stream=f'khl{r}')
                tqh = P.emit("sync", lambda e, ec=ec, r=r: e.dma_start(
                    out=qh_ring.ap()[:, SLABT * r: SLABT * (r + 1)],
                    in_=dr["sp_qhat"].ap()[128 * ec:128 * (ec + 1), :]),
                    waits=(kq_free.get(r), t_qhat), stream=f'qhl{r}')
                av_tok = {}
                for ic, (i0, iw) in enumerate(I_CHUNKS):
                    bR0, bR1 = bank[4], bank[5]
                    tevprev = (bfree(bR0), bfree(bR1))
                    pend = None
                    for jt in range(NT):
                        sp = pA if jt % 2 == 0 else pB
                        spk = ("S", jt % 2)
                        wA = (tkh, tqh, t_vproj, tnb2, H["bank_free"].get(spk))
                        if jt >= 2:
                            wA = (H["bank_free"].get(spk),)
                        P.emit("tensor", lambda e, r=r, jt=jt, i0=i0, iw=iw, sp=sp: e.matmul(
                            sp.ap()[:, 0:iw],
                            kh_ring.ap()[0:64, SEQ * r + 128 * jt: SEQ * r + 128 * (jt + 1)],
                            qh_ring.ap()[0:64, SLABT * r + i0: SLABT * r + i0 + iw],
                            start=True, stop=True, tile_position=(0, 0)),
                            waits=wA, inc=None)
                        tqk = P.emit("tensor", lambda e, r=r, jt=jt, i0=i0, iw=iw, sp=sp: e.matmul(
                            sp.ap()[:, 512:512 + iw],
                            kh_ring.ap()[64:128, SEQ * r + 128 * jt: SEQ * r + 128 * (jt + 1)],
                            qh_ring.ap()[64:128, SLABT * r + i0: SLABT * r + i0 + iw],
                            start=True, stop=True, tile_position=(64, 0)),
                            inc="mm")
                        er = jt % 3
                        if iw == 512:
                            tex = P.emit("scalar", lambda e, er=er, sp=sp: e.activation(
                                e_ring.ap()[:, 1024 * er: 1024 * (er + 1)], sp.ap()[:, :],
                                AF.Exp, bias=negB.ap()[:, 0:1], scale=0.125),
                                waits=(tqk, av_tok.get(er)), inc="sc")
                        else:
                            P.emit("scalar", lambda e, er=er, iw=iw, sp=sp: e.activation(
                                e_ring.ap()[:, 1024 * er: 1024 * er + iw], sp.ap()[:, 0:iw],
                                AF.Exp, bias=negB.ap()[:, 0:1], scale=0.125),
                                waits=(tqk, av_tok.get(er)), inc="sc")
                            tex = P.emit("scalar", lambda e, er=er, iw=iw, sp=sp: e.activation(
                                e_ring.ap()[:, 1024 * er + 512: 1024 * er + 512 + iw],
                                sp.ap()[:, 512:512 + iw],
                                AF.Exp, bias=negB.ap()[:, 0:1], scale=0.125), inc="sc")
                        H["bank_free"][spk] = tex

                        def emit_av(jt, er, tex, iw=iw, ec=ec, bR0=bR0, bR1=bR1,
                                    tevprev=tevprev):
                            P.emit("tensor", lambda e, jt=jt, er=er: e.matmul(
                                bR0.ap()[0:65, 0:iw],
                                v_sb.ap()[:, 1040 * jt + 65 * (2 * ec): 1040 * jt + 65 * (2 * ec) + 65],
                                e_ring.ap()[:, 1024 * er: 1024 * er + iw],
                                start=(jt == 0), stop=(jt == NT - 1)),
                                waits=(tex,) + (tevprev if jt == 0 else ()), inc=None)
                            tav = P.emit("tensor", lambda e, jt=jt, er=er: e.matmul(
                                bR1.ap()[0:65, 0:iw],
                                v_sb.ap()[:, 1040 * jt + 65 * (2 * ec + 1): 1040 * jt + 65 * (2 * ec + 1) + 65],
                                e_ring.ap()[:, 1024 * er + 512: 1024 * er + 512 + iw],
                                start=(jt == 0), stop=(jt == NT - 1)),
                                inc="mm")
                            av_tok[er] = tav
                            return tav

                        if pend is not None:
                            tav = emit_av(*pend)
                        pend = (jt, er, tex)
                    tav = emit_av(*pend)
                    tr0 = P.emit("vector", lambda e, iw=iw, bR0=bR0: e.tensor_copy(
                        rec_sb.ap()[64:65, 0:iw], bR0.ap()[64:65, 0:iw]), waits=(tav,), inc="v")
                    tr1 = P.emit("vector", lambda e, iw=iw, bR1=bR1: e.tensor_copy(
                        rec_sb.ap()[64:65, 512:512 + iw], bR1.ap()[64:65, 0:iw]), inc="v")
                    tb0 = P.emit("tensor", lambda e, iw=iw: e.matmul(
                        bank[6].ap()[0:64, 0:iw], ones64.ap()[64:65, :],
                        rec_sb.ap()[64:65, 0:iw], start=True, stop=True),
                        waits=(tr0, tr1, bfree(bank[6]), bfree(bank[7])), inc="mm")
                    tb1 = P.emit("tensor", lambda e, iw=iw: e.matmul(
                        bank[7].ap()[0:64, 0:iw], ones64.ap()[64:65, :],
                        rec_sb.ap()[64:65, 512:512 + iw], start=True, stop=True),
                        inc="mm")
                    td0 = P.emit("vector", lambda e, iw=iw: e.reciprocal(
                        den_sb.ap()[0:64, 0:iw], bank[6].ap()[0:64, 0:iw]),
                        waits=(tb0, tb1), inc="v")
                    td1 = P.emit("vector", lambda e, iw=iw: e.reciprocal(
                        den_sb.ap()[0:64, 512:512 + iw], bank[7].ap()[0:64, 0:iw]),
                        inc="v")
                    te0 = P.emit("vector", lambda e, ec=ec, i0=i0, iw=iw, bR0=bR0: e.tensor_tensor(
                        attnout[ec].ap()[0:64, i0:i0 + iw], bR0.ap()[0:64, 0:iw],
                        den_sb.ap()[0:64, 0:iw], ALU.mult), waits=(td0,), inc="v")
                    te1 = P.emit("vector", lambda e, iw=iw, bR1=bR1: e.tensor_tensor(
                        h1tmp.ap()[0:64, 0:iw], bR1.ap()[0:64, 0:iw],
                        den_sb.ap()[0:64, 512:512 + iw], ALU.mult),
                        waits=(td1, kq_free.get(("h1",))), inc="v")
                    tdm = P.emit("scalar", lambda e, ec=ec, i0=i0, iw=iw: e.dma_start(
                        out=attnout[ec].ap()[64:128, i0:i0 + iw], in_=h1tmp.ap()[0:64, 0:iw]),
                        waits=(te1,), stream='h1s')
                    kq_free[("h1",)] = tdm
                    bset(bank[6], td0); bset(bank[7], td1)
                    bset(bR0, te0); bset(bR1, te1)
                kq_free[r] = P.tok("mm")
                pump_tern(3)
            t_attn_done = (P.tok("v"), P.tok("q_h1s"))
            P.barrier()

        if upto == "attn":
            P.barrier()
            phv.close()
            phWQ.close()
            block = E(nc.Block())
            P.run_all(block)
            return nc

        with ExitStack() as ph1c:
            E1c = ph1c.enter_context
            wo_sb = E1c(nc.sbuf_tensor("wo_sb", [128, NFC * 1024], BF16))
            x0ring = E1c(nc.sbuf_tensor("x0ring", [128, 4 * 512], F32))
            x1stg = E1c(nc.sbuf_tensor("x1stg", [128, 4 * 512], F32))
            P.emit("gpsimd", lambda e: e.dma_start(
                out=wo_sb.ap().rearrange("p (f n) -> p f n", f=NFC),
                in_=dr["WoT0"].ap().rearrange("(f p) n -> p f n", p=128)), stream='gw0')
            t_wo = P.tok("q_gw0")
            ring_free = {}
            for ic, (i0, iw) in enumerate(I_CHUNKS):
                for dc in range(NFC):
                    r = (ic * NFC + dc) % 2
                    tx0 = P.emit("sync", lambda e, dc=dc, i0=i0, iw=iw, r=r: e.dma_start(
                        out=x0ring.ap()[:, 1024 * r: 1024 * r + iw],
                        in_=dr["x0T_slab"].ap()[128 * dc:128 * (dc + 1), i0:i0 + iw]),
                        waits=(ring_free.get(("x0", r)),), stream=f'x0l{r}')
                    bk = bank[0] if dc % 2 == 0 else bank[1]
                    for ec in range(NFC):
                        P.emit("tensor", lambda e, dc=dc, ec=ec, i0=i0, iw=iw, bk=bk: e.matmul(
                            bk.ap()[:, 0:iw],
                            wo_sb.ap()[:, 1024 * ec + 128 * dc: 1024 * ec + 128 * (dc + 1)],
                            attnout[ec].ap()[:, i0:i0 + iw],
                            start=(ec == 0), stop=(ec == NFC - 1)),
                            waits=((t_wo, bfree(bk)) + t_attn_done) if ec == 0 else (),
                            inc="mm" if ec == NFC - 1 else None)
                    tmm = P.tok("mm")
                    tev = P.emit("vector", lambda e, dc=dc, r=r, iw=iw, bk=bk: e.scalar_tensor_tensor(
                        x1stg.ap()[:, 1024 * r: 1024 * r + iw],
                        bk.ap()[:, 0:iw], bo_sb.ap()[:, dc:dc + 1],
                        x0ring.ap()[:, 1024 * r: 1024 * r + iw],
                        ALU.add, ALU.add),
                        waits=(tmm, tx0, t_small, ring_free.get(("x1", r))), inc="v")
                    bset(bk, tev)
                    ring_free[("x0", r)] = tev
                    ring_free[("x1", r)] = P.emit(
                        "scalar", lambda e, dc=dc, i0=i0, iw=iw, r=r: e.dma_start(
                            out=dr["sp_x1"].ap()[128 * dc:128 * (dc + 1), i0:i0 + iw],
                            in_=x1stg.ap()[:, 1024 * r: 1024 * r + iw]),
                        waits=(tev,), stream=f'x1s{r}')
                pump_tern(3)
            t_x1_sp = (P.tok("q_x1s0"), P.tok("q_x1s1"))
            if "x1" in dbg:
                dout("dbg_x1", (D, SLABT), F32)
                P.wait("sync", t_x1_sp)
                P.emit("sync", lambda e: e.dma_start(
                    out=dr["dbg_x1"].ap()[:, :], in_=dr["sp_x1"].ap()[:, :]), stream='dbg')
            P.barrier()

        wq_finish()
        P.barrier()
        phv.close()
        phWQ.close()

        if upto == "x1":
            P.wait("sync", *P.all_toks())
            block = E(nc.Block())
            P.run_all(block)
            return nc

        t_w1t = wq_state[("done", 0)]
        t_w2t = wq_state[("done", 1)]
        dr["sp_y"] = nc.dram_tensor("sp_y", [HID, OWN], BF16)
        dr["sp_yq"] = nc.dram_tensor("sp_yq", [HID, OWN], BF16)
        cw_sb = E(nc.sbuf_tensor("cw_sb", [128, NHC * 3], F32))
        cb_sb = E(nc.sbuf_tensor("cb_sb", [128, NHC], F32))
        amel = E(nc.sbuf_tensor("amel", [128, 17], F32))
        amtm = E(nc.sbuf_tensor("amtm", [128, 17], F32))
        P.emit("sync", lambda e: e.dma_start(
            out=cw_sb.ap().rearrange("p (h t) -> p h t", h=NHC),
            in_=dr["convw"].ap().rearrange("(h p) t -> p h t", p=128)), stream='sm0')
        P.emit("sync", lambda e: e.dma_start(
            out=cb_sb.ap().rearrange("p h -> p h ()"),
            in_=dr["convb"].ap().rearrange("(h p) o -> p h o", p=128),
            allow_slow_non_contiguous=True), stream='sm1')
        tam = P.emit("sync", lambda e: e.dma_start(out=amtm.ap()[:, :], in_=dr["amask_tm"].ap()[:, :]),
                     stream='sm2')
        P.emit("vector", lambda e: e.tensor_single_scalar(
            amel.ap()[:, :], amtm.ap()[:, :], 1.0 / 127.0, ALU.mult), waits=(tam,), inc="v")
        t_cw = (P.tok("q_sm0"), P.tok("q_sm1"), P.tok("q_sm2"))
        t_amel = P.tok("v")

        with ExitStack() as ph2:
            E2o = ph2.enter_context
            xq2 = [E2o(nc.sbuf_tensor(f"xq2_{fc}", [128, SLABT], BF16)) for fc in range(NFC)]
            alpha = E2o(nc.sbuf_tensor("alpha_sb", [128, SLABT], F32))
            my_sb = E2o(nc.sbuf_tensor("my_sb", [128, OWN], F32))

            with ExitStack() as ph2a:
                E2 = ph2a.enter_context
                ring2 = E2(nc.sbuf_tensor("ln1_ring", [128, 2 * NFC * 512], F32))
                rows2 = E2(nc.sbuf_tensor("ln1_rows", [1, 2 * 4 * 512], F32))
                sq2 = E2(nc.sbuf_tensor("ln1_sq", [128, NFC * 512], BF16))
                xh2r = E2(nc.sbuf_tensor("xh2r", [128, NFC * 512], F32))
                mfold = E2(nc.sbuf_tensor("mfold", [128, 512], F32))
                fscr2 = E2(nc.sbuf_tensor("fscr2", [128, 512], F32))
                amax2 = E2(nc.sbuf_tensor("amax2", [128, 4], F32))
                srep = E2(nc.sbuf_tensor("srep2", [128, 129], F32))
                s1b = E2(nc.sbuf_tensor("s1b", [128, 512], F32))
                P.emit("gpsimd", lambda e: e.memset(amax2.ap()[:, :], 1.0), inc="g")
                t_am2i = P.tok("g")
                lng = {"mf": None, "sr": None}

                def ln1_src(cb, c0, w, r, free_tok):
                    ld = P.emit("sync", lambda e, c0=c0, w=w, r=r: e.dma_start(
                        out=mkap(ring2.ap().tensor, NFC * 512 * r,
                                 [[2 * NFC * 512, 128], [512, NFC], [1, w]]),
                        in_=dr["sp_x1"].ap()[:, c0:c0 + w].rearrange("(f p) w -> p f w", p=128)),
                        waits=(free_tok, t_x1_sp), stream=f'lnl{r}')
                    def tile(fc, r=r, w=w):
                        return ring2.ap()[:, NFC * 512 * r + 512 * fc: NFC * 512 * r + 512 * fc + w]
                    return ld, tile

                def ln1_out(cb, c0, w, r, bA, bB, toks, tile, ld):
                    P.wait("vector", *toks)
                    I32 = mybir.dt.int32
                    tts = []
                    for fc in range(NFC):
                        P.emit("vector", lambda e, fc=fc, w=w, bA=bA, tile=tile: e.tensor_tensor(
                            xh2r.ap()[:, 512 * fc:512 * fc + w], tile(fc), bA.ap()[:, 0:w],
                            ALU.mult), inc="v")
                        tt = P.emit("vector", lambda e, fc=fc, w=w, bB=bB: e.tensor_tensor(
                            xh2r.ap()[:, 512 * fc:512 * fc + w], xh2r.ap()[:, 512 * fc:512 * fc + w],
                            bB.ap()[:, 0:w], ALU.add), inc="v")
                        tts.append(tt)
                    if LN1_STOP and LN1_STOP < 2:
                        return tts[-1]
                    tf = P.emit("vector", lambda e, w=w: e.tensor_single_scalar(
                        mfold.ap()[:, 0:w].bitcast(I32), xh2r.ap()[:, 0:w].bitcast(I32),
                        0x7FFFFFFF, ALU.bitwise_and), waits=(tts[0], lng["mf"]), inc="v")
                    for fc in range(1, NFC):
                        P.emit("vector", lambda e, fc=fc, w=w: e.tensor_single_scalar(
                            fscr2.ap()[:, 0:w].bitcast(I32),
                            xh2r.ap()[:, 512 * fc:512 * fc + w].bitcast(I32),
                            0x7FFFFFFF, ALU.bitwise_and), waits=(tts[fc],), inc="v")
                        tf = P.emit("vector", lambda e, w=w: e.tensor_tensor(
                            mfold.ap()[:, 0:w], mfold.ap()[:, 0:w], fscr2.ap()[:, 0:w], ALU.max),
                            inc="v")
                    if LN1_STOP and LN1_STOP < 3:
                        return tf
                    ng = (w + 127) // 128
                    tam_ = []
                    for g in range(ng):
                        gw = min(128, w - 128 * g)
                        ttr = P.emit("tensor", lambda e, g=g, gw=gw: e.transpose(
                            bank[2].ap()[0:gw, 0:128],
                            mfold.ap()[:, 128 * g:128 * g + gw], ident.ap()[:, :]),
                            waits=(tf, t_am2i, t_const, bfree(bank[2])),
                            inc="mm")
                        tam_.append(P.emit("vector", lambda e, g=g, gw=gw: e.reduce_max(
                            amax2.ap()[0:gw, g:g + 1], bank[2].ap()[0:gw, 0:128],
                            axis=AXX), waits=(ttr,), inc="v"))
                        bset(bank[2], tam_[-1])
                        lng["mf"] = ttr
                    tcl = P.emit("vector", lambda e, ng=ng: e.tensor_single_scalar(
                        amax2.ap()[:, 0:ng], amax2.ap()[:, 0:ng], 1e-5, ALU.max),
                        waits=tuple(tam_), inc="v")
                    if LN1_STOP and LN1_STOP < 4:
                        return tcl
                    for g in range(ng):
                        gw = min(128, w - 128 * g)
                        gabs = c0 // 128 + g
                        trc = P.emit("vector", lambda e, g=g: e.reciprocal(
                            srep.ap()[:, 128:129], amax2.ap()[:, g:g + 1]), waits=(tcl,), inc="v")
                        trp = P.emit("vector", lambda e: e.tensor_scalar(
                            srep.ap()[:, 0:128], ones128.ap()[:, :],
                            srep.ap()[:, 128:129], 127.0, ALU.mult, ALU.mult),
                            waits=(trc, lng["sr"]), inc="v")
                        ttr = P.emit("tensor", lambda e: e.transpose(
                            bank[3].ap()[:, 0:128], srep.ap()[:, 0:128], ident.ap()[:, :]),
                            waits=(trp, bfree(bank[3])), inc="mm")
                        tcp = P.emit("vector", lambda e, g=g, gw=gw: e.tensor_copy(
                            s1b.ap()[:, 128 * g:128 * g + gw], bank[3].ap()[:, 0:gw]),
                            waits=(ttr,), inc="v")
                        bset(bank[3], tcp)
                        if LN1_STOP and LN1_STOP < 5:
                            last = tcp
                            continue
                        ta1 = P.emit("vector", lambda e, g=g, gabs=gabs: e.tensor_scalar(
                            srep.ap()[:, 128:129], amax2.ap()[:, g:g + 1],
                            thr_cols.ap()[:, 1:2], amel.ap()[:, gabs:gabs + 1],
                            ALU.mult, ALU.mult),
                            waits=(tcl, t_amel), inc="v")
                        tra = P.emit("vector", lambda e: e.tensor_scalar(
                            srep.ap()[:, 0:128], ones128.ap()[:, :],
                            srep.ap()[:, 128:129], None, ALU.mult),
                            waits=(ta1, ttr), inc="v")
                        ttr2 = P.emit("tensor", lambda e: e.transpose(
                            bank[3].ap()[:, 0:128], srep.ap()[:, 0:128], ident.ap()[:, :]),
                            waits=(tra, bfree(bank[3])), inc="mm")
                        lng["sr"] = ttr2
                        tca = P.emit("vector", lambda e, gabs=gabs, gw=gw: e.tensor_copy(
                            alpha.ap()[:, 128 * gabs:128 * gabs + gw], bank[3].ap()[:, 0:gw]),
                            waits=(ttr2,), inc="v")
                        bset(bank[3], tca)
                        last = tca
                        if LN1_STOP and LN1_STOP < 6:
                            continue
                        for fc in range(NFC):
                            P.emit("vector", lambda e, fc=fc, g=g, gw=gw: e.tensor_tensor(
                                fscr2.ap()[:, 0:gw],
                                xh2r.ap()[:, 512 * fc + 128 * g: 512 * fc + 128 * g + gw],
                                s1b.ap()[:, 128 * g:128 * g + gw], ALU.mult),
                                waits=(tcp,), inc="v")
                            last = P.emit("vector", lambda e, fc=fc, c0=c0, g=g, gw=gw: e.tensor_scalar(
                                xq2[fc].ap()[:, c0 + 128 * g: c0 + 128 * g + gw],
                                fscr2.ap()[:, 0:gw],
                                MRND, MRND, ALU.add, ALU.subtract), inc="v")
                    return last

                emit_ln("ln1", ln1_src, SLABT, ln1_out, rows2, sq2)
                t_xq2 = P.tok("v")
                P.barrier()
                if "ln1dbg" in dbg:
                    dout("dbg_amax2", (128, 4), F32)
                    dout("dbg_mfold", (128, 512), F32)
                    dout("dbg_xh2r", (128, NFC * 512), F32)
                    P.emit("sync", lambda e: e.dma_start(
                        out=dr["dbg_amax2"].ap()[:, :], in_=amax2.ap()[:, :]), stream='dbg')
                    P.emit("sync", lambda e: e.dma_start(
                        out=dr["dbg_mfold"].ap()[:, :], in_=mfold.ap()[:, :]), stream='dbg')
                    P.emit("sync", lambda e: e.dma_start(
                        out=dr["dbg_xh2r"].ap()[:, :], in_=xh2r.ap()[:, :]), stream='dbg')
                    P.barrier()
                if "ffn1" in dbg:
                    dout("dbg_xq2", (D, SLABT), BF16)
                    dout("dbg_alpha", (128, SLABT), F32)
                    for fc in range(NFC):
                        P.emit("sync", lambda e, fc=fc: e.dma_start(
                            out=dr["dbg_xq2"].ap()[128 * fc:128 * (fc + 1), :],
                            in_=xq2[fc].ap()[:, :]), stream='dbg')
                    P.emit("sync", lambda e: e.dma_start(
                        out=dr["dbg_alpha"].ap()[:, :], in_=alpha.ap()[:, :]), stream='dbg')
                    P.barrier()

            if upto == "ln1":
                P.barrier()
                ph2.close()
                block = E(nc.Block())
                P.run_all(block)
                return nc

            with ExitStack() as ph2b:
                E2 = ph2b.enter_context
                w1ring = E2(nc.sbuf_tensor("w1ring", [128, 2 * NFC * 128], BF16))
                hrow = E2(nc.sbuf_tensor("hrow", [128, 2 * SLABT], F32))
                cscr = E2(nc.sbuf_tensor("cscr", [128, OWN], F32))
                ystg = E2(nc.sbuf_tensor("ystg", [128, 2 * OWN], BF16))
                yabs = E2(nc.sbuf_tensor("yabs", [128, OWN], BF16))
                w1_free = {}
                hrow_free = {}
                yst_free = {}
                y_tok = {}
                tmy = None
                for hc in range(NHC):
                    r = hc % 2
                    tw = P.emit("sync", lambda e, hc=hc, r=r: e.dma_start(
                        out=w1ring.ap()[:, NFC * 128 * r: NFC * 128 * (r + 1)].rearrange(
                            "p (f w) -> p f w", f=NFC),
                        in_=dr["sp_w1t"].ap()[:, 128 * hc:128 * (hc + 1)].rearrange(
                            "(f p) w -> p f w", p=128)),
                        waits=(w1_free.get(r), t_w1t), stream=f'w1l{r}')
                    rh = hc % 2
                    tev_all = []
                    for tb, (i0, iw) in enumerate(I_CHUNKS):
                        bk = bank[tb % 4]
                        for fc in range(NFC):
                            P.emit("tensor", lambda e, fc=fc, r=r, i0=i0, iw=iw, bk=bk: e.matmul(
                                bk.ap()[:, 0:iw],
                                w1ring.ap()[:, NFC * 128 * r + 128 * fc: NFC * 128 * r + 128 * (fc + 1)],
                                xq2[fc].ap()[:, i0:i0 + iw],
                                start=(fc == 0), stop=(fc == NFC - 1)),
                                waits=(tw, t_xq2, bfree(bk)) if fc == 0 else (),
                                inc="mm" if fc == NFC - 1 else None)
                        tmm = P.tok("mm")
                        tev = P.emit("vector", lambda e, rh=rh, i0=i0, iw=iw, bk=bk: e.tensor_tensor(
                            hrow.ap()[:, SLABT * rh + i0: SLABT * rh + i0 + iw],
                            bk.ap()[:, 0:iw], alpha.ap()[:, i0:i0 + iw], ALU.mult),
                            waits=(tmm, hrow_free.get(rh)), inc="v")
                        bset(bk, tev)
                        tev_all.append(tev)
                    w1_free[r] = tev_all[-1]
                    h3d = hrow.ap()[:, SLABT * rh: SLABT * (rh + 1)].rearrange(
                        "p (s u) -> p s u", u=SLABW)
                    cv = cscr.ap()[:, 0:OWN].rearrange("p (s u) -> p s u", u=128)
                    tc1 = P.emit("vector", lambda e, hc=hc, h3d=h3d, cv=cv: e.tensor_scalar(
                        cv, h3d[:, :, 0:128], cw_sb.ap()[:, 3 * hc: 3 * hc + 1],
                        cb_sb.ap()[:, hc:hc + 1], ALU.mult, ALU.add),
                        waits=tuple(tev_all) + (t_cw, hrow_free.get(("cs",))), inc="v")
                    tc2 = P.emit("vector", lambda e, hc=hc, h3d=h3d, cv=cv: e.scalar_tensor_tensor(
                        cv, h3d[:, :, 1:129], cw_sb.ap()[:, 3 * hc + 1: 3 * hc + 2],
                        cv, ALU.mult, ALU.add), inc="v")
                    tc3 = P.emit("vector", lambda e, hc=hc, h3d=h3d, cv=cv: e.scalar_tensor_tensor(
                        cv, h3d[:, :, 2:130], cw_sb.ap()[:, 3 * hc + 2: 3 * hc + 3],
                        cv, ALU.mult, ALU.add), inc="v")
                    hrow_free[rh] = tc3
                    tsi = P.emit("scalar", lambda e, hc=hc, r=r: e.activation(
                        ystg.ap()[:, OWN * r: OWN * (r + 1)], cscr.ap()[:, 0:OWN],
                        AF.Silu, bias=0.0, scale=1.0),
                        waits=(tc3, yst_free.get(("d", r)), yst_free.get(("f", r))), inc="sc")
                    hrow_free[("cs",)] = tsi
                    yst_free[("d", r)] = P.emit("scalar", lambda e, hc=hc, r=r: e.dma_start(
                        out=dr["sp_y"].ap()[128 * hc:128 * (hc + 1), :],
                        in_=ystg.ap()[:, OWN * r: OWN * (r + 1)]),
                        waits=(tsi,), stream=f'ys{r}')
                    y_tok[hc] = yst_free[("d", r)]
                    tya = P.emit("scalar", lambda e, r=r: e.activation(
                        yabs.ap()[:, :], ystg.ap()[:, OWN * r: OWN * (r + 1)],
                        AF.Abs, bias=0.0, scale=1.0), waits=(tsi, tmy), inc="sc")
                    if hc == 0:
                        tmy = P.emit("vector", lambda e: e.tensor_copy(my_sb.ap()[:, :], yabs.ap()[:, :]),
                                     waits=(tya,), inc="v")
                    else:
                        tmy = P.emit("vector", lambda e: e.tensor_tensor(
                            my_sb.ap()[:, :], my_sb.ap()[:, :], yabs.ap()[:, :], ALU.max),
                            waits=(tya, tmy), inc="v")
                    yst_free[("f", r)] = tmy
                t_y_sp = (P.tok("q_ys0"), P.tok("q_ys1"))
                t_my = tmy
                P.barrier()

            if upto == "w1y":
                P.barrier()
                ph2.close()
                block = E(nc.Block())
                P.run_all(block)
                return nc

            with ExitStack() as ph2c:
                E2 = ph2c.enter_context
                amaxg = E2(nc.sbuf_tensor("amaxg", [128, 16], F32))
                s2b = E2(nc.sbuf_tensor("s2b", [128, OWN], F32))
                gmb = E2(nc.sbuf_tensor("gmb", [128, OWN], F32))
                srepc = E2(nc.sbuf_tensor("srep_c", [128, 129], F32))
                yring = E2(nc.sbuf_tensor("yring", [128, 2 * OWN], BF16))
                yqs = E2(nc.sbuf_tensor("yqs", [128, 2 * OWN], BF16))
                qscr = E2(nc.sbuf_tensor("qscr", [128, OWN], F32))
                rowtp = E2(nc.sbuf_tensor("rowtp", [16, 128], F32))
                rowsc = E2(nc.sbuf_tensor("rowsc", [1, 3 * OWN], F32))
                P.emit("gpsimd", lambda e: e.memset(amaxg.ap()[:, :], 1.0), inc="g")
                t_agi = P.tok("g")
                trd = None
                for g in range(16):
                    bkt = bank[2] if g % 2 == 0 else bank[3]
                    ttr = P.emit("tensor", lambda e, g=g, bkt=bkt: e.transpose(
                        bkt.ap()[:, 0:128], my_sb.ap()[:, 128 * g:128 * (g + 1)],
                        ident.ap()[:, :]), waits=(t_my, t_agi, bfree(bkt)), inc="mm")
                    trd = P.emit("vector", lambda e, g=g, bkt=bkt: e.reduce_max(
                        amaxg.ap()[:, g:g + 1], bkt.ap()[:, 0:128], axis=AXX),
                        waits=(ttr,), inc="v")
                    bset(bkt, trd)
                tcl = P.emit("vector", lambda e: e.tensor_single_scalar(
                    amaxg.ap()[:, :], amaxg.ap()[:, :], 1e-5, ALU.max), waits=(trd,), inc="v")
                ttra = P.emit("tensor", lambda e: e.transpose(
                    bank[2].ap()[0:16, 0:128], amaxg.ap()[:, :], ident.ap()[:, :]),
                    waits=(tcl, bfree(bank[2])), inc="mm")
                tcpr = P.emit("vector", lambda e: e.tensor_copy(
                    rowtp.ap()[:, :], bank[2].ap()[0:16, 0:128]), waits=(ttra,), inc="v")
                bset(bank[2], tcpr)
                tdm = P.emit("scalar", lambda e: e.dma_start(
                    out=rowsc.ap()[0:1, 0:OWN], in_=rowtp.ap()[:, :]),
                    waits=(tcpr,), stream='sm3')
                trec = P.emit("vector", lambda e: e.reciprocal(
                    rowsc.ap()[0:1, OWN:2 * OWN], rowsc.ap()[0:1, 0:OWN]),
                    waits=(tdm,), inc="v")
                ts127 = P.emit("vector", lambda e: e.tensor_single_scalar(
                    rowsc.ap()[0:1, OWN:2 * OWN], rowsc.ap()[0:1, OWN:2 * OWN],
                    127.0, ALU.mult), waits=(trec,), inc="v")
                tgam = P.emit("vector", lambda e: e.tensor_scalar(
                    rowsc.ap()[0:1, 2 * OWN:3 * OWN], rowsc.ap()[0:1, 0:OWN],
                    thr_cols.ap()[0:1, 3:4], 1.0 / 127.0, ALU.mult, ALU.mult),
                    waits=(tdm,), inc="v")
                for q in range(4):
                    bkt = bank[2] if q % 2 == 0 else bank[3]
                    tb1 = P.emit("tensor", lambda e, q=q, bkt=bkt: e.matmul(
                        bkt.ap()[:, 0:512], ones_row.ap()[0:1, :],
                        rowsc.ap()[0:1, OWN + 512 * q: OWN + 512 * (q + 1)],
                        start=True, stop=True), waits=(ts127, bfree(bkt)), inc="mm")
                    tc1 = P.emit("vector", lambda e, q=q, bkt=bkt: e.tensor_copy(
                        s2b.ap()[:, 512 * q:512 * (q + 1)], bkt.ap()[:, 0:512]),
                        waits=(tb1,), inc="v")
                    tb2 = P.emit("tensor", lambda e, q=q, bkt=bkt: e.matmul(
                        bkt.ap()[:, 0:512], ones_row.ap()[0:1, :],
                        rowsc.ap()[0:1, 2 * OWN + 512 * q: 2 * OWN + 512 * (q + 1)],
                        start=True, stop=True), waits=(tgam, tc1), inc="mm")
                    tc2 = P.emit("vector", lambda e, q=q, bkt=bkt: e.tensor_copy(
                        gmb.ap()[:, 512 * q:512 * (q + 1)], bkt.ap()[:, 0:512]),
                        waits=(tb2,), inc="v")
                    bset(bkt, tc2)
                t_s2 = P.tok("v")
                yq_free = {}
                yq_tok = {}
                for hc in range(NHC):
                    r = hc % 2
                    tyl = P.emit("sync", lambda e, hc=hc, r=r: e.dma_start(
                        out=yring.ap()[:, OWN * r: OWN * (r + 1)],
                        in_=dr["sp_y"].ap()[128 * hc:128 * (hc + 1), :]),
                        waits=(yq_free.get(("l", r)), y_tok[hc]), stream=f'yl{r}')
                    tq1 = P.emit("vector", lambda e, r=r: e.tensor_tensor(
                        qscr.ap()[:, :], yring.ap()[:, OWN * r: OWN * (r + 1)],
                        s2b.ap()[:, :], ALU.mult), waits=(tyl, t_s2), inc="v")
                    yq_free[("l", r)] = tq1
                    tq2 = P.emit("vector", lambda e, r=r: e.tensor_scalar(
                        yqs.ap()[:, OWN * r: OWN * (r + 1)], qscr.ap()[:, :],
                        MRND, MRND, ALU.add, ALU.subtract),
                        waits=(yq_free.get(("s", r)),), inc="v")
                    yq_free[("s", r)] = P.emit("scalar", lambda e, hc=hc, r=r: e.dma_start(
                        out=dr["sp_yq"].ap()[128 * hc:128 * (hc + 1), :],
                        in_=yqs.ap()[:, OWN * r: OWN * (r + 1)]),
                        waits=(tq2,), stream=f'yqs{r}')
                    yq_tok[hc] = yq_free[("s", r)]
                t_yq_sp = (P.tok("q_yqs0"), P.tok("q_yqs1"))
                if upto == "yq":
                    P.barrier()
                    ph2c.close()
                    ph2.close()
                    block = E(nc.Block())
                    P.run_all(block)
                    return nc

                w2ring = E2(nc.sbuf_tensor("w2ring", [128, 2 * 4 * 128], BF16))
                yqring = E2(nc.sbuf_tensor("yqring", [128, 2 * 512], BF16))
                x1ring = E2(nc.sbuf_tensor("x1iring", [128, 512], F32))
                x2stg = E2(nc.sbuf_tensor("x2stg", [128, 2 * 512], F32))
                w2_free = {}
                mm_slot = {}
                for quad in range(2):
                    for tb, (i0, iw) in enumerate(T_CHUNKS):
                        bks = [bank[2 * i + (tb % 2)] for i in range(4)]
                        for hc in range(NHC):
                            r = hc % 2
                            tw2 = P.emit("sync", lambda e, quad=quad, hc=hc, r=r: e.dma_start(
                                out=w2ring.ap()[:, 512 * r: 512 * (r + 1)].rearrange(
                                    "p (d w) -> p d w", d=4),
                                in_=dr["sp_w2t"].ap()[128 * hc:128 * (hc + 1),
                                                      512 * quad:512 * (quad + 1)].rearrange(
                                    "p (d w) -> p d w", d=4)),
                                waits=(mm_slot.get(r), t_w2t), stream=f'w2l{r}')
                            tyq = P.emit("scalar", lambda e, hc=hc, i0=i0, r=r: e.dma_start(
                                out=yqring.ap()[:, 512 * r: 512 * (r + 1)],
                                in_=dr["sp_yq"].ap()[128 * hc:128 * (hc + 1), i0:i0 + 512]),
                                waits=(mm_slot.get(r), yq_tok[hc]), stream=f'yql{r}')
                            for dx in range(4):
                                P.emit("tensor", lambda e, dx=dx, r=r, hc=hc, bk=bks[dx]: e.matmul(
                                    bk.ap()[:, 0:512],
                                    w2ring.ap()[:, 512 * r + 128 * dx: 512 * r + 128 * (dx + 1)],
                                    yqring.ap()[:, 512 * r: 512 * (r + 1)],
                                    start=(hc == 0), stop=(hc == NHC - 1)),
                                    waits=((tw2, tyq) +
                                           tuple(bfree(bks[i]) for i in range(4))) if dx == 0 else (),
                                    inc="mm" if dx == 3 else None)
                            mm_slot[r] = P.tok("mm")
                        tmm = P.tok("mm")
                        for dx in range(4):
                            dc = 4 * quad + dx
                            bk = bks[dx]
                            tx1 = P.emit("scalar", lambda e, dc=dc, i0=i0: e.dma_start(
                                out=x1ring.ap()[:, 0:512],
                                in_=mkap(dr["sp_x1"].ap().tensor,
                                         128 * dc * SLABT + 1 + (i0 // 128) * SLABW,
                                         [[SLABT, 128], [SLABW, 4], [1, 128]])),
                                waits=(w2_free.get(("x1",)),), stream='x1i')
                            tg1 = P.emit("vector", lambda e, dx=dx, i0=i0, bk=bk: e.tensor_tensor(
                                x2stg.ap()[:, 512 * (dx % 2): 512 * (dx % 2) + 512],
                                bk.ap()[:, 0:512], gmb.ap()[:, i0:i0 + 512], ALU.mult),
                                waits=(tmm, w2_free.get(("st", dx % 2))), inc="v")
                            tg2 = P.emit("vector", lambda e, dx=dx: e.scalar_tensor_tensor(
                                x2stg.ap()[:, 512 * (dx % 2): 512 * (dx % 2) + 512],
                                x1ring.ap()[:, 0:512], 1.0,
                                x2stg.ap()[:, 512 * (dx % 2): 512 * (dx % 2) + 512],
                                ALU.mult, ALU.add), waits=(tx1,), inc="v")
                            bset(bk, tg2)
                            w2_free[("x1",)] = tg2
                            w2_free[("st", dx % 2)] = P.emit(
                                "scalar", lambda e, dc=dc, i0=i0, dx=dx: e.dma_start(
                                    out=dr["sp_x2"].ap()[128 * dc:128 * (dc + 1), i0:i0 + 512],
                                    in_=x2stg.ap()[:, 512 * (dx % 2): 512 * (dx % 2) + 512]),
                                waits=(tg2,), stream=f'x2s{dx % 2}')
                t_x2_sp = (P.tok("q_x2s0"), P.tok("q_x2s1"))
                if "x2" in dbg:
                    dout("dbg_x2", (D, OWN), F32)
                    P.wait("sync", t_x2_sp)
                    P.emit("sync", lambda e: e.dma_start(
                        out=dr["dbg_x2"].ap()[:, :], in_=dr["sp_x2"].ap()[:, :]), stream='dbg')
                P.barrier()

        if upto == "x2":
            P.wait("sync", *P.all_toks())
            block = E(nc.Block())
            P.run_all(block)
            return nc

        ph4o = ExitStack()
        E4o = ph4o.enter_context
        bq_sb = E4o(nc.sbuf_tensor("bq_sb", [128, 8], F32))
        bk_sb = E4o(nc.sbuf_tensor("bk_sb", [128, 8], F32))
        bv_sb = E4o(nc.sbuf_tensor("bv_sb", [128, 8], F32))
        for bi_, (nm, sbb) in enumerate((("bq1", bq_sb), ("bk1", bk_sb), ("bv1", bv_sb))):
            P.emit("sync", lambda e, nm=nm, sbb=sbb: e.dma_start(
                out=sbb.ap().rearrange("p a -> p a ()"),
                in_=dr[nm].ap().rearrange("(a p) o -> p a o", p=128),
                allow_slow_non_contiguous=True), stream=f'sm{bi_}')
        t_b1 = (P.tok("q_sm0"), P.tok("q_sm1"), P.tok("q_sm2"))

        e_off = {}
        _off = 0
        for jt in range(NT):
            e_off[jt] = _off
            _off += (4 - jt // 8) * 512
        E_COLS = _off

        negm_sb = E4o(nc.sbuf_tensor("negm_sb", [128, NT], F32))
        sown_sb = E4o(nc.sbuf_tensor("sown_sb", [128, NT], F32))
        fst_sb = E4o(nc.sbuf_tensor("fst_sb", [128, NT], F32))
        fbf_sb = E4o(nc.sbuf_tensor("fbf_sb", [128, NT], BF16))

        ph45 = ExitStack()
        qT = [ph45.enter_context(nc.sbuf_tensor(f"qT_{fc}", [128, OWN], BF16))
              for fc in range(NFC)]

        ph34 = ExitStack()
        h3own = [ph34.enter_context(nc.sbuf_tensor(f"h3o_{fc}", [128, OWN], BF16))
                 for fc in range(NFC)]
        with ExitStack() as ph3:
            E3 = ph3.enter_context
            ring3 = E3(nc.sbuf_tensor("ln2_ring", [128, 2 * NFC * 512], F32))
            rows3 = E3(nc.sbuf_tensor("ln2_rows", [1, 2 * 4 * 512], F32))
            sq3 = E3(nc.sbuf_tensor("ln2_sq", [128, NFC * 512], BF16))
            h3scr = E3(nc.sbuf_tensor("h3scr", [128, 512], F32))
            zstg = E3(nc.sbuf_tensor("zstg", [128, 2 * 512], F32))
            zs_g = {}

            def ln2_src(cb, c0, w, r, free_tok):
                ld = P.emit("sync", lambda e, c0=c0, w=w, r=r: e.dma_start(
                    out=mkap(ring3.ap().tensor, NFC * 512 * r,
                             [[2 * NFC * 512, 128], [512, NFC], [1, w]]),
                    in_=dr["sp_x2"].ap()[:, c0:c0 + w].rearrange("(f p) w -> p f w", p=128)),
                    waits=(free_tok, t_x2_sp), stream=f'lnl{r}')
                def tile(fc, r=r, w=w):
                    return ring3.ap()[:, NFC * 512 * r + 512 * fc: NFC * 512 * r + 512 * fc + w]
                return ld, tile

            def ln2_out(cb, c0, w, r, bA, bB, toks, tile, ld):
                P.wait("vector", *toks)
                for fc in range(NFC):
                    sl = fc % 2
                    P.emit("vector", lambda e, fc=fc, w=w, bA=bA, tile=tile: e.tensor_tensor(
                        h3scr.ap()[:, 0:w], tile(fc), bA.ap()[:, 0:w], ALU.mult), inc="v")
                    P.emit("vector", lambda e, fc=fc, w=w, bB=bB: e.tensor_tensor(
                        h3scr.ap()[:, 0:w], h3scr.ap()[:, 0:w], bB.ap()[:, 0:w], ALU.add),
                        inc="v")
                    P.emit("vector", lambda e, fc=fc, c0=c0, w=w: e.tensor_copy(
                        h3own[fc].ap()[:, c0:c0 + w], h3scr.ap()[:, 0:w]), inc="v")
                    tz = P.emit("vector", lambda e, fc=fc, sl=sl, w=w, tile=tile: e.tensor_tensor(
                        zstg.ap()[:, 512 * sl:512 * sl + w], tile(fc), h3scr.ap()[:, 0:w],
                        ALU.add), waits=(zs_g.get(sl),), inc="v")
                    zs_g[sl] = P.emit("scalar", lambda e, fc=fc, c0=c0, w=w, sl=sl: e.dma_start(
                        out=dr["sp_z"].ap()[128 * fc:128 * (fc + 1), c0:c0 + w],
                        in_=zstg.ap()[:, 512 * sl:512 * sl + w]),
                        waits=(tz,), stream=f'zs{sl}')
                    th3s = P.emit("scalar", lambda e, fc=fc, c0=c0, w=w: e.dma_start(
                        out=dr["cc_h3_in"].ap()[128 * fc:128 * (fc + 1), c0:c0 + w],
                        in_=h3own[fc].ap()[:, c0:c0 + w]),
                        waits=(P.tok("v"),), stream='h3s')
                return th3s

            emit_ln("ln2", ln2_src, OWN, ln2_out, rows3, sq3)
            t_h3 = P.tok("v")
            t_h3_sp = (P.tok("q_zs0"), P.tok("q_zs1"), P.tok("q_h3s"))
            P.wait("gpsimd", t_h3_sp)
            P.emit("gpsimd", lambda e: e.collective_compute(
                "AllGather", ALU.bypass,
                replica_groups=[[0, 1], [2, 3], [4, 5], [6, 7]],
                ins=[dr["cc_h3_in"].ap().opt()],
                outs=[dr["cc_h3_out"].ap().opt()]), stream="cc1", stream_k=1)
            t_ag = P.tok("q_cc1")

        with ExitStack() as phq:
            t_ph3q = tuple(t for t in P.all_toks() if t[0] != "q_cc1")
            wq1_sb = phq.enter_context(nc.sbuf_tensor("wq1_sb", [128, NFC * 1024], BF16))
            P.emit("gpsimd", lambda e: e.dma_start(
                out=wq1_sb.ap().rearrange("p (f n) -> p f n", f=NFC),
                in_=dr["Wq1"].ap().rearrange("(f p) n -> p f n", p=128)),
                waits=(t_ph3q,), stream='gw0')
            t_wq1 = P.tok("q_gw0")
            for tc, (i0, iw) in enumerate(T_CHUNKS):
                for ec in range(NFC):
                    bk = bank[0] if ec % 2 == 0 else bank[1]
                    for fc in range(NFC):
                        P.emit("tensor", lambda e, ec=ec, fc=fc, i0=i0, bk=bk: e.matmul(
                            bk.ap()[:, 0:512],
                            wq1_sb.ap()[:, 1024 * fc + 128 * ec: 1024 * fc + 128 * (ec + 1)],
                            h3own[fc].ap()[:, i0:i0 + 512],
                            start=(fc == 0), stop=(fc == NFC - 1)),
                            waits=(t_wq1, t_h3, bfree(bk)) if fc == 0 else (),
                            inc="mm" if fc == NFC - 1 else None)
                    tmm = P.tok("mm")
                    tev = P.emit("vector", lambda e, ec=ec, i0=i0, bk=bk: e.tensor_scalar(
                        qT[ec].ap()[:, i0:i0 + 512], bk.ap()[:, 0:512],
                        bq_sb.ap()[:, ec:ec + 1], None, ALU.add),
                        waits=(tmm, t_b1), inc="v")
                    bset(bk, tev)
            t_qT = P.tok("v")

        with ExitStack() as phkv:
            E4 = phkv.enter_context
            wk1_sb = E4(nc.sbuf_tensor("wk1_sb", [128, NFC * 1024], BF16))
            wv1_sb = E4(nc.sbuf_tensor("wv1_sb", [128, NFC * 1024], BF16))
            t_phqq = tuple(t for t in P.all_toks() if t[0] != "q_cc1")
            for ki_, (nm, sbw) in enumerate((("Wk1", wk1_sb), ("Wv1", wv1_sb))):
                P.emit("gpsimd", lambda e, nm=nm, sbw=sbw: e.dma_start(
                    out=sbw.ap().rearrange("p (f n) -> p f n", f=NFC),
                    in_=dr[nm].ap().rearrange("(f p) n -> p f n", p=128)),
                    waits=(t_phqq,), stream=f'gw{ki_ + 1}')
            t_w1l = (P.tok("q_gw1"), P.tok("q_gw2"))
            gring = E4(nc.sbuf_tensor("gring", [128, 2 * NFC * 512], BF16))
            kvstg = E4(nc.sbuf_tensor("kvstg", [128, 4 * 512], BF16))
            kvg = {}
            gfree = {}
            for gc in range(NT // 4):
                r = gc % 2
                rank, loc = gc // 4, (gc % 4) * 512
                tld = P.emit("sync", lambda e, rank=rank, loc=loc, r=r: e.dma_start(
                    out=gring.ap()[:, NFC * 512 * r: NFC * 512 * (r + 1)].rearrange(
                        "p (f w) -> p f w", f=NFC),
                    in_=dr["cc_h3_out"].ap()[1024 * rank:1024 * (rank + 1), loc:loc + 512].rearrange(
                        "(f p) w -> p f w", p=128)),
                    waits=(gfree.get(r), t_ag), stream=f'gl{r}')
                last = None
                for ec in range(NFC):
                    bk = bank[0] if ec % 2 == 0 else bank[1]
                    for fc in range(NFC):
                        P.emit("tensor", lambda e, ec=ec, fc=fc, r=r, bk=bk: e.matmul(
                            bk.ap()[:, 0:512],
                            wk1_sb.ap()[:, 1024 * fc + 128 * ec: 1024 * fc + 128 * (ec + 1)],
                            gring.ap()[:, NFC * 512 * r + 512 * fc: NFC * 512 * r + 512 * fc + 512],
                            start=(fc == 0), stop=(fc == NFC - 1)),
                            waits=(tld, t_w1l, bfree(bk)) if fc == 0 else (),
                            inc="mm" if fc == NFC - 1 else None)
                    tmm = P.tok("mm")
                    rr = ec % 2
                    tev = P.emit("vector", lambda e, ec=ec, rr=rr, bk=bk: e.tensor_scalar(
                        kvstg.ap()[:, 512 * rr:512 * (rr + 1)], bk.ap()[:, 0:512],
                        bk_sb.ap()[:, ec:ec + 1], None, ALU.add),
                        waits=(tmm, t_b1, kvg.get(("k", rr))), inc="v")
                    bset(bk, tev)
                    last = P.emit("scalar", lambda e, ec=ec, gc=gc, rr=rr: e.dma_start(
                        out=dr["sp_k1"].ap()[128 * ec:128 * (ec + 1), 512 * gc:512 * (gc + 1)],
                        in_=kvstg.ap()[:, 512 * rr:512 * (rr + 1)]),
                        waits=(tev,), stream=f'ks{rr}')
                    kvg[("k", rr)] = last
                for sub in range(4):
                    for en in range(2):
                        bk = bank[0] if en == 0 else bank[1]
                        for fc in range(NFC):
                            P.emit("tensor", lambda e, sub=sub, en=en, fc=fc, r=r, bk=bk: e.matmul(
                                bk.ap()[:, 0:512],
                                gring.ap()[:, NFC * 512 * r + 512 * fc + 128 * sub:
                                            NFC * 512 * r + 512 * fc + 128 * (sub + 1)],
                                wv1_sb.ap()[:, 1024 * fc + 512 * en: 1024 * fc + 512 * (en + 1)],
                                start=(fc == 0), stop=(fc == NFC - 1)),
                                waits=(tld, t_w1l, bfree(bk)) if fc == 0 else (),
                                inc="mm" if fc == NFC - 1 else None)
                        tmm = P.tok("mm")
                        rr = 2 + en
                        tev = P.emit("vector", lambda e, rr=rr, bk=bk: e.tensor_copy(
                            kvstg.ap()[:, 512 * rr:512 * (rr + 1)], bk.ap()[:, 0:512]),
                            waits=(tmm, kvg.get(("v", rr))), inc="v")
                        bset(bk, tev)
                        last = P.emit("scalar", lambda e, gc=gc, sub=sub, en=en, rr=rr: e.dma_start(
                            out=dr["sp_v1"].ap()[512 * gc + 128 * sub: 512 * gc + 128 * (sub + 1),
                                                 512 * en:512 * (en + 1)],
                            in_=kvstg.ap()[:, 512 * rr:512 * (rr + 1)]),
                            waits=(tev,), stream=f'vs{en}')
                        kvg[("v", rr)] = last
                gfree[r] = last
            t_kv_sp = (P.tok("q_ks0"), P.tok("q_ks1"), P.tok("q_vs0"), P.tok("q_vs1"))
            P.barrier()
        ph34.close()

        ph4e = ExitStack()
        E4e = ph4e.enter_context
        e_sb = E4e(nc.sbuf_tensor("e_sb", [128, E_COLS], BF16))
        ststg = E4e(nc.sbuf_tensor("ststg", [128, 128], F32))
        cmb = E4e(nc.sbuf_tensor("cmb", [128, 5 * NT], F32))

        with ExitStack() as ph4ab:
            E4 = ph4ab.enter_context
            stair_sb = E4(nc.sbuf_tensor("stair_sb", [128, 8 * 512], BF16))
            P.emit("gpsimd", lambda e: e.dma_start(
                out=stair_sb.ap().rearrange("p (s w) -> p s w", s=8),
                in_=dr["stair"].ap().rearrange("(s p) w -> p s w", p=128)), stream='gw0')
            t_stair = P.tok("q_gw0")

            kring = E4(nc.sbuf_tensor("kring", [128, 2 * NFC * 128], BF16))
            mscr = E4(nc.sbuf_tensor("mscr", [128, 6], F32))
            kfree = {}
            tacc_prev = None
            for jt in range(NT):
                r = jt % 2
                gt = _gtile(jt)
                tk = P.emit("sync", lambda e, gt=gt, r=r: e.dma_start(
                    out=kring.ap()[:, NFC * 128 * r: NFC * 128 * (r + 1)].rearrange(
                        "p (f w) -> p f w", f=NFC),
                    in_=dr["sp_k1"].ap()[:, 128 * gt:128 * (gt + 1)].rearrange(
                        "(f p) w -> p f w", p=128)),
                    waits=(kfree.get(r), t_kv_sp), stream=f'kl{r}')
                cmin = jt // 8
                sm = jt % 8
                trd = None
                for ci, C in enumerate(range(cmin, 4)):
                    i0 = 512 * C
                    bk = bank[0] if C % 2 == 0 else bank[2]
                    for fc in range(NFC):
                        P.emit("tensor", lambda e, fc=fc, r=r, i0=i0, bk=bk: e.matmul(
                            bk.ap()[:, 0:512],
                            kring.ap()[:, NFC * 128 * r + 128 * fc: NFC * 128 * r + 128 * (fc + 1)],
                            qT[fc].ap()[:, i0:i0 + 512],
                            start=(fc == 0), stop=(fc == NFC - 1)),
                            waits=(tk, t_qT, t_stair, bfree(bk)) if fc == 0 else (),
                            inc="mm" if fc == NFC - 1 else None)
                    tmm = P.tok("mm")
                    tlast = tmm
                    if C == cmin:
                        tlast = P.emit("vector", lambda e, sm=sm, bk=bk: e.tensor_tensor(
                            bk.ap()[:, 0:512], bk.ap()[:, 0:512],
                            stair_sb.ap()[:, 512 * sm:512 * (sm + 1)], ALU.add),
                            waits=(tmm,), inc="v")
                    if ci == 0:
                        trd = P.emit("vector", lambda e, jt=jt, bk=bk: e.reduce_max(
                            negm_sb.ap()[:, jt:jt + 1], bk.ap()[:, 0:512], axis=AXX,
                            negate=True), waits=(tlast,), inc="v")
                    else:
                        trd0 = P.emit("vector", lambda e, bk=bk: e.reduce_max(
                            mscr.ap()[:, 0:1], bk.ap()[:, 0:512], axis=AXX, negate=True),
                            waits=(tlast,), inc="v")
                        trd = P.emit("vector", lambda e, jt=jt: e.tensor_tensor(
                            negm_sb.ap()[:, jt:jt + 1], negm_sb.ap()[:, jt:jt + 1],
                            mscr.ap()[:, 0:1], ALU.min), waits=(trd0,), inc="v")
                    bset(bk, trd)
                for ci, C in enumerate(range(cmin, 4)):
                    i0 = 512 * C
                    bk = bank[0] if C % 2 == 0 else bank[2]
                    for fc in range(NFC):
                        P.emit("tensor", lambda e, fc=fc, r=r, i0=i0, bk=bk: e.matmul(
                            bk.ap()[:, 0:512],
                            kring.ap()[:, NFC * 128 * r + 128 * fc: NFC * 128 * r + 128 * (fc + 1)],
                            qT[fc].ap()[:, i0:i0 + 512],
                            start=(fc == 0), stop=(fc == NFC - 1)),
                            waits=(trd, bfree(bk)) if fc == 0 else (),
                            inc="mm" if fc == NFC - 1 else None)
                    tmm = P.tok("mm")
                    tlast = tmm
                    if C == cmin:
                        tlast = P.emit("vector", lambda e, sm=sm, bk=bk: e.tensor_tensor(
                            bk.ap()[:, 0:512], bk.ap()[:, 0:512],
                            stair_sb.ap()[:, 512 * sm:512 * (sm + 1)], ALU.add),
                            waits=(tmm,), inc="v")
                    eoff = e_off[jt] + ci * 512
                    if ci == 0:
                        tex = P.emit("scalar", lambda e, jt=jt, eoff=eoff, bk=bk: e.activation(
                            e_sb.ap()[:, eoff:eoff + 512], bk.ap()[:, 0:512], AF.Exp,
                            bias=negm_sb.ap()[:, jt:jt + 1], scale=1.0,
                            accum_out=sown_sb.ap()[:, jt:jt + 1]),
                            waits=(tlast, tacc_prev), inc="sc")
                        tacc = tex
                    else:
                        tex = P.emit("scalar", lambda e, jt=jt, eoff=eoff, ci=ci, bk=bk: e.activation(
                            e_sb.ap()[:, eoff:eoff + 512], bk.ap()[:, 0:512], AF.Exp,
                            bias=negm_sb.ap()[:, jt:jt + 1], scale=1.0,
                            accum_out=mscr.ap()[:, 1 + ci:2 + ci]),
                            waits=(tlast,), inc="sc")
                        tacc = P.emit("vector", lambda e, jt=jt, ci=ci: e.tensor_tensor(
                            sown_sb.ap()[:, jt:jt + 1], sown_sb.ap()[:, jt:jt + 1],
                            mscr.ap()[:, 1 + ci:2 + ci], ALU.add), waits=(tex,), inc="v")
                    bset(bk, tex)
                tacc_prev = tacc
                kfree[r] = P.tok("sc")
            t_ab = (P.tok("v"), P.tok("sc"))

            P.wait("vector", *t_ab)
            tp1 = P.emit("vector", lambda e: e.tensor_copy(
                ststg.ap()[:, 0:NT], negm_sb.ap()[:, :]), inc="v")
            tp2 = P.emit("vector", lambda e: e.tensor_copy(
                ststg.ap()[:, NT:2 * NT], sown_sb.ap()[:, :]), inc="v")
            tsp = P.emit("scalar", lambda e: e.dma_start(
                out=dr["cc_st_in"].ap()[:, :], in_=ststg.ap()[:, 0:64]),
                waits=(tp1, tp2), stream='sts')
            P.wait("gpsimd", tsp)
            tag2 = P.emit("gpsimd", lambda e: e.collective_compute(
                "AllGather", ALU.bypass,
                replica_groups=[[0, 1], [2, 3], [4, 5], [6, 7]],
                ins=[dr["cc_st_in"].ap().opt()],
                outs=[dr["cc_st_out"].ap().opt()]), stream="cc2", stream_k=1)
            tld2 = P.emit("sync", lambda e: e.dma_start(
                out=ststg.ap()[:, 0:128].rearrange("p (r c) -> p r c", r=2),
                in_=dr["cc_st_out"].ap().rearrange("(r p) c -> p r c", p=128)),
                waits=(tag2,), stream='stl')
            u1 = P.emit("vector", lambda e: e.tensor_tensor(
                cmb.ap()[:, 0:NT], ststg.ap()[:, 0:NT], ststg.ap()[:, 64:64 + NT], ALU.min),
                waits=(tld2,), inc="v")
            u2 = P.emit("vector", lambda e: e.tensor_tensor(
                cmb.ap()[:, NT:2 * NT], cmb.ap()[:, 0:NT], ststg.ap()[:, 0:NT], ALU.subtract),
                waits=(u1,), inc="v")
            u3 = P.emit("vector", lambda e: e.tensor_tensor(
                cmb.ap()[:, 2 * NT:3 * NT], cmb.ap()[:, 0:NT], ststg.ap()[:, 64:64 + NT],
                ALU.subtract), waits=(u1,), inc="v")
            u4 = P.emit("scalar", lambda e: e.activation(
                cmb.ap()[:, NT:2 * NT], cmb.ap()[:, NT:2 * NT], AF.Exp, bias=0.0, scale=1.0),
                waits=(u2,), inc="sc")
            u5 = P.emit("scalar", lambda e: e.activation(
                cmb.ap()[:, 2 * NT:3 * NT], cmb.ap()[:, 2 * NT:3 * NT], AF.Exp, bias=0.0,
                scale=1.0), waits=(u3,), inc="sc")
            u6 = P.emit("vector", lambda e: e.tensor_tensor(
                cmb.ap()[:, NT:2 * NT], cmb.ap()[:, NT:2 * NT], ststg.ap()[:, NT:2 * NT],
                ALU.mult), waits=(u4,), inc="v")
            u7 = P.emit("vector", lambda e: e.tensor_tensor(
                cmb.ap()[:, 2 * NT:3 * NT], cmb.ap()[:, 2 * NT:3 * NT],
                ststg.ap()[:, 64 + NT:64 + 2 * NT], ALU.mult), waits=(u5,), inc="v")
            u8 = P.emit("vector", lambda e: e.tensor_tensor(
                cmb.ap()[:, 3 * NT:4 * NT], cmb.ap()[:, NT:2 * NT], cmb.ap()[:, 2 * NT:3 * NT],
                ALU.add), waits=(u6, u7), inc="v")
            u9 = P.emit("vector", lambda e: e.tensor_tensor(
                cmb.ap()[:, 4 * NT:5 * NT], cmb.ap()[:, 0:NT], negm_sb.ap()[:, :], ALU.subtract),
                waits=(u8,), inc="v")
            u10 = P.emit("scalar", lambda e: e.activation(
                cmb.ap()[:, 4 * NT:5 * NT], cmb.ap()[:, 4 * NT:5 * NT], AF.Exp, bias=0.0,
                scale=1.0), waits=(u9,), inc="sc")
            u11 = P.emit("vector", lambda e: e.reciprocal(
                cmb.ap()[:, 3 * NT:4 * NT], cmb.ap()[:, 3 * NT:4 * NT]), waits=(u8,), inc="v")
            u12 = P.emit("vector", lambda e: e.tensor_tensor(
                fst_sb.ap()[:, :], cmb.ap()[:, 4 * NT:5 * NT], cmb.ap()[:, 3 * NT:4 * NT],
                ALU.mult), waits=(u10, u11), inc="v")
            u13 = P.emit("vector", lambda e: e.tensor_single_scalar(
                fst_sb.ap()[:, :], fst_sb.ap()[:, :], 1.0 / 32.0, ALU.mult),
                waits=(u12,), inc="v")
            u14 = P.emit("vector", lambda e: e.tensor_copy(fbf_sb.ap()[:, :], fst_sb.ap()[:, :]),
                         waits=(u13,), inc="v")
            t_f = u14
            P.barrier()

        with ExitStack() as ph4c:
            E4 = ph4c.enter_context
            vring = E4(nc.sbuf_tensor("vring", [128, 2 * 1024], BF16))
            zring = E4(nc.sbuf_tensor("zring", [128, NFC * 512], F32))
            rrow = E4(nc.sbuf_tensor("rrow", [1, 512], F32))
            rb_sb = E4(nc.sbuf_tensor("rb_sb", [128, 512], F32))
            c3scr = E4(nc.sbuf_tensor("c3scr", [128, 512], F32))
            x3stg = E4(nc.sbuf_tensor("x3stg", [128, 2 * 512], F32))
            vg = {}
            for C in range(4):
                i0 = 512 * C
                njt = 8 * C + 8
                tz = P.emit("sync", lambda e, i0=i0: e.dma_start(
                    out=zring.ap().rearrange("p (f w) -> p f w", f=NFC),
                    in_=dr["sp_z"].ap()[:, i0:i0 + 512].rearrange("(f p) w -> p f w", p=128)),
                    waits=(P.tok("v"),), stream='zl')
                for jx in range(njt):
                    jt = jx
                    P.emit("tensor", lambda e, jt=jt, C=C, jx=jx, njt=njt: e.matmul(
                        bank[6].ap()[0:1, 0:512], fbf_sb.ap()[:, jt:jt + 1],
                        e_sb.ap()[:, e_off[jt] + (C - jt // 8) * 512:
                                  e_off[jt] + (C - jt // 8) * 512 + 512],
                        start=(jx == 0), stop=(jx == njt - 1)),
                        waits=(t_f, bfree(bank[6])) if jx == 0 else (),
                        inc="mm" if jx == njt - 1 else None)
                trho = P.tok("mm")
                trr = P.emit("vector", lambda e: e.tensor_copy(
                    rrow.ap()[0:1, :], bank[6].ap()[0:1, 0:512]), waits=(trho,), inc="v")
                bset(bank[6], trr)
                tbc = P.emit("tensor", lambda e: e.matmul(
                    bank[7].ap()[:, 0:512], ones_row.ap()[0:1, :], rrow.ap()[0:1, :],
                    start=True, stop=True), waits=(trr, bfree(bank[7])), inc="mm")
                trb = P.emit("vector", lambda e: e.tensor_copy(
                    rb_sb.ap()[:, :], bank[7].ap()[:, 0:512]), waits=(tbc,), inc="v")
                bset(bank[7], trb)
                for half in range(2):
                    for jx in range(njt):
                        jt = jx
                        r = jt % 2
                        gt = _gtile(jt)
                        tv = P.emit("sync", lambda e, gt=gt, r=r: e.dma_start(
                            out=vring.ap()[:, 1024 * r:1024 * (r + 1)],
                            in_=dr["sp_v1"].ap()[128 * gt:128 * (gt + 1), :]),
                            waits=(vg.get(("v", r)), t_kv_sp), stream=f'vl{r}')
                        tvs = P.emit("vector", lambda e, jt=jt, r=r: e.tensor_scalar(
                            vring.ap()[:, 1024 * r:1024 * (r + 1)],
                            vring.ap()[:, 1024 * r:1024 * (r + 1)],
                            fst_sb.ap()[:, jt:jt + 1], None, ALU.mult),
                            waits=(tv, t_f, vg.get(("vt", r))), inc="v")
                        for vx in range(4):
                            vc = 4 * half + vx
                            bk = bank[2 * vx]
                            P.emit("tensor", lambda e, jt=jt, C=C, vc=vc, r=r, bk=bk, jx=jx, njt=njt: e.matmul(
                                bk.ap()[:, 0:512],
                                vring.ap()[:, 1024 * r + 128 * vc: 1024 * r + 128 * (vc + 1)],
                                e_sb.ap()[:, e_off[jt] + (C - jt // 8) * 512:
                                          e_off[jt] + (C - jt // 8) * 512 + 512],
                                start=(jx == 0), stop=(jx == njt - 1)),
                                waits=(tvs, bfree(bk)) if jx == 0 else (tvs,),
                                inc="mm" if jx == njt - 1 else None)
                        vg[("vt", r)] = P.tok("mm")
                        vg[("v", r)] = vg[("vt", r)]
                    tmm_half = P.tok("mm")
                    for vx in range(4):
                        vc = 4 * half + vx
                        bk = bank[2 * vx]
                        sl = vc % 2
                        tc1 = P.emit("vector", lambda e, vc=vc, bk=bk: e.scalar_tensor_tensor(
                            c3scr.ap()[:, 0:512], rb_sb.ap()[:, :],
                            bv_sb.ap()[:, vc:vc + 1], bk.ap()[:, 0:512], ALU.mult, ALU.add),
                            waits=(tmm_half, trb, t_b1), inc="v")
                        tc2 = P.emit("vector", lambda e, vc=vc, sl=sl: e.tensor_tensor(
                            x3stg.ap()[:, 512 * sl:512 * (sl + 1)], c3scr.ap()[:, 0:512],
                            zring.ap()[:, 512 * vc:512 * (vc + 1)], ALU.add),
                            waits=(tz, vg.get(("x3", sl))), inc="v")
                        bset(bk, tc2)
                        vg[("x3", sl)] = P.emit(
                            "scalar", lambda e, vc=vc, i0=i0, sl=sl: e.dma_start(
                                out=dr["sp_x3"].ap()[128 * vc:128 * (vc + 1), i0:i0 + 512],
                                in_=x3stg.ap()[:, 512 * sl:512 * (sl + 1)]),
                            waits=(tc2,), stream=f'x3s{sl}')
            t_x3_sp = (P.tok("q_x3s0"), P.tok("q_x3s1"))
            P.barrier()
        ph4e.close()
        ph45.close()
        ph4o.close()

        if upto == "x3":
            if "x3" in dbg:
                dout("dbg_x3", (D, OWN), F32)
                P.wait("sync", t_x3_sp)
                P.emit("sync", lambda e: e.dma_start(
                    out=dr["dbg_x3"].ap()[:, :], in_=dr["sp_x3"].ap()[:, :]),
                    stream='dbg')
            P.wait("sync", *P.all_toks())
            block = E(nc.Block())
            P.run_all(block)
            return nc

        t_wgt = wq_state[("done", 2)]
        t_wut = wq_state[("done", 3)]
        t_wdt = wq_state[("done", 4)]
        with ExitStack() as ph5:
            E5 = ph5.enter_context
            xq4 = [E5(nc.sbuf_tensor(f"xq4_{fc}", [128, OWN], BF16)) for fc in range(NFC)]
            am5 = E5(nc.sbuf_tensor("am5", [128, 4], F32))
            am5b = E5(nc.sbuf_tensor("am5b", [128, 16], F32))
            srep5 = E5(nc.sbuf_tensor("srep5", [128, 129], F32))
            s4b = E5(nc.sbuf_tensor("s4b", [128, OWN], F32))
            dgb = E5(nc.sbuf_tensor("dgb", [128, OWN], F32))
            dub = E5(nc.sbuf_tensor("dub", [128, OWN], F32))
            ph5a = ExitStack()
            E5a = ph5a.enter_context
            ring5 = E5a(nc.sbuf_tensor("ln3_ring", [128, 2 * NFC * 512], F32))
            rows5 = E5a(nc.sbuf_tensor("ln3_rows", [1, 2 * 4 * 512], F32))
            sq5 = E5a(nc.sbuf_tensor("ln3_sq", [128, NFC * 512], BF16))
            xh4r = E5a(nc.sbuf_tensor("xh4r", [128, NFC * 512], F32))
            mf5 = E5a(nc.sbuf_tensor("mf5", [128, 512], F32))
            f5scr = E5a(nc.sbuf_tensor("f5scr", [128, 512], F32))
            P.emit("gpsimd", lambda e: e.memset(am5.ap()[:, :], 1.0), inc="g")
            P.emit("gpsimd", lambda e: e.memset(am5b.ap()[:, :], 1.0), inc="g")
            t_am5i = P.tok("g")
            I32 = mybir.dt.int32
            lg5 = {"mf": None, "sr": None}

            def ln3_src(cb, c0, w, r, free_tok):
                ld = P.emit("sync", lambda e, c0=c0, w=w, r=r: e.dma_start(
                    out=mkap(ring5.ap().tensor, NFC * 512 * r,
                             [[2 * NFC * 512, 128], [512, NFC], [1, w]]),
                    in_=dr["sp_x3"].ap()[:, c0:c0 + w].rearrange("(f p) w -> p f w", p=128)),
                    waits=(free_tok, t_x3_sp), stream=f'lnl{r}')
                def tile(fc, r=r, w=w):
                    return ring5.ap()[:, NFC * 512 * r + 512 * fc: NFC * 512 * r + 512 * fc + w]
                return ld, tile

            def ln3_out(cb, c0, w, r, bA, bB, toks, tile, ld):
                P.wait("vector", *toks)
                tts = []
                for fc in range(NFC):
                    P.emit("vector", lambda e, fc=fc, w=w, bA=bA, tile=tile: e.tensor_tensor(
                        xh4r.ap()[:, 512 * fc:512 * fc + w], tile(fc), bA.ap()[:, 0:w],
                        ALU.mult), inc="v")
                    tt = P.emit("vector", lambda e, fc=fc, w=w, bB=bB: e.tensor_tensor(
                        xh4r.ap()[:, 512 * fc:512 * fc + w], xh4r.ap()[:, 512 * fc:512 * fc + w],
                        bB.ap()[:, 0:w], ALU.add), inc="v")
                    tts.append(tt)
                tf = P.emit("vector", lambda e, w=w: e.tensor_single_scalar(
                    mf5.ap()[:, 0:w].bitcast(I32), xh4r.ap()[:, 0:w].bitcast(I32),
                    0x7FFFFFFF, ALU.bitwise_and), waits=(tts[0], lg5["mf"]), inc="v")
                for fc in range(1, NFC):
                    P.emit("vector", lambda e, fc=fc, w=w: e.tensor_single_scalar(
                        f5scr.ap()[:, 0:w].bitcast(I32),
                        xh4r.ap()[:, 512 * fc:512 * fc + w].bitcast(I32),
                        0x7FFFFFFF, ALU.bitwise_and), waits=(tts[fc],), inc="v")
                    tf = P.emit("vector", lambda e, w=w: e.tensor_tensor(
                        mf5.ap()[:, 0:w], mf5.ap()[:, 0:w], f5scr.ap()[:, 0:w], ALU.max),
                        inc="v")
                trd = None
                for g in range(4):
                    ttr = P.emit("tensor", lambda e, g=g: e.transpose(
                        bank[2].ap()[:, 0:128], mf5.ap()[:, 128 * g:128 * (g + 1)],
                        ident.ap()[:, :]), waits=(tf, t_am5i, bfree(bank[2])), inc="mm")
                    trd = P.emit("vector", lambda e, g=g: e.reduce_max(
                        am5.ap()[:, g:g + 1], bank[2].ap()[:, 0:128], axis=AXX),
                        waits=(ttr,), inc="v")
                    bset(bank[2], trd)
                    lg5["mf"] = ttr
                tcl = P.emit("vector", lambda e: e.tensor_single_scalar(
                    am5.ap()[:, :], am5.ap()[:, :], 1e-5, ALU.max), waits=(trd,), inc="v")
                tcp = None
                for g in range(4):
                    gabs = c0 // 128 + g
                    trc = P.emit("vector", lambda e, g=g: e.reciprocal(
                        srep5.ap()[:, 128:129], am5.ap()[:, g:g + 1]), waits=(tcl,), inc="v")
                    trp = P.emit("vector", lambda e: e.tensor_scalar(
                        srep5.ap()[:, 0:128], ones128.ap()[:, :],
                        srep5.ap()[:, 128:129], 127.0, ALU.mult, ALU.mult),
                        waits=(trc, lg5["sr"]), inc="v")
                    ttr = P.emit("tensor", lambda e: e.transpose(
                        bank[3].ap()[:, 0:128], srep5.ap()[:, 0:128], ident.ap()[:, :]),
                        waits=(trp, bfree(bank[3])), inc="mm")
                    tcp = P.emit("vector", lambda e, gabs=gabs: e.tensor_copy(
                        s4b.ap()[:, 128 * gabs:128 * (gabs + 1)], bank[3].ap()[:, 0:128]),
                        waits=(ttr,), inc="v")
                    bset(bank[3], tcp)
                    lg5["sr"] = ttr
                    for (dst, wqi) in ((dgb, 5), (dub, 7)):
                        ta1 = P.emit("vector", lambda e, g=g, wqi=wqi: e.tensor_scalar(
                            srep5.ap()[:, 128:129], am5.ap()[:, g:g + 1],
                            thr_cols.ap()[:, wqi:wqi + 1], 1.0 / 127.0, ALU.mult, ALU.mult),
                            waits=(tcl,), inc="v")
                        tra = P.emit("vector", lambda e: e.tensor_scalar(
                            srep5.ap()[:, 0:128], ones128.ap()[:, :],
                            srep5.ap()[:, 128:129], None, ALU.mult),
                            waits=(ta1, lg5["sr"]), inc="v")
                        ttr2 = P.emit("tensor", lambda e: e.transpose(
                            bank[3].ap()[:, 0:128], srep5.ap()[:, 0:128], ident.ap()[:, :]),
                            waits=(tra, bfree(bank[3])), inc="mm")
                        tcg = P.emit("vector", lambda e, gabs=gabs, dst=dst: e.tensor_copy(
                            dst.ap()[:, 128 * gabs:128 * (gabs + 1)], bank[3].ap()[:, 0:128]),
                            waits=(ttr2,), inc="v")
                        bset(bank[3], tcg)
                        lg5["sr"] = ttr2
                last = None
                for fc in range(NFC):
                    P.emit("vector", lambda e, fc=fc, c0=c0, w=w: e.tensor_tensor(
                        xh4r.ap()[:, 512 * fc:512 * fc + w], xh4r.ap()[:, 512 * fc:512 * fc + w],
                        s4b.ap()[:, c0:c0 + w], ALU.mult), waits=(tcp,), inc="v")
                    last = P.emit("vector", lambda e, fc=fc, c0=c0, w=w: e.tensor_scalar(
                        xq4[fc].ap()[:, c0:c0 + w], xh4r.ap()[:, 512 * fc:512 * fc + w],
                        MRND, MRND, ALU.add, ALU.subtract), inc="v")
                return last

            emit_ln("ln3", ln3_src, OWN, ln3_out, rows5, sq5)
            t_xq4 = P.tok("v")
            P.barrier()
            ph5a.close()
            mp5 = E5(nc.sbuf_tensor("mp5x", [128, 512], F32))
            row5tp = E5(nc.sbuf_tensor("row5tp", [4, 128], F32))
            row5sc = E5(nc.sbuf_tensor("row5sc", [1, 3 * 512], F32))

            wgr = E5(nc.sbuf_tensor("wgr", [128, 2 * NFC * 128], BF16))
            wur = E5(nc.sbuf_tensor("wur", [128, 2 * NFC * 128], BF16))
            wdr = E5(nc.sbuf_tensor("wdr", [128, 2 * NHC * 128], BF16))
            psl = E5(nc.sbuf_tensor("psl", [128, NHC * 512], BF16))
            gscr = E5(nc.sbuf_tensor("gscr", [128, 2 * 512], F32))
            uscr = E5(nc.sbuf_tensor("uscr", [128, 2 * 512], F32))
            pabs = E5(nc.sbuf_tensor("pabs", [128, 512], F32))
            s5b = E5(nc.sbuf_tensor("s5b", [128, 512], F32))
            ztb = E5(nc.sbuf_tensor("ztb", [128, 512], F32))
            outstg = E5(nc.sbuf_tensor("outstg", [128, 2 * 512], F32))
            x3iring = E5(nc.sbuf_tensor("x3iring", [128, 2 * 512], F32))
            t_prev_wd = None
            og = {}

            for tb, (i0, iw) in enumerate(T_CHUNKS):
                wfree = {}
                pm_tok = None
                for ic in range(NHC):
                    r = ic % 2
                    twg = P.emit("sync", lambda e, ic=ic, r=r: e.dma_start(
                        out=wgr.ap()[:, NFC * 128 * r: NFC * 128 * (r + 1)].rearrange(
                            "p (f w) -> p f w", f=NFC),
                        in_=dr["sp_wgt"].ap()[:, 128 * ic:128 * (ic + 1)].rearrange(
                            "(f p) w -> p f w", p=128)),
                        waits=(wfree.get(("g", r)), t_wgt), stream=f'wgl{r}')
                    twu = P.emit("sync", lambda e, ic=ic, r=r: e.dma_start(
                        out=wur.ap()[:, NFC * 128 * r: NFC * 128 * (r + 1)].rearrange(
                            "p (f w) -> p f w", f=NFC),
                        in_=dr["sp_wut"].ap()[:, 128 * ic:128 * (ic + 1)].rearrange(
                            "(f p) w -> p f w", p=128)),
                        waits=(wfree.get(("u", r)), t_wut), stream=f'wul{r}')
                    bg, bu = bank[0], bank[1]
                    for fc in range(NFC):
                        P.emit("tensor", lambda e, fc=fc, r=r, i0=i0, bg=bg: e.matmul(
                            bg.ap()[:, 0:512],
                            wgr.ap()[:, NFC * 128 * r + 128 * fc: NFC * 128 * r + 128 * (fc + 1)],
                            xq4[fc].ap()[:, i0:i0 + 512],
                            start=(fc == 0), stop=(fc == NFC - 1)),
                            waits=(twg, t_xq4, bfree(bg)) if fc == 0 else (),
                            inc="mm" if fc == NFC - 1 else None)
                    tg = P.tok("mm")
                    for fc in range(NFC):
                        P.emit("tensor", lambda e, fc=fc, r=r, i0=i0, bu=bu: e.matmul(
                            bu.ap()[:, 0:512],
                            wur.ap()[:, NFC * 128 * r + 128 * fc: NFC * 128 * r + 128 * (fc + 1)],
                            xq4[fc].ap()[:, i0:i0 + 512],
                            start=(fc == 0), stop=(fc == NFC - 1)),
                            waits=(twu, bfree(bu)) if fc == 0 else (),
                            inc="mm" if fc == NFC - 1 else None)
                    tu = P.tok("mm")
                    wfree[("g", r)] = tg
                    wfree[("u", r)] = tu
                    rr = ic % 2
                    tg1 = P.emit("vector", lambda e, rr=rr, i0=i0, bg=bg: e.tensor_tensor(
                        gscr.ap()[:, 512 * rr:512 * (rr + 1)], bg.ap()[:, 0:512],
                        dgb.ap()[:, i0:i0 + 512], ALU.mult),
                        waits=(tg, og.get(("gs", rr)), t_prev_wd), inc="v")
                    tsi = P.emit("scalar", lambda e, rr=rr: e.activation(
                        gscr.ap()[:, 512 * rr:512 * (rr + 1)], gscr.ap()[:, 512 * rr:512 * (rr + 1)],
                        AF.Silu, bias=0.0, scale=1.0), waits=(tg1,), inc="sc")
                    tu1 = P.emit("vector", lambda e, rr=rr, i0=i0, bu=bu: e.tensor_tensor(
                        uscr.ap()[:, 512 * rr:512 * (rr + 1)], bu.ap()[:, 0:512],
                        dub.ap()[:, i0:i0 + 512], ALU.mult),
                        waits=(tu, og.get(("us", rr))), inc="v")
                    bset(bg, tg1); bset(bu, tu1)
                    tp = P.emit("vector", lambda e, ic=ic, rr=rr: e.tensor_tensor(
                        psl.ap()[:, 512 * ic:512 * (ic + 1)], gscr.ap()[:, 512 * rr:512 * (rr + 1)],
                        uscr.ap()[:, 512 * rr:512 * (rr + 1)], ALU.mult),
                        waits=(tsi, tu1), inc="v")
                    og[("gs", rr)] = tp
                    og[("us", rr)] = tp
                    tya = P.emit("scalar", lambda e, ic=ic: e.activation(
                        pabs.ap()[:, :], psl.ap()[:, 512 * ic:512 * (ic + 1)],
                        AF.Abs, bias=0.0, scale=1.0), waits=(tp, pm_tok), inc="sc")
                    if ic == 0:
                        pm_tok = P.emit("vector", lambda e: e.tensor_copy(
                            mp5.ap()[:, :], pabs.ap()[:, :]),
                            waits=(tya, lg5.get("mp")), inc="v")
                    else:
                        pm_tok = P.emit("vector", lambda e: e.tensor_tensor(
                            mp5.ap()[:, :], mp5.ap()[:, :], pabs.ap()[:, :], ALU.max),
                            waits=(tya, pm_tok), inc="v")
                trd = None
                for g in range(4):
                    bkt = bank[2] if g % 2 == 0 else bank[3]
                    ttr = P.emit("tensor", lambda e, g=g, bkt=bkt: e.transpose(
                        bkt.ap()[:, 0:128], mp5.ap()[:, 128 * g:128 * (g + 1)],
                        ident.ap()[:, :]), waits=(pm_tok, t_am5i, bfree(bkt)), inc="mm")
                    trd = P.emit("vector", lambda e, g=g, bkt=bkt: e.reduce_max(
                        am5b.ap()[:, g:g + 1], bkt.ap()[:, 0:128], axis=AXX),
                        waits=(ttr,), inc="v")
                    bset(bkt, trd)
                    lg5["mp"] = ttr
                tcl = P.emit("vector", lambda e: e.tensor_single_scalar(
                    am5b.ap()[:, 0:4], am5b.ap()[:, 0:4], 1e-5, ALU.max), waits=(trd,), inc="v")
                ttra = P.emit("tensor", lambda e: e.transpose(
                    bank[2].ap()[0:4, 0:128], am5b.ap()[:, 0:4], ident.ap()[:, :]),
                    waits=(tcl, bfree(bank[2])), inc="mm")
                tcpr = P.emit("vector", lambda e: e.tensor_copy(
                    row5tp.ap()[:, :], bank[2].ap()[0:4, 0:128]), waits=(ttra,), inc="v")
                bset(bank[2], tcpr)
                tdm5 = P.emit("scalar", lambda e: e.dma_start(
                    out=row5sc.ap()[0:1, 0:512], in_=row5tp.ap()[:, :]),
                    waits=(tcpr, og.get(("r5",))), stream='sm4')
                trec = P.emit("vector", lambda e: e.reciprocal(
                    row5sc.ap()[0:1, 512:1024], row5sc.ap()[0:1, 0:512]),
                    waits=(tdm5,), inc="v")
                ts127 = P.emit("vector", lambda e: e.tensor_single_scalar(
                    row5sc.ap()[0:1, 512:1024], row5sc.ap()[0:1, 512:1024],
                    127.0, ALU.mult), waits=(trec,), inc="v")
                tgam = P.emit("vector", lambda e: e.tensor_scalar(
                    row5sc.ap()[0:1, 1024:1536], row5sc.ap()[0:1, 0:512],
                    thr_cols.ap()[0:1, 9:10], 1.0 / 127.0, ALU.mult, ALU.mult),
                    waits=(tdm5,), inc="v")

                tb1 = P.emit("tensor", lambda e: e.matmul(
                    bank[2].ap()[:, 0:512], ones_row.ap()[0:1, :],
                    row5sc.ap()[0:1, 512:1024],
                    start=True, stop=True), waits=(ts127, bfree(bank[2])), inc="mm")
                tcp = P.emit("vector", lambda e: e.tensor_copy(
                    s5b.ap()[:, :], bank[2].ap()[:, 0:512]),
                    waits=(tb1, og.get(("s5",)), og.get(("s5g",))), inc="v")
                tb2 = P.emit("tensor", lambda e: e.matmul(
                    bank[3].ap()[:, 0:512], ones_row.ap()[0:1, :],
                    row5sc.ap()[0:1, 1024:1536],
                    start=True, stop=True), waits=(tgam, bfree(bank[3])), inc="mm")
                tcz_last = P.emit("vector", lambda e: e.tensor_copy(
                    ztb.ap()[:, :], bank[3].ap()[:, 0:512]),
                    waits=(tb2,), inc="v")
                bset(bank[2], tcp); bset(bank[3], tcz_last)
                og[("r5",)] = tcz_last
                t_s5 = P.tok("v")
                for ic in range(NHC):
                    rr = ic % 2
                    P.emit("vector", lambda e, ic=ic, rr=rr: e.tensor_tensor(
                        gscr.ap()[:, 512 * rr:512 * (rr + 1)], psl.ap()[:, 512 * ic:512 * (ic + 1)],
                        s5b.ap()[:, :], ALU.mult), waits=(t_s5, pm_tok), inc="v")
                    P.emit("vector", lambda e, ic=ic, rr=rr: e.tensor_scalar(
                        psl.ap()[:, 512 * ic:512 * (ic + 1)], gscr.ap()[:, 512 * rr:512 * (rr + 1)],
                        MRND, MRND, ALU.add, ALU.subtract), inc="v")
                t_pq = (P.tok("v"), P.tok("g"))
                og[("s5",)] = t_pq[0]
                og[("s5g",)] = t_pq[1]
                og[("gs", 0)] = t_pq[0]
                og[("gs", 1)] = t_pq[1]
                og[("us", 0)] = t_pq[0]
                og[("us", 1)] = t_pq[1]
                wdfree = {}
                for dc in range(NFC):
                    r = dc % 2
                    twd = P.emit("sync", lambda e, dc=dc, r=r: e.dma_start(
                        out=wdr.ap()[:, NHC * 128 * r: NHC * 128 * (r + 1)].rearrange(
                            "p (h w) -> p h w", h=NHC),
                        in_=dr["sp_wdt"].ap()[:, 128 * dc:128 * (dc + 1)].rearrange(
                            "(h p) w -> p h w", p=128)),
                        waits=(wdfree.get(r), t_wdt), stream=f'wdl{r}')
                    bk = bank[0] if dc % 2 == 0 else bank[1]
                    for ic in range(NHC):
                        P.emit("tensor", lambda e, ic=ic, dc=dc, r=r, bk=bk: e.matmul(
                            bk.ap()[:, 0:512],
                            wdr.ap()[:, NHC * 128 * r + 128 * ic: NHC * 128 * r + 128 * (ic + 1)],
                            psl.ap()[:, 512 * ic:512 * (ic + 1)],
                            start=(ic == 0), stop=(ic == NHC - 1)),
                            waits=((twd, bfree(bk)) + t_pq) if ic == 0 else (),
                            inc="mm" if ic == NHC - 1 else None)
                    tmm = P.tok("mm")
                    wdfree[r] = tmm
                    t_prev_wd = tmm
                    rr = dc % 2
                    tx3 = P.emit("scalar", lambda e, dc=dc, i0=i0, rr=rr: e.dma_start(
                        out=x3iring.ap()[:, 512 * rr:512 * (rr + 1)],
                        in_=dr["sp_x3"].ap()[128 * dc:128 * (dc + 1), i0:i0 + 512]),
                        waits=(og.get(("x3", rr)),), stream=f'x3i{rr}')
                    to1 = P.emit("vector", lambda e, rr=rr, bk=bk: e.tensor_tensor(
                        outstg.ap()[:, 512 * rr:512 * (rr + 1)], bk.ap()[:, 0:512],
                        ztb.ap()[:, :], ALU.mult),
                        waits=(tmm, tcz_last, og.get(("os", rr))), inc="v")
                    to2 = P.emit("vector", lambda e, rr=rr: e.tensor_tensor(
                        outstg.ap()[:, 512 * rr:512 * (rr + 1)],
                        outstg.ap()[:, 512 * rr:512 * (rr + 1)],
                        x3iring.ap()[:, 512 * rr:512 * (rr + 1)], ALU.add),
                        waits=(tx3,), inc="v")
                    og[("x3", rr)] = to2
                    bset(bk, to2)
                    og[("os", rr)] = P.emit("scalar", lambda e, dc=dc, i0=i0, rr=rr: e.dma_start(
                        out=dr["out_shard"].ap()[128 * dc:128 * (dc + 1), i0:i0 + 512],
                        in_=outstg.ap()[:, 512 * rr:512 * (rr + 1)]),
                        waits=(to2,), stream=f'os{rr}')

        P.wait("sync", *P.all_toks())
        block = E(nc.Block())
        P.run_all(block)
    return nc


def kernel(**inputs):
    maps = prep_core_inputs(inputs)
    nc = build_nc()
    res = run_bass_kernel_spmd(nc, maps, core_ids=list(range(NCORES)))
    return assemble_output(res.results)


# version 10
# speedup vs baseline: 1.0827x; 1.0780x over previous
import os

import numpy as np
import ml_dtypes
from contextlib import ExitStack

import concourse.bass as bass
import concourse.mybir as mybir
from concourse.bass_isa import ReduceOp
from concourse.ap import AP
from concourse.bass_utils import run_bass_kernel_spmd

F32 = mybir.dt.float32
BF16 = mybir.dt.bfloat16
AF = mybir.ActivationFunctionType
ALU = mybir.AluOpType
AXX = mybir.AxisListType.X

D = 1024
SEQ = 4096
NBATCH = 4
NCORES = 8
HID = 4096
NT = 32
OWN_NT = 16
OWN = 2048
SLABW = 130
SLABT = OWN_NT * SLABW
NFC = 8
NHC = 32
MRND = 12582912.0
NEGBIG = -1.0e30
EPS_LN = 1e-5

ENGINES = ("sync", "gpsimd", "vector", "scalar", "tensor")

STRICT_SYNC = os.environ.get("BASS_SIM_FP64") == "1"

HAT_LIMIT = None
HAT_ECS = None
LN1_STOP = None
HAT_STOP = None

I_CHUNKS = [(0, 512), (512, 512), (1024, 512), (1536, 512), (2048, 32)]
T_CHUNKS = [(0, 512), (512, 512), (1024, 512), (1536, 512)]


def _true_tiles(par):
    return [2 * k + par for k in range(OWN_NT)]


def _gtile(jt):
    return (jt % 2) * OWN_NT + jt // 2



def prep_core_inputs(I):
    x0 = np.ascontiguousarray(np.asarray(I["L0_attn_x"], dtype=np.float32))
    f32 = lambda a: np.ascontiguousarray(np.asarray(a, dtype=np.float32))
    T = lambda a: np.ascontiguousarray(np.asarray(a, dtype=np.float32).T)
    col = lambda a: np.ascontiguousarray(np.asarray(a, dtype=np.float32).reshape(-1, 1))

    shared = {
        "WqT0": T(I["L0_attn_Wq"]), "WkT0": T(I["L0_attn_Wk"]),
        "WvT0": T(I["L0_attn_Wv"]), "WoT0": T(I["L0_attn_Wo"]),
        "bo0": col(I["L0_attn_bo"]),
        "qs_col": col(np.asarray(I["L0_attn_q_scale"]).reshape(-1)),
        "ks_col": col(np.asarray(I["L0_attn_k_scale"]).reshape(-1)),
        "qs16": f32(np.asarray(I["L0_attn_q_scale"]).reshape(16, 64)),
        "ks16": f32(np.asarray(I["L0_attn_k_scale"]).reshape(16, 64)),
        "w1T": T(I["L0_ffn_w1"]), "w2T": T(I["L0_ffn_w2"]),
        "convw": f32(np.asarray(I["L0_ffn_conv_w"]).reshape(HID, 3)),
        "convb": col(I["L0_ffn_conv_b"]),
        "Wq1": f32(I["L1_attn_Wq"]), "Wk1": f32(I["L1_attn_Wk"]), "Wv1": f32(I["L1_attn_Wv"]),
        "bq1": col(I["L1_attn_bq"]), "bk1": col(I["L1_attn_bk"]), "bv1": col(I["L1_attn_bv"]),
        "wgT": T(I["L1_ffn_w_gate"]), "wuT": T(I["L1_ffn_w_up"]), "wdT": T(I["L1_ffn_w_down"]),
    }

    maps = []
    for c in range(NCORES):
        b, par = c // 2, c % 2
        xT = np.ascontiguousarray(x0[b].T)
        slab = np.zeros((D, SLABT), np.float32)
        for k, t in enumerate(_true_tiles(par)):
            lo, hi = 128 * t - 1, 128 * t + 129
            s0, s1 = max(lo, 0), min(hi, SEQ)
            slab[:, SLABW * k + (s0 - lo): SLABW * k + (s1 - lo)] = xT[:, s0:s1]
        amask = np.ones(SLABT, np.float32)
        if par == 0:
            amask[0] = 0.0
        else:
            amask[SLABT - 1] = 0.0
        ampad = np.zeros(17 * 128, np.float32)
        ampad[:SLABT] = amask
        amask_tm = np.ascontiguousarray(ampad.reshape(17, 128).T)
        stair = np.full((8, 128, 512), NEGBIG, np.float32)
        jj = np.arange(128)[:, None]
        ii = np.arange(512)[None, :]
        itrue = (2 * (ii // 128) + par) * 128 + (ii % 128)
        for s in range(8):
            stair[s][(s * 128 + jj) <= itrue] = 0.0
        m = dict(shared)
        m["x0T_full"] = xT
        m["x0T_slab"] = slab
        m["amask_tm"] = amask_tm
        m["stair"] = np.ascontiguousarray(stair.reshape(8 * 128, 512))
        maps.append(m)
    return maps


def assemble_output(results):
    out = np.empty((NBATCH, SEQ, D), np.float32)
    for c in range(NCORES):
        b, par = c // 2, c % 2
        sh = results[c]["out_shard"]
        for k, t in enumerate(_true_tiles(par)):
            out[b, 128 * t:128 * (t + 1), :] = sh[:, 128 * k:128 * (k + 1)].T
    return out



class Prog:
    def __init__(self, nc, ctx):
        self.nc = nc
        self.q = {e: [] for e in ENGINES}
        self.cnt = {}
        self.sem = {}
        self._last_wait = {e: {} for e in ENGINES}
        self.ctx = ctx

    def sem_new(self, *names):
        for n in names:
            self.sem[n] = self.ctx.enter_context(self.nc.semaphore(n))
            self.cnt[n] = 0

    def tok(self, sem):
        return (sem, self.cnt[sem])

    ENG_SEM = {"vector": "v", "scalar": "sc", "tensor": "mm", "gpsimd": "g",
               "sync": "sy"}

    @staticmethod
    def _flat(waits):
        out = []
        for wt in waits:
            if wt is None:
                continue
            if isinstance(wt[0], str):
                out.append(wt)
            else:
                out.extend(w for w in wt if w is not None)
        return out

    def all_toks(self):
        return [(n, self.cnt[n]) for n in self.cnt]

    def emit(self, eng, fn, waits=(), inc=None, inc_k=1, stream=None, stream_k=16):
        waits = self._flat(waits)
        auto = self.ENG_SEM[eng]
        if stream is not None:
            sname = "q_" + stream
            if sname not in self.sem:
                self.sem_new(sname)
            waits = waits + [(sname, self.cnt[sname])]
            if STRICT_SYNC or eng != "tensor":
                waits = waits + [(auto, self.cnt[auto])]
            inc, inc_k = sname, stream_k
        else:
            if STRICT_SYNC or eng != "tensor":
                waits = waits + [(auto, self.cnt[auto])]
            if inc is None:
                inc = auto
                inc_k = 1
        tok = None
        if inc is not None:
            self.cnt[inc] += inc_k
            tok = (inc, self.cnt[inc])
        lw = self._last_wait[eng]
        eff = []
        for wt in waits:
            s, v = wt
            if v <= 0 or lw.get(s, 0) >= v:
                continue
            eff.append((s, v))
            lw[s] = v
        sem = self.sem

        def thunk(e, eff=tuple(eff), fn=fn, inc=inc, inc_k=inc_k):
            for (s, v) in eff:
                e.wait_ge(sem[s], v)
            r = fn(e)
            if inc is not None:
                r.then_inc(sem[inc], inc_k)
        self.q[eng].append(thunk)
        return tok

    def wait(self, eng, *toks):
        lw = self._last_wait[eng]
        need = [wt for wt in self._flat(toks)
                if wt[1] > 0 and lw.get(wt[0], 0) < wt[1]]
        if not need:
            return
        self.emit(eng, lambda e: e.nop(), waits=need)

    def barrier(self):
        toks = [(n, self.cnt[n]) for n in self.cnt]
        for e in ENGINES:
            self.wait(e, *toks)

    def run_all(self, block):
        qs = self.q

        @block.sync
        def _(e):
            for f in qs["sync"]:
                f(e)

        @block.gpsimd
        def _(e):
            for f in qs["gpsimd"]:
                f(e)

        @block.vector
        def _(e):
            for f in qs["vector"]:
                f(e)

        @block.scalar
        def _(e):
            for f in qs["scalar"]:
                f(e)

        @block.tensor
        def _(e):
            for f in qs["tensor"]:
                f(e)



def mkap(t, offset, dims):
    return AP(t, offset, [list(d) for d in dims])



def build_nc(upto="full", dbg=()):
    nc = bass.Bass("TRN2", target_bir_lowering=False)
    dbg = set(dbg)

    dr = {}
    def din(name, shape, dt=F32):
        dr[name] = nc.dram_tensor(name, list(shape), dt, kind="ExternalInput")
    def dout(name, shape, dt=F32):
        dr[name] = nc.dram_tensor(name, list(shape), dt, kind="ExternalOutput")
    def dint(name, shape, dt=F32):
        dr[name] = nc.dram_tensor(name, list(shape), dt)

    din("x0T_full", (D, SEQ)); din("x0T_slab", (D, SLABT))
    din("WqT0", (D, D)); din("WkT0", (D, D)); din("WvT0", (D, D)); din("WoT0", (D, D))
    din("bo0", (D, 1)); din("qs_col", (D, 1)); din("ks_col", (D, 1))
    din("qs16", (16, 64)); din("ks16", (16, 64))
    din("w1T", (D, HID)); din("w2T", (HID, D)); din("convw", (HID, 3)); din("convb", (HID, 1))
    din("Wq1", (D, D)); din("Wk1", (D, D)); din("Wv1", (D, D))
    din("bq1", (D, 1)); din("bk1", (D, 1)); din("bv1", (D, 1))
    din("wgT", (D, HID)); din("wuT", (D, HID)); din("wdT", (HID, D))
    din("amask_tm", (128, 17)); din("stair", (8 * 128, 512))
    dout("out_shard", (D, OWN))

    dint("sp_xh1", (D, SEQ), BF16)
    dint("sp_xh1s", (D, SLABT), BF16)
    dint("sp_khat", (D, SEQ), BF16)
    dint("sp_qhat", (D, SLABT), BF16)
    dint("sp_x1", (D, SLABT))
    dint("sp_x2", (D, OWN))
    dint("sp_z", (D, OWN))
    dint("sp_x3", (D, OWN))
    dint("sp_k1", (D, SEQ), BF16)
    dint("sp_v1", (SEQ, D), BF16)
    dint("sp_w1t", (D, HID), BF16)
    dint("sp_w2t", (HID, D), BF16)
    dint("sp_wgt", (D, HID), BF16)
    dint("sp_wut", (D, HID), BF16)
    dint("sp_wdt", (HID, D), BF16)
    dint("cc_h3_in", (D, OWN), BF16)
    dint("cc_h3_out", (2 * D, OWN), BF16)
    dint("cc_st_in", (128, 64))
    dint("cc_st_out", (2 * 128, 64))

    with ExitStack() as ctx:
        E = ctx.enter_context
        P = Prog(nc, ctx)
        P.sem_new("mm", "v", "sc", "g", "sy")
        H = {"bank_free": {}}

        ones_col = E(nc.sbuf_tensor("ones_col", [128, 1], F32))
        ones_colb = E(nc.sbuf_tensor("ones_colb", [128, 1], BF16))
        ones_row = E(nc.sbuf_tensor("ones_row", [1, 128], F32))
        ident = E(nc.sbuf_tensor("ident", [128, 128], F32))
        onesA = E(nc.sbuf_tensor("onesA", [128, 1], F32))
        onesB = E(nc.sbuf_tensor("onesB", [128, 1], F32))
        rowsel0 = E(nc.sbuf_tensor("rowsel0", [1, 128], F32))
        rowsel1 = E(nc.sbuf_tensor("rowsel1", [1, 128], F32))
        ones64 = E(nc.sbuf_tensor("ones64", [128, 64], F32))
        ones128 = E(nc.sbuf_tensor("ones128", [128, 128], F32))
        thr_cols = E(nc.sbuf_tensor("thr_cols", [128, 16], F32))
        negB = E(nc.sbuf_tensor("negB", [128, 1], F32))
        epsl_col = E(nc.sbuf_tensor("epsl_col", [128, 1], F32))
        eps24_col = E(nc.sbuf_tensor("eps24_col", [128, 1], F32))

        P.emit("gpsimd", lambda e: e.memset(ones_col.ap()[:, :], 1.0), inc="g")
        P.emit("gpsimd", lambda e: e.memset(epsl_col.ap()[:, :], EPS_LN), inc="g")
        P.emit("gpsimd", lambda e: e.memset(eps24_col.ap()[:, :], 1e-24), inc="g")
        P.emit("gpsimd", lambda e: e.memset(ones_colb.ap()[:, :], 1.0), inc="g")
        P.emit("gpsimd", lambda e: e.memset(ones_row.ap()[:, :], 1.0), inc="g")
        P.emit("gpsimd", lambda e: e.memset(ident.ap()[:, :], 0.0), inc="g")
        P.emit("gpsimd", lambda e: e.affine_select(
            out=ident.ap()[:, :], in_=ident.ap()[:, :],
            compare_op=ALU.not_equal, fill=1.0, base=0,
            pattern=[[-1, 128]], channel_multiplier=1), inc="g")
        P.emit("gpsimd", lambda e: e.memset(onesA.ap()[:, :], 0.0), inc="g")
        P.emit("gpsimd", lambda e: e.memset(onesA.ap()[0:64, 0:1], 1.0), inc="g")
        P.emit("gpsimd", lambda e: e.memset(onesB.ap()[:, :], 0.0), inc="g")
        P.emit("gpsimd", lambda e: e.memset(onesB.ap()[64:128, 0:1], 1.0), inc="g")
        P.emit("gpsimd", lambda e: e.memset(rowsel0.ap()[:, :], 0.0), inc="g")
        P.emit("gpsimd", lambda e: e.memset(rowsel0.ap()[0:1, 0:64], 1.0), inc="g")
        P.emit("gpsimd", lambda e: e.memset(rowsel1.ap()[:, :], 0.0), inc="g")
        P.emit("gpsimd", lambda e: e.memset(rowsel1.ap()[0:1, 64:128], 1.0), inc="g")
        P.emit("gpsimd", lambda e: e.memset(ones128.ap()[:, :], 1.0), inc="g")
        P.emit("gpsimd", lambda e: e.memset(ones64.ap()[:, :], 0.0), inc="g")
        P.emit("gpsimd", lambda e: e.memset(ones64.ap()[64:65, :], 1.0), inc="g")
        t_const = P.tok("g")

        pA = E(nc.psum_tensor("pA", [128, 1024], F32))
        pB = E(nc.psum_tensor("pB", [128, 1024], F32))
        bank_s = [E(nc.psum_tensor(f"pb{i}", [128, 512], F32)) for i in range(4)]

        class Bk:
            __slots__ = ("t", "off")
            def __init__(self, t, off):
                self.t, self.off = t, off
            def ap(self):
                a = self.t.ap()
                return a[:, self.off:self.off + 512]

        bank = [Bk(pA, 0), Bk(pA, 512), Bk(pB, 0), Bk(pB, 512)] + \
               [Bk(b, 0) for b in bank_s]

        def bfree(b):
            return H["bank_free"].get(id(b))

        def bset(b, tok):
            H["bank_free"][id(b)] = tok


        WQ = [("w1T", "sp_w1t", D, HID), ("w2T", "sp_w2t", HID, D),
              ("wgT", "sp_wgt", D, HID), ("wuT", "sp_wut", D, HID),
              ("wdT", "sp_wdt", HID, D)]
        phWQ = ExitStack()
        EWQ = phWQ.enter_context
        wring = EWQ(nc.sbuf_tensor("wring", [128, 2 * 4096], F32))
        wacc = EWQ(nc.sbuf_tensor("wacc", [128, 32], F32))
        wrow = EWQ(nc.sbuf_tensor("wrow", [1, 32], F32))
        wb1 = EWQ(nc.sbuf_tensor("wb1", [128, 2048], F32))
        wb2 = EWQ(nc.sbuf_tensor("wb2", [128, 2048], F32))
        wtn = EWQ(nc.sbuf_tensor("wtn", [128, 2 * 2048], BF16))
        wq_mean_steps = []
        wq_tern_steps = []
        wq_state = {}

        def _wq_build():
            for wi, (src, dst, R, C) in enumerate(WQ):
                nch = R // 128
                colw = C
                nsl = colw // 512

                def mean_step(ch, wi=wi, src=src, R=R, C=C, nch=nch, colw=colw, nsl=nsl):
                    sl = ch % 2
                    tld = P.emit("sync", lambda e, src=src, ch=ch, colw=colw, sl=sl: e.dma_start(
                        out=wring.ap()[:, 4096 * sl:4096 * sl + colw],
                        in_=dr[src].ap()[128 * ch:128 * (ch + 1), :]),
                        waits=(wq_state.get(("mg", sl)),),
                        stream=f'wqm{sl}')
                    ta = P.emit("vector", lambda e, ch=ch, colw=colw, sl=sl: e.reduce_sum(
                        wacc.ap()[:, ch:ch + 1], wring.ap()[:, 4096 * sl:4096 * sl + colw],
                        axis=AXX, apply_absolute_value=True),
                        waits=(tld, wq_state.get(("wacc_t",)) if ch == 0 else None), inc="v")
                    wq_state[("mg", sl)] = ta
                    wq_state[("macc", ch)] = ta
                    if ch == nch - 1:
                        tmm = P.emit("tensor", lambda e, nch=nch: e.matmul(
                            bank[6].ap()[0:1, 0:nch], ones_col.ap()[:, :],
                            wacc.ap()[:, 0:nch], start=True, stop=True),
                            waits=tuple(wq_state[("macc", c)] for c in range(nch)) +
                                  (t_const, bfree(bank[6])), inc="mm")
                        wq_state[("wacc_t",)] = tmm
                        trr = P.emit("vector", lambda e, wi=wi, nch=nch: e.reduce_sum(
                            wrow.ap()[0:1, 2 * wi:2 * wi + 1].rearrange("p a -> p a ()"),
                            bank[6].ap()[0:1, 0:nch].rearrange("p a -> p () a"), axis=AXX),
                            waits=(tmm,), inc="v")
                        tm = P.emit("vector", lambda e, wi=wi, R=R, C=C: e.tensor_scalar(
                            wrow.ap()[0:1, 2 * wi:2 * wi + 1], wrow.ap()[0:1, 2 * wi:2 * wi + 1],
                            1.0 / (R * C), 1e-5, ALU.mult, ALU.max), waits=(trr,), inc="v")
                        tb1 = P.emit("tensor", lambda e, wi=wi: e.matmul(
                            bank[6].ap()[:, 0:1], ones_row.ap()[0:1, :],
                            wrow.ap()[0:1, 2 * wi:2 * wi + 1],
                            start=True, stop=True), waits=(tm, trr), inc="mm")
                        tc1 = P.emit("vector", lambda e, wi=wi: e.tensor_copy(
                            thr_cols.ap()[:, 2 * wi + 1:2 * wi + 2], bank[6].ap()[:, 0:1]),
                            waits=(tb1,), inc="v")
                        tc2 = P.emit("vector", lambda e, wi=wi: e.tensor_single_scalar(
                            thr_cols.ap()[:, 2 * wi:2 * wi + 1], bank[6].ap()[:, 0:1], 0.5,
                            ALU.mult), waits=(tb1,), inc="v")
                        bset(bank[6], tc2)
                        tc3 = P.emit("vector", lambda e, wi=wi: e.tensor_single_scalar(
                            thr_cols.ap()[:, 10 + wi:11 + wi],
                            thr_cols.ap()[:, 2 * wi:2 * wi + 1], -1.0, ALU.mult),
                            waits=(tc2,), inc="v")
                        wq_state[("thr", wi)] = tc3

                for ch in range(nch):
                    wq_mean_steps.append((mean_step, ch))

                def tern_step(ch, wi=wi, src=src, dst=dst, colw=colw, nch=nch):
                    sl = ch % 2
                    tld = P.emit("gpsimd", lambda e, src=src, ch=ch, colw=colw, sl=sl: e.dma_start(
                        out=wring.ap()[:, 4096 * sl:4096 * sl + colw],
                        in_=dr[src].ap()[128 * ch:128 * (ch + 1), :]),
                        waits=(wq_state.get(("tg", sl)), wq_state.get(("mg", sl))),
                        stream=f'wqt{sl}')
                    tc2 = wq_state[("thr", wi)]
                    nhb = max(1, colw // 2048)
                    last = None
                    for hb in range(nhb):
                        hw = min(2048, colw)
                        off = 4096 * sl + 2048 * hb
                        rr = hb % 2
                        tg1 = P.emit("vector", lambda e, off=off, hw=hw, wi=wi: e.tensor_scalar(
                            wb1.ap()[:, 0:hw], wring.ap()[:, off:off + hw],
                            thr_cols.ap()[:, 2 * wi:2 * wi + 1], None, ALU.is_ge),
                            waits=(tld, tc2), inc="v")
                        tg2 = P.emit("vector", lambda e, off=off, hw=hw, wi=wi: e.tensor_scalar(
                            wb2.ap()[:, 0:hw], wring.ap()[:, off:off + hw],
                            thr_cols.ap()[:, 10 + wi:11 + wi], None, ALU.is_le),
                            inc="v")
                        tg3 = P.emit("vector", lambda e, off=off, hw=hw, rr=rr: e.tensor_tensor(
                            wtn.ap()[:, 2048 * rr:2048 * rr + hw], wb1.ap()[:, 0:hw],
                            wb2.ap()[:, 0:hw], ALU.subtract),
                            waits=(tg1, tg2, wq_state.get(("tsp", rr))), inc="v")
                        last = tg3
                        wq_state[("tsp", rr)] = P.emit(
                            "scalar", lambda e, dst=dst, ch=ch, hb=hb, hw=hw, rr=rr: e.dma_start(
                                out=dr[dst].ap()[128 * ch:128 * (ch + 1),
                                                 2048 * hb:2048 * hb + hw],
                                in_=wtn.ap()[:, 2048 * rr:2048 * rr + hw]),
                            waits=(tg3,), stream=f'wqs{rr}')
                    wq_state[("tg", sl)] = last
                    if ch == (128 if False else (dst and 0)) or True:
                        pass
                    if ch == nch - 1:
                        wq_state[("done", wi)] = (P.tok("q_wqs0"),
                                                  P.tok("q_wqs1") if "q_wqs1" in P.cnt else None)

                for ch in range(nch):
                    wq_tern_steps.append((tern_step, ch))

        _wq_build()
        _wq_i = [0, 0]

        def pump_mean(n):
            while n > 0 and _wq_i[0] < len(wq_mean_steps):
                f, a = wq_mean_steps[_wq_i[0]]
                f(a)
                _wq_i[0] += 1
                n -= 1

        def pump_tern(n):
            while n > 0 and _wq_i[1] < len(wq_tern_steps):
                if _wq_i[0] < len(wq_mean_steps):
                    pump_mean(1)
                    n -= 1
                    continue
                f, a = wq_tern_steps[_wq_i[1]]
                f(a)
                _wq_i[1] += 1
                n -= 1

        def wq_finish():
            pump_mean(10 ** 9)
            pump_tern(10 ** 9)

        def emit_ln(name, src_getter, n_cols, out_cb, rows, sqr, state=None):
            nblk = (n_cols + 511) // 512
            state = state if state is not None else {}
            ring_free = state.setdefault("ring", {})
            sq_guard = state.setdefault("sq", [None])
            ab_guard = state.setdefault("ab", [None, None])
            pend = state.setdefault("pend", [None])

            def finish(ctx2):
                cb, c0, w, r, tile, ld, t7, t8 = ctx2
                bA, bB = bank[4], bank[5]
                tA = P.emit("tensor", lambda e, w=w, bA=bA: e.matmul(
                    bA.ap()[:, 0:w], ones_row.ap()[0:1, :],
                    rows.ap()[0:1, 2048 * (cb % 2) + 1536: 2048 * (cb % 2) + 1536 + w],
                    start=True, stop=True), waits=(t7, bfree(bA)), inc="mm")
                tB = P.emit("tensor", lambda e, w=w, bB=bB: e.matmul(
                    bB.ap()[:, 0:w], ones_row.ap()[0:1, :],
                    rows.ap()[0:1, 2048 * (cb % 2) + 512: 2048 * (cb % 2) + 512 + w],
                    start=True, stop=True), waits=(t8, bfree(bB)), inc="mm")
                ab_guard[0] = tA
                ab_guard[1] = tB
                last = out_cb(cb, c0, w, r, bA, bB, (tA, tB), tile, ld)
                ring_free[r] = last
                bset(bA, last); bset(bB, last)

            for cb in range(nblk):
                c0 = 512 * cb
                w = min(512, n_cols - c0)
                r = cb % 2
                ro = 2048 * r
                bsum, bsq = bank[0 + 2 * r], bank[1 + 2 * r]
                ld, tile = src_getter(cb, c0, w, r, ring_free.get(r))
                sq_toks = []
                for fc in range(NFC):
                    st = P.emit("scalar", lambda e, fc=fc, w=w, tile=tile: e.activation(
                        sqr.ap()[:, 512 * fc:512 * fc + w], tile(fc), AF.Square,
                        bias=0.0, scale=1.0), waits=(ld, sq_guard[0]), inc="sc")
                    sq_toks.append(st)
                for fc in range(NFC):
                    P.emit("tensor", lambda e, fc=fc, w=w, tile=tile, bsum=bsum: e.matmul(
                        bsum.ap()[0:1, 0:w], ones_col.ap()[:, :], tile(fc),
                        start=(fc == 0), stop=(fc == NFC - 1)),
                        waits=(ld, t_const, bfree(bsum)) if fc == 0 else (),
                        inc="mm" if fc == NFC - 1 else None)
                t_sum = P.tok("mm")
                for fc in range(NFC):
                    P.emit("tensor", lambda e, fc=fc, w=w, bsq=bsq: e.matmul(
                        bsq.ap()[0:1, 0:w], ones_colb.ap()[:, :],
                        sqr.ap()[:, 512 * fc:512 * fc + w],
                        start=(fc == 0), stop=(fc == NFC - 1)),
                        waits=(sq_toks[fc], bfree(bsq)) if fc == 0 else (sq_toks[fc],),
                        inc="mm" if fc == NFC - 1 else None)
                t_sq = P.tok("mm")
                sq_guard[0] = t_sq
                t1 = P.emit("vector", lambda e, w=w, ro=ro, bsum=bsum: e.tensor_single_scalar(
                    rows.ap()[0:1, ro:ro + w], bsum.ap()[0:1, 0:w], 1.0 / D, ALU.mult),
                    waits=(t_sum,), inc="v")
                t2 = P.emit("vector", lambda e, w=w, ro=ro, bsq=bsq: e.tensor_single_scalar(
                    rows.ap()[0:1, ro + 512:ro + 512 + w], bsq.ap()[0:1, 0:w], 1.0 / D, ALU.mult),
                    waits=(t_sq, ab_guard[1]), inc="v")
                bset(bsum, t1); bset(bsq, t2)
                t3 = P.emit("vector", lambda e, w=w, ro=ro: e.tensor_tensor(
                    rows.ap()[0:1, ro + 1024:ro + 1024 + w], rows.ap()[0:1, ro:ro + w],
                    rows.ap()[0:1, ro:ro + w], ALU.mult), waits=(t1,), inc="v")
                t4 = P.emit("vector", lambda e, w=w, ro=ro: e.tensor_tensor(
                    rows.ap()[0:1, ro + 1024:ro + 1024 + w], rows.ap()[0:1, ro + 512:ro + 512 + w],
                    rows.ap()[0:1, ro + 1024:ro + 1024 + w], ALU.subtract), waits=(t2, t3), inc="v")
                t5 = P.emit("vector", lambda e, w=w, ro=ro: e.tensor_single_scalar(
                    rows.ap()[0:1, ro + 1024:ro + 1024 + w], rows.ap()[0:1, ro + 1024:ro + 1024 + w],
                    EPS_LN, ALU.add), waits=(t4,), inc="v")
                t6 = P.emit("vector", lambda e, w=w, ro=ro: e.reciprocal(
                    rows.ap()[0:1, ro + 1536:ro + 1536 + w], rows.ap()[0:1, ro + 1024:ro + 1024 + w]),
                    waits=(t5, ab_guard[0]), inc="v")
                t7 = P.emit("scalar", lambda e, w=w, ro=ro: e.activation(
                    rows.ap()[0:1, ro + 1536:ro + 1536 + w], rows.ap()[0:1, ro + 1536:ro + 1536 + w],
                    AF.Sqrt, bias=0.0, scale=1.0), waits=(t6,), inc="sc")
                t8 = P.emit("vector", lambda e, w=w, ro=ro: e.scalar_tensor_tensor(
                    rows.ap()[0:1, ro + 512:ro + 512 + w], rows.ap()[0:1, ro:ro + w], -1.0,
                    rows.ap()[0:1, ro + 1536:ro + 1536 + w], ALU.mult, ALU.mult),
                    waits=(t7,), inc="v")
                if pend[0] is not None:
                    finish(pend[0])
                pend[0] = (cb, c0, w, r, tile, ld, t7, t8)
            finish(pend[0])
            pend[0] = None

        with ExitStack() as ph0:
            E0 = ph0.enter_context
            xring = E0(nc.sbuf_tensor("ln0_ring", [128, 2 * NFC * 512], F32))
            rows0 = E0(nc.sbuf_tensor("ln0_rows", [1, 2 * 4 * 512], F32))
            sq0 = E0(nc.sbuf_tensor("ln0_sq", [128, NFC * 512], BF16))
            prod0 = E0(nc.sbuf_tensor("ln0_prod", [128, 512], F32))
            stg0 = E0(nc.sbuf_tensor("ln0_stg", [128, 2 * NFC * 512], BF16))
            stg_guard = {}

            def mk_ln0(src_name, spill_name, tag):
                def src(cb, c0, w, r, free_tok):
                    ld = P.emit("sync", lambda e, c0=c0, w=w, r=r: e.dma_start(
                        out=mkap(xring.ap().tensor, NFC * 512 * r,
                                 [[2 * NFC * 512, 128], [512, NFC], [1, w]]),
                        in_=dr[src_name].ap()[:, c0:c0 + w].rearrange("(f p) w -> p f w", p=128)),
                        waits=(free_tok,), stream=f'lnl{r}')
                    def tile(fc, r=r, w=w):
                        return xring.ap()[:, NFC * 512 * r + 512 * fc: NFC * 512 * r + 512 * fc + w]
                    return ld, tile

                def outcb(cb, c0, w, r, bA, bB, toks, tile, ld):
                    P.wait("vector", *toks)
                    P.wait("vector", stg_guard.get((tag, r)))
                    last = None
                    for fc in range(NFC):
                        P.emit("vector", lambda e, fc=fc, w=w, bA=bA, tile=tile: e.tensor_tensor(
                            prod0.ap()[:, 0:w], tile(fc), bA.ap()[:, 0:w], ALU.mult), inc="v")
                        last = P.emit("vector", lambda e, fc=fc, r=r, w=w, bB=bB: e.tensor_tensor(
                            stg0.ap()[:, NFC * 512 * r + 512 * fc: NFC * 512 * r + 512 * fc + w],
                            prod0.ap()[:, 0:w], bB.ap()[:, 0:w], ALU.add), inc="v")
                    tsp_a = tsp_b = None
                    for fc in range(NFC):
                        if fc % 2 == 0:
                            tsp_a = P.emit("scalar", lambda e, fc=fc, c0=c0, w=w, r=r: e.dma_start(
                                out=dr[spill_name].ap()[128 * fc:128 * (fc + 1), c0:c0 + w],
                                in_=stg0.ap()[:, NFC * 512 * r + 512 * fc: NFC * 512 * r + 512 * fc + w]),
                                waits=(last,), stream=f'ln0s{r}')
                        else:
                            tsp_b = P.emit("gpsimd", lambda e, fc=fc, c0=c0, w=w, r=r: e.dma_start(
                                out=dr[spill_name].ap()[128 * fc:128 * (fc + 1), c0:c0 + w],
                                in_=stg0.ap()[:, NFC * 512 * r + 512 * fc: NFC * 512 * r + 512 * fc + w]),
                                waits=(last,), stream=f'ln0g{r}')
                    tspl = (tsp_a, tsp_b)
                    stg_guard[(tag, r)] = tspl
                    pump_mean(1)
                    return tspl
                return src, outcb

            ln0_state = {}
            s0, o0 = mk_ln0("x0T_full", "sp_xh1", "a")
            emit_ln("ln0", s0, SEQ, o0, rows0, sq0, state=ln0_state)
            s1, o1 = mk_ln0("x0T_slab", "sp_xh1s", "a")
            emit_ln("ln0s", s1, SLABT, o1, rows0, sq0, state=ln0_state)
            t_xh1_sp = (P.tok("q_ln0s0"), P.tok("q_ln0s1"),
                        P.tok("q_ln0g0"), P.tok("q_ln0g1"))
            if "xh1" in dbg:
                dout("dbg_xh1", (D, SEQ), BF16)
                P.wait("sync", t_xh1_sp)
                P.emit("sync", lambda e: e.dma_start(
                    out=dr["dbg_xh1"].ap()[:, :], in_=dr["sp_xh1"].ap()[:, :]),
                    stream='dbg')
            P.barrier()

        if upto == "xh1":
            phWQ.close()
            P.wait("sync", *P.all_toks())
            block = E(nc.Block())
            P.run_all(block)
            return nc

        phv = ExitStack()
        v_sb = phv.enter_context(nc.sbuf_tensor("v_sb", [128, NT * 1040], BF16))
        qs_sb = phv.enter_context(nc.sbuf_tensor("qs_sb", [128, 8], F32))
        ks_sb = phv.enter_context(nc.sbuf_tensor("ks_sb", [128, 8], F32))
        bo_sb = phv.enter_context(nc.sbuf_tensor("bo_sb", [128, 8], F32))
        P.emit("gpsimd", lambda e: e.memset(
            v_sb.ap().rearrange("p (a b) -> p a b", b=65)[:, :, 64:65], 1.0), inc="g")
        t_vones = P.tok("g")
        P.emit("sync", lambda e: e.dma_start(
            out=qs_sb.ap().rearrange("p a -> p a ()"),
            in_=dr["qs_col"].ap().rearrange("(a p) o -> p a o", p=128),
            allow_slow_non_contiguous=True), stream='sm0')
        P.emit("sync", lambda e: e.dma_start(
            out=ks_sb.ap().rearrange("p a -> p a ()"),
            in_=dr["ks_col"].ap().rearrange("(a p) o -> p a o", p=128),
            allow_slow_non_contiguous=True), stream='sm1')
        P.emit("sync", lambda e: e.dma_start(
            out=bo_sb.ap().rearrange("p a -> p a ()"),
            in_=dr["bo0"].ap().rearrange("(a p) o -> p a o", p=128),
            allow_slow_non_contiguous=True), stream='sm2')
        t_small = (P.tok("q_sm0"), P.tok("q_sm1"), P.tok("q_sm2"))

        with ExitStack() as ph1:
            E1 = ph1.enter_context
            qs16r = E1(nc.sbuf_tensor("qs16r", [16, 64], F32))
            ks16r = E1(nc.sbuf_tensor("ks16r", [16, 64], F32))
            hmax = E1(nc.sbuf_tensor("hmax", [16, 3], F32))
            t1 = P.emit("sync", lambda e: e.dma_start(out=qs16r.ap()[:, :], in_=dr["qs16"].ap()[:, :]), stream='sm3')
            t2 = P.emit("sync", lambda e: e.dma_start(out=ks16r.ap()[:, :], in_=dr["ks16"].ap()[:, :]), stream='sm4')
            ta = P.emit("vector", lambda e: e.reduce_max(
                hmax.ap()[:, 0:1], qs16r.ap()[:, :], axis=AXX, apply_absolute_value=True),
                waits=(t1,), inc="v")
            tb = P.emit("vector", lambda e: e.reduce_max(
                hmax.ap()[:, 1:2], ks16r.ap()[:, :], axis=AXX, apply_absolute_value=True),
                waits=(t2,), inc="v")
            tc_ = P.emit("vector", lambda e: e.tensor_tensor(
                hmax.ap()[:, 2:3], hmax.ap()[:, 0:1], hmax.ap()[:, 1:2], ALU.mult),
                waits=(ta, tb), inc="v")
            hrow16 = E1(nc.sbuf_tensor("hrow16", [1, 16], F32))
            tdh = P.emit("scalar", lambda e: e.dma_start(
                out=hrow16.ap()[0:1, :], in_=hmax.ap()[:, 2:3],
                allow_slow_non_contiguous=True),
                waits=(tc_,), stream='sm3')
            trdh = P.emit("vector", lambda e: e.reduce_max(
                hmax.ap()[0:1, 2:3], hrow16.ap()[0:1, :], axis=AXX),
                waits=(tdh,), inc="v")
            tnb = P.emit("vector", lambda e: e.tensor_single_scalar(
                negB.ap()[0:1, 0:1], hmax.ap()[0:1, 2:3], -0.125, ALU.mult),
                waits=(trdh,), inc="v")
            tmm = P.emit("tensor", lambda e: e.matmul(
                bank[7].ap()[:, 0:1], ones_row.ap()[0:1, :], negB.ap()[0:1, 0:1],
                start=True, stop=True), waits=(tnb, t_const, bfree(bank[7])), inc="mm")
            tnb2 = P.emit("vector", lambda e: e.tensor_copy(negB.ap()[:, 0:1], bank[7].ap()[:, 0:1]),
                          waits=(tmm,), inc="v")
            bset(bank[7], tnb2)

            wq_sb = E1(nc.sbuf_tensor("wq_sb", [128, NFC * 1024], BF16))
            wk_sb = E1(nc.sbuf_tensor("wk_sb", [128, NFC * 1024], BF16))
            wv_sb = E1(nc.sbuf_tensor("wv_sb", [128, NFC * 1024], BF16))
            for wi_, (nm, sbw) in enumerate((("WqT0", wq_sb), ("WkT0", wk_sb), ("WvT0", wv_sb))):
                P.emit("gpsimd", lambda e, nm=nm, sbw=sbw: e.dma_start(
                    out=sbw.ap().rearrange("p (f n) -> p f n", f=NFC),
                    in_=dr[nm].ap().rearrange("(f p) n -> p f n", p=128)), stream=f'gw{wi_}')
            t_wload = (P.tok("q_gw0"), P.tok("q_gw1"), P.tok("q_gw2"))

            xr1 = E1(nc.sbuf_tensor("p1_xring", [128, 2 * NFC * 512], BF16))
            praw = E1(nc.sbuf_tensor("p1_raw", [128, 2 * 512], F32))
            psq1 = E1(nc.sbuf_tensor("p1_sq", [128, 512], F32))
            stg1 = E1(nc.sbuf_tensor("p1_stg", [128, 2 * 512], BF16))
            sumr = E1(nc.sbuf_tensor("p1_sumr", [1, 1024], F32))
            hg = {}

            def hat_pass(src_spill, n_cols, wsb, scol, spill):
                nblk = (n_cols + 511) // 512
                if HAT_LIMIT:
                    nblk = min(nblk, HAT_LIMIT)
                for cb in range(nblk):
                    c0 = 512 * cb
                    w = min(512, n_cols - c0)
                    r = cb % 2
                    ld = P.emit("sync", lambda e, c0=c0, w=w, r=r, src_spill=src_spill: e.dma_start(
                        out=mkap(xr1.ap().tensor, NFC * 512 * r,
                                 [[2 * NFC * 512, 128], [512, NFC], [1, w]]),
                        in_=dr[src_spill].ap()[:, c0:c0 + w].rearrange("(f p) w -> p f w", p=128)),
                        waits=(hg.get(("x", r)), t_xh1_sp), stream=f'lnl{r}')
                    last_r = None
                    hpend = [None]

                    def hat_norm(ec, bk, tmm, w=w, c0=c0):
                        nonlocal last_r
                        rr = ec % 2
                        trw = P.emit("vector", lambda e, rr=rr, w=w, bk=bk: e.tensor_copy(
                            praw.ap()[:, 512 * rr: 512 * rr + w], bk.ap()[:, 0:w]),
                            waits=(tmm, hg.get(("praw", rr))), inc="v")
                        bset(bk, trw)
                        if HAT_STOP and HAT_STOP < 2:
                            hg[("praw", rr)] = trw
                            last_r = trw
                            return
                        tsq = P.emit("scalar", lambda e, w=w, rr=rr: e.activation(
                            psq1.ap()[:, 0:w], praw.ap()[:, 512 * rr: 512 * rr + w],
                            AF.Square, bias=0.0, scale=1.0),
                            waits=(trw, hg.get(("psq",))), inc="sc")
                        if HAT_STOP and HAT_STOP < 3:
                            hg[("psq",)] = tsq
                            hg[("praw", rr)] = trw
                            last_r = tsq
                            return
                        P.emit("tensor", lambda e, w=w: e.matmul(
                            bank[2].ap()[0:1, 0:w], onesA.ap()[:, :], psq1.ap()[:, 0:w],
                            start=True, stop=True),
                            waits=(tsq, t_const, bfree(bank[2]), bfree(bank[3])), inc="mm")
                        tss = P.emit("tensor", lambda e, w=w: e.matmul(
                            bank[3].ap()[0:1, 0:w], onesB.ap()[:, :],
                            psq1.ap()[:, 0:w],
                            start=True, stop=True), inc="mm")
                        hg[("psq",)] = tss
                        if HAT_STOP and HAT_STOP < 4:
                            bset(bank[2], tss); bset(bank[3], tss)
                            hg[("praw", rr)] = trw
                            last_r = tss
                            return
                        tsc0 = P.emit("vector", lambda e, w=w: e.tensor_copy(
                            sumr.ap()[0:1, 0:w], bank[2].ap()[0:1, 0:w]),
                            waits=(tss, hg.get(("sumr",))), inc="v")
                        tsc1 = P.emit("vector", lambda e, w=w: e.tensor_copy(
                            sumr.ap()[0:1, 512:512 + w], bank[3].ap()[0:1, 0:w]), inc="v")
                        bset(bank[2], tsc1); bset(bank[3], tsc1)
                        ts1 = P.emit("scalar", lambda e, w=w: e.activation(
                            sumr.ap()[0:1, 0:w], sumr.ap()[0:1, 0:w],
                            AF.Sqrt, bias=0.0, scale=1.0),
                            waits=(tsc0,), inc="sc")
                        ts1b = P.emit("scalar", lambda e, w=w: e.activation(
                            sumr.ap()[0:1, 512:512 + w], sumr.ap()[0:1, 512:512 + w],
                            AF.Sqrt, bias=0.0, scale=1.0), waits=(tsc1,), inc="sc")
                        ts2 = P.emit("vector", lambda e, w=w: e.tensor_single_scalar(
                            sumr.ap()[0:1, 0:1024], sumr.ap()[0:1, 0:1024], 1e-12, ALU.max),
                            waits=(ts1, ts1b), inc="v")
                        ts3b = ts3 = P.emit("vector", lambda e, w=w: e.reciprocal(
                            sumr.ap()[0:1, 0:1024], sumr.ap()[0:1, 0:1024]),
                            waits=(ts2,), inc="v")
                        if HAT_STOP and HAT_STOP < 5:
                            hg[("sumr",)] = ts3b
                            hg[("praw", rr)] = trw
                            last_r = ts3b
                            return
                        if HAT_STOP and HAT_STOP < 6:
                            hg[("sumr",)] = ts3b
                            hg[("praw", rr)] = trw
                            last_r = ts3b
                            return
                        P.emit("tensor", lambda e, w=w: e.matmul(
                            bank[4].ap()[:, 0:w], rowsel0.ap()[0:1, :],
                            sumr.ap()[0:1, 0:w],
                            start=True, stop=False),
                            waits=(ts3, ts3b, bfree(bank[4])), inc="mm")
                        tbc = P.emit("tensor", lambda e, w=w: e.matmul(
                            bank[4].ap()[:, 0:w], rowsel1.ap()[0:1, :],
                            sumr.ap()[0:1, 512:512 + w],
                            start=False, stop=True), inc="mm")
                        hg[("sumr",)] = tbc
                        if HAT_STOP and HAT_STOP < 7:
                            bset(bank[4], tbc)
                            hg[("praw", rr)] = trw
                            last_r = tbc
                            return
                        tht = P.emit("vector", lambda e, ec=ec, rr=rr, w=w, scol=scol: e.scalar_tensor_tensor(
                            stg1.ap()[:, 512 * rr: 512 * rr + w],
                            praw.ap()[:, 512 * rr: 512 * rr + w], scol.ap()[:, ec:ec + 1],
                            bank[4].ap()[:, 0:w], ALU.mult, ALU.mult),
                            waits=(tbc, trw, t_small, hg.get(("stg", rr))), inc="v")
                        bset(bank[4], tht)
                        hg[("praw", rr)] = tht
                        if HAT_STOP and HAT_STOP < 8:
                            last_r = tht
                            hg[("stg", rr)] = tht
                            return
                        last_r = P.emit("scalar", lambda e, ec=ec, c0=c0, w=w, rr=rr, spill=spill: e.dma_start(
                            out=dr[spill].ap()[128 * ec:128 * (ec + 1), c0:c0 + w],
                            in_=stg1.ap()[:, 512 * rr: 512 * rr + w]),
                            waits=(tht,), stream=f'hats{rr}')
                        hg[("stg", rr)] = last_r

                    for ec in range(HAT_ECS or NFC):
                        bk = bank[0] if ec % 2 == 0 else bank[1]
                        for fc in range(NFC):
                            P.emit("tensor", lambda e, ec=ec, fc=fc, r=r, w=w, bk=bk, wsb=wsb: e.matmul(
                                bk.ap()[:, 0:w],
                                wsb.ap()[:, 1024 * fc + 128 * ec: 1024 * fc + 128 * (ec + 1)],
                                xr1.ap()[:, NFC * 512 * r + 512 * fc: NFC * 512 * r + 512 * fc + w],
                                start=(fc == 0), stop=(fc == NFC - 1)),
                                waits=(ld, t_wload, bfree(bk)) if fc == 0 else (),
                                inc="mm" if fc == NFC - 1 else None)
                        tmm = P.tok("mm")
                        if hpend[0] is not None:
                            hat_norm(*hpend[0])
                        hpend[0] = (ec, bk, tmm)
                    hat_norm(*hpend[0])
                    hg[("x", r)] = last_r
                    pump_mean(1)

            if upto != "negb":
                hat_pass("sp_xh1", SEQ, wk_sb, ks_sb, "sp_khat")
                t_khat = tuple(P.tok(q) for q in ("q_hats0", "q_hats1") if q in P.cnt)
            if upto in ("negb", "khat"):
                P.barrier()
                ph1.close()
                phv.close()
                phWQ.close()
                block = E(nc.Block())
                P.run_all(block)
                return nc
            hat_pass("sp_xh1s", SLABT, wq_sb, qs_sb, "sp_qhat")
            t_qhat = tuple(P.tok(q) for q in ("q_hats0", "q_hats1") if q in P.cnt)

            for cb in range(SEQ // 512):
                c0 = 512 * cb
                r = cb % 2
                ld = P.emit("sync", lambda e, c0=c0, r=r: e.dma_start(
                    out=xr1.ap()[:, NFC * 512 * r: NFC * 512 * (r + 1)].rearrange(
                        "p (f w) -> p f w", f=NFC),
                    in_=dr["sp_xh1"].ap()[:, c0:c0 + 512].rearrange("(f p) w -> p f w", p=128)),
                    waits=(hg.get(("x", r)), P.tok("mm") if cb < 2 else None),
                    stream=f'lnl{r}')
                last_mm = None
                for sub in range(4):
                    jt = 4 * cb + sub
                    for en in range(2):
                        bk = bank[0] if en == 0 else bank[1]
                        for fc in range(NFC):
                            P.emit("tensor", lambda e, sub=sub, en=en, fc=fc, r=r, bk=bk: e.matmul(
                                bk.ap()[:, 0:512],
                                xr1.ap()[:, NFC * 512 * r + 512 * fc + 128 * sub:
                                          NFC * 512 * r + 512 * fc + 128 * (sub + 1)],
                                wv_sb.ap()[:, 1024 * fc + 512 * en: 1024 * fc + 512 * (en + 1)],
                                start=(fc == 0), stop=(fc == NFC - 1)),
                                waits=(ld, t_wload, t_vones, bfree(bk)) if fc == 0 else (),
                                inc="mm" if fc == NFC - 1 else None)
                        tmm = P.tok("mm")
                        tev = P.emit("vector", lambda e, jt=jt, en=en, bk=bk: e.tensor_copy(
                            v_sb.ap()[:, 1040 * jt + 520 * en: 1040 * jt + 520 * (en + 1)].rearrange(
                                "p (h x) -> p h x", h=8)[:, :, 0:64],
                            bk.ap()[:, 0:512].rearrange("p (h x) -> p h x", h=8)),
                            waits=(tmm,), inc="v")
                        bset(bk, tev)
                hg[("x", r)] = P.tok("v")
                pump_mean(1)
            t_vproj = P.tok("v")
            P.barrier()

        if upto == "hat":
            P.barrier()
            phv.close()
            phWQ.close()
            block = E(nc.Block())
            P.run_all(block)
            return nc

        pump_mean(10 ** 9)

        attnout = [phv.enter_context(nc.sbuf_tensor(f"ao_{ec}", [128, SLABT], BF16))
                   for ec in range(NFC)]
        with ExitStack() as ph1b:
            E1b = ph1b.enter_context
            kh_ring = E1b(nc.sbuf_tensor("kh_ring", [128, 2 * SEQ], BF16))
            qh_ring = E1b(nc.sbuf_tensor("qh_ring", [128, 2 * SLABT], BF16))
            e_ring = E1b(nc.sbuf_tensor("e_ring", [128, 3 * 1024], BF16))
            rec_sb = E1b(nc.sbuf_tensor("rec_sb", [128, 2 * 512], F32))
            den_sb = E1b(nc.sbuf_tensor("den_sb", [128, 2 * 512], F32))
            h1tmp = E1b(nc.sbuf_tensor("h1tmp", [128, 512], BF16))
            kq_free = {}
            for ec in range(NFC):
                r = ec % 2
                tkh = P.emit("sync", lambda e, ec=ec, r=r: e.dma_start(
                    out=kh_ring.ap()[:, SEQ * r: SEQ * (r + 1)],
                    in_=dr["sp_khat"].ap()[128 * ec:128 * (ec + 1), :]),
                    waits=(kq_free.get(r), t_khat), stream=f'khl{r}')
                tqh = P.emit("sync", lambda e, ec=ec, r=r: e.dma_start(
                    out=qh_ring.ap()[:, SLABT * r: SLABT * (r + 1)],
                    in_=dr["sp_qhat"].ap()[128 * ec:128 * (ec + 1), :]),
                    waits=(kq_free.get(r), t_qhat), stream=f'qhl{r}')
                av_tok = {}
                for ic, (i0, iw) in enumerate(I_CHUNKS):
                    bR0, bR1 = bank[4], bank[5]
                    tevprev = (bfree(bR0), bfree(bR1))
                    pend = None
                    for jt in range(NT):
                        sp = pA if jt % 2 == 0 else pB
                        spk = ("S", jt % 2)
                        wA = (tkh, tqh, t_vproj, tnb2, H["bank_free"].get(spk))
                        if jt >= 2:
                            wA = (H["bank_free"].get(spk),)
                        P.emit("tensor", lambda e, r=r, jt=jt, i0=i0, iw=iw, sp=sp: e.matmul(
                            sp.ap()[:, 0:iw],
                            kh_ring.ap()[0:64, SEQ * r + 128 * jt: SEQ * r + 128 * (jt + 1)],
                            qh_ring.ap()[0:64, SLABT * r + i0: SLABT * r + i0 + iw],
                            start=True, stop=True, tile_position=(0, 0)),
                            waits=wA, inc=None)
                        tqk = P.emit("tensor", lambda e, r=r, jt=jt, i0=i0, iw=iw, sp=sp: e.matmul(
                            sp.ap()[:, 512:512 + iw],
                            kh_ring.ap()[64:128, SEQ * r + 128 * jt: SEQ * r + 128 * (jt + 1)],
                            qh_ring.ap()[64:128, SLABT * r + i0: SLABT * r + i0 + iw],
                            start=True, stop=True, tile_position=(64, 0)),
                            inc="mm")
                        er = jt % 3
                        if iw == 512:
                            tex = P.emit("scalar", lambda e, er=er, sp=sp: e.activation(
                                e_ring.ap()[:, 1024 * er: 1024 * (er + 1)], sp.ap()[:, :],
                                AF.Exp, bias=negB.ap()[:, 0:1], scale=0.125),
                                waits=(tqk, av_tok.get(er)), inc="sc")
                        else:
                            P.emit("scalar", lambda e, er=er, iw=iw, sp=sp: e.activation(
                                e_ring.ap()[:, 1024 * er: 1024 * er + iw], sp.ap()[:, 0:iw],
                                AF.Exp, bias=negB.ap()[:, 0:1], scale=0.125),
                                waits=(tqk, av_tok.get(er)), inc="sc")
                            tex = P.emit("scalar", lambda e, er=er, iw=iw, sp=sp: e.activation(
                                e_ring.ap()[:, 1024 * er + 512: 1024 * er + 512 + iw],
                                sp.ap()[:, 512:512 + iw],
                                AF.Exp, bias=negB.ap()[:, 0:1], scale=0.125), inc="sc")
                        H["bank_free"][spk] = tex

                        def emit_av(jt, er, tex, iw=iw, ec=ec, bR0=bR0, bR1=bR1,
                                    tevprev=tevprev):
                            P.emit("tensor", lambda e, jt=jt, er=er: e.matmul(
                                bR0.ap()[0:65, 0:iw],
                                v_sb.ap()[:, 1040 * jt + 65 * (2 * ec): 1040 * jt + 65 * (2 * ec) + 65],
                                e_ring.ap()[:, 1024 * er: 1024 * er + iw],
                                start=(jt == 0), stop=(jt == NT - 1)),
                                waits=(tex,) + (tevprev if jt == 0 else ()), inc=None)
                            tav = P.emit("tensor", lambda e, jt=jt, er=er: e.matmul(
                                bR1.ap()[0:65, 0:iw],
                                v_sb.ap()[:, 1040 * jt + 65 * (2 * ec + 1): 1040 * jt + 65 * (2 * ec + 1) + 65],
                                e_ring.ap()[:, 1024 * er + 512: 1024 * er + 512 + iw],
                                start=(jt == 0), stop=(jt == NT - 1)),
                                inc="mm")
                            av_tok[er] = tav
                            return tav

                        if pend is not None:
                            tav = emit_av(*pend)
                        pend = (jt, er, tex)
                    tav = emit_av(*pend)
                    tr0 = P.emit("vector", lambda e, iw=iw, bR0=bR0: e.tensor_copy(
                        rec_sb.ap()[64:65, 0:iw], bR0.ap()[64:65, 0:iw]), waits=(tav,), inc="v")
                    tr1 = P.emit("vector", lambda e, iw=iw, bR1=bR1: e.tensor_copy(
                        rec_sb.ap()[64:65, 512:512 + iw], bR1.ap()[64:65, 0:iw]), inc="v")
                    tb0 = P.emit("tensor", lambda e, iw=iw: e.matmul(
                        bank[6].ap()[0:64, 0:iw], ones64.ap()[64:65, :],
                        rec_sb.ap()[64:65, 0:iw], start=True, stop=True),
                        waits=(tr0, tr1, bfree(bank[6]), bfree(bank[7])), inc="mm")
                    tb1 = P.emit("tensor", lambda e, iw=iw: e.matmul(
                        bank[7].ap()[0:64, 0:iw], ones64.ap()[64:65, :],
                        rec_sb.ap()[64:65, 512:512 + iw], start=True, stop=True),
                        inc="mm")
                    td0 = P.emit("vector", lambda e, iw=iw: e.reciprocal(
                        den_sb.ap()[0:64, 0:iw], bank[6].ap()[0:64, 0:iw]),
                        waits=(tb0, tb1), inc="v")
                    td1 = P.emit("vector", lambda e, iw=iw: e.reciprocal(
                        den_sb.ap()[0:64, 512:512 + iw], bank[7].ap()[0:64, 0:iw]),
                        inc="v")
                    te0 = P.emit("vector", lambda e, ec=ec, i0=i0, iw=iw, bR0=bR0: e.tensor_tensor(
                        attnout[ec].ap()[0:64, i0:i0 + iw], bR0.ap()[0:64, 0:iw],
                        den_sb.ap()[0:64, 0:iw], ALU.mult), waits=(td0,), inc="v")
                    te1 = P.emit("vector", lambda e, iw=iw, bR1=bR1: e.tensor_tensor(
                        h1tmp.ap()[0:64, 0:iw], bR1.ap()[0:64, 0:iw],
                        den_sb.ap()[0:64, 512:512 + iw], ALU.mult),
                        waits=(td1, kq_free.get(("h1",))), inc="v")
                    tdm = P.emit("scalar", lambda e, ec=ec, i0=i0, iw=iw: e.dma_start(
                        out=attnout[ec].ap()[64:128, i0:i0 + iw], in_=h1tmp.ap()[0:64, 0:iw]),
                        waits=(te1,), stream='h1s')
                    kq_free[("h1",)] = tdm
                    bset(bank[6], td0); bset(bank[7], td1)
                    bset(bR0, te0); bset(bR1, te1)
                kq_free[r] = P.tok("mm")
                pump_tern(3)
            t_attn_done = (P.tok("v"), P.tok("q_h1s"))
            P.barrier()

        if upto == "attn":
            P.barrier()
            phv.close()
            phWQ.close()
            block = E(nc.Block())
            P.run_all(block)
            return nc

        with ExitStack() as ph1c:
            E1c = ph1c.enter_context
            wo_sb = E1c(nc.sbuf_tensor("wo_sb", [128, NFC * 1024], BF16))
            x0ring = E1c(nc.sbuf_tensor("x0ring", [128, 4 * 512], F32))
            x1stg = E1c(nc.sbuf_tensor("x1stg", [128, 4 * 512], F32))
            P.emit("gpsimd", lambda e: e.dma_start(
                out=wo_sb.ap().rearrange("p (f n) -> p f n", f=NFC),
                in_=dr["WoT0"].ap().rearrange("(f p) n -> p f n", p=128)), stream='gw0')
            t_wo = P.tok("q_gw0")
            ring_free = {}
            for ic, (i0, iw) in enumerate(I_CHUNKS):
                for dc in range(NFC):
                    r = (ic * NFC + dc) % 2
                    tx0 = P.emit("sync", lambda e, dc=dc, i0=i0, iw=iw, r=r: e.dma_start(
                        out=x0ring.ap()[:, 1024 * r: 1024 * r + iw],
                        in_=dr["x0T_slab"].ap()[128 * dc:128 * (dc + 1), i0:i0 + iw]),
                        waits=(ring_free.get(("x0", r)),), stream=f'x0l{r}')
                    bk = bank[0] if dc % 2 == 0 else bank[1]
                    for ec in range(NFC):
                        P.emit("tensor", lambda e, dc=dc, ec=ec, i0=i0, iw=iw, bk=bk: e.matmul(
                            bk.ap()[:, 0:iw],
                            wo_sb.ap()[:, 1024 * ec + 128 * dc: 1024 * ec + 128 * (dc + 1)],
                            attnout[ec].ap()[:, i0:i0 + iw],
                            start=(ec == 0), stop=(ec == NFC - 1)),
                            waits=((t_wo, bfree(bk)) + t_attn_done) if ec == 0 else (),
                            inc="mm" if ec == NFC - 1 else None)
                    tmm = P.tok("mm")
                    tev = P.emit("vector", lambda e, dc=dc, r=r, iw=iw, bk=bk: e.scalar_tensor_tensor(
                        x1stg.ap()[:, 1024 * r: 1024 * r + iw],
                        bk.ap()[:, 0:iw], bo_sb.ap()[:, dc:dc + 1],
                        x0ring.ap()[:, 1024 * r: 1024 * r + iw],
                        ALU.add, ALU.add),
                        waits=(tmm, tx0, t_small, ring_free.get(("x1", r))), inc="v")
                    bset(bk, tev)
                    ring_free[("x0", r)] = tev
                    ring_free[("x1", r)] = P.emit(
                        "scalar", lambda e, dc=dc, i0=i0, iw=iw, r=r: e.dma_start(
                            out=dr["sp_x1"].ap()[128 * dc:128 * (dc + 1), i0:i0 + iw],
                            in_=x1stg.ap()[:, 1024 * r: 1024 * r + iw]),
                        waits=(tev,), stream=f'x1s{r}')
                pump_tern(3)
            t_x1_sp = (P.tok("q_x1s0"), P.tok("q_x1s1"))
            if "x1" in dbg:
                dout("dbg_x1", (D, SLABT), F32)
                P.wait("sync", t_x1_sp)
                P.emit("sync", lambda e: e.dma_start(
                    out=dr["dbg_x1"].ap()[:, :], in_=dr["sp_x1"].ap()[:, :]), stream='dbg')
            P.barrier()

        wq_finish()
        P.barrier()
        phv.close()
        phWQ.close()

        if upto == "x1":
            P.wait("sync", *P.all_toks())
            block = E(nc.Block())
            P.run_all(block)
            return nc

        t_w1t = wq_state[("done", 0)]
        t_w2t = wq_state[("done", 1)]
        dr["sp_y"] = nc.dram_tensor("sp_y", [HID, OWN], BF16)
        dr["sp_yq"] = nc.dram_tensor("sp_yq", [HID, OWN], BF16)
        cw_sb = E(nc.sbuf_tensor("cw_sb", [128, NHC * 3], F32))
        cb_sb = E(nc.sbuf_tensor("cb_sb", [128, NHC], F32))
        amel = E(nc.sbuf_tensor("amel", [128, 17], F32))
        amtm = E(nc.sbuf_tensor("amtm", [128, 17], F32))
        P.emit("sync", lambda e: e.dma_start(
            out=cw_sb.ap().rearrange("p (h t) -> p h t", h=NHC),
            in_=dr["convw"].ap().rearrange("(h p) t -> p h t", p=128)), stream='sm0')
        P.emit("sync", lambda e: e.dma_start(
            out=cb_sb.ap().rearrange("p h -> p h ()"),
            in_=dr["convb"].ap().rearrange("(h p) o -> p h o", p=128),
            allow_slow_non_contiguous=True), stream='sm1')
        tam = P.emit("sync", lambda e: e.dma_start(out=amtm.ap()[:, :], in_=dr["amask_tm"].ap()[:, :]),
                     stream='sm2')
        P.emit("vector", lambda e: e.tensor_single_scalar(
            amel.ap()[:, :], amtm.ap()[:, :], 1.0 / 127.0, ALU.mult), waits=(tam,), inc="v")
        t_cw = (P.tok("q_sm0"), P.tok("q_sm1"), P.tok("q_sm2"))
        t_amel = P.tok("v")

        with ExitStack() as ph2:
            E2o = ph2.enter_context
            xq2 = [E2o(nc.sbuf_tensor(f"xq2_{fc}", [128, SLABT], BF16)) for fc in range(NFC)]
            alpha = E2o(nc.sbuf_tensor("alpha_sb", [128, SLABT], F32))
            my_sb = E2o(nc.sbuf_tensor("my_sb", [128, OWN], F32))

            with ExitStack() as ph2a:
                E2 = ph2a.enter_context
                ring2 = E2(nc.sbuf_tensor("ln1_ring", [128, 2 * NFC * 512], F32))
                rows2 = E2(nc.sbuf_tensor("ln1_rows", [1, 2 * 4 * 512], F32))
                sq2 = E2(nc.sbuf_tensor("ln1_sq", [128, NFC * 512], BF16))
                xh2r = E2(nc.sbuf_tensor("xh2r", [128, NFC * 512], F32))
                mfold = E2(nc.sbuf_tensor("mfold", [128, 512], F32))
                fscr2 = E2(nc.sbuf_tensor("fscr2", [128, 512], F32))
                amax2 = E2(nc.sbuf_tensor("amax2", [128, 4], F32))
                srep = E2(nc.sbuf_tensor("srep2", [128, 129], F32))
                s1b = E2(nc.sbuf_tensor("s1b", [128, 512], F32))
                P.emit("gpsimd", lambda e: e.memset(amax2.ap()[:, :], 1.0), inc="g")
                t_am2i = P.tok("g")
                lng = {"mf": None, "sr": None}

                def ln1_src(cb, c0, w, r, free_tok):
                    ld = P.emit("sync", lambda e, c0=c0, w=w, r=r: e.dma_start(
                        out=mkap(ring2.ap().tensor, NFC * 512 * r,
                                 [[2 * NFC * 512, 128], [512, NFC], [1, w]]),
                        in_=dr["sp_x1"].ap()[:, c0:c0 + w].rearrange("(f p) w -> p f w", p=128)),
                        waits=(free_tok, t_x1_sp), stream=f'lnl{r}')
                    def tile(fc, r=r, w=w):
                        return ring2.ap()[:, NFC * 512 * r + 512 * fc: NFC * 512 * r + 512 * fc + w]
                    return ld, tile

                def ln1_out(cb, c0, w, r, bA, bB, toks, tile, ld):
                    P.wait("vector", *toks)
                    I32 = mybir.dt.int32
                    tts = []
                    for fc in range(NFC):
                        P.emit("vector", lambda e, fc=fc, w=w, bA=bA, tile=tile: e.tensor_tensor(
                            xh2r.ap()[:, 512 * fc:512 * fc + w], tile(fc), bA.ap()[:, 0:w],
                            ALU.mult), inc="v")
                        tt = P.emit("vector", lambda e, fc=fc, w=w, bB=bB: e.tensor_tensor(
                            xh2r.ap()[:, 512 * fc:512 * fc + w], xh2r.ap()[:, 512 * fc:512 * fc + w],
                            bB.ap()[:, 0:w], ALU.add), inc="v")
                        tts.append(tt)
                    if LN1_STOP and LN1_STOP < 2:
                        return tts[-1]
                    tf = P.emit("vector", lambda e, w=w: e.tensor_single_scalar(
                        mfold.ap()[:, 0:w].bitcast(I32), xh2r.ap()[:, 0:w].bitcast(I32),
                        0x7FFFFFFF, ALU.bitwise_and), waits=(tts[0], lng["mf"]), inc="v")
                    for fc in range(1, NFC):
                        P.emit("vector", lambda e, fc=fc, w=w: e.tensor_single_scalar(
                            fscr2.ap()[:, 0:w].bitcast(I32),
                            xh2r.ap()[:, 512 * fc:512 * fc + w].bitcast(I32),
                            0x7FFFFFFF, ALU.bitwise_and), waits=(tts[fc],), inc="v")
                        tf = P.emit("vector", lambda e, w=w: e.tensor_tensor(
                            mfold.ap()[:, 0:w], mfold.ap()[:, 0:w], fscr2.ap()[:, 0:w], ALU.max),
                            inc="v")
                    if LN1_STOP and LN1_STOP < 3:
                        return tf
                    ng = (w + 127) // 128
                    tam_ = []
                    for g in range(ng):
                        gw = min(128, w - 128 * g)
                        ttr = P.emit("tensor", lambda e, g=g, gw=gw: e.transpose(
                            bank[2].ap()[0:gw, 0:128],
                            mfold.ap()[:, 128 * g:128 * g + gw], ident.ap()[:, :]),
                            waits=(tf, t_am2i, t_const, bfree(bank[2])),
                            inc="mm")
                        tam_.append(P.emit("vector", lambda e, g=g, gw=gw: e.reduce_max(
                            amax2.ap()[0:gw, g:g + 1], bank[2].ap()[0:gw, 0:128],
                            axis=AXX), waits=(ttr,), inc="v"))
                        bset(bank[2], tam_[-1])
                        lng["mf"] = ttr
                    tcl = P.emit("vector", lambda e, ng=ng: e.tensor_single_scalar(
                        amax2.ap()[:, 0:ng], amax2.ap()[:, 0:ng], 1e-5, ALU.max),
                        waits=tuple(tam_), inc="v")
                    if LN1_STOP and LN1_STOP < 4:
                        return tcl
                    for g in range(ng):
                        gw = min(128, w - 128 * g)
                        gabs = c0 // 128 + g
                        trc = P.emit("vector", lambda e, g=g: e.reciprocal(
                            srep.ap()[:, 128:129], amax2.ap()[:, g:g + 1]), waits=(tcl,), inc="v")
                        trp = P.emit("vector", lambda e: e.tensor_scalar(
                            srep.ap()[:, 0:128], ones128.ap()[:, :],
                            srep.ap()[:, 128:129], 127.0, ALU.mult, ALU.mult),
                            waits=(trc, lng["sr"]), inc="v")
                        ttr = P.emit("tensor", lambda e: e.transpose(
                            bank[3].ap()[:, 0:128], srep.ap()[:, 0:128], ident.ap()[:, :]),
                            waits=(trp, bfree(bank[3])), inc="mm")
                        tcp = P.emit("vector", lambda e, g=g, gw=gw: e.tensor_copy(
                            s1b.ap()[:, 128 * g:128 * g + gw], bank[3].ap()[:, 0:gw]),
                            waits=(ttr,), inc="v")
                        bset(bank[3], tcp)
                        if LN1_STOP and LN1_STOP < 5:
                            last = tcp
                            continue
                        ta1 = P.emit("vector", lambda e, g=g, gabs=gabs: e.tensor_scalar(
                            srep.ap()[:, 128:129], amax2.ap()[:, g:g + 1],
                            thr_cols.ap()[:, 1:2], amel.ap()[:, gabs:gabs + 1],
                            ALU.mult, ALU.mult),
                            waits=(tcl, t_amel), inc="v")
                        tra = P.emit("vector", lambda e: e.tensor_scalar(
                            srep.ap()[:, 0:128], ones128.ap()[:, :],
                            srep.ap()[:, 128:129], None, ALU.mult),
                            waits=(ta1, ttr), inc="v")
                        ttr2 = P.emit("tensor", lambda e: e.transpose(
                            bank[3].ap()[:, 0:128], srep.ap()[:, 0:128], ident.ap()[:, :]),
                            waits=(tra, bfree(bank[3])), inc="mm")
                        lng["sr"] = ttr2
                        tca = P.emit("vector", lambda e, gabs=gabs, gw=gw: e.tensor_copy(
                            alpha.ap()[:, 128 * gabs:128 * gabs + gw], bank[3].ap()[:, 0:gw]),
                            waits=(ttr2,), inc="v")
                        bset(bank[3], tca)
                        last = tca
                        if LN1_STOP and LN1_STOP < 6:
                            continue
                        for fc in range(NFC):
                            P.emit("vector", lambda e, fc=fc, g=g, gw=gw: e.tensor_tensor(
                                fscr2.ap()[:, 0:gw],
                                xh2r.ap()[:, 512 * fc + 128 * g: 512 * fc + 128 * g + gw],
                                s1b.ap()[:, 128 * g:128 * g + gw], ALU.mult),
                                waits=(tcp,), inc="v")
                            last = P.emit("vector", lambda e, fc=fc, c0=c0, g=g, gw=gw: e.tensor_scalar(
                                xq2[fc].ap()[:, c0 + 128 * g: c0 + 128 * g + gw],
                                fscr2.ap()[:, 0:gw],
                                MRND, MRND, ALU.add, ALU.subtract), inc="v")
                    return last

                emit_ln("ln1", ln1_src, SLABT, ln1_out, rows2, sq2)
                t_xq2 = P.tok("v")
                P.barrier()
                if "ln1dbg" in dbg:
                    dout("dbg_amax2", (128, 4), F32)
                    dout("dbg_mfold", (128, 512), F32)
                    dout("dbg_xh2r", (128, NFC * 512), F32)
                    P.emit("sync", lambda e: e.dma_start(
                        out=dr["dbg_amax2"].ap()[:, :], in_=amax2.ap()[:, :]), stream='dbg')
                    P.emit("sync", lambda e: e.dma_start(
                        out=dr["dbg_mfold"].ap()[:, :], in_=mfold.ap()[:, :]), stream='dbg')
                    P.emit("sync", lambda e: e.dma_start(
                        out=dr["dbg_xh2r"].ap()[:, :], in_=xh2r.ap()[:, :]), stream='dbg')
                    P.barrier()
                if "ffn1" in dbg:
                    dout("dbg_xq2", (D, SLABT), BF16)
                    dout("dbg_alpha", (128, SLABT), F32)
                    for fc in range(NFC):
                        P.emit("sync", lambda e, fc=fc: e.dma_start(
                            out=dr["dbg_xq2"].ap()[128 * fc:128 * (fc + 1), :],
                            in_=xq2[fc].ap()[:, :]), stream='dbg')
                    P.emit("sync", lambda e: e.dma_start(
                        out=dr["dbg_alpha"].ap()[:, :], in_=alpha.ap()[:, :]), stream='dbg')
                    P.barrier()

            if upto == "ln1":
                P.barrier()
                ph2.close()
                block = E(nc.Block())
                P.run_all(block)
                return nc

            with ExitStack() as ph2b:
                E2 = ph2b.enter_context
                w1ring = E2(nc.sbuf_tensor("w1ring", [128, 2 * NFC * 128], BF16))
                hrow = E2(nc.sbuf_tensor("hrow", [128, 2 * SLABT], F32))
                cscr = E2(nc.sbuf_tensor("cscr", [128, OWN], F32))
                ystg = E2(nc.sbuf_tensor("ystg", [128, 2 * OWN], BF16))
                yabs = E2(nc.sbuf_tensor("yabs", [128, OWN], BF16))
                w1_free = {}
                hrow_free = {}
                yst_free = {}
                y_tok = {}
                tmy = None
                for hc in range(NHC):
                    r = hc % 2
                    tw = P.emit("sync", lambda e, hc=hc, r=r: e.dma_start(
                        out=w1ring.ap()[:, NFC * 128 * r: NFC * 128 * (r + 1)].rearrange(
                            "p (f w) -> p f w", f=NFC),
                        in_=dr["sp_w1t"].ap()[:, 128 * hc:128 * (hc + 1)].rearrange(
                            "(f p) w -> p f w", p=128)),
                        waits=(w1_free.get(r), t_w1t), stream=f'w1l{r}')
                    rh = hc % 2
                    tev_all = []
                    for tb, (i0, iw) in enumerate(I_CHUNKS):
                        bk = bank[tb % 4]
                        for fc in range(NFC):
                            P.emit("tensor", lambda e, fc=fc, r=r, i0=i0, iw=iw, bk=bk: e.matmul(
                                bk.ap()[:, 0:iw],
                                w1ring.ap()[:, NFC * 128 * r + 128 * fc: NFC * 128 * r + 128 * (fc + 1)],
                                xq2[fc].ap()[:, i0:i0 + iw],
                                start=(fc == 0), stop=(fc == NFC - 1)),
                                waits=(tw, t_xq2, bfree(bk)) if fc == 0 else (),
                                inc="mm" if fc == NFC - 1 else None)
                        tmm = P.tok("mm")
                        tev = P.emit("vector", lambda e, rh=rh, i0=i0, iw=iw, bk=bk: e.tensor_tensor(
                            hrow.ap()[:, SLABT * rh + i0: SLABT * rh + i0 + iw],
                            bk.ap()[:, 0:iw], alpha.ap()[:, i0:i0 + iw], ALU.mult),
                            waits=(tmm, hrow_free.get(rh)), inc="v")
                        bset(bk, tev)
                        tev_all.append(tev)
                    w1_free[r] = tev_all[-1]
                    h3d = hrow.ap()[:, SLABT * rh: SLABT * (rh + 1)].rearrange(
                        "p (s u) -> p s u", u=SLABW)
                    cv = cscr.ap()[:, 0:OWN].rearrange("p (s u) -> p s u", u=128)
                    tc1 = P.emit("vector", lambda e, hc=hc, h3d=h3d, cv=cv: e.tensor_scalar(
                        cv, h3d[:, :, 0:128], cw_sb.ap()[:, 3 * hc: 3 * hc + 1],
                        cb_sb.ap()[:, hc:hc + 1], ALU.mult, ALU.add),
                        waits=tuple(tev_all) + (t_cw, hrow_free.get(("cs",))), inc="v")
                    tc2 = P.emit("vector", lambda e, hc=hc, h3d=h3d, cv=cv: e.scalar_tensor_tensor(
                        cv, h3d[:, :, 1:129], cw_sb.ap()[:, 3 * hc + 1: 3 * hc + 2],
                        cv, ALU.mult, ALU.add), inc="v")
                    tc3 = P.emit("vector", lambda e, hc=hc, h3d=h3d, cv=cv: e.scalar_tensor_tensor(
                        cv, h3d[:, :, 2:130], cw_sb.ap()[:, 3 * hc + 2: 3 * hc + 3],
                        cv, ALU.mult, ALU.add), inc="v")
                    hrow_free[rh] = tc3
                    tsi = P.emit("scalar", lambda e, hc=hc, r=r: e.activation(
                        ystg.ap()[:, OWN * r: OWN * (r + 1)], cscr.ap()[:, 0:OWN],
                        AF.Silu, bias=0.0, scale=1.0),
                        waits=(tc3, yst_free.get(("d", r)), yst_free.get(("f", r))), inc="sc")
                    hrow_free[("cs",)] = tsi
                    yst_free[("d", r)] = P.emit("scalar", lambda e, hc=hc, r=r: e.dma_start(
                        out=dr["sp_y"].ap()[128 * hc:128 * (hc + 1), :],
                        in_=ystg.ap()[:, OWN * r: OWN * (r + 1)]),
                        waits=(tsi,), stream=f'ys{r}')
                    y_tok[hc] = yst_free[("d", r)]
                    tya = P.emit("scalar", lambda e, r=r: e.activation(
                        yabs.ap()[:, :], ystg.ap()[:, OWN * r: OWN * (r + 1)],
                        AF.Abs, bias=0.0, scale=1.0), waits=(tsi, tmy), inc="sc")
                    if hc == 0:
                        tmy = P.emit("vector", lambda e: e.tensor_copy(my_sb.ap()[:, :], yabs.ap()[:, :]),
                                     waits=(tya,), inc="v")
                    else:
                        tmy = P.emit("vector", lambda e: e.tensor_tensor(
                            my_sb.ap()[:, :], my_sb.ap()[:, :], yabs.ap()[:, :], ALU.max),
                            waits=(tya, tmy), inc="v")
                    yst_free[("f", r)] = tmy
                t_y_sp = (P.tok("q_ys0"), P.tok("q_ys1"))
                t_my = tmy
                P.barrier()

            if upto == "w1y":
                P.barrier()
                ph2.close()
                block = E(nc.Block())
                P.run_all(block)
                return nc

            with ExitStack() as ph2c:
                E2 = ph2c.enter_context
                amaxg = E2(nc.sbuf_tensor("amaxg", [128, 16], F32))
                s2b = E2(nc.sbuf_tensor("s2b", [128, OWN], F32))
                gmb = E2(nc.sbuf_tensor("gmb", [128, OWN], F32))
                srepc = E2(nc.sbuf_tensor("srep_c", [128, 129], F32))
                yring = E2(nc.sbuf_tensor("yring", [128, 2 * OWN], BF16))
                yqs = E2(nc.sbuf_tensor("yqs", [128, 2 * OWN], BF16))
                qscr = E2(nc.sbuf_tensor("qscr", [128, OWN], F32))
                rowtp = E2(nc.sbuf_tensor("rowtp", [16, 128], F32))
                rowsc = E2(nc.sbuf_tensor("rowsc", [1, 3 * OWN], F32))
                P.emit("gpsimd", lambda e: e.memset(amaxg.ap()[:, :], 1.0), inc="g")
                t_agi = P.tok("g")
                trd = None
                for g in range(16):
                    bkt = bank[2] if g % 2 == 0 else bank[3]
                    ttr = P.emit("tensor", lambda e, g=g, bkt=bkt: e.transpose(
                        bkt.ap()[:, 0:128], my_sb.ap()[:, 128 * g:128 * (g + 1)],
                        ident.ap()[:, :]), waits=(t_my, t_agi, bfree(bkt)), inc="mm")
                    trd = P.emit("vector", lambda e, g=g, bkt=bkt: e.reduce_max(
                        amaxg.ap()[:, g:g + 1], bkt.ap()[:, 0:128], axis=AXX),
                        waits=(ttr,), inc="v")
                    bset(bkt, trd)
                tcl = P.emit("vector", lambda e: e.tensor_single_scalar(
                    amaxg.ap()[:, :], amaxg.ap()[:, :], 1e-5, ALU.max), waits=(trd,), inc="v")
                ttra = P.emit("tensor", lambda e: e.transpose(
                    bank[2].ap()[0:16, 0:128], amaxg.ap()[:, :], ident.ap()[:, :]),
                    waits=(tcl, bfree(bank[2])), inc="mm")
                tcpr = P.emit("vector", lambda e: e.tensor_copy(
                    rowtp.ap()[:, :], bank[2].ap()[0:16, 0:128]), waits=(ttra,), inc="v")
                bset(bank[2], tcpr)
                tdm = P.emit("scalar", lambda e: e.dma_start(
                    out=rowsc.ap()[0:1, 0:OWN], in_=rowtp.ap()[:, :]),
                    waits=(tcpr,), stream='sm3')
                trec = P.emit("vector", lambda e: e.reciprocal(
                    rowsc.ap()[0:1, OWN:2 * OWN], rowsc.ap()[0:1, 0:OWN]),
                    waits=(tdm,), inc="v")
                ts127 = P.emit("vector", lambda e: e.tensor_single_scalar(
                    rowsc.ap()[0:1, OWN:2 * OWN], rowsc.ap()[0:1, OWN:2 * OWN],
                    127.0, ALU.mult), waits=(trec,), inc="v")
                tgam = P.emit("vector", lambda e: e.tensor_scalar(
                    rowsc.ap()[0:1, 2 * OWN:3 * OWN], rowsc.ap()[0:1, 0:OWN],
                    thr_cols.ap()[0:1, 3:4], 1.0 / 127.0, ALU.mult, ALU.mult),
                    waits=(tdm,), inc="v")
                for q in range(4):
                    bkt = bank[2] if q % 2 == 0 else bank[3]
                    tb1 = P.emit("tensor", lambda e, q=q, bkt=bkt: e.matmul(
                        bkt.ap()[:, 0:512], ones_row.ap()[0:1, :],
                        rowsc.ap()[0:1, OWN + 512 * q: OWN + 512 * (q + 1)],
                        start=True, stop=True), waits=(ts127, bfree(bkt)), inc="mm")
                    tc1 = P.emit("vector", lambda e, q=q, bkt=bkt: e.tensor_copy(
                        s2b.ap()[:, 512 * q:512 * (q + 1)], bkt.ap()[:, 0:512]),
                        waits=(tb1,), inc="v")
                    tb2 = P.emit("tensor", lambda e, q=q, bkt=bkt: e.matmul(
                        bkt.ap()[:, 0:512], ones_row.ap()[0:1, :],
                        rowsc.ap()[0:1, 2 * OWN + 512 * q: 2 * OWN + 512 * (q + 1)],
                        start=True, stop=True), waits=(tgam, tc1), inc="mm")
                    tc2 = P.emit("vector", lambda e, q=q, bkt=bkt: e.tensor_copy(
                        gmb.ap()[:, 512 * q:512 * (q + 1)], bkt.ap()[:, 0:512]),
                        waits=(tb2,), inc="v")
                    bset(bkt, tc2)
                t_s2 = P.tok("v")
                yq_free = {}
                yq_tok = {}
                for hc in range(NHC):
                    r = hc % 2
                    tyl = P.emit("sync", lambda e, hc=hc, r=r: e.dma_start(
                        out=yring.ap()[:, OWN * r: OWN * (r + 1)],
                        in_=dr["sp_y"].ap()[128 * hc:128 * (hc + 1), :]),
                        waits=(yq_free.get(("l", r)), y_tok[hc]), stream=f'yl{r}')
                    tq1 = P.emit("vector", lambda e, r=r: e.tensor_tensor(
                        qscr.ap()[:, :], yring.ap()[:, OWN * r: OWN * (r + 1)],
                        s2b.ap()[:, :], ALU.mult), waits=(tyl, t_s2), inc="v")
                    yq_free[("l", r)] = tq1
                    tq2 = P.emit("vector", lambda e, r=r: e.tensor_scalar(
                        yqs.ap()[:, OWN * r: OWN * (r + 1)], qscr.ap()[:, :],
                        MRND, MRND, ALU.add, ALU.subtract),
                        waits=(yq_free.get(("s", r)),), inc="v")
                    yq_free[("s", r)] = P.emit("scalar", lambda e, hc=hc, r=r: e.dma_start(
                        out=dr["sp_yq"].ap()[128 * hc:128 * (hc + 1), :],
                        in_=yqs.ap()[:, OWN * r: OWN * (r + 1)]),
                        waits=(tq2,), stream=f'yqs{r}')
                    yq_tok[hc] = yq_free[("s", r)]
                t_yq_sp = (P.tok("q_yqs0"), P.tok("q_yqs1"))
                if upto == "yq":
                    P.barrier()
                    ph2c.close()
                    ph2.close()
                    block = E(nc.Block())
                    P.run_all(block)
                    return nc

                w2ring = E2(nc.sbuf_tensor("w2ring", [128, 2 * 4 * 128], BF16))
                yqring = E2(nc.sbuf_tensor("yqring", [128, 2 * 512], BF16))
                x1ring = E2(nc.sbuf_tensor("x1iring", [128, 512], F32))
                x2stg = E2(nc.sbuf_tensor("x2stg", [128, 2 * 512], F32))
                w2_free = {}
                mm_slot = {}
                for quad in range(2):
                    for tb, (i0, iw) in enumerate(T_CHUNKS):
                        bks = [bank[2 * i + (tb % 2)] for i in range(4)]
                        for hc in range(NHC):
                            r = hc % 2
                            tw2 = P.emit("sync", lambda e, quad=quad, hc=hc, r=r: e.dma_start(
                                out=w2ring.ap()[:, 512 * r: 512 * (r + 1)].rearrange(
                                    "p (d w) -> p d w", d=4),
                                in_=dr["sp_w2t"].ap()[128 * hc:128 * (hc + 1),
                                                      512 * quad:512 * (quad + 1)].rearrange(
                                    "p (d w) -> p d w", d=4)),
                                waits=(mm_slot.get(r), t_w2t), stream=f'w2l{r}')
                            tyq = P.emit("scalar", lambda e, hc=hc, i0=i0, r=r: e.dma_start(
                                out=yqring.ap()[:, 512 * r: 512 * (r + 1)],
                                in_=dr["sp_yq"].ap()[128 * hc:128 * (hc + 1), i0:i0 + 512]),
                                waits=(mm_slot.get(r), yq_tok[hc]), stream=f'yql{r}')
                            for dx in range(4):
                                P.emit("tensor", lambda e, dx=dx, r=r, hc=hc, bk=bks[dx]: e.matmul(
                                    bk.ap()[:, 0:512],
                                    w2ring.ap()[:, 512 * r + 128 * dx: 512 * r + 128 * (dx + 1)],
                                    yqring.ap()[:, 512 * r: 512 * (r + 1)],
                                    start=(hc == 0), stop=(hc == NHC - 1)),
                                    waits=((tw2, tyq) +
                                           tuple(bfree(bks[i]) for i in range(4))) if dx == 0 else (),
                                    inc="mm" if dx == 3 else None)
                            mm_slot[r] = P.tok("mm")
                        tmm = P.tok("mm")
                        for dx in range(4):
                            dc = 4 * quad + dx
                            bk = bks[dx]
                            tx1 = P.emit("scalar", lambda e, dc=dc, i0=i0: e.dma_start(
                                out=x1ring.ap()[:, 0:512],
                                in_=mkap(dr["sp_x1"].ap().tensor,
                                         128 * dc * SLABT + 1 + (i0 // 128) * SLABW,
                                         [[SLABT, 128], [SLABW, 4], [1, 128]])),
                                waits=(w2_free.get(("x1",)),), stream='x1i')
                            tg1 = P.emit("vector", lambda e, dx=dx, i0=i0, bk=bk: e.tensor_tensor(
                                x2stg.ap()[:, 512 * (dx % 2): 512 * (dx % 2) + 512],
                                bk.ap()[:, 0:512], gmb.ap()[:, i0:i0 + 512], ALU.mult),
                                waits=(tmm, w2_free.get(("st", dx % 2))), inc="v")
                            tg2 = P.emit("vector", lambda e, dx=dx: e.scalar_tensor_tensor(
                                x2stg.ap()[:, 512 * (dx % 2): 512 * (dx % 2) + 512],
                                x1ring.ap()[:, 0:512], 1.0,
                                x2stg.ap()[:, 512 * (dx % 2): 512 * (dx % 2) + 512],
                                ALU.mult, ALU.add), waits=(tx1,), inc="v")
                            bset(bk, tg2)
                            w2_free[("x1",)] = tg2
                            w2_free[("st", dx % 2)] = P.emit(
                                "scalar", lambda e, dc=dc, i0=i0, dx=dx: e.dma_start(
                                    out=dr["sp_x2"].ap()[128 * dc:128 * (dc + 1), i0:i0 + 512],
                                    in_=x2stg.ap()[:, 512 * (dx % 2): 512 * (dx % 2) + 512]),
                                waits=(tg2,), stream=f'x2s{dx % 2}')
                t_x2_sp = (P.tok("q_x2s0"), P.tok("q_x2s1"))
                if "x2" in dbg:
                    dout("dbg_x2", (D, OWN), F32)
                    P.wait("sync", t_x2_sp)
                    P.emit("sync", lambda e: e.dma_start(
                        out=dr["dbg_x2"].ap()[:, :], in_=dr["sp_x2"].ap()[:, :]), stream='dbg')
                P.barrier()

        if upto == "x2":
            P.wait("sync", *P.all_toks())
            block = E(nc.Block())
            P.run_all(block)
            return nc

        ph4o = ExitStack()
        E4o = ph4o.enter_context
        bq_sb = E4o(nc.sbuf_tensor("bq_sb", [128, 8], F32))
        bk_sb = E4o(nc.sbuf_tensor("bk_sb", [128, 8], F32))
        bv_sb = E4o(nc.sbuf_tensor("bv_sb", [128, 8], F32))
        for bi_, (nm, sbb) in enumerate((("bq1", bq_sb), ("bk1", bk_sb), ("bv1", bv_sb))):
            P.emit("sync", lambda e, nm=nm, sbb=sbb: e.dma_start(
                out=sbb.ap().rearrange("p a -> p a ()"),
                in_=dr[nm].ap().rearrange("(a p) o -> p a o", p=128),
                allow_slow_non_contiguous=True), stream=f'sm{bi_}')
        t_b1 = (P.tok("q_sm0"), P.tok("q_sm1"), P.tok("q_sm2"))

        e_off = {}
        _off = 0
        for jt in range(NT):
            e_off[jt] = _off
            _off += (4 - jt // 8) * 512
        E_COLS = _off

        negm_sb = E4o(nc.sbuf_tensor("negm_sb", [128, NT], F32))
        sown_sb = E4o(nc.sbuf_tensor("sown_sb", [128, NT], F32))
        fst_sb = E4o(nc.sbuf_tensor("fst_sb", [128, NT], F32))
        fbf_sb = E4o(nc.sbuf_tensor("fbf_sb", [128, NT], BF16))

        ph45 = ExitStack()
        qT = [ph45.enter_context(nc.sbuf_tensor(f"qT_{fc}", [128, OWN], BF16))
              for fc in range(NFC)]

        ph34 = ExitStack()
        h3own = [ph34.enter_context(nc.sbuf_tensor(f"h3o_{fc}", [128, OWN], BF16))
                 for fc in range(NFC)]
        with ExitStack() as ph3:
            E3 = ph3.enter_context
            ring3 = E3(nc.sbuf_tensor("ln2_ring", [128, 2 * NFC * 512], F32))
            rows3 = E3(nc.sbuf_tensor("ln2_rows", [1, 2 * 4 * 512], F32))
            sq3 = E3(nc.sbuf_tensor("ln2_sq", [128, NFC * 512], BF16))
            h3scr = E3(nc.sbuf_tensor("h3scr", [128, 512], F32))
            zstg = E3(nc.sbuf_tensor("zstg", [128, 2 * 512], F32))
            zs_g = {}

            def ln2_src(cb, c0, w, r, free_tok):
                ld = P.emit("sync", lambda e, c0=c0, w=w, r=r: e.dma_start(
                    out=mkap(ring3.ap().tensor, NFC * 512 * r,
                             [[2 * NFC * 512, 128], [512, NFC], [1, w]]),
                    in_=dr["sp_x2"].ap()[:, c0:c0 + w].rearrange("(f p) w -> p f w", p=128)),
                    waits=(free_tok, t_x2_sp), stream=f'lnl{r}')
                def tile(fc, r=r, w=w):
                    return ring3.ap()[:, NFC * 512 * r + 512 * fc: NFC * 512 * r + 512 * fc + w]
                return ld, tile

            def ln2_out(cb, c0, w, r, bA, bB, toks, tile, ld):
                P.wait("vector", *toks)
                for fc in range(NFC):
                    sl = fc % 2
                    P.emit("vector", lambda e, fc=fc, w=w, bA=bA, tile=tile: e.tensor_tensor(
                        h3scr.ap()[:, 0:w], tile(fc), bA.ap()[:, 0:w], ALU.mult), inc="v")
                    P.emit("vector", lambda e, fc=fc, w=w, bB=bB: e.tensor_tensor(
                        h3scr.ap()[:, 0:w], h3scr.ap()[:, 0:w], bB.ap()[:, 0:w], ALU.add),
                        inc="v")
                    P.emit("vector", lambda e, fc=fc, c0=c0, w=w: e.tensor_copy(
                        h3own[fc].ap()[:, c0:c0 + w], h3scr.ap()[:, 0:w]), inc="v")
                    tz = P.emit("vector", lambda e, fc=fc, sl=sl, w=w, tile=tile: e.tensor_tensor(
                        zstg.ap()[:, 512 * sl:512 * sl + w], tile(fc), h3scr.ap()[:, 0:w],
                        ALU.add), waits=(zs_g.get(sl),), inc="v")
                    zs_g[sl] = P.emit("scalar", lambda e, fc=fc, c0=c0, w=w, sl=sl: e.dma_start(
                        out=dr["sp_z"].ap()[128 * fc:128 * (fc + 1), c0:c0 + w],
                        in_=zstg.ap()[:, 512 * sl:512 * sl + w]),
                        waits=(tz,), stream=f'zs{sl}')
                    th3s = P.emit("scalar", lambda e, fc=fc, c0=c0, w=w: e.dma_start(
                        out=dr["cc_h3_in"].ap()[128 * fc:128 * (fc + 1), c0:c0 + w],
                        in_=h3own[fc].ap()[:, c0:c0 + w]),
                        waits=(P.tok("v"),), stream='h3s')
                return th3s

            emit_ln("ln2", ln2_src, OWN, ln2_out, rows3, sq3)
            t_h3 = P.tok("v")
            t_h3_sp = (P.tok("q_zs0"), P.tok("q_zs1"), P.tok("q_h3s"))
            P.wait("gpsimd", t_h3_sp)
            P.emit("gpsimd", lambda e: e.collective_compute(
                "AllGather", ALU.bypass,
                replica_groups=[[0, 1], [2, 3], [4, 5], [6, 7]],
                ins=[dr["cc_h3_in"].ap().opt()],
                outs=[dr["cc_h3_out"].ap().opt()]), stream="cc1", stream_k=1)
            t_ag = P.tok("q_cc1")

        with ExitStack() as phq:
            t_ph3q = tuple(t for t in P.all_toks() if t[0] != "q_cc1")
            wq1_sb = phq.enter_context(nc.sbuf_tensor("wq1_sb", [128, NFC * 1024], BF16))
            P.emit("gpsimd", lambda e: e.dma_start(
                out=wq1_sb.ap().rearrange("p (f n) -> p f n", f=NFC),
                in_=dr["Wq1"].ap().rearrange("(f p) n -> p f n", p=128)),
                waits=(t_ph3q,), stream='gw0')
            t_wq1 = P.tok("q_gw0")
            for tc, (i0, iw) in enumerate(T_CHUNKS):
                for ec in range(NFC):
                    bk = bank[0] if ec % 2 == 0 else bank[1]
                    for fc in range(NFC):
                        P.emit("tensor", lambda e, ec=ec, fc=fc, i0=i0, bk=bk: e.matmul(
                            bk.ap()[:, 0:512],
                            wq1_sb.ap()[:, 1024 * fc + 128 * ec: 1024 * fc + 128 * (ec + 1)],
                            h3own[fc].ap()[:, i0:i0 + 512],
                            start=(fc == 0), stop=(fc == NFC - 1)),
                            waits=(t_wq1, t_h3, bfree(bk)) if fc == 0 else (),
                            inc="mm" if fc == NFC - 1 else None)
                    tmm = P.tok("mm")
                    tev = P.emit("vector", lambda e, ec=ec, i0=i0, bk=bk: e.tensor_scalar(
                        qT[ec].ap()[:, i0:i0 + 512], bk.ap()[:, 0:512],
                        bq_sb.ap()[:, ec:ec + 1], None, ALU.add),
                        waits=(tmm, t_b1), inc="v")
                    bset(bk, tev)
            t_qT = P.tok("v")

        with ExitStack() as phkv:
            E4 = phkv.enter_context
            wk1_sb = E4(nc.sbuf_tensor("wk1_sb", [128, NFC * 1024], BF16))
            wv1_sb = E4(nc.sbuf_tensor("wv1_sb", [128, NFC * 1024], BF16))
            t_phqq = tuple(t for t in P.all_toks() if t[0] != "q_cc1")
            for ki_, (nm, sbw) in enumerate((("Wk1", wk1_sb), ("Wv1", wv1_sb))):
                P.emit("gpsimd", lambda e, nm=nm, sbw=sbw: e.dma_start(
                    out=sbw.ap().rearrange("p (f n) -> p f n", f=NFC),
                    in_=dr[nm].ap().rearrange("(f p) n -> p f n", p=128)),
                    waits=(t_phqq,), stream=f'gw{ki_ + 1}')
            t_w1l = (P.tok("q_gw1"), P.tok("q_gw2"))
            gring = E4(nc.sbuf_tensor("gring", [128, 2 * NFC * 512], BF16))
            kvstg = E4(nc.sbuf_tensor("kvstg", [128, 4 * 512], BF16))
            kvg = {}
            gfree = {}
            for gc in range(NT // 4):
                r = gc % 2
                rank, loc = gc // 4, (gc % 4) * 512
                tld = P.emit("sync", lambda e, rank=rank, loc=loc, r=r: e.dma_start(
                    out=gring.ap()[:, NFC * 512 * r: NFC * 512 * (r + 1)].rearrange(
                        "p (f w) -> p f w", f=NFC),
                    in_=dr["cc_h3_out"].ap()[1024 * rank:1024 * (rank + 1), loc:loc + 512].rearrange(
                        "(f p) w -> p f w", p=128)),
                    waits=(gfree.get(r), t_ag), stream=f'gl{r}')
                last = None
                for ec in range(NFC):
                    bk = bank[0] if ec % 2 == 0 else bank[1]
                    for fc in range(NFC):
                        P.emit("tensor", lambda e, ec=ec, fc=fc, r=r, bk=bk: e.matmul(
                            bk.ap()[:, 0:512],
                            wk1_sb.ap()[:, 1024 * fc + 128 * ec: 1024 * fc + 128 * (ec + 1)],
                            gring.ap()[:, NFC * 512 * r + 512 * fc: NFC * 512 * r + 512 * fc + 512],
                            start=(fc == 0), stop=(fc == NFC - 1)),
                            waits=(tld, t_w1l, bfree(bk)) if fc == 0 else (),
                            inc="mm" if fc == NFC - 1 else None)
                    tmm = P.tok("mm")
                    rr = ec % 2
                    tev = P.emit("vector", lambda e, ec=ec, rr=rr, bk=bk: e.tensor_scalar(
                        kvstg.ap()[:, 512 * rr:512 * (rr + 1)], bk.ap()[:, 0:512],
                        bk_sb.ap()[:, ec:ec + 1], None, ALU.add),
                        waits=(tmm, t_b1, kvg.get(("k", rr))), inc="v")
                    bset(bk, tev)
                    last = P.emit("scalar", lambda e, ec=ec, gc=gc, rr=rr: e.dma_start(
                        out=dr["sp_k1"].ap()[128 * ec:128 * (ec + 1), 512 * gc:512 * (gc + 1)],
                        in_=kvstg.ap()[:, 512 * rr:512 * (rr + 1)]),
                        waits=(tev,), stream=f'ks{rr}')
                    kvg[("k", rr)] = last
                for sub in range(4):
                    for en in range(2):
                        bk = bank[0] if en == 0 else bank[1]
                        for fc in range(NFC):
                            P.emit("tensor", lambda e, sub=sub, en=en, fc=fc, r=r, bk=bk: e.matmul(
                                bk.ap()[:, 0:512],
                                gring.ap()[:, NFC * 512 * r + 512 * fc + 128 * sub:
                                            NFC * 512 * r + 512 * fc + 128 * (sub + 1)],
                                wv1_sb.ap()[:, 1024 * fc + 512 * en: 1024 * fc + 512 * (en + 1)],
                                start=(fc == 0), stop=(fc == NFC - 1)),
                                waits=(tld, t_w1l, bfree(bk)) if fc == 0 else (),
                                inc="mm" if fc == NFC - 1 else None)
                        tmm = P.tok("mm")
                        rr = 2 + en
                        tev = P.emit("vector", lambda e, rr=rr, bk=bk: e.tensor_copy(
                            kvstg.ap()[:, 512 * rr:512 * (rr + 1)], bk.ap()[:, 0:512]),
                            waits=(tmm, kvg.get(("v", rr))), inc="v")
                        bset(bk, tev)
                        last = P.emit("scalar", lambda e, gc=gc, sub=sub, en=en, rr=rr: e.dma_start(
                            out=dr["sp_v1"].ap()[512 * gc + 128 * sub: 512 * gc + 128 * (sub + 1),
                                                 512 * en:512 * (en + 1)],
                            in_=kvstg.ap()[:, 512 * rr:512 * (rr + 1)]),
                            waits=(tev,), stream=f'vs{en}')
                        kvg[("v", rr)] = last
                gfree[r] = last
            t_kv_sp = (P.tok("q_ks0"), P.tok("q_ks1"), P.tok("q_vs0"), P.tok("q_vs1"))
            P.barrier()
        ph34.close()

        ph4e = ExitStack()
        E4e = ph4e.enter_context
        e_sb = E4e(nc.sbuf_tensor("e_sb", [128, E_COLS], BF16))
        ststg = E4e(nc.sbuf_tensor("ststg", [128, 128], F32))
        cmb = E4e(nc.sbuf_tensor("cmb", [128, 5 * NT], F32))

        with ExitStack() as ph4ab:
            E4 = ph4ab.enter_context
            stair_sb = E4(nc.sbuf_tensor("stair_sb", [128, 8 * 512], BF16))
            P.emit("gpsimd", lambda e: e.dma_start(
                out=stair_sb.ap().rearrange("p (s w) -> p s w", s=8),
                in_=dr["stair"].ap().rearrange("(s p) w -> p s w", p=128)), stream='gw0')
            t_stair = P.tok("q_gw0")

            kring = E4(nc.sbuf_tensor("kring", [128, 2 * NFC * 128], BF16))
            mscr = E4(nc.sbuf_tensor("mscr", [128, 6], F32))
            kfree = {}
            tacc_prev = None
            for jt in range(NT):
                r = jt % 2
                gt = _gtile(jt)
                tk = P.emit("sync", lambda e, gt=gt, r=r: e.dma_start(
                    out=kring.ap()[:, NFC * 128 * r: NFC * 128 * (r + 1)].rearrange(
                        "p (f w) -> p f w", f=NFC),
                    in_=dr["sp_k1"].ap()[:, 128 * gt:128 * (gt + 1)].rearrange(
                        "(f p) w -> p f w", p=128)),
                    waits=(kfree.get(r), t_kv_sp), stream=f'kl{r}')
                cmin = jt // 8
                sm = jt % 8
                trd = None
                for ci, C in enumerate(range(cmin, 4)):
                    i0 = 512 * C
                    bk = bank[0] if C % 2 == 0 else bank[2]
                    for fc in range(NFC):
                        P.emit("tensor", lambda e, fc=fc, r=r, i0=i0, bk=bk: e.matmul(
                            bk.ap()[:, 0:512],
                            kring.ap()[:, NFC * 128 * r + 128 * fc: NFC * 128 * r + 128 * (fc + 1)],
                            qT[fc].ap()[:, i0:i0 + 512],
                            start=(fc == 0), stop=(fc == NFC - 1)),
                            waits=(tk, t_qT, t_stair, bfree(bk)) if fc == 0 else (),
                            inc="mm" if fc == NFC - 1 else None)
                    tmm = P.tok("mm")
                    tlast = tmm
                    if C == cmin:
                        tlast = P.emit("vector", lambda e, sm=sm, bk=bk: e.tensor_tensor(
                            bk.ap()[:, 0:512], bk.ap()[:, 0:512],
                            stair_sb.ap()[:, 512 * sm:512 * (sm + 1)], ALU.add),
                            waits=(tmm,), inc="v")
                    if ci == 0:
                        trd = P.emit("vector", lambda e, jt=jt, bk=bk: e.reduce_max(
                            negm_sb.ap()[:, jt:jt + 1], bk.ap()[:, 0:512], axis=AXX,
                            negate=True), waits=(tlast,), inc="v")
                    else:
                        trd0 = P.emit("vector", lambda e, bk=bk: e.reduce_max(
                            mscr.ap()[:, 0:1], bk.ap()[:, 0:512], axis=AXX, negate=True),
                            waits=(tlast,), inc="v")
                        trd = P.emit("vector", lambda e, jt=jt: e.tensor_tensor(
                            negm_sb.ap()[:, jt:jt + 1], negm_sb.ap()[:, jt:jt + 1],
                            mscr.ap()[:, 0:1], ALU.min), waits=(trd0,), inc="v")
                    bset(bk, trd)
                for ci, C in enumerate(range(cmin, 4)):
                    i0 = 512 * C
                    bk = bank[0] if C % 2 == 0 else bank[2]
                    for fc in range(NFC):
                        P.emit("tensor", lambda e, fc=fc, r=r, i0=i0, bk=bk: e.matmul(
                            bk.ap()[:, 0:512],
                            kring.ap()[:, NFC * 128 * r + 128 * fc: NFC * 128 * r + 128 * (fc + 1)],
                            qT[fc].ap()[:, i0:i0 + 512],
                            start=(fc == 0), stop=(fc == NFC - 1)),
                            waits=(trd, bfree(bk)) if fc == 0 else (),
                            inc="mm" if fc == NFC - 1 else None)
                    tmm = P.tok("mm")
                    tlast = tmm
                    if C == cmin:
                        tlast = P.emit("vector", lambda e, sm=sm, bk=bk: e.tensor_tensor(
                            bk.ap()[:, 0:512], bk.ap()[:, 0:512],
                            stair_sb.ap()[:, 512 * sm:512 * (sm + 1)], ALU.add),
                            waits=(tmm,), inc="v")
                    eoff = e_off[jt] + ci * 512
                    if ci == 0:
                        tex = P.emit("scalar", lambda e, jt=jt, eoff=eoff, bk=bk: e.activation(
                            e_sb.ap()[:, eoff:eoff + 512], bk.ap()[:, 0:512], AF.Exp,
                            bias=negm_sb.ap()[:, jt:jt + 1], scale=1.0,
                            accum_out=sown_sb.ap()[:, jt:jt + 1]),
                            waits=(tlast, tacc_prev), inc="sc")
                        tacc = tex
                    else:
                        tex = P.emit("scalar", lambda e, jt=jt, eoff=eoff, ci=ci, bk=bk: e.activation(
                            e_sb.ap()[:, eoff:eoff + 512], bk.ap()[:, 0:512], AF.Exp,
                            bias=negm_sb.ap()[:, jt:jt + 1], scale=1.0,
                            accum_out=mscr.ap()[:, 1 + ci:2 + ci]),
                            waits=(tlast,), inc="sc")
                        tacc = P.emit("vector", lambda e, jt=jt, ci=ci: e.tensor_tensor(
                            sown_sb.ap()[:, jt:jt + 1], sown_sb.ap()[:, jt:jt + 1],
                            mscr.ap()[:, 1 + ci:2 + ci], ALU.add), waits=(tex,), inc="v")
                    bset(bk, tex)
                tacc_prev = tacc
                kfree[r] = P.tok("sc")
            t_ab = (P.tok("v"), P.tok("sc"))

            P.wait("vector", *t_ab)
            tp1 = P.emit("vector", lambda e: e.tensor_copy(
                ststg.ap()[:, 0:NT], negm_sb.ap()[:, :]), inc="v")
            tp2 = P.emit("vector", lambda e: e.tensor_copy(
                ststg.ap()[:, NT:2 * NT], sown_sb.ap()[:, :]), inc="v")
            tsp = P.emit("scalar", lambda e: e.dma_start(
                out=dr["cc_st_in"].ap()[:, :], in_=ststg.ap()[:, 0:64]),
                waits=(tp1, tp2), stream='sts')
            P.wait("gpsimd", tsp)
            tag2 = P.emit("gpsimd", lambda e: e.collective_compute(
                "AllGather", ALU.bypass,
                replica_groups=[[0, 1], [2, 3], [4, 5], [6, 7]],
                ins=[dr["cc_st_in"].ap().opt()],
                outs=[dr["cc_st_out"].ap().opt()]), stream="cc2", stream_k=1)
            tld2 = P.emit("sync", lambda e: e.dma_start(
                out=ststg.ap()[:, 0:128].rearrange("p (r c) -> p r c", r=2),
                in_=dr["cc_st_out"].ap().rearrange("(r p) c -> p r c", p=128)),
                waits=(tag2,), stream='stl')
            u1 = P.emit("vector", lambda e: e.tensor_tensor(
                cmb.ap()[:, 0:NT], ststg.ap()[:, 0:NT], ststg.ap()[:, 64:64 + NT], ALU.min),
                waits=(tld2,), inc="v")
            u2 = P.emit("vector", lambda e: e.tensor_tensor(
                cmb.ap()[:, NT:2 * NT], cmb.ap()[:, 0:NT], ststg.ap()[:, 0:NT], ALU.subtract),
                waits=(u1,), inc="v")
            u3 = P.emit("vector", lambda e: e.tensor_tensor(
                cmb.ap()[:, 2 * NT:3 * NT], cmb.ap()[:, 0:NT], ststg.ap()[:, 64:64 + NT],
                ALU.subtract), waits=(u1,), inc="v")
            u4 = P.emit("scalar", lambda e: e.activation(
                cmb.ap()[:, NT:2 * NT], cmb.ap()[:, NT:2 * NT], AF.Exp, bias=0.0, scale=1.0),
                waits=(u2,), inc="sc")
            u5 = P.emit("scalar", lambda e: e.activation(
                cmb.ap()[:, 2 * NT:3 * NT], cmb.ap()[:, 2 * NT:3 * NT], AF.Exp, bias=0.0,
                scale=1.0), waits=(u3,), inc="sc")
            u6 = P.emit("vector", lambda e: e.tensor_tensor(
                cmb.ap()[:, NT:2 * NT], cmb.ap()[:, NT:2 * NT], ststg.ap()[:, NT:2 * NT],
                ALU.mult), waits=(u4,), inc="v")
            u7 = P.emit("vector", lambda e: e.tensor_tensor(
                cmb.ap()[:, 2 * NT:3 * NT], cmb.ap()[:, 2 * NT:3 * NT],
                ststg.ap()[:, 64 + NT:64 + 2 * NT], ALU.mult), waits=(u5,), inc="v")
            u8 = P.emit("vector", lambda e: e.tensor_tensor(
                cmb.ap()[:, 3 * NT:4 * NT], cmb.ap()[:, NT:2 * NT], cmb.ap()[:, 2 * NT:3 * NT],
                ALU.add), waits=(u6, u7), inc="v")
            u9 = P.emit("vector", lambda e: e.tensor_tensor(
                cmb.ap()[:, 4 * NT:5 * NT], cmb.ap()[:, 0:NT], negm_sb.ap()[:, :], ALU.subtract),
                waits=(u8,), inc="v")
            u10 = P.emit("scalar", lambda e: e.activation(
                cmb.ap()[:, 4 * NT:5 * NT], cmb.ap()[:, 4 * NT:5 * NT], AF.Exp, bias=0.0,
                scale=1.0), waits=(u9,), inc="sc")
            u11 = P.emit("vector", lambda e: e.reciprocal(
                cmb.ap()[:, 3 * NT:4 * NT], cmb.ap()[:, 3 * NT:4 * NT]), waits=(u8,), inc="v")
            u12 = P.emit("vector", lambda e: e.tensor_tensor(
                fst_sb.ap()[:, :], cmb.ap()[:, 4 * NT:5 * NT], cmb.ap()[:, 3 * NT:4 * NT],
                ALU.mult), waits=(u10, u11), inc="v")
            u13 = P.emit("vector", lambda e: e.tensor_single_scalar(
                fst_sb.ap()[:, :], fst_sb.ap()[:, :], 1.0 / 32.0, ALU.mult),
                waits=(u12,), inc="v")
            u14 = P.emit("vector", lambda e: e.tensor_copy(fbf_sb.ap()[:, :], fst_sb.ap()[:, :]),
                         waits=(u13,), inc="v")
            t_f = u14
            P.barrier()

        with ExitStack() as ph4c:
            E4 = ph4c.enter_context
            vring = E4(nc.sbuf_tensor("vring", [128, 2 * 1024], BF16))
            zring = E4(nc.sbuf_tensor("zring", [128, NFC * 512], F32))
            rrow = E4(nc.sbuf_tensor("rrow", [1, 512], F32))
            rb_sb = E4(nc.sbuf_tensor("rb_sb", [128, 512], F32))
            c3scr = E4(nc.sbuf_tensor("c3scr", [128, 512], F32))
            x3stg = E4(nc.sbuf_tensor("x3stg", [128, 2 * 512], F32))
            vg = {}
            for C in range(4):
                i0 = 512 * C
                njt = 8 * C + 8
                tz = P.emit("sync", lambda e, i0=i0: e.dma_start(
                    out=zring.ap().rearrange("p (f w) -> p f w", f=NFC),
                    in_=dr["sp_z"].ap()[:, i0:i0 + 512].rearrange("(f p) w -> p f w", p=128)),
                    waits=(P.tok("v"),), stream='zl')
                for jx in range(njt):
                    jt = jx
                    P.emit("tensor", lambda e, jt=jt, C=C, jx=jx, njt=njt: e.matmul(
                        bank[6].ap()[0:1, 0:512], fbf_sb.ap()[:, jt:jt + 1],
                        e_sb.ap()[:, e_off[jt] + (C - jt // 8) * 512:
                                  e_off[jt] + (C - jt // 8) * 512 + 512],
                        start=(jx == 0), stop=(jx == njt - 1)),
                        waits=(t_f, bfree(bank[6])) if jx == 0 else (),
                        inc="mm" if jx == njt - 1 else None)
                trho = P.tok("mm")
                trr = P.emit("vector", lambda e: e.tensor_copy(
                    rrow.ap()[0:1, :], bank[6].ap()[0:1, 0:512]), waits=(trho,), inc="v")
                bset(bank[6], trr)
                tbc = P.emit("tensor", lambda e: e.matmul(
                    bank[7].ap()[:, 0:512], ones_row.ap()[0:1, :], rrow.ap()[0:1, :],
                    start=True, stop=True), waits=(trr, bfree(bank[7])), inc="mm")
                trb = P.emit("vector", lambda e: e.tensor_copy(
                    rb_sb.ap()[:, :], bank[7].ap()[:, 0:512]), waits=(tbc,), inc="v")
                bset(bank[7], trb)
                for half in range(2):
                    for jx in range(njt):
                        jt = jx
                        r = jt % 2
                        gt = _gtile(jt)
                        tv = P.emit("sync", lambda e, gt=gt, r=r: e.dma_start(
                            out=vring.ap()[:, 1024 * r:1024 * (r + 1)],
                            in_=dr["sp_v1"].ap()[128 * gt:128 * (gt + 1), :]),
                            waits=(vg.get(("v", r)), t_kv_sp), stream=f'vl{r}')
                        tvs = P.emit("vector", lambda e, jt=jt, r=r: e.tensor_scalar(
                            vring.ap()[:, 1024 * r:1024 * (r + 1)],
                            vring.ap()[:, 1024 * r:1024 * (r + 1)],
                            fst_sb.ap()[:, jt:jt + 1], None, ALU.mult),
                            waits=(tv, t_f, vg.get(("vt", r))), inc="v")
                        for vx in range(4):
                            vc = 4 * half + vx
                            bk = bank[2 * vx]
                            P.emit("tensor", lambda e, jt=jt, C=C, vc=vc, r=r, bk=bk, jx=jx, njt=njt: e.matmul(
                                bk.ap()[:, 0:512],
                                vring.ap()[:, 1024 * r + 128 * vc: 1024 * r + 128 * (vc + 1)],
                                e_sb.ap()[:, e_off[jt] + (C - jt // 8) * 512:
                                          e_off[jt] + (C - jt // 8) * 512 + 512],
                                start=(jx == 0), stop=(jx == njt - 1)),
                                waits=(tvs, bfree(bk)) if jx == 0 else (tvs,),
                                inc="mm" if jx == njt - 1 else None)
                        vg[("vt", r)] = P.tok("mm")
                        vg[("v", r)] = vg[("vt", r)]
                    tmm_half = P.tok("mm")
                    for vx in range(4):
                        vc = 4 * half + vx
                        bk = bank[2 * vx]
                        sl = vc % 2
                        tc1 = P.emit("vector", lambda e, vc=vc, bk=bk: e.scalar_tensor_tensor(
                            c3scr.ap()[:, 0:512], rb_sb.ap()[:, :],
                            bv_sb.ap()[:, vc:vc + 1], bk.ap()[:, 0:512], ALU.mult, ALU.add),
                            waits=(tmm_half, trb, t_b1), inc="v")
                        tc2 = P.emit("vector", lambda e, vc=vc, sl=sl: e.tensor_tensor(
                            x3stg.ap()[:, 512 * sl:512 * (sl + 1)], c3scr.ap()[:, 0:512],
                            zring.ap()[:, 512 * vc:512 * (vc + 1)], ALU.add),
                            waits=(tz, vg.get(("x3", sl))), inc="v")
                        bset(bk, tc2)
                        vg[("x3", sl)] = P.emit(
                            "scalar", lambda e, vc=vc, i0=i0, sl=sl: e.dma_start(
                                out=dr["sp_x3"].ap()[128 * vc:128 * (vc + 1), i0:i0 + 512],
                                in_=x3stg.ap()[:, 512 * sl:512 * (sl + 1)]),
                            waits=(tc2,), stream=f'x3s{sl}')
            t_x3_sp = (P.tok("q_x3s0"), P.tok("q_x3s1"))
            P.barrier()
        ph4e.close()
        ph45.close()
        ph4o.close()

        if upto == "x3":
            if "x3" in dbg:
                dout("dbg_x3", (D, OWN), F32)
                P.wait("sync", t_x3_sp)
                P.emit("sync", lambda e: e.dma_start(
                    out=dr["dbg_x3"].ap()[:, :], in_=dr["sp_x3"].ap()[:, :]),
                    stream='dbg')
            P.wait("sync", *P.all_toks())
            block = E(nc.Block())
            P.run_all(block)
            return nc

        t_wgt = wq_state[("done", 2)]
        t_wut = wq_state[("done", 3)]
        t_wdt = wq_state[("done", 4)]
        with ExitStack() as ph5:
            E5 = ph5.enter_context
            xq4 = [E5(nc.sbuf_tensor(f"xq4_{fc}", [128, OWN], BF16)) for fc in range(NFC)]
            am5 = E5(nc.sbuf_tensor("am5", [128, 4], F32))
            am5b = E5(nc.sbuf_tensor("am5b", [128, 16], F32))
            srep5 = E5(nc.sbuf_tensor("srep5", [128, 129], F32))
            s4b = E5(nc.sbuf_tensor("s4b", [128, OWN], F32))
            dgb = E5(nc.sbuf_tensor("dgb", [128, OWN], F32))
            dub = E5(nc.sbuf_tensor("dub", [128, OWN], F32))
            ph5a = ExitStack()
            E5a = ph5a.enter_context
            ring5 = E5a(nc.sbuf_tensor("ln3_ring", [128, 2 * NFC * 512], F32))
            rows5 = E5a(nc.sbuf_tensor("ln3_rows", [1, 2 * 4 * 512], F32))
            sq5 = E5a(nc.sbuf_tensor("ln3_sq", [128, NFC * 512], BF16))
            xh4r = E5a(nc.sbuf_tensor("xh4r", [128, NFC * 512], F32))
            mf5 = E5a(nc.sbuf_tensor("mf5", [128, 512], F32))
            f5scr = E5a(nc.sbuf_tensor("f5scr", [128, 512], F32))
            P.emit("gpsimd", lambda e: e.memset(am5.ap()[:, :], 1.0), inc="g")
            P.emit("gpsimd", lambda e: e.memset(am5b.ap()[:, :], 1.0), inc="g")
            t_am5i = P.tok("g")
            I32 = mybir.dt.int32
            lg5 = {"mf": None, "sr": None}

            def ln3_src(cb, c0, w, r, free_tok):
                ld = P.emit("sync", lambda e, c0=c0, w=w, r=r: e.dma_start(
                    out=mkap(ring5.ap().tensor, NFC * 512 * r,
                             [[2 * NFC * 512, 128], [512, NFC], [1, w]]),
                    in_=dr["sp_x3"].ap()[:, c0:c0 + w].rearrange("(f p) w -> p f w", p=128)),
                    waits=(free_tok, t_x3_sp), stream=f'lnl{r}')
                def tile(fc, r=r, w=w):
                    return ring5.ap()[:, NFC * 512 * r + 512 * fc: NFC * 512 * r + 512 * fc + w]
                return ld, tile

            def ln3_out(cb, c0, w, r, bA, bB, toks, tile, ld):
                P.wait("vector", *toks)
                tts = []
                for fc in range(NFC):
                    P.emit("vector", lambda e, fc=fc, w=w, bA=bA, tile=tile: e.tensor_tensor(
                        xh4r.ap()[:, 512 * fc:512 * fc + w], tile(fc), bA.ap()[:, 0:w],
                        ALU.mult), inc="v")
                    tt = P.emit("vector", lambda e, fc=fc, w=w, bB=bB: e.tensor_tensor(
                        xh4r.ap()[:, 512 * fc:512 * fc + w], xh4r.ap()[:, 512 * fc:512 * fc + w],
                        bB.ap()[:, 0:w], ALU.add), inc="v")
                    tts.append(tt)
                tf = P.emit("vector", lambda e, w=w: e.tensor_single_scalar(
                    mf5.ap()[:, 0:w].bitcast(I32), xh4r.ap()[:, 0:w].bitcast(I32),
                    0x7FFFFFFF, ALU.bitwise_and), waits=(tts[0], lg5["mf"]), inc="v")
                for fc in range(1, NFC):
                    P.emit("vector", lambda e, fc=fc, w=w: e.tensor_single_scalar(
                        f5scr.ap()[:, 0:w].bitcast(I32),
                        xh4r.ap()[:, 512 * fc:512 * fc + w].bitcast(I32),
                        0x7FFFFFFF, ALU.bitwise_and), waits=(tts[fc],), inc="v")
                    tf = P.emit("vector", lambda e, w=w: e.tensor_tensor(
                        mf5.ap()[:, 0:w], mf5.ap()[:, 0:w], f5scr.ap()[:, 0:w], ALU.max),
                        inc="v")
                trd = None
                for g in range(4):
                    ttr = P.emit("tensor", lambda e, g=g: e.transpose(
                        bank[2].ap()[:, 0:128], mf5.ap()[:, 128 * g:128 * (g + 1)],
                        ident.ap()[:, :]), waits=(tf, t_am5i, bfree(bank[2])), inc="mm")
                    trd = P.emit("vector", lambda e, g=g: e.reduce_max(
                        am5.ap()[:, g:g + 1], bank[2].ap()[:, 0:128], axis=AXX),
                        waits=(ttr,), inc="v")
                    bset(bank[2], trd)
                    lg5["mf"] = ttr
                tcl = P.emit("vector", lambda e: e.tensor_single_scalar(
                    am5.ap()[:, :], am5.ap()[:, :], 1e-5, ALU.max), waits=(trd,), inc="v")
                tcp = None
                for g in range(4):
                    gabs = c0 // 128 + g
                    trc = P.emit("vector", lambda e, g=g: e.reciprocal(
                        srep5.ap()[:, 128:129], am5.ap()[:, g:g + 1]), waits=(tcl,), inc="v")
                    trp = P.emit("vector", lambda e: e.tensor_scalar(
                        srep5.ap()[:, 0:128], ones128.ap()[:, :],
                        srep5.ap()[:, 128:129], 127.0, ALU.mult, ALU.mult),
                        waits=(trc, lg5["sr"]), inc="v")
                    ttr = P.emit("tensor", lambda e: e.transpose(
                        bank[3].ap()[:, 0:128], srep5.ap()[:, 0:128], ident.ap()[:, :]),
                        waits=(trp, bfree(bank[3])), inc="mm")
                    tcp = P.emit("vector", lambda e, gabs=gabs: e.tensor_copy(
                        s4b.ap()[:, 128 * gabs:128 * (gabs + 1)], bank[3].ap()[:, 0:128]),
                        waits=(ttr,), inc="v")
                    bset(bank[3], tcp)
                    lg5["sr"] = ttr
                    for (dst, wqi) in ((dgb, 5), (dub, 7)):
                        ta1 = P.emit("vector", lambda e, g=g, wqi=wqi: e.tensor_scalar(
                            srep5.ap()[:, 128:129], am5.ap()[:, g:g + 1],
                            thr_cols.ap()[:, wqi:wqi + 1], 1.0 / 127.0, ALU.mult, ALU.mult),
                            waits=(tcl,), inc="v")
                        tra = P.emit("vector", lambda e: e.tensor_scalar(
                            srep5.ap()[:, 0:128], ones128.ap()[:, :],
                            srep5.ap()[:, 128:129], None, ALU.mult),
                            waits=(ta1, lg5["sr"]), inc="v")
                        ttr2 = P.emit("tensor", lambda e: e.transpose(
                            bank[3].ap()[:, 0:128], srep5.ap()[:, 0:128], ident.ap()[:, :]),
                            waits=(tra, bfree(bank[3])), inc="mm")
                        tcg = P.emit("vector", lambda e, gabs=gabs, dst=dst: e.tensor_copy(
                            dst.ap()[:, 128 * gabs:128 * (gabs + 1)], bank[3].ap()[:, 0:128]),
                            waits=(ttr2,), inc="v")
                        bset(bank[3], tcg)
                        lg5["sr"] = ttr2
                last = None
                for fc in range(NFC):
                    P.emit("vector", lambda e, fc=fc, c0=c0, w=w: e.tensor_tensor(
                        xh4r.ap()[:, 512 * fc:512 * fc + w], xh4r.ap()[:, 512 * fc:512 * fc + w],
                        s4b.ap()[:, c0:c0 + w], ALU.mult), waits=(tcp,), inc="v")
                    last = P.emit("vector", lambda e, fc=fc, c0=c0, w=w: e.tensor_scalar(
                        xq4[fc].ap()[:, c0:c0 + w], xh4r.ap()[:, 512 * fc:512 * fc + w],
                        MRND, MRND, ALU.add, ALU.subtract), inc="v")
                return last

            emit_ln("ln3", ln3_src, OWN, ln3_out, rows5, sq5)
            t_xq4 = P.tok("v")
            P.barrier()
            ph5a.close()
            mp5 = E5(nc.sbuf_tensor("mp5x", [128, 512], F32))
            row5tp = E5(nc.sbuf_tensor("row5tp", [4, 128], F32))
            row5sc = E5(nc.sbuf_tensor("row5sc", [1, 3 * 512], F32))

            wgr = E5(nc.sbuf_tensor("wgr", [128, 2 * NFC * 128], BF16))
            wur = E5(nc.sbuf_tensor("wur", [128, 2 * NFC * 128], BF16))
            wdr = E5(nc.sbuf_tensor("wdr", [128, 2 * NHC * 128], BF16))
            psl = E5(nc.sbuf_tensor("psl", [128, NHC * 512], BF16))
            gscr = E5(nc.sbuf_tensor("gscr", [128, 2 * 512], F32))
            uscr = E5(nc.sbuf_tensor("uscr", [128, 2 * 512], F32))
            pabs = E5(nc.sbuf_tensor("pabs", [128, 512], F32))
            s5b = E5(nc.sbuf_tensor("s5b", [128, 512], F32))
            ztb = E5(nc.sbuf_tensor("ztb", [128, 512], F32))
            outstg = E5(nc.sbuf_tensor("outstg", [128, 2 * 512], F32))
            x3iring = E5(nc.sbuf_tensor("x3iring", [128, 2 * 512], F32))
            t_prev_wd = None
            og = {}

            for tb, (i0, iw) in enumerate(T_CHUNKS):
                wfree = {}
                pm_tok = None
                for ic in range(NHC):
                    r = ic % 2
                    twg = P.emit("sync", lambda e, ic=ic, r=r: e.dma_start(
                        out=wgr.ap()[:, NFC * 128 * r: NFC * 128 * (r + 1)].rearrange(
                            "p (f w) -> p f w", f=NFC),
                        in_=dr["sp_wgt"].ap()[:, 128 * ic:128 * (ic + 1)].rearrange(
                            "(f p) w -> p f w", p=128)),
                        waits=(wfree.get(("g", r)), t_wgt), stream=f'wgl{r}')
                    twu = P.emit("sync", lambda e, ic=ic, r=r: e.dma_start(
                        out=wur.ap()[:, NFC * 128 * r: NFC * 128 * (r + 1)].rearrange(
                            "p (f w) -> p f w", f=NFC),
                        in_=dr["sp_wut"].ap()[:, 128 * ic:128 * (ic + 1)].rearrange(
                            "(f p) w -> p f w", p=128)),
                        waits=(wfree.get(("u", r)), t_wut), stream=f'wul{r}')
                    bg, bu = bank[0], bank[1]
                    for fc in range(NFC):
                        P.emit("tensor", lambda e, fc=fc, r=r, i0=i0, bg=bg: e.matmul(
                            bg.ap()[:, 0:512],
                            wgr.ap()[:, NFC * 128 * r + 128 * fc: NFC * 128 * r + 128 * (fc + 1)],
                            xq4[fc].ap()[:, i0:i0 + 512],
                            start=(fc == 0), stop=(fc == NFC - 1)),
                            waits=(twg, t_xq4, bfree(bg)) if fc == 0 else (),
                            inc="mm" if fc == NFC - 1 else None)
                    tg = P.tok("mm")
                    for fc in range(NFC):
                        P.emit("tensor", lambda e, fc=fc, r=r, i0=i0, bu=bu: e.matmul(
                            bu.ap()[:, 0:512],
                            wur.ap()[:, NFC * 128 * r + 128 * fc: NFC * 128 * r + 128 * (fc + 1)],
                            xq4[fc].ap()[:, i0:i0 + 512],
                            start=(fc == 0), stop=(fc == NFC - 1)),
                            waits=(twu, bfree(bu)) if fc == 0 else (),
                            inc="mm" if fc == NFC - 1 else None)
                    tu = P.tok("mm")
                    wfree[("g", r)] = tg
                    wfree[("u", r)] = tu
                    rr = ic % 2
                    tg1 = P.emit("vector", lambda e, rr=rr, i0=i0, bg=bg: e.tensor_tensor(
                        gscr.ap()[:, 512 * rr:512 * (rr + 1)], bg.ap()[:, 0:512],
                        dgb.ap()[:, i0:i0 + 512], ALU.mult),
                        waits=(tg, og.get(("gs", rr)), t_prev_wd), inc="v")
                    tsi = P.emit("scalar", lambda e, rr=rr: e.activation(
                        gscr.ap()[:, 512 * rr:512 * (rr + 1)], gscr.ap()[:, 512 * rr:512 * (rr + 1)],
                        AF.Silu, bias=0.0, scale=1.0), waits=(tg1,), inc="sc")
                    tu1 = P.emit("vector", lambda e, rr=rr, i0=i0, bu=bu: e.tensor_tensor(
                        uscr.ap()[:, 512 * rr:512 * (rr + 1)], bu.ap()[:, 0:512],
                        dub.ap()[:, i0:i0 + 512], ALU.mult),
                        waits=(tu, og.get(("us", rr))), inc="v")
                    bset(bg, tg1); bset(bu, tu1)
                    tp = P.emit("vector", lambda e, ic=ic, rr=rr: e.tensor_tensor(
                        psl.ap()[:, 512 * ic:512 * (ic + 1)], gscr.ap()[:, 512 * rr:512 * (rr + 1)],
                        uscr.ap()[:, 512 * rr:512 * (rr + 1)], ALU.mult),
                        waits=(tsi, tu1), inc="v")
                    og[("gs", rr)] = tp
                    og[("us", rr)] = tp
                    tya = P.emit("scalar", lambda e, ic=ic: e.activation(
                        pabs.ap()[:, :], psl.ap()[:, 512 * ic:512 * (ic + 1)],
                        AF.Abs, bias=0.0, scale=1.0), waits=(tp, pm_tok), inc="sc")
                    if ic == 0:
                        pm_tok = P.emit("vector", lambda e: e.tensor_copy(
                            mp5.ap()[:, :], pabs.ap()[:, :]),
                            waits=(tya, lg5.get("mp")), inc="v")
                    else:
                        pm_tok = P.emit("vector", lambda e: e.tensor_tensor(
                            mp5.ap()[:, :], mp5.ap()[:, :], pabs.ap()[:, :], ALU.max),
                            waits=(tya, pm_tok), inc="v")
                trd = None
                for g in range(4):
                    bkt = bank[2] if g % 2 == 0 else bank[3]
                    ttr = P.emit("tensor", lambda e, g=g, bkt=bkt: e.transpose(
                        bkt.ap()[:, 0:128], mp5.ap()[:, 128 * g:128 * (g + 1)],
                        ident.ap()[:, :]), waits=(pm_tok, t_am5i, bfree(bkt)), inc="mm")
                    trd = P.emit("vector", lambda e, g=g, bkt=bkt: e.reduce_max(
                        am5b.ap()[:, g:g + 1], bkt.ap()[:, 0:128], axis=AXX),
                        waits=(ttr,), inc="v")
                    bset(bkt, trd)
                    lg5["mp"] = ttr
                tcl = P.emit("vector", lambda e: e.tensor_single_scalar(
                    am5b.ap()[:, 0:4], am5b.ap()[:, 0:4], 1e-5, ALU.max), waits=(trd,), inc="v")
                ttra = P.emit("tensor", lambda e: e.transpose(
                    bank[2].ap()[0:4, 0:128], am5b.ap()[:, 0:4], ident.ap()[:, :]),
                    waits=(tcl, bfree(bank[2])), inc="mm")
                tcpr = P.emit("vector", lambda e: e.tensor_copy(
                    row5tp.ap()[:, :], bank[2].ap()[0:4, 0:128]), waits=(ttra,), inc="v")
                bset(bank[2], tcpr)
                tdm5 = P.emit("scalar", lambda e: e.dma_start(
                    out=row5sc.ap()[0:1, 0:512], in_=row5tp.ap()[:, :]),
                    waits=(tcpr, og.get(("r5",))), stream='sm4')
                trec = P.emit("vector", lambda e: e.reciprocal(
                    row5sc.ap()[0:1, 512:1024], row5sc.ap()[0:1, 0:512]),
                    waits=(tdm5,), inc="v")
                ts127 = P.emit("vector", lambda e: e.tensor_single_scalar(
                    row5sc.ap()[0:1, 512:1024], row5sc.ap()[0:1, 512:1024],
                    127.0, ALU.mult), waits=(trec,), inc="v")
                tgam = P.emit("vector", lambda e: e.tensor_scalar(
                    row5sc.ap()[0:1, 1024:1536], row5sc.ap()[0:1, 0:512],
                    thr_cols.ap()[0:1, 9:10], 1.0 / 127.0, ALU.mult, ALU.mult),
                    waits=(tdm5,), inc="v")

                tb1 = P.emit("tensor", lambda e: e.matmul(
                    bank[2].ap()[:, 0:512], ones_row.ap()[0:1, :],
                    row5sc.ap()[0:1, 512:1024],
                    start=True, stop=True), waits=(ts127, bfree(bank[2])), inc="mm")
                tcp = P.emit("vector", lambda e: e.tensor_copy(
                    s5b.ap()[:, :], bank[2].ap()[:, 0:512]),
                    waits=(tb1, og.get(("s5",)), og.get(("s5g",))), inc="v")
                tb2 = P.emit("tensor", lambda e: e.matmul(
                    bank[3].ap()[:, 0:512], ones_row.ap()[0:1, :],
                    row5sc.ap()[0:1, 1024:1536],
                    start=True, stop=True), waits=(tgam, bfree(bank[3])), inc="mm")
                tcz_last = P.emit("vector", lambda e: e.tensor_copy(
                    ztb.ap()[:, :], bank[3].ap()[:, 0:512]),
                    waits=(tb2,), inc="v")
                bset(bank[2], tcp); bset(bank[3], tcz_last)
                og[("r5",)] = tcz_last
                t_s5 = P.tok("v")
                for ic in range(NHC):
                    rr = ic % 2
                    P.emit("vector", lambda e, ic=ic, rr=rr: e.tensor_tensor(
                        gscr.ap()[:, 512 * rr:512 * (rr + 1)], psl.ap()[:, 512 * ic:512 * (ic + 1)],
                        s5b.ap()[:, :], ALU.mult), waits=(t_s5, pm_tok), inc="v")
                    P.emit("vector", lambda e, ic=ic, rr=rr: e.tensor_scalar(
                        psl.ap()[:, 512 * ic:512 * (ic + 1)], gscr.ap()[:, 512 * rr:512 * (rr + 1)],
                        MRND, MRND, ALU.add, ALU.subtract), inc="v")
                t_pq = (P.tok("v"), P.tok("g"))
                og[("s5",)] = t_pq[0]
                og[("s5g",)] = t_pq[1]
                og[("gs", 0)] = t_pq[0]
                og[("gs", 1)] = t_pq[1]
                og[("us", 0)] = t_pq[0]
                og[("us", 1)] = t_pq[1]
                wdfree = {}
                for dc in range(NFC):
                    r = dc % 2
                    twd = P.emit("sync", lambda e, dc=dc, r=r: e.dma_start(
                        out=wdr.ap()[:, NHC * 128 * r: NHC * 128 * (r + 1)].rearrange(
                            "p (h w) -> p h w", h=NHC),
                        in_=dr["sp_wdt"].ap()[:, 128 * dc:128 * (dc + 1)].rearrange(
                            "(h p) w -> p h w", p=128)),
                        waits=(wdfree.get(r), t_wdt), stream=f'wdl{r}')
                    bk = bank[0] if dc % 2 == 0 else bank[1]
                    for ic in range(NHC):
                        P.emit("tensor", lambda e, ic=ic, dc=dc, r=r, bk=bk: e.matmul(
                            bk.ap()[:, 0:512],
                            wdr.ap()[:, NHC * 128 * r + 128 * ic: NHC * 128 * r + 128 * (ic + 1)],
                            psl.ap()[:, 512 * ic:512 * (ic + 1)],
                            start=(ic == 0), stop=(ic == NHC - 1)),
                            waits=((twd, bfree(bk)) + t_pq) if ic == 0 else (),
                            inc="mm" if ic == NHC - 1 else None)
                    tmm = P.tok("mm")
                    wdfree[r] = tmm
                    t_prev_wd = tmm
                    rr = dc % 2
                    tx3 = P.emit("scalar", lambda e, dc=dc, i0=i0, rr=rr: e.dma_start(
                        out=x3iring.ap()[:, 512 * rr:512 * (rr + 1)],
                        in_=dr["sp_x3"].ap()[128 * dc:128 * (dc + 1), i0:i0 + 512]),
                        waits=(og.get(("x3", rr)),), stream=f'x3i{rr}')
                    to1 = P.emit("vector", lambda e, rr=rr, bk=bk: e.tensor_tensor(
                        outstg.ap()[:, 512 * rr:512 * (rr + 1)], bk.ap()[:, 0:512],
                        ztb.ap()[:, :], ALU.mult),
                        waits=(tmm, tcz_last, og.get(("os", rr))), inc="v")
                    to2 = P.emit("vector", lambda e, rr=rr: e.tensor_tensor(
                        outstg.ap()[:, 512 * rr:512 * (rr + 1)],
                        outstg.ap()[:, 512 * rr:512 * (rr + 1)],
                        x3iring.ap()[:, 512 * rr:512 * (rr + 1)], ALU.add),
                        waits=(tx3,), inc="v")
                    og[("x3", rr)] = to2
                    bset(bk, to2)
                    og[("os", rr)] = P.emit("scalar", lambda e, dc=dc, i0=i0, rr=rr: e.dma_start(
                        out=dr["out_shard"].ap()[128 * dc:128 * (dc + 1), i0:i0 + 512],
                        in_=outstg.ap()[:, 512 * rr:512 * (rr + 1)]),
                        waits=(to2,), stream=f'os{rr}')

        P.wait("sync", *P.all_toks())
        block = E(nc.Block())
        P.run_all(block)
    return nc


def kernel(**inputs):
    maps = prep_core_inputs(inputs)
    nc = build_nc()
    res = run_bass_kernel_spmd(nc, maps, core_ids=list(range(NCORES)))
    return assemble_output(res.results)
